# version 04
# baseline (speedup 1.0000x reference)
import numpy as np
import ml_dtypes
import concourse.bass as bass
import concourse.mybir as mybir
from concourse.bass_utils import run_bass_kernel_spmd

F32 = mybir.dt.float32
BF16 = mybir.dt.bfloat16
U32 = mybir.dt.uint32
U8 = mybir.dt.uint8
AF = mybir.ActivationFunctionType
ALU = mybir.AluOpType
AX = mybir.AxisListType

B, S, D, H, DH = 2, 4096, 2048, 16, 128
HPC = 4
NT = 1024
DFF, E, RK = 5632, 8, 16
KC = D // 128
FC = DFF // 128
TBLK = 512
NTB = S // TBLK
SBLK = 256
EPS_LN = 1e-5
EPS_RMS = 1e-6

DEBUG_TAPS = ()
STOP_AFTER = None


EVENTS = {}


class MK:
    def __init__(self, eng_name, handles, sems):
        self.me = eng_name
        self.h = handles
        self.sems = sems
        self.cnt = {k: 0 for k in sems}
        self.ev = []
        EVENTS[eng_name] = self.ev

    def emit(self, eng, fn, inc=None):
        if self.me == eng:
            ins = fn(self.h[eng])
            if inc is not None and ins is not None:
                ins.then_inc(self.sems[inc[0]], inc[1])
            if inc is not None:
                self.ev.append(("inc", inc[0], inc[1]))
        if inc is not None:
            self.cnt[inc[0]] += inc[1]

    def wait(self, eng, sem, val=None):
        v = self.cnt[sem] if val is None else val
        if self.me == eng and v > 0:
            self.h[eng].wait_ge(self.sems[sem], v)
            self.ev.append(("wait", sem, v))

    def drain(self, eng):
        if self.me == eng:
            self.h[eng].drain()


def build_program(taps=()):
    nc = bass.Bass("TRN2", target_bir_lowering=False)
    d = {}

    def din(name, shape, dt=F32):
        d[name] = nc.dram_tensor(name, shape, dt, kind="ExternalInput").ap()

    def dint(name, shape, dt=F32):
        d[name] = nc.dram_tensor(name, shape, dt).ap()

    din("x_b", [S, D]);  din("x_m", [NT, D]);  din("xT_m", [D, NT])
    din("c2", [128, KC, B]);   din("wmod_sl", [128, KC, 1536]);  din("modidx", [8, 1], U32)
    din("wqkv_T", [128, 12, KC, 128]);  din("slopes", [1, HPC])
    din("wg_T", [KC, 128, KC, 128]);    din("wo_T", [KC, 128, KC, 128])
    din("nw", [128, KC]);         din("aidx", [128, KC], U32)
    din("Wg_T", [FC, 128, KC, 128]);    din("Wu_T", [FC, 128, KC, 128]);  din("Wd_T", [KC, 128, FC, 128])
    din("wr_T", [128, KC, E])
    din("agp1", [128, KC, 128]); din("agp2", [128, KC, 128]); din("aup1", [128, KC, 128]); din("aup2", [128, KC, 128])
    din("bgp", [FC, 32, E, 128]);  din("bup", [FC, 32, E, 128])
    din("adS", [128, FC, E, 16]);  din("bdS", [128, 2, D]);  din("pm2", [128, 1])
    d["y"] = nc.dram_tensor("y", [D, NT], F32, kind="ExternalOutput").ap()

    dint("xnb_park", [S, D], BF16)
    dint("xnm_park", [NT, D], BF16)
    dint("xp_park", [D, NT])
    dint("mod_gi", [B, 1536]);  dint("mod_go", [16, 1536])
    dint("att_gi", [HPC * DH, S], BF16);  dint("att_go", [D, S], BF16)

    tapd = {}
    for tname, shape, dt in taps:
        tapd[tname] = nc.dram_tensor("tap_" + tname, shape, dt, kind="ExternalOutput").ap()

    guards = []
    arena_cm = nc.sbuf_tensor("arena", [128, 212800], U8)
    arena_cm.__enter__(); guards.append(arena_cm)
    BASE = 16512
    RA = BASE + 0
    RB = BASE + 49152
    RC = BASE + 81920
    RD = BASE + 116736
    RE = BASE + 141312
    RF = BASE + 157696

    T = {}

    def at(name, shape, dt, off):
        T[name] = nc.alloc_sbuf_tensor_at(name, shape, dt, offset=off)

    at("wmod_sb", [128, KC, 1536], BF16, RA)
    at("g8", [8, 1536], F32, RA)
    at("wqkv_sb", [128, 12, KC, 128], BF16, RA)
    at("attnz", [128, KC, NT], BF16, RA)
    at("rstat", [4, 1024], F32, RA + 40960)
    at("lrow", [1, 2048], F32, RA + 40960)
    at("rl8", [8, NT], F32, RA)
    at("wef", [8, NT], BF16, RA + 8192)
    at("hbase", [128, FC, TBLK], BF16, RA)

    at("gate_h2", [128, KC, NT], BF16, RB)
    at("modrows", [2, 1536], F32, RB)
    at("kv", [128, HPC, 128], F32, RB + 8192)
    at("kvb", [128, HPC, 128], BF16, RB + 10240)
    at("dtt", [128, HPC, 2, SBLK], BF16, RB + 11264)
    at("qdec", [128, HPC, SBLK], BF16, RB + 15360)
    at("kdec", [128, HPC, 2], F32, RB + 17408)

    at("qkvblk", [128, 2, 12, TBLK], BF16, RC)
    at("scanbuf", [128, HPC, 5, SBLK], BF16, RC + 24576)
    at("h1m", [128, KC, NT], BF16, RC)
    at("agp_sb", [128, 4, KC, 128], BF16, RC)
    at("wr_sb", [128, KC, E], BF16, RC + 16384)
    at("xagP", [128, 2, NT], BF16, RC + 16672)
    at("xauP", [128, 2, NT], BF16, RC + 20768)
    at("hexp", [128, 2, TBLK], BF16, RC + 24864)
    at("sgb", [128, 2, TBLK], BF16, RC + 26912)
    at("sgbb", [128, TBLK], BF16, RC + 28960)
    at("tadd", [128, 2, TBLK], BF16, RC + 29984)
    at("usb", [128, 2, TBLK], BF16, RC + 32032)
    at("wexpE", [128, NT], BF16, RA + 47104)
    at("gbb", [128, TBLK], BF16, RC)
    at("hAwE", [128, TBLK], BF16, RA + 45056)
    at("hAwO", [128, TBLK], BF16, RA + 46080)

    at("xstage", [128, 3, D], F32, RD)
    at("wcolA", [128, 2, KC, 128], BF16, RD)
    at("wcolB", [128, 2, KC, 128], BF16, RD + 8192)
    at("bgp_sb", [32, 2, E, 128], BF16, RD + 16384)
    at("bup_sb", [32, 2, E, 128], BF16, RD + 20480)
    at("wdcol", [128, 2, FC, 128], BF16, RD)

    at("hblk", [128, 2, KC, TBLK], BF16, RE)
    at("xat", [32, 2 * E, TBLK], BF16, RE)

    off = RF
    def fx(name, shape, dt):
        nonlocal off
        n = int(np.prod(shape[1:]))
        sz = n * (4 if dt == F32 else (2 if dt in (BF16,) else 4))
        at(name, shape, dt, off)
        off += (sz + 31) // 32 * 32

    fx("resid", [128, 2, NT], F32)
    fx("smallb", [128, 3072], BF16)
    fx("small", [128, 784], F32)
    fx("rbc", [128, NT], BF16)
    fx("adS_sb", [128, FC, E, 16], BF16)
    fx("bdS_sb", [128, 2, D], BF16)
    fx("c2_sb", [128, KC, B], F32)
    fx("c2sil", [128, KC, B], BF16)
    fx("modc", [128, 96], F32)
    fx("modd", [128, 96], F32)
    fx("stats", [128, 64], F32)
    fx("statm", [128, 16], F32)
    fx("stat6", [128, 48], F32)
    fx("isc", [128, 64], F32)
    fx("iscm", [128, 16], F32)
    fx("wexpO", [128, NT], BF16)
    fx("ubb", [128, TBLK], BF16)
    fx("consts", [128, 8], F32)
    fx("nw_sb", [128, KC], F32)
    fx("rt", [128, 8, E], F32)
    fx("rsc", [128, 8, E], F32)
    fx("rcol", [128, 8, 4], F32)
    fx("ident", [128, 128], BF16)
    fx("identf", [128, 128], F32)
    fx("onesb", [1, 128], BF16)
    fx("onecol", [128, 1], BF16)
    fx("slop", [1, 8], F32)
    fx("pm2c", [128, 1], F32)
    fx("slopc", [128, 8], F32)
    fx("idx_m", [8, 1], U32)
    fx("idx_a", [128, KC], U32)
    fx("exp16", [8, 2, 128], BF16)
    assert off <= 229344, off

    sm = T["small"]

    psl = []
    for i in range(8):
        cm = nc.psum_tensor("psb%d" % i, [128, 512], F32)
        psl.append(cm.__enter__()); guards.append(cm)

    semnames = ["Wld", "Xld", "Park", "Trp", "Bnc", "Gth", "CC", "Tmm", "Sev",
                "Vev", "Gev", "Out", "Hfree", "Qfree", "PkA", "PkB", "WlA", "WlB",
                "WdA", "WdB", "XsA", "XsB", "XlA", "XlB", "TrA", "TrB", "Wq", "Wst"]
    sems = {}
    for name in semnames:
        cm = nc.semaphore(name)
        sems[name] = cm.__enter__(); guards.append(cm)

    def walk(mk: MK):
        emit, wait, drain = mk.emit, mk.wait, mk.drain
        P = psl
        pstr = psl[6][:, :].bitcast(BF16)[:, 0:512]

        def tap(name, src_fn, after=None):
            if name in tapd:
                if after:
                    wait("sync", after[0], after[1])
                emit("sync", lambda e: e.dma_start(out=tapd[name], in_=src_fn()), inc=("Out", 16))

        emit("gpsimd", lambda g: g.memset(T["consts"][:, 0:1], EPS_LN))
        emit("gpsimd", lambda g: g.memset(T["consts"][:, 1:2], -0.5))
        emit("gpsimd", lambda g: g.memset(T["consts"][:, 2:3], EPS_RMS))
        emit("gpsimd", lambda g: g.memset(T["consts"][:, 3:4], 1.0 / 2048.0))
        emit("gpsimd", lambda g: g.memset(T["onesb"][:, :], 1.0))
        emit("gpsimd", lambda g: g.memset(T["onecol"][:, :], 1.0))
        emit("gpsimd", lambda g: g.memset(T["ident"][:, :], 1.0))
        drain("gpsimd")
        emit("gpsimd", lambda g: g.affine_select(
            T["ident"][:, :], T["ident"][:, :], compare_op=ALU.is_equal, fill=0.0,
            pattern=[[1, 128]], base=0, channel_multiplier=-1))
        emit("gpsimd", lambda g: g.memset(T["identf"][:, :], 1.0))
        drain("gpsimd")
        emit("gpsimd", lambda g: g.affine_select(
            T["identf"][:, :], T["identf"][:, :], compare_op=ALU.is_equal, fill=0.0,
            pattern=[[1, 128]], base=0, channel_multiplier=-1))
        emit("gpsimd", lambda g: g.memset(T["exp16"][:, :, :], 2.0))
        drain("gpsimd")
        emit("gpsimd", lambda g: g.affine_select(
            T["exp16"][:, 0, :], T["exp16"][:, 0, :], compare_op=ALU.is_equal, fill=0.0,
            pattern=[[2, 4], [0, 32]], base=0, channel_multiplier=-1))
        drain("gpsimd")
        emit("gpsimd", lambda g: g.affine_select(
            T["exp16"][:, 1, :], T["exp16"][:, 1, :], compare_op=ALU.is_equal, fill=0.0,
            pattern=[[2, 4], [0, 32]], base=1, channel_multiplier=-1))
        drain("gpsimd")
        emit("gpsimd", lambda g: g.nop(), inc=("Gev", 1))

        emit("sync", lambda e: e.dma_start(out=T["slop"][:, 0:HPC], in_=d["slopes"]), inc=("Xld", 16))
        emit("sync", lambda e: e.dma_start(out=T["pm2c"][:, :], in_=d["pm2"]), inc=("Xld", 16))
        emit("sync", lambda e: e.dma_start(out=T["idx_m"][:, :], in_=d["modidx"]), inc=("Xld", 16))
        emit("sync", lambda e: e.dma_start(out=T["idx_a"][:, :], in_=d["aidx"]), inc=("Xld", 16))
        emit("sync", lambda e: e.dma_start(out=T["nw_sb"][:, :], in_=d["nw"]), inc=("Xld", 16))
        emit("sync", lambda e: e.dma_start(
            out=T["c2_sb"][:, :, :], in_=d["c2"]), inc=("Xld", 16))
        XLD0 = mk.cnt["Xld"]
        emit("gpsimd", lambda g: g.dma_start(
            out=T["wmod_sb"][:, :, :], in_=d["wmod_sl"]),
            inc=("Wst", 16))
        WMOD_W = mk.cnt["Wst"]

        wait("scalar", "Xld", XLD0)
        emit("scalar", lambda s: s.activation(T["c2sil"][:, :, :], T["c2_sb"][:, :, :], AF.Silu),
             inc=("Sev", 1))
        C2S_S = mk.cnt["Sev"]
        wait("tensor", "Sev", C2S_S)
        wait("tensor", "Wst", WMOD_W)
        for n0 in range(3):
            def modmm(e, n0=n0):
                for k in range(KC):
                    last = e.matmul(P[n0][0:B, :], T["c2sil"][:, k, :],
                             T["wmod_sb"][:, k, n0 * 512:(n0 + 1) * 512],
                             start=(k == 0), stop=(k == KC - 1))
                return last
            emit("tensor", modmm, inc=("Tmm", 1))
        wait("vector", "Tmm", mk.cnt["Tmm"])
        for n0 in range(3):
            emit("vector", lambda v, n0=n0: v.tensor_copy(
                T["modrows"][0:B, n0 * 512:(n0 + 1) * 512], P[n0][0:B, :]))
        drain("vector")
        emit("vector", lambda v: v.nop(), inc=("Vev", 1))
        wait("sync", "Vev", mk.cnt["Vev"])
        emit("sync", lambda e: e.dma_start(out=d["mod_gi"], in_=T["modrows"][0:B, 0:1536]), inc=("Park", 16))
        MODGI_P = mk.cnt["Park"]
        wait("gpsimd", "Park", MODGI_P)
        emit("gpsimd", lambda g: g.collective_compute(
            "AllGather", ALU.bypass, replica_groups=[list(range(8))],
            ins=[d["mod_gi"].opt()], outs=[d["mod_go"].opt()]), inc=("CC", 1))
        wait("gpsimd", "CC", 1)
        wait("gpsimd", "Xld", XLD0)
        emit("gpsimd", lambda g: g.indirect_dma_start(
            out=T["g8"][:, :], out_offset=None, in_=d["mod_go"],
            in_offset=bass.IndirectOffsetOnAxis(ap=T["idx_m"][:, :], axis=0)), inc=("Gth", 16))
        MODG_G = mk.cnt["Gth"]
        wait("tensor", "Gth", MODG_G)
        for j in range(12):
            emit("tensor", lambda e, j=j: e.transpose(
                P[1][:, j * 8:(j + 1) * 8], T["g8"][:, j * 128:(j + 1) * 128],
                T["identf"][0:8, 0:8]), inc=("Tmm", 1))
        MODTR_T = mk.cnt["Tmm"]
        wait("vector", "Tmm", MODTR_T)
        for j in range(12):
            emit("vector", lambda v, j=j: v.tensor_copy(
                T["modc"][:, j:96:12], P[1][:, j * 8:(j + 1) * 8]))
        drain("vector")
        emit("vector", lambda v: v.tensor_copy(T["modd"][:, 0:16], T["modc"][:, 0:16]))
        emit("vector", lambda v: v.tensor_scalar(T["modd"][:, 16:32], T["modc"][:, 16:32],
                                                 1.0, None, ALU.add))
        emit("vector", lambda v: v.tensor_copy(T["modd"][:, 48:64], T["modc"][:, 48:64]))
        emit("vector", lambda v: v.tensor_scalar(T["modd"][:, 64:80], T["modc"][:, 64:80],
                                                 1.0, None, ALU.add))
        drain("vector")
        emit("vector", lambda v: v.nop(), inc=("Vev", 1))
        MODC_V = mk.cnt["Vev"]
        wait("scalar", "Vev", MODC_V)
        emit("scalar", lambda s: s.sign(T["isc"][:, 0:16], T["modc"][:, 32:48]))
        emit("scalar", lambda s: s.sign(T["isc"][:, 16:32], T["modc"][:, 80:96]))
        drain("scalar")
        emit("scalar", lambda s: s.nop(), inc=("Sev", 1))
        wait("vector", "Sev", mk.cnt["Sev"])
        emit("vector", lambda v: v.tensor_tensor(T["isc"][:, 32:48], T["modc"][:, 32:48],
                                                 T["isc"][:, 0:16], ALU.mult))
        emit("vector", lambda v: v.tensor_tensor(T["isc"][:, 48:64], T["modc"][:, 80:96],
                                                 T["isc"][:, 16:32], ALU.mult))
        drain("vector")
        emit("vector", lambda v: v.tensor_scalar(T["isc"][:, 32:48], T["isc"][:, 32:48],
                                                 0.25, None, ALU.max))
        emit("vector", lambda v: v.tensor_scalar(T["isc"][:, 48:64], T["isc"][:, 48:64],
                                                 0.25, None, ALU.max))
        drain("vector")
        emit("vector", lambda v: v.tensor_tensor(T["modd"][:, 32:48], T["isc"][:, 32:48],
                                                 T["isc"][:, 0:16], ALU.mult))
        emit("vector", lambda v: v.tensor_tensor(T["modd"][:, 80:96], T["isc"][:, 48:64],
                                                 T["isc"][:, 16:32], ALU.mult))
        drain("vector")
        emit("vector", lambda v: v.nop(), inc=("Vev", 1))
        MODD_V = mk.cnt["Vev"]
        tap("modd", lambda: T["modd"][:, :], after=("Vev" if mk.me == "sync" else "Vev", MODD_V))

        if STOP_AFTER == "P1":
            return
        wait("vector", "Xld", XLD0)
        emit("vector", lambda v: v.tensor_scalar(T["slop"][:, 4:8], T["slop"][:, 0:4],
                                                 -1.0, None, ALU.mult))
        drain("vector")
        emit("vector", lambda v: v.nop(), inc=("Vev", 1))
        NEGS_V = mk.cnt["Vev"]
        wait("scalar", "Vev", NEGS_V)
        emit("scalar", lambda s: s.copy(T["smallb"][0:1, 0:8], T["slop"][0:1, 0:8]))
        drain("scalar")
        emit("scalar", lambda s: s.nop(), inc=("Sev", 1))
        wait("tensor", "Sev", mk.cnt["Sev"])
        wait("tensor", "Gev", 1)
        emit("tensor", lambda e: e.matmul(P[3][:, 0:8], T["onesb"][0:1, :],
                                          T["smallb"][0:1, 0:8], start=True, stop=True),
             inc=("Tmm", 1))
        wait("vector", "Tmm", mk.cnt["Tmm"])
        emit("vector", lambda v: v.tensor_copy(T["slopc"][:, :], P[3][:, 0:8]))
        drain("vector")
        emit("vector", lambda v: v.nop(), inc=("Vev", 1))
        SLOPC_V = mk.cnt["Vev"]
        if STOP_AFTER == "P2a":
            return
        wait("gpsimd", "Park", MODGI_P)
        emit("gpsimd", lambda g: g.iota(sm[0:1, 0:SBLK], pattern=[[1, SBLK]], base=1,
                                        channel_multiplier=0, allow_small_or_imprecise_dtypes=True))
        emit("gpsimd", lambda g: g.iota(sm[:, SBLK:SBLK + 2], pattern=[[-128, 2]], base=255,
                                        channel_multiplier=-1, allow_small_or_imprecise_dtypes=True))
        drain("gpsimd")
        emit("gpsimd", lambda g: g.nop(), inc=("Gev", 1))
        IOTA1_G = mk.cnt["Gev"]
        wait("scalar", "Gev", IOTA1_G)
        wait("scalar", "Vev", SLOPC_V)
        for hh in range(HPC):
            emit("scalar", lambda s, hh=hh: s.activation(
                T["smallb"][0:1, hh * SBLK:(hh + 1) * SBLK], sm[0:1, 0:SBLK],
                AF.Exp, bias=0.0, scale=T["slop"][0:1, 4 + hh:5 + hh]))
            emit("scalar", lambda s, hh=hh: s.activation(
                T["kdec"][:, hh, :], sm[:, SBLK:SBLK + 2],
                AF.Exp, bias=0.0, scale=T["slopc"][:, 4 + hh:5 + hh]))
        drain("scalar")
        emit("scalar", lambda s: s.nop(), inc=("Sev", 1))
        QDR_S = mk.cnt["Sev"]
        if STOP_AFTER == "P2b":
            return
        wait("tensor", "Sev", QDR_S)
        wait("tensor", "Gev", 1)
        QDEC_BASE_V = mk.cnt["Vev"]
        for hh in range(HPC):
            if hh >= 1:
                wait("tensor", "Vev", QDEC_BASE_V + hh)
            emit("tensor", lambda e, hh=hh: e.matmul(
                P[2][:, (hh % 2) * SBLK:(hh % 2 + 1) * SBLK], T["onesb"][0:1, :],
                T["smallb"][0:1, hh * SBLK:(hh + 1) * SBLK], start=True, stop=True),
                inc=("Tmm", 1))
            wait("vector", "Tmm", mk.cnt["Tmm"])
            emit("vector", lambda v, hh=hh: v.tensor_copy(
                T["qdec"][:, hh, :], P[2][:, (hh % 2) * SBLK:(hh % 2 + 1) * SBLK]))
            drain("vector")
            emit("vector", lambda v: v.nop(), inc=("Vev", 1))
        QDEC_V = mk.cnt["Vev"]
        if STOP_AFTER == "P2c":
            return
        for half in range(2):
            wait("gpsimd", "Sev", mk.cnt["Sev"])
            emit("gpsimd", lambda g, half=half: g.iota(
                sm[:, 0:SBLK], pattern=[[1, SBLK]], base=-128 * half,
                channel_multiplier=-1, allow_small_or_imprecise_dtypes=True))
            drain("gpsimd")
            emit("gpsimd", lambda g: g.nop(), inc=("Gev", 1))
            iota_g = mk.cnt["Gev"]
            for hh in range(HPC):
                wait("scalar", "Gev", iota_g)
                wait("scalar", "Gev", mk.cnt["Gev"])
                emit("scalar", lambda s, hh=hh: s.activation(
                    sm[:, 512:512 + SBLK], sm[:, 0:SBLK], AF.Exp, bias=0.0,
                    scale=T["slopc"][:, 4 + hh:5 + hh]))
                drain("scalar")
                emit("scalar", lambda s: s.nop(), inc=("Sev", 1))
                wait("gpsimd", "Sev", mk.cnt["Sev"])
                emit("gpsimd", lambda g, hh=hh, half=half: g.affine_select(
                    T["dtt"][:, hh, half, :], sm[:, 512:512 + SBLK],
                    compare_op=ALU.is_ge, fill=0.0, pattern=[[1, SBLK]],
                    base=-128 * half, channel_multiplier=-1))
                drain("gpsimd")
                emit("gpsimd", lambda g: g.nop(), inc=("Gev", 1))
        DTT_G = mk.cnt["Gev"]
        if STOP_AFTER == "P2d":
            return
        emit("scalar", lambda s: s.activation(sm[0:1, 776:780], T["slop"][0:1, 4:8],
                                              AF.Exp, bias=0.0, scale=256.0))
        drain("scalar")
        emit("scalar", lambda s: s.copy(T["smallb"][0:1, 1024:1028], sm[0:1, 776:780]))
        drain("scalar")
        emit("scalar", lambda s: s.nop(), inc=("Sev", 1))
        wait("tensor", "Sev", mk.cnt["Sev"])
        wait("tensor", "Vev", QDEC_V)
        emit("tensor", lambda e: e.matmul(P[2][:, 0:4], T["onesb"][0:1, :],
                                          T["smallb"][0:1, 1024:1028], start=True, stop=True),
             inc=("Tmm", 1))
        wait("vector", "Tmm", mk.cnt["Tmm"])
        emit("vector", lambda v: v.tensor_copy(sm[:, 772:776], P[2][:, 0:4]))
        drain("vector")
        emit("vector", lambda v: v.memset(T["kv"][:, :, :], 0.0))
        emit("vector", lambda v: v.memset(T["kvb"][:, :, :], 0.0))
        drain("vector")
        emit("vector", lambda v: v.nop(), inc=("Vev", 1))
        BD_V = mk.cnt["Vev"]

        if STOP_AFTER == "P2":
            return
        wait("gpsimd", "Tmm", MODTR_T)
        emit("gpsimd", lambda g: g.dma_start(
            out=T["wqkv_sb"][:, :, :, :],
            in_=d["wqkv_T"]), inc=("Wq", 16))
        WQKV_W = mk.cnt["Wq"]

        xn_sev = []
        xs_sems = ["XsA", "XsB", "TrA"]
        stat_v = []
        all_tiles = [(d["x_b"], tt, T["stats"]) for tt in range(32)] + \
                    [(d["x_m"], tt, T["statm"]) for tt in range(8)]
        for gi, (src_t, tt, statcol) in enumerate(all_tiles):
            buf = gi % 3
            xsem = xs_sems[buf]
            if gi >= 3:
                wait("sync", "Vev", stat_v[gi - 3])
            emit("sync", lambda e, src_t=src_t, tt=tt, buf=buf: e.dma_start(
                out=T["xstage"][:, buf, :], in_=src_t[tt * 128:(tt + 1) * 128, :]),
                inc=(xsem, 16))
            wait("vector", xsem, mk.cnt[xsem])
            s6 = (gi % 2) * 24
            for pc in range(4):
                emit("vector", lambda v, buf=buf, pc=pc, s6=s6: v.bn_stats(
                    T["stat6"][:, s6 + pc * 6:s6 + (pc + 1) * 6],
                    T["xstage"][:, buf, pc * 512:(pc + 1) * 512]))
            drain("vector")
            emit("vector", lambda v, tt=tt, statcol=statcol, s6=s6: v.bn_aggr(
                statcol[:, 2 * tt:2 * tt + 2], T["stat6"][:, s6:s6 + 24]), inc=("Vev", 1))
            stat_v.append(mk.cnt["Vev"])
        STATS_V = mk.cnt["Vev"]
        wait("scalar", "Vev", STATS_V)
        wait("scalar", "Gev", 1)
        for statcol, n in ((T["stats"], 32), (T["statm"], 8)):
            dst = T["isc"] if n == 32 else T["iscm"]
            emit("scalar", lambda s, statcol=statcol, dst=dst, n=n: s.activation(
                dst[:, 0:2 * n:2], statcol[:, 1:2 * n:2],
                AF.Ln, bias=T["consts"][:, 0:1], scale=1.0))
            drain("scalar")
            emit("scalar", lambda s, dst=dst, n=n: s.activation(
                dst[:, 0:2 * n:2], dst[:, 0:2 * n:2],
                AF.Exp, bias=0.0, scale=T["consts"][:, 1:2]))
            drain("scalar")
        emit("scalar", lambda s: s.nop(), inc=("Sev", 1))
        wait("vector", "Sev", mk.cnt["Sev"])
        for statcol, n in ((T["stats"], 32), (T["statm"], 8)):
            dst = T["isc"] if n == 32 else T["iscm"]
            emit("vector", lambda v, statcol=statcol, dst=dst, n=n: v.tensor_tensor(
                dst[:, 1:2 * n:2], statcol[:, 0:2 * n:2], dst[:, 0:2 * n:2], ALU.mult))
            drain("vector")
            emit("vector", lambda v, dst=dst, n=n: v.tensor_scalar(
                dst[:, 1:2 * n:2], dst[:, 1:2 * n:2], -1.0, None, ALU.mult))
            drain("vector")
        emit("vector", lambda v: v.nop(), inc=("Vev", 1))
        ISTD_V = mk.cnt["Vev"]
        wait("scalar", "Vev", ISTD_V)
        for gi, (src_t, tt, statcol) in enumerate(all_tiles):
            buf = gi % 3
            xsem = xs_sems[buf]
            if gi >= 3:
                wait("sync", "Sev", xn_sev[gi - 3])
            else:
                wait("sync", "Vev", STATS_V)
            emit("sync", lambda e, src_t=src_t, tt=tt, buf=buf: e.dma_start(
                out=T["xstage"][:, buf, :], in_=src_t[tt * 128:(tt + 1) * 128, :]),
                inc=(xsem, 16))
            isc = T["isc"] if gi < 32 else T["iscm"]
            hb = gi % 2
            pk_sem = "PkA" if hb == 0 else "PkB"
            wait("scalar", xsem, mk.cnt[xsem])
            if gi >= 2:
                wait("scalar", pk_sem, mk.cnt[pk_sem])
            emit("scalar", lambda s, tt=tt, buf=buf, hb=hb, isc=isc: s.activation(
                T["hblk"][:, hb, :, :].rearrange("p a b -> p (a b)")[:, 0:D],
                T["xstage"][:, buf, :], AF.Identity,
                bias=isc[:, 2 * tt + 1:2 * tt + 2],
                scale=isc[:, 2 * tt:2 * tt + 1]), inc=("Sev", 1))
            xn_sev.append(mk.cnt["Sev"])
            park = d["xnb_park"] if gi < 32 else d["xnm_park"]
            wait("sync", "Sev", mk.cnt["Sev"])
            emit("sync", lambda e, tt=tt, hb=hb, park=park: e.dma_start(
                out=park[tt * 128:(tt + 1) * 128, :],
                in_=T["hblk"][:, hb, :, :].rearrange("p a b -> p (a b)")[:, 0:D]),
                inc=(pk_sem, 16))
        PKA_ALL = mk.cnt["PkA"]
        PKB_ALL = mk.cnt["PkB"]

        if STOP_AFTER == "P3":
            return
        hv = [0, 0]
        qkv_sev = [0] * NTB
        scan_state = dict(tr_evac_v=0, mask_v=0, kvupd_v=0)
        att_evac = {3: 0, 4: 0}
        kvb_sev = [0] * HPC
        bank_evac_s = [0, 0]

        for tb in range(NTB):
            buf = tb % 2
            tr_sem = "TrA" if buf == 0 else "TrB"
            wait("sync", "PkA", PKA_ALL)
            wait("sync", "PkB", PKB_ALL)
            if tb >= 2:
                wait("sync", "Hfree", tb - 1)
            wait("sync", "Bnc", mk.cnt["Bnc"])
            emit("sync", lambda e, tb=tb, buf=buf: e.dma_start_transpose(
                T["hblk"][:, buf, :, :], d["xnb_park"][tb * TBLK:(tb + 1) * TBLK, :]),
                inc=(tr_sem, 16))
            wait("vector", tr_sem, mk.cnt[tr_sem])
            wait("vector", "Vev", MODD_V)
            for k in range(KC):
                emit("vector", lambda v, buf=buf, k=k: v.tensor_scalar(
                    T["hblk"][:, buf, k, :], T["hblk"][:, buf, k, :],
                    T["modd"][:, 16 + k:17 + k], T["modd"][:, 0 + k:1 + k], ALU.mult, ALU.add),
                    inc=("Vev", 1) if k == KC - 1 else None)
            hv[buf] = mk.cnt["Vev"]

            wait("tensor", "Vev", hv[buf])
            wait("tensor", "Wq", WQKV_W)
            qbuf = tb % 2
            if tb >= 2:
                wait("tensor", "Qfree", tb - 1)
            for m in range(12):
                bk = m % 2
                wait("tensor", "Sev", bank_evac_s[bk])
                def qmm(e, m=m, buf=buf, bk=bk):
                    for k in range(KC):
                        last = e.matmul(P[bk][:, :], T["wqkv_sb"][:, m, k, :],
                                 T["hblk"][:, buf, k, :], start=(k == 0), stop=(k == KC - 1))
                    return last
                emit("tensor", qmm, inc=("Tmm", 1))
                wait("scalar", "Tmm", mk.cnt["Tmm"])
                emit("scalar", lambda s, m=m, qbuf=qbuf, bk=bk: s.activation(
                    T["qkvblk"][:, qbuf, m, :], P[bk][:, :], AF.Silu), inc=("Sev", 1))
                bank_evac_s[bk] = mk.cnt["Sev"]
            qkv_sev[tb] = mk.cnt["Sev"]
            emit("tensor", lambda e: e.nop(), inc=("Hfree", 1))

            for jj in range(2):
                j = 2 * tb + jj
                for hh in range(HPC):
                    wait("tensor", "Sev", qkv_sev[tb])
                    wait("tensor", "Vev", scan_state["tr_evac_v"])
                    def tr4(e, hh=hh, jj=jj, qbuf=qbuf):
                        for half in range(2):
                            last = e.transpose(pstr[:, half * 128:(half + 1) * 128],
                                        T["qkvblk"][:, qbuf, 4 + hh,
                                                    jj * SBLK + half * 128:jj * SBLK + (half + 1) * 128],
                                        T["ident"][:, :])
                        for half in range(2):
                            last = e.transpose(pstr[:, 256 + half * 128:256 + (half + 1) * 128],
                                        T["qkvblk"][:, qbuf, 8 + hh,
                                                    jj * SBLK + half * 128:jj * SBLK + (half + 1) * 128],
                                        T["ident"][:, :])
                        return last
                    emit("tensor", tr4, inc=("Tmm", 1))
                    wait("vector", "Tmm", mk.cnt["Tmm"])
                    wait("vector", "Vev", QDEC_V)
                    for half in range(2):
                        emit("vector", lambda v, hh=hh, half=half: v.tensor_scalar(
                            T["scanbuf"][:, hh, 2, half * 128:(half + 1) * 128],
                            pstr[:, half * 128:(half + 1) * 128],
                            T["kdec"][:, hh, half:half + 1], None, ALU.mult))
                    for half in range(2):
                        emit("vector", lambda v, hh=hh, half=half: v.tensor_copy(
                            T["scanbuf"][:, hh, 3, half * 128:(half + 1) * 128],
                            pstr[:, 256 + half * 128:256 + (half + 1) * 128]))
                    emit("vector", lambda v, hh=hh, jj=jj, qbuf=qbuf: v.tensor_tensor(
                        T["scanbuf"][:, hh, 4, :],
                        T["qkvblk"][:, qbuf, hh, jj * SBLK:(jj + 1) * SBLK],
                        T["qdec"][:, hh, :], ALU.mult), inc=("Vev", 1))
                    scan_state["tr_evac_v"] = mk.cnt["Vev"]
                    wait("tensor", "Vev", scan_state["tr_evac_v"])
                    wait("tensor", "Vev", scan_state["mask_v"])
                    def smm(e, hh=hh, jj=jj, qbuf=qbuf):
                        for half in range(2):
                            last = e.matmul(P[2][:, half * SBLK:(half + 1) * SBLK],
                                     T["qkvblk"][:, qbuf, 4 + hh,
                                                 jj * SBLK + half * 128:jj * SBLK + (half + 1) * 128],
                                     T["qkvblk"][:, qbuf, hh, jj * SBLK:(jj + 1) * SBLK],
                                     start=True, stop=True)
                        return last
                    emit("tensor", smm, inc=("Tmm", 1))
                    wait("vector", "Tmm", mk.cnt["Tmm"])
                    wait("vector", "Gev", DTT_G)
                    for half in range(2):
                        ii = emit("vector", lambda v, hh=hh, half=half: v.tensor_tensor(
                            T["scanbuf"][:, hh, half, :], P[2][:, half * SBLK:(half + 1) * SBLK],
                            T["dtt"][:, hh, half, :], ALU.mult),
                            inc=("Vev", 1) if half == 1 else None)
                    scan_state["mask_v"] = mk.cnt["Vev"]
                    abank = 3 + (hh % 2)
                    wait("tensor", "Vev", scan_state["mask_v"])
                    wait("tensor", "Sev", kvb_sev[hh])
                    wait("tensor", "Sev", att_evac[abank])
                    def amm(e, hh=hh, abank=abank):
                        last = e.matmul(P[abank][:, 0:SBLK], T["kvb"][:, hh, :],
                                 T["scanbuf"][:, hh, 4, :], start=True, stop=False)
                        for half in range(2):
                            last = e.matmul(P[abank][:, 0:SBLK],
                                     T["scanbuf"][:, hh, 3, half * 128:(half + 1) * 128],
                                     T["scanbuf"][:, hh, half, :],
                                     start=False, stop=(half == 1))
                        return last
                    emit("tensor", amm, inc=("Tmm", 1))
                    att_t = mk.cnt["Tmm"]
                    wait("tensor", "Vev", scan_state["kvupd_v"])
                    def kmm(e, hh=hh):
                        for half in range(2):
                            last = e.matmul(P[5][:, 0:128],
                                     T["scanbuf"][:, hh, 2, half * 128:(half + 1) * 128],
                                     T["scanbuf"][:, hh, 3, half * 128:(half + 1) * 128],
                                     start=(half == 0), stop=(half == 1))
                        return last
                    emit("tensor", kmm, inc=("Tmm", 1))
                    kv_t = mk.cnt["Tmm"]
                    wait("scalar", "Tmm", att_t)
                    wait("scalar", "Bnc", mk.cnt["Bnc"])
                    emit("scalar", lambda s, hh=hh, abank=abank: s.activation(
                        T["smallb"][:, 2048 + hh * SBLK:2048 + (hh + 1) * SBLK],
                        P[abank][:, 0:SBLK], AF.Copy), inc=("Sev", 1))
                    att_evac[abank] = mk.cnt["Sev"]
                    wait("sync", "Sev", att_evac[abank])
                    emit("sync", lambda e, hh=hh, j=j: e.dma_start(
                        out=d["att_gi"][hh * 128:(hh + 1) * 128, j * SBLK:(j + 1) * SBLK],
                        in_=T["smallb"][:, 2048 + hh * SBLK:2048 + (hh + 1) * SBLK]),
                        inc=("Bnc", 16))
                    wait("vector", "Tmm", kv_t)
                    wait("vector", "Vev", BD_V)
                    emit("vector", lambda v, hh=hh: v.scalar_tensor_tensor(
                        T["kv"][:, hh, :], T["kv"][:, hh, :], sm[:, 772 + hh:773 + hh],
                        P[5][:, 0:128], ALU.mult, ALU.add), inc=("Vev", 1))
                    scan_state["kvupd_v"] = mk.cnt["Vev"]
                    wait("scalar", "Vev", scan_state["kvupd_v"])
                    emit("scalar", lambda s, hh=hh: s.copy(T["kvb"][:, hh, :], T["kv"][:, hh, :]),
                         inc=("Sev", 1))
                    kvb_sev[hh] = mk.cnt["Sev"]
            emit("vector", lambda v: v.nop(), inc=("Qfree", 1))
        BNC_ALL = mk.cnt["Bnc"]

        if STOP_AFTER == "P4":
            return
        wait("sync", "Tmm", mk.cnt["Tmm"])
        wait("sync", "Vev", mk.cnt["Vev"])
        wait("sync", "Sev", mk.cnt["Sev"])
        wait("sync", "Bnc", mk.cnt["Bnc"])
        for half in range(2):
            emit("sync", lambda e, half=half: e.dma_start_transpose(
                T["h1m"][:, :, half * TBLK:(half + 1) * TBLK],
                d["xnm_park"][half * TBLK:(half + 1) * TBLK, :]), inc=("Trp", 16))
        wait("vector", "Trp", mk.cnt["Trp"])
        for k in range(KC):
            emit("vector", lambda v, k=k: v.tensor_scalar(
                T["h1m"][:, k, :], T["h1m"][:, k, :],
                T["modd"][:, 16 + k:17 + k], T["modd"][:, 0 + k:1 + k], ALU.mult, ALU.add),
                inc=("Vev", 1) if k == KC - 1 else None)
        H1M_V = mk.cnt["Vev"]

        wait("gpsimd", "Bnc", BNC_ALL)
        emit("gpsimd", lambda g: g.collective_compute(
            "AllGather", ALU.bypass, replica_groups=[[0, 1, 2, 3], [4, 5, 6, 7]],
            ins=[d["att_gi"].opt()], outs=[d["att_go"].opt()]), inc=("CC", 1))
        ATTCC = mk.cnt["CC"]

        wait("gpsimd", "Sev", xn_sev[-1])
        emit("gpsimd", lambda g: g.dma_start(
            out=T["wcolA"][:, 0, :, :], in_=d["wg_T"][0]),
            inc=("WlA", 16))
        gate_evac = [0, 0]
        gate_t = []
        for m in range(KC):
            if m + 1 < KC:
                if m >= 1:
                    wait("gpsimd", "Tmm", gate_t[m - 1])
                wl = "WlA" if (m + 1) % 2 == 0 else "WlB"
                emit("gpsimd", lambda g, m=m: g.dma_start(
                    out=T["wcolA"][:, (m + 1) % 2, :, :],
                    in_=d["wg_T"][m + 1]), inc=(wl, 16))
            wl_m = "WlA" if m % 2 == 0 else "WlB"
            wait("tensor", wl_m, mk.cnt[wl_m])
            wait("tensor", "Vev", H1M_V)
            wait("tensor", "Sev", gate_evac[m % 2])
            def gmm(e, m=m):
                for half in range(2):
                    for k in range(KC):
                        last = e.matmul(P[2 * (m % 2) + half][:, :], T["wcolA"][:, m % 2, k, :],
                                 T["h1m"][:, k, half * TBLK:(half + 1) * TBLK],
                                 start=(k == 0), stop=(k == KC - 1))
                return last
            emit("tensor", gmm, inc=("Tmm", 1))
            gate_t.append(mk.cnt["Tmm"])
            wait("scalar", "Tmm", mk.cnt["Tmm"])
            for half in range(2):
                emit("scalar", lambda s, m=m, half=half: s.activation(
                    T["gate_h2"][:, m, half * TBLK:(half + 1) * TBLK],
                    P[2 * (m % 2) + half][:, :], AF.Sigmoid))
            drain("scalar")
            emit("scalar", lambda s: s.nop(), inc=("Sev", 1))
            gate_evac[m % 2] = mk.cnt["Sev"]
        GATE_S = mk.cnt["Sev"]
        tap("gate", lambda: T["gate_h2"][:, :, :], after=("Sev", GATE_S))

        if STOP_AFTER == "P5":
            return
        wait("gpsimd", "CC", ATTCC)
        for k in range(KC):
            emit("gpsimd", lambda g, k=k: g.indirect_dma_start(
                out=T["attnz"][:, k, :], out_offset=None,
                in_=d["att_go"].rearrange("a (q c) -> (a q) c", c=NT),
                in_offset=bass.IndirectOffsetOnAxis(ap=T["idx_a"][:, k:k + 1], axis=0)),
                inc=("Gth", 16))
        GTH_A = mk.cnt["Gth"]
        tap("attng", lambda: T["attnz"][:, :, :], after=("Gth", GTH_A))

        rms_t = 0
        for k in range(KC):
            wait("vector", "Gth", GTH_A)
            if k > 0:
                wait("vector", "Tmm", rms_t)
            emit("vector", lambda v, k=k: v.tensor_tensor(
                T["smallb"][:, 0:NT], T["attnz"][:, k, :], T["attnz"][:, k, :], ALU.mult),
                inc=("Vev", 1))
            wait("tensor", "Vev", mk.cnt["Vev"])
            def rmm(e, k=k):
                last = e.matmul(P[4][0:1, :], T["onecol"][:, 0:1], T["smallb"][:, 0:512],
                         start=(k == 0), stop=(k == KC - 1))
                last = e.matmul(P[5][0:1, :], T["onecol"][:, 0:1], T["smallb"][:, 512:1024],
                         start=(k == 0), stop=(k == KC - 1))
                return last
            emit("tensor", rmm, inc=("Tmm", 1))
            rms_t = mk.cnt["Tmm"]
        wait("scalar", "Tmm", rms_t)
        for half, bank in ((0, 4), (1, 5)):
            emit("scalar", lambda s, half=half, bank=bank: s.activation(
                T["rstat"][0:1, half * 512:(half + 1) * 512], P[bank][0:1, :],
                AF.Ln, bias=T["consts"][0:1, 2:3], scale=T["consts"][0:1, 3:4]))
        drain("scalar")
        for half in range(2):
            emit("scalar", lambda s, half=half: s.activation(
                T["rstat"][0:1, half * 512:(half + 1) * 512],
                T["rstat"][0:1, half * 512:(half + 1) * 512],
                AF.Exp, bias=0.0, scale=T["consts"][0:1, 1:2]))
        drain("scalar")
        emit("scalar", lambda s: s.copy(T["smallb"][0:1, 0:NT], T["rstat"][0:1, 0:NT]))
        drain("scalar")
        emit("scalar", lambda s: s.nop(), inc=("Sev", 1))
        wait("tensor", "Sev", mk.cnt["Sev"])
        for half in range(2):
            if half == 1:
                wait("tensor", "Vev", mk.cnt["Vev"])
            emit("tensor", lambda e, half=half: e.matmul(
                P[6][:, :], T["onesb"][0:1, :], T["smallb"][0:1, half * 512:(half + 1) * 512],
                start=True, stop=True), inc=("Tmm", 1))
            wait("vector", "Tmm", mk.cnt["Tmm"])
            emit("vector", lambda v, half=half: v.tensor_copy(
                T["rbc"][:, half * 512:(half + 1) * 512], P[6][:, :]))
            drain("vector")
            emit("vector", lambda v: v.nop(), inc=("Vev", 1))
        RBC_V = mk.cnt["Vev"]

        wait("vector", "Sev", GATE_S)
        wait("vector", "Vev", RBC_V)
        for k in range(KC):
            emit("vector", lambda v, k=k: v.scalar_tensor_tensor(
                T["attnz"][:, k, :], T["attnz"][:, k, :], T["nw_sb"][:, k:k + 1],
                T["gate_h2"][:, k, :], ALU.mult, ALU.mult),
                inc=("Vev", 1) if k == KC - 1 else None)
        wait("vector", "Vev", mk.cnt["Vev"])
        for k in range(KC):
            emit("vector", lambda v, k=k: v.tensor_tensor(
                T["attnz"][:, k, :], T["attnz"][:, k, :], T["rbc"][:, :], ALU.mult),
                inc=("Vev", 1) if k == KC - 1 else None)
        Z_V = mk.cnt["Vev"]
        tap("z", lambda: T["attnz"][:, :, :], after=("Vev", Z_V))

        wait("gpsimd", "Tmm", gate_t[-1])
        emit("gpsimd", lambda g: g.dma_start(
            out=T["wcolA"][:, 0, :, :], in_=d["wo_T"][0]),
            inc=("WlA", 16))
        xp_evac = [0, 0]
        xp_done = [0, 0]
        op_t = []
        ln2_t = 0
        for m in range(KC):
            if m + 1 < KC:
                if m >= 1:
                    wait("gpsimd", "Tmm", op_t[m - 1])
                wl = "WlA" if (m + 1) % 2 == 0 else "WlB"
                emit("gpsimd", lambda g, m=m: g.dma_start(
                    out=T["wcolA"][:, (m + 1) % 2, :, :],
                    in_=d["wo_T"][m + 1]), inc=(wl, 16))
            xl_m = "XlA" if m % 2 == 0 else "XlB"
            if m >= 2:
                wait("sync", "Vev", xp_done[m % 2])
                wait("sync", "Park", mk.cnt["Park"])
            emit("sync", lambda e, m=m: e.dma_start(
                out=T["resid"][:, m % 2, :], in_=d["xT_m"][m * 128:(m + 1) * 128, :]),
                inc=(xl_m, 16))
            xld_m = mk.cnt[xl_m]
            wl_m = "WlA" if m % 2 == 0 else "WlB"
            wait("tensor", wl_m, mk.cnt[wl_m])
            wait("tensor", "Vev", Z_V)
            for half in range(2):
                wait("tensor", "Vev", xp_evac[half])
                def omm(e, m=m, half=half):
                    for k in range(KC):
                        last = e.matmul(P[half][:, :], T["wcolA"][:, m % 2, k, :],
                                 T["attnz"][:, k, half * TBLK:(half + 1) * TBLK],
                                 start=(k == 0), stop=(k == KC - 1))
                    return last
                emit("tensor", omm, inc=("Tmm", 1))
                omm_t = mk.cnt["Tmm"]
                wait("vector", "Tmm", omm_t)
                wait("vector", xl_m, xld_m)
                emit("vector", lambda v, m=m, half=half: v.scalar_tensor_tensor(
                    T["resid"][:, m % 2, half * TBLK:(half + 1) * TBLK],
                    P[half][:, :], T["modd"][:, 32 + m:33 + m],
                    T["resid"][:, m % 2, half * TBLK:(half + 1) * TBLK], ALU.mult, ALU.add))
                drain("vector")
                emit("vector", lambda v: v.nop(), inc=("Vev", 1))
                xp_evac[half] = mk.cnt["Vev"]
            emit("vector", lambda v, m=m: v.tensor_copy(
                T["smallb"][:, 0:NT], T["resid"][:, m % 2, :]))
            emit("vector", lambda v, m=m: v.tensor_tensor(
                T["smallb"][:, NT:2 * NT], T["resid"][:, m % 2, :], T["resid"][:, m % 2, :],
                ALU.mult), inc=("Vev", 1))
            xp_done[m % 2] = mk.cnt["Vev"]
            wait("sync", "Vev", xp_done[m % 2])
            emit("sync", lambda e, m=m: e.dma_start(
                out=d["xp_park"][m * 128:(m + 1) * 128, :], in_=T["resid"][:, m % 2, :]),
                inc=("Park", 16))
            wait("tensor", "Vev", xp_done[m % 2])
            def lsmm(e, m=m):
                last = e.matmul(P[2][0:1, :], T["onecol"][:, 0:1], T["smallb"][:, 0:512],
                         start=(m == 0), stop=(m == KC - 1))
                last = e.matmul(P[3][0:1, :], T["onecol"][:, 0:1], T["smallb"][:, 512:1024],
                         start=(m == 0), stop=(m == KC - 1))
                last = e.matmul(P[6][0:1, :], T["onecol"][:, 0:1], T["smallb"][:, NT:NT + 512],
                         start=(m == 0), stop=(m == KC - 1))
                last = e.matmul(P[4][0:1, :], T["onecol"][:, 0:1], T["smallb"][:, NT + 512:2 * NT],
                         start=(m == 0), stop=(m == KC - 1))
                return last
            emit("tensor", lsmm, inc=("Tmm", 1))
            ln2_t = mk.cnt["Tmm"]
            op_t.append(mk.cnt["Tmm"])
        XPARK_ALL = mk.cnt["Park"]

        wait("vector", "Tmm", ln2_t)
        emit("vector", lambda v: v.tensor_scalar(T["lrow"][0:1, 0:512], P[2][0:1, :],
                                                 1.0 / 2048.0, None, ALU.mult))
        emit("vector", lambda v: v.tensor_scalar(T["lrow"][0:1, 512:1024], P[3][0:1, :],
                                                 1.0 / 2048.0, None, ALU.mult))
        emit("vector", lambda v: v.tensor_scalar(T["lrow"][0:1, 1024:1536], P[6][0:1, :],
                                                 1.0 / 2048.0, None, ALU.mult))
        emit("vector", lambda v: v.tensor_scalar(T["lrow"][0:1, 1536:2048], P[4][0:1, :],
                                                 1.0 / 2048.0, None, ALU.mult))
        drain("vector")
        emit("vector", lambda v: v.tensor_tensor(sm[0:1, 0:512], T["lrow"][0:1, 0:512],
                                                 T["lrow"][0:1, 0:512], ALU.mult))
        drain("vector")
        emit("vector", lambda v: v.tensor_tensor(T["lrow"][0:1, 1024:1536],
                                                 T["lrow"][0:1, 1024:1536],
                                                 sm[0:1, 0:512], ALU.subtract))
        drain("vector")
        emit("vector", lambda v: v.tensor_tensor(sm[0:1, 0:512], T["lrow"][0:1, 512:1024],
                                                 T["lrow"][0:1, 512:1024], ALU.mult))
        drain("vector")
        emit("vector", lambda v: v.tensor_tensor(T["lrow"][0:1, 1536:2048],
                                                 T["lrow"][0:1, 1536:2048],
                                                 sm[0:1, 0:512], ALU.subtract))
        drain("vector")
        emit("vector", lambda v: v.nop(), inc=("Vev", 1))
        wait("scalar", "Vev", mk.cnt["Vev"])
        emit("scalar", lambda s: s.activation(T["lrow"][0:1, 1024:2048], T["lrow"][0:1, 1024:2048],
                                              AF.Ln, bias=T["consts"][0:1, 0:1], scale=1.0))
        drain("scalar")
        emit("scalar", lambda s: s.activation(T["lrow"][0:1, 1024:2048], T["lrow"][0:1, 1024:2048],
                                              AF.Exp, bias=0.0, scale=T["consts"][0:1, 1:2]))
        drain("scalar")
        emit("scalar", lambda s: s.nop(), inc=("Sev", 1))
        wait("vector", "Sev", mk.cnt["Sev"])
        emit("vector", lambda v: v.tensor_tensor(T["lrow"][0:1, 0:1024], T["lrow"][0:1, 0:1024],
                                                 T["lrow"][0:1, 1024:2048], ALU.mult))
        drain("vector")
        emit("vector", lambda v: v.tensor_scalar(T["lrow"][0:1, 0:1024], T["lrow"][0:1, 0:1024],
                                                 -1.0, None, ALU.mult))
        drain("vector")
        emit("vector", lambda v: v.tensor_copy(T["smallb"][0:1, 0:NT], T["lrow"][0:1, 1024:2048]))
        emit("vector", lambda v: v.tensor_copy(T["smallb"][0:1, NT:2 * NT], T["lrow"][0:1, 0:1024]))
        drain("vector")
        emit("vector", lambda v: v.nop(), inc=("Vev", 1))
        wait("tensor", "Vev", mk.cnt["Vev"])
        for half in range(2):
            if half == 1:
                wait("tensor", "Vev", mk.cnt["Vev"])
            emit("tensor", lambda e, half=half: e.matmul(
                P[5][:, :], T["onesb"][0:1, :], T["smallb"][0:1, half * 512:(half + 1) * 512],
                start=True, stop=True), inc=("Tmm", 1))
            wait("vector", "Tmm", mk.cnt["Tmm"])
            emit("vector", lambda v, half=half: v.tensor_copy(
                T["rbc"][:, half * 512:(half + 1) * 512], P[5][:, :]))
            drain("vector")
            emit("vector", lambda v: v.nop(), inc=("Vev", 1))
            wait("tensor", "Vev", mk.cnt["Vev"])
            emit("tensor", lambda e, half=half: e.matmul(
                P[5][:, :], T["onesb"][0:1, :],
                T["smallb"][0:1, NT + half * 512:NT + (half + 1) * 512],
                start=True, stop=True), inc=("Tmm", 1))
            wait("vector", "Tmm", mk.cnt["Tmm"])
            emit("vector", lambda v, half=half: v.tensor_copy(
                T["smallb"][:, 2048 + half * 512:2048 + (half + 1) * 512], P[5][:, :]))
            drain("vector")
            emit("vector", lambda v: v.nop(), inc=("Vev", 1))
        IBC_V = mk.cnt["Vev"]

        for m in range(KC):
            xl_m = "XlA" if m % 2 == 0 else "XlB"
            wait("sync", "Park", XPARK_ALL)
            if m >= 2:
                wait("sync", "Vev", mk.cnt["Vev"])
            emit("sync", lambda e, m=m: e.dma_start(
                out=T["resid"][:, m % 2, :], in_=d["xp_park"][m * 128:(m + 1) * 128, :]),
                inc=(xl_m, 16))
            wait("vector", xl_m, mk.cnt[xl_m])
            wait("vector", "Vev", IBC_V)
            emit("vector", lambda v, m=m: v.tensor_tensor(
                T["gate_h2"][:, m, :], T["resid"][:, m % 2, :], T["rbc"][:, :], ALU.mult),
                inc=("Vev", 1))
            wait("vector", "Vev", mk.cnt["Vev"])
            emit("vector", lambda v, m=m: v.tensor_tensor(
                T["gate_h2"][:, m, :], T["gate_h2"][:, m, :],
                T["smallb"][:, 2048:3072], ALU.add), inc=("Vev", 1))
            wait("vector", "Vev", mk.cnt["Vev"])
            emit("vector", lambda v, m=m: v.tensor_scalar(
                T["gate_h2"][:, m, :], T["gate_h2"][:, m, :],
                T["modd"][:, 64 + m:65 + m], T["modd"][:, 48 + m:49 + m], ALU.mult, ALU.add),
                inc=("Vev", 1))
        H2_V = mk.cnt["Vev"]
        tap("h2", lambda: T["gate_h2"][:, :, :], after=("Vev", H2_V))

        if STOP_AFTER == "P6":
            return
        wait("gpsimd", "Sev", GATE_S)
        emit("gpsimd", lambda g: g.dma_start(
            out=T["wr_sb"][:, :, :], in_=d["wr_T"]),
            inc=("Wst", 16))
        for i, nm in enumerate(["agp1", "agp2", "aup1", "aup2"]):
            emit("gpsimd", lambda g, i=i, nm=nm: g.dma_start(
                out=T["agp_sb"][:, i, :, :], in_=d[nm]),
                inc=("Wst", 16))
        emit("gpsimd", lambda g: g.dma_start(out=T["adS_sb"][:, :, :, :],
                                             in_=d["adS"]), inc=("Wst", 16))
        emit("gpsimd", lambda g: g.dma_start(out=T["bdS_sb"][:, :, :], in_=d["bdS"]), inc=("Wst", 16))
        WFFN_W = mk.cnt["Wst"]

        wait("tensor", "Wst", WFFN_W)
        wait("tensor", "Vev", H2_V)
        def rlmm2(e):
            for half in range(2):
                bank = P[4] if half == 0 else P[5]
                for k in range(KC):
                    last = e.matmul(bank[0:E, :], T["wr_sb"][:, k, :],
                             T["gate_h2"][:, k, half * TBLK:(half + 1) * TBLK],
                             start=(k == 0), stop=(k == KC - 1))
            return last
        emit("tensor", rlmm2, inc=("Tmm", 1))
        wait("vector", "Tmm", mk.cnt["Tmm"])
        emit("vector", lambda v: v.tensor_copy(T["rl8"][:, 0:512], P[4][0:E, :]))
        emit("vector", lambda v: v.tensor_copy(T["rl8"][:, 512:1024], P[5][0:E, :]))
        drain("vector")
        emit("vector", lambda v: v.nop(), inc=("Vev", 1))
        wait("tensor", "Vev", mk.cnt["Vev"])
        for c in range(8):
            emit("tensor", lambda e, c=c: e.transpose(
                P[4][:, c * 8:(c + 1) * 8], T["rl8"][:, c * 128:(c + 1) * 128],
                T["identf"][0:8, 0:8]), inc=("Tmm", 1))
        wait("vector", "Tmm", mk.cnt["Tmm"])
        for c in range(8):
            emit("vector", lambda v, c=c: v.tensor_copy(T["rt"][:, c, :], P[4][:, c * 8:(c + 1) * 8]))
        drain("vector")
        for c in range(8):
            emit("vector", lambda v, c=c: v.reduce_max(T["rcol"][:, c, 0:1], T["rt"][:, c, :], AX.X))
        drain("vector")
        for c in range(8):
            emit("vector", lambda v, c=c: v.tensor_scalar(T["rcol"][:, c, 1:2], T["rcol"][:, c, 0:1],
                                                          -1.0, None, ALU.mult))
        drain("vector")
        emit("vector", lambda v: v.nop(), inc=("Vev", 1))
        wait("scalar", "Vev", mk.cnt["Vev"])
        for c in range(8):
            emit("scalar", lambda s, c=c: s.activation(
                T["rsc"][:, c, :], T["rt"][:, c, :], AF.Exp, bias=T["rcol"][:, c, 1:2], scale=1.0))
        drain("scalar")
        emit("scalar", lambda s: s.nop(), inc=("Sev", 1))
        wait("vector", "Sev", mk.cnt["Sev"])
        for c in range(8):
            emit("vector", lambda v, c=c: v.tensor_reduce(T["rcol"][:, c, 0:1], T["rsc"][:, c, :],
                                                          AX.X, ALU.add))
        drain("vector")
        for c in range(8):
            emit("vector", lambda v, c=c: v.reciprocal(T["rcol"][:, c, 1:2], T["rcol"][:, c, 0:1]))
        drain("vector")
        for c in range(8):
            emit("vector", lambda v, c=c: v.tensor_scalar(T["rt"][:, c, :], T["rsc"][:, c, :],
                                                          T["rcol"][:, c, 1:2], None, ALU.mult))
        drain("vector")
        for c in range(8):
            emit("vector", lambda v, c=c: v.reduce_max(T["rcol"][:, c, 0:1], T["rt"][:, c, :], AX.X))
        drain("vector")
        for c in range(8):
            emit("vector", lambda v, c=c: v.tensor_scalar(T["rsc"][:, c, :], T["rt"][:, c, :],
                                                          T["rcol"][:, c, 0:1], None, ALU.is_equal))
        drain("vector")
        for c in range(8):
            emit("vector", lambda v, c=c: v.scalar_tensor_tensor(
                T["rsc"][:, c, :], T["rsc"][:, c, :], -1e30, T["rt"][:, c, :], ALU.mult, ALU.add))
        drain("vector")
        for c in range(8):
            emit("vector", lambda v, c=c: v.reduce_max(T["rcol"][:, c, 2:3], T["rsc"][:, c, :], AX.X))
        drain("vector")
        for c in range(8):
            emit("vector", lambda v, c=c: v.tensor_scalar(T["rsc"][:, c, :], T["rt"][:, c, :],
                                                          T["rcol"][:, c, 2:3], None, ALU.is_ge))
        drain("vector")
        for c in range(8):
            emit("vector", lambda v, c=c: v.tensor_tensor(T["rt"][:, c, :], T["rt"][:, c, :],
                                                          T["rsc"][:, c, :], ALU.mult))
        drain("vector")
        emit("vector", lambda v: v.nop(), inc=("Vev", 1))
        TOPW_V = mk.cnt["Vev"]
        tap("topw", lambda: T["rt"][:, :, :], after=("Vev", TOPW_V))
        wait("tensor", "Vev", TOPW_V)
        for c in range(8):
            bank = P[4] if c < 4 else P[5]
            emit("tensor", lambda e, c=c, bank=bank: e.transpose(
                bank[0:8, (c % 4) * 128:(c % 4 + 1) * 128], T["rt"][:, c, :], T["identf"][:, :]),
                inc=("Tmm", 1))
        wait("vector", "Tmm", mk.cnt["Tmm"])
        emit("vector", lambda v: v.tensor_copy(T["wef"][:, 0:512], P[4][0:8, 0:512]))
        emit("vector", lambda v: v.tensor_copy(T["wef"][:, 512:1024], P[5][0:8, 0:512]))
        drain("vector")
        emit("vector", lambda v: v.nop(), inc=("Vev", 1))
        wait("tensor", "Vev", mk.cnt["Vev"])
        for par, dst in ((0, "wexpE"), (1, "wexpO")):
            for half in range(2):
                if not (par == 0 and half == 0):
                    wait("tensor", "Vev", mk.cnt["Vev"])
                emit("tensor", lambda e, par=par, half=half: e.matmul(
                    P[6][:, :], T["exp16"][:, par, :], T["wef"][:, half * 512:(half + 1) * 512],
                    start=True, stop=True), inc=("Tmm", 1))
                wait("vector", "Tmm", mk.cnt["Tmm"])
                emit("vector", lambda v, dst=dst, half=half: v.tensor_copy(
                    T[dst][:, half * 512:(half + 1) * 512], P[6][:, :]))
                drain("vector")
                emit("vector", lambda v: v.nop(), inc=("Vev", 1))
        WEXP_V = mk.cnt["Vev"]

        xa_dst = [("xagP", 0), ("xagP", 1), ("xauP", 0), ("xauP", 1)]
        for si in range(4):
            dstn, dsti = xa_dst[si]
            for half in range(2):
                if not (si == 0 and half == 0):
                    wait("tensor", "Vev", mk.cnt["Vev"])
                def xamm(e, si=si, half=half):
                    for k in range(KC):
                        last = e.matmul(P[3][:, :], T["agp_sb"][:, si, k, :],
                                 T["gate_h2"][:, k, half * TBLK:(half + 1) * TBLK],
                                 start=(k == 0), stop=(k == KC - 1))
                    return last
                emit("tensor", xamm, inc=("Tmm", 1))
                wait("vector", "Tmm", mk.cnt["Tmm"])
                wait("vector", "Xld", XLD0)
                emit("vector", lambda v, dstn=dstn, dsti=dsti, half=half: v.tensor_scalar(
                    T[dstn][:, dsti, half * TBLK:(half + 1) * TBLK], P[3][:, :],
                    T["pm2c"][:, 0:1], None, ALU.mult))
                drain("vector")
                emit("vector", lambda v: v.nop(), inc=("Vev", 1))
        XA_V = mk.cnt["Vev"]
        tap("xa", lambda: T["xagP"][:, :, :], after=("Vev", XA_V))

        if STOP_AFTER == "P7":
            return
        xat_gth = 0
        for half in range(2):
            wait("gpsimd", "Vev", XA_V)
            if half == 1:
                wait("gpsimd", "Tmm", mk.cnt["Tmm"])
            for tg in range(E):
                srcn = "xagP"; srci = 0 if tg < 4 else 1
                emit("gpsimd", lambda g, tg=tg, srcn=srcn, srci=srci, half=half: g.dma_start(
                    out=T["xat"][:, tg, :],
                    in_=T[srcn][32 * (tg % 4):32 * (tg % 4) + 32, srci,
                                half * TBLK:(half + 1) * TBLK]), inc=("Gth", 16))
            for tg in range(E):
                srcn = "xauP"; srci = 0 if tg < 4 else 1
                emit("gpsimd", lambda g, tg=tg, srcn=srcn, srci=srci, half=half: g.dma_start(
                    out=T["xat"][:, E + tg, :],
                    in_=T[srcn][32 * (tg % 4):32 * (tg % 4) + 32, srci,
                                half * TBLK:(half + 1) * TBLK]), inc=("Gth", 16))
            xat_gth = mk.cnt["Gth"]

            if half == 1:
                wait("gpsimd", "Tmm", mk.cnt["Tmm"])
            wait("gpsimd", "Tmm", op_t[-1])
            emit("gpsimd", lambda g: g.dma_start(
                out=T["wcolA"][:, 0, :, :], in_=d["Wg_T"][0]),
                inc=("WlA", 16))
            emit("gpsimd", lambda g: g.dma_start(
                out=T["wcolB"][:, 0, :, :], in_=d["Wu_T"][0]),
                inc=("WlA", 16))
            emit("gpsimd", lambda g: g.dma_start(
                out=T["bgp_sb"][:, 0, :, :], in_=d["bgp"][0]),
                inc=("WdA", 16))
            emit("gpsimd", lambda g: g.dma_start(
                out=T["bup_sb"][:, 0, :, :], in_=d["bup"][0]),
                inc=("WdA", 16))
            wait("vector", "Tmm", mk.cnt["Tmm"])
            emit("vector", lambda v: v.memset(P[6][:, :], 0.0))
            emit("vector", lambda v: v.memset(P[7][:, :], 0.0))
            drain("vector")
            emit("vector", lambda v: v.nop(), inc=("Vev", 1))
            HAZ_V = mk.cnt["Vev"]
            ffn_t = []

            chunk_t = [0] * (FC + 1)
            base_done_t = [0] * (FC + 1)
            silu_base_s = [0] * FC
            mult_base_v = [0] * FC
            base_read_v = mk.cnt["Vev"]
            gub_read_v = mk.cnt["Vev"]
            lg_v = {}
            lu_v = {}
            for i in range(FC + 1):
                if i + 1 < FC:
                    if i >= 1:
                        wait("gpsimd", "Tmm", base_done_t[i - 1])
                    wl = "WlA" if (i + 1) % 2 == 0 else "WlB"
                    for nm, dst in (("Wg_T", "wcolA"), ("Wu_T", "wcolB")):
                        emit("gpsimd", lambda g, i=i, nm=nm, dst=dst: g.dma_start(
                            out=T[dst][:, (i + 1) % 2, :, :], in_=d[nm][i + 1]), inc=(wl, 16))
                silu_s = [0] * E
                gadd_v = [0] * E
                uadd_v = [0] * E
                mult_v = [0] * E
                ad_t = [0] * E
                if i >= 1:
                    if i >= 2:
                        wait("scalar", "Vev", mult_base_v[i - 2])
                        wait("scalar", "Vev", gub_read_v)
                    wait("scalar", "Tmm", base_done_t[i - 1])
                    emit("scalar", lambda s: s.activation(
                        T["sgbb"][:, :], P[0][:, :], AF.Silu))
                    emit("scalar", lambda s: s.copy(T["gbb"][:, :], P[0][:, :]))
                    emit("scalar", lambda s: s.copy(T["ubb"][:, :], P[1][:, :]), inc=("Sev", 1))
                    silu_base_s[i - 1] = mk.cnt["Sev"]
                    wait("vector", "Tmm", base_done_t[i - 1])
                    wait("vector", "Sev", silu_base_s[i - 1])
                    wait("vector", "Sev", silu_base_s[i - 1])
                    emit("vector", lambda v, i=i: v.tensor_tensor(
                        T["hbase"][:, i - 1, :], T["sgbb"][:, :], T["ubb"][:, :],
                        ALU.mult), inc=("Vev", 1))
                    mult_base_v[i - 1] = mk.cnt["Vev"]
                if i < FC:
                    wl_fc = "WlA" if i % 2 == 0 else "WlB"
                    wait("tensor", wl_fc, mk.cnt[wl_fc])
                    wait("tensor", "Vev", H2_V)
                    wait("tensor", "Vev", base_read_v)
                    if i >= 1:
                        wait("tensor", "Sev", silu_base_s[i - 1])
                        wait("tensor", "Vev", mult_base_v[i - 1])
                    def bmm(e, i=i, half=half):
                        for k in range(KC):
                            e.matmul(P[0][:, :], T["wcolA"][:, i % 2, k, :],
                                     T["gate_h2"][:, k, half * TBLK:(half + 1) * TBLK],
                                     start=(k == 0), stop=(k == KC - 1))
                        for k in range(KC):
                            last = e.matmul(P[1][:, :], T["wcolB"][:, i % 2, k, :],
                                            T["gate_h2"][:, k, half * TBLK:(half + 1) * TBLK],
                                            start=(k == 0), stop=(k == KC - 1))
                        return last
                    emit("tensor", bmm, inc=("Tmm", 1))
                    base_done_t[i] = mk.cnt["Tmm"]
                if i >= 1:
                    wdp = "WdA" if (i - 1) % 2 == 0 else "WdB"
                    wait("tensor", wdp, mk.cnt[wdp])
                    wait("tensor", "Gth", xat_gth)
                    wait("tensor", "Vev", HAZ_V)
                    for e_i in range(E):
                        lgb = 2 + (e_i % 2)
                        lub = 4 + (e_i % 2)
                        if e_i >= 2:
                            wait("tensor", "Vev", gadd_v[e_i - 2])
                        elif (i - 1, e_i - 2 + E) in lg_v:
                            wait("tensor", "Vev", lg_v[(i - 1, e_i - 2 + E)])
                        def lmm(e, i=i, e_i=e_i, lgb=lgb, lub=lub):
                            e.matmul(P[lgb][:, :], T["bgp_sb"][:, (i - 1) % 2, e_i, :],
                                     T["xat"][:, e_i, :],
                                     start=True, stop=True, skip_group_check=True)
                            last = e.matmul(P[lub][:, :], T["bup_sb"][:, (i - 1) % 2, e_i, :],
                                            T["xat"][:, E + e_i, :],
                                            start=True, stop=True, skip_group_check=True)
                            return last
                        emit("tensor", lmm, inc=("Tmm", 1))
                        lora_t = mk.cnt["Tmm"]
                        if e_i >= 1:
                            wait("tensor", "Vev", mult_v[e_i - 1])
                            emit("tensor", lambda e, i=i, e_i=e_i: e.matmul(
                                P[6 + ((e_i - 1) % 2)][32 * ((e_i - 1) // 2):32 * ((e_i - 1) // 2) + 16, :],
                                T["adS_sb"][:, i - 1, e_i - 1, :], T["hexp"][:, (e_i - 1) % 2, :],
                                start=False, stop=True, skip_group_check=True,
                                tile_position=(0, 32 * ((e_i - 1) // 2))), inc=("Tmm", 1))
                            ad_t[e_i - 1] = mk.cnt["Tmm"]
                        wait("vector", "Tmm", lora_t)
                        if e_i >= 2:
                            wait("vector", "Sev", silu_s[e_i - 2])
                        emit("vector", lambda v, e_i=e_i, lgb=lgb: v.tensor_tensor(
                            T["tadd"][:, e_i % 2, :], P[lgb][:, :], T["gbb"][:, :], ALU.add),
                            inc=("Vev", 1))
                        gadd_v[e_i] = mk.cnt["Vev"]
                        if e_i >= 2:
                            wait("vector", "Vev", mult_v[e_i - 2])
                        emit("vector", lambda v, e_i=e_i, lub=lub: v.tensor_tensor(
                            T["usb"][:, e_i % 2, :], P[lub][:, :], T["ubb"][:, :], ALU.add),
                            inc=("Vev", 1))
                        uadd_v[e_i] = mk.cnt["Vev"]
                        wait("scalar", "Vev", gadd_v[e_i])
                        if e_i >= 2:
                            wait("scalar", "Vev", mult_v[e_i - 2])
                        emit("scalar", lambda s, e_i=e_i: s.activation(
                            T["sgb"][:, e_i % 2, :], T["tadd"][:, e_i % 2, :], AF.Silu),
                            inc=("Sev", 1))
                        silu_s[e_i] = mk.cnt["Sev"]
                        wait("vector", "Vev", uadd_v[e_i])
                        wait("vector", "Sev", silu_s[e_i])
                        if e_i >= 2:
                            wait("vector", "Tmm", ad_t[e_i - 2])
                        emit("vector", lambda v, e_i=e_i: v.tensor_tensor(
                            T["hexp"][:, e_i % 2, :], T["sgb"][:, e_i % 2, :],
                            T["usb"][:, e_i % 2, :], ALU.mult), inc=("Vev", 1))
                        mult_v[e_i] = mk.cnt["Vev"]
                    wait("tensor", "Vev", mult_v[E - 1])
                    emit("tensor", lambda e, i=i: e.matmul(
                        P[6 + ((E - 1) % 2)][32 * ((E - 1) // 2):32 * ((E - 1) // 2) + 16, :],
                        T["adS_sb"][:, i - 1, E - 1, :], T["hexp"][:, (E - 1) % 2, :],
                        start=False, stop=True, skip_group_check=True,
                        tile_position=(0, 32 * ((E - 1) // 2))), inc=("Tmm", 1))
                    ad_t[E - 1] = mk.cnt["Tmm"]
                    base_read_v = max(uadd_v[E - 1], mult_base_v[i - 1])
                    gub_read_v = base_read_v
                chunk_t[i] = mk.cnt["Tmm"]
                if i + 1 < FC:
                    wait("gpsimd", "Tmm", chunk_t[i])
                    wlp = "WdA" if (i + 1) % 2 == 0 else "WdB"
                    for nm, dst in (("bgp", "bgp_sb"), ("bup", "bup_sb")):
                        emit("gpsimd", lambda g, i=i, nm=nm, dst=dst: g.dma_start(
                            out=T[dst][:, (i + 1) % 2, :, :], in_=d[nm][i + 1]), inc=(wlp, 16))
            AD_T = mk.cnt["Tmm"]
            wait("vector", "Tmm", AD_T)
            wait("vector", "Vev", WEXP_V)
            emit("vector", lambda v, half=half: v.tensor_tensor(
                T["hAwE"][:, :], P[6][:, :], T["wexpE"][:, half * TBLK:(half + 1) * TBLK], ALU.mult))
            emit("vector", lambda v, half=half: v.tensor_tensor(
                T["hAwO"][:, :], P[7][:, :], T["wexpO"][:, half * TBLK:(half + 1) * TBLK], ALU.mult))
            drain("vector")
            emit("vector", lambda v: v.nop(), inc=("Vev", 1))
            HAW_V = mk.cnt["Vev"]

            wait("gpsimd", "Tmm", AD_T)
            emit("gpsimd", lambda g: g.dma_start(
                out=T["wdcol"][:, 0, :, :], in_=d["Wd_T"][0]),
                inc=("WdA", 16))
            out_evac = [0, 0]
            dm_t = []
            for mo in range(KC):
                if mo + 1 < KC:
                    if mo >= 1:
                        wait("gpsimd", "Tmm", dm_t[mo - 1])
                    wd = "WdA" if (mo + 1) % 2 == 0 else "WdB"
                    emit("gpsimd", lambda g, mo=mo: g.dma_start(
                        out=T["wdcol"][:, (mo + 1) % 2, :, :],
                        in_=d["Wd_T"][mo + 1]), inc=(wd, 16))
                xl_mo = "XlA" if mo % 2 == 0 else "XlB"
                if mo >= 2:
                    wait("sync", "Vev", out_evac[mo % 2])
                else:
                    wait("sync", "Vev", H2_V)
                    if half == 1:
                        wait("sync", "Vev", out_evac[mo % 2])
                wait("sync", "Out", mk.cnt["Out"])
                emit("sync", lambda e, mo=mo, half=half: e.dma_start(
                    out=T["resid"][:, mo % 2, 0:TBLK],
                    in_=d["xp_park"][mo * 128:(mo + 1) * 128, half * TBLK:(half + 1) * TBLK]),
                    inc=(xl_mo, 16))
                wd_mo = "WdA" if mo % 2 == 0 else "WdB"
                wait("tensor", wd_mo, mk.cnt[wd_mo])
                wait("tensor", "Vev", HAW_V)
                if mo >= 2:
                    wait("tensor", "Vev", out_evac[mo % 2])
                def dmm(e, mo=mo):
                    for k in range(FC):
                        last = e.matmul(P[4 + (mo % 2)][:, :], T["wdcol"][:, mo % 2, k, :],
                                 T["hbase"][:, k, :], start=(k == 0), stop=False)
                    last = e.matmul(P[4 + (mo % 2)][:, :], T["bdS_sb"][:, 0, mo * 128:(mo + 1) * 128],
                             T["hAwE"][:, :], start=False, stop=False)
                    last = e.matmul(P[4 + (mo % 2)][:, :], T["bdS_sb"][:, 1, mo * 128:(mo + 1) * 128],
                             T["hAwO"][:, :], start=False, stop=True)
                    return last
                emit("tensor", dmm, inc=("Tmm", 1))
                dm_t.append(mk.cnt["Tmm"])
                wait("vector", "Tmm", mk.cnt["Tmm"])
                wait("vector", xl_mo, mk.cnt[xl_mo])
                emit("vector", lambda v, mo=mo: v.scalar_tensor_tensor(
                    T["resid"][:, mo % 2, 0:TBLK], P[4 + (mo % 2)][:, :],
                    T["modd"][:, 80 + mo:81 + mo], T["resid"][:, mo % 2, 0:TBLK],
                    ALU.mult, ALU.add))
                drain("vector")
                emit("vector", lambda v: v.nop(), inc=("Vev", 1))
                out_evac[mo % 2] = mk.cnt["Vev"]
                wait("sync", "Vev", out_evac[mo % 2])
                emit("sync", lambda e, mo=mo, half=half: e.dma_start(
                    out=d["y"][mo * 128:(mo + 1) * 128, half * TBLK:(half + 1) * TBLK],
                    in_=T["resid"][:, mo % 2, 0:TBLK]), inc=("Out", 16))
        wait("sync", "Out", mk.cnt["Out"])
        wait("gpsimd", "Out", mk.cnt["Out"])
        wait("scalar", "Out", mk.cnt["Out"])
        wait("vector", "Out", mk.cnt["Out"])
        wait("tensor", "Out", mk.cnt["Out"])

    handles = {}
    with nc.Block() as block:
        def runner(eng):
            def f(h):
                hs = {eng: h}
                mk = MK(eng, hs, sems)
                walk(mk)
            return f
        block.gpsimd(runner("gpsimd"))
        block.sync(runner("sync"))
        block.scalar(runner("scalar"))
        block.vector(runner("vector"))
        block.tensor(runner("tensor"))

    for cm in reversed(guards):
        cm.__exit__(None, None, None)
    return nc



def host_prep(inp):
    x = np.asarray(inp["L0_attn_hidden_states"], np.float32)
    c = np.asarray(inp["L0_c"], np.float32)
    wmod = np.asarray(inp["L0_wmod"], np.float32)
    wqkv = np.asarray(inp["L0_attn_w_qkv"], np.float32)
    slope = np.asarray(inp["L0_attn_slope_rate"], np.float32)
    wgate = np.asarray(inp["L0_attn_w_gate"], np.float32)
    wout = np.asarray(inp["L0_attn_w_out"], np.float32)
    normw = np.asarray(inp["L0_attn_norm_w"], np.float32)
    Wg = np.asarray(inp["L0_ffn_Wg"], np.float32)
    Wu = np.asarray(inp["L0_ffn_Wu"], np.float32)
    Wd = np.asarray(inp["L0_ffn_Wd"], np.float32)
    Wr = np.asarray(inp["L0_ffn_Wr"], np.float32)
    Ag = np.asarray(inp["L0_ffn_Ag"], np.float32)
    Bg = np.asarray(inp["L0_ffn_Bg"], np.float32)
    Au = np.asarray(inp["L0_ffn_Au"], np.float32)
    Bu = np.asarray(inp["L0_ffn_Bu"], np.float32)
    Ad = np.asarray(inp["L0_ffn_Ad"], np.float32)
    Bd = np.asarray(inp["L0_ffn_Bd"], np.float32)

    co = np.ascontiguousarray

    def mblocks(wT, nblk):
        K, N = wT.shape
        kc = K // 128
        return co(wT.reshape(kc, 128, nblk, 128).transpose(2, 1, 0, 3))

    wg_T = mblocks(wgate.T, KC)
    wo_T = mblocks(wout.T, KC)
    Wg_T = mblocks(Wg.T, FC)
    Wu_T = mblocks(Wu.T, FC)
    Wd_T = mblocks(Wd.T, KC)
    wr_T = co(Wr.T.reshape(KC, 128, E).transpose(1, 0, 2))
    nw_cols = co(normw.reshape(KC, 128).T)

    def pair_cols(A):
        AT = A.transpose(2, 0, 1)
        def mk(tt0):
            cols = []
            for tt in range(tt0, tt0 + 4):
                cols.append(AT[:, (tt - 1) % E, :])
                cols.append(AT[:, tt, :])
            m = np.concatenate(cols, axis=1)
            return co(m.reshape(KC, 128, 128).transpose(1, 0, 2))
        return mk(0), mk(4)
    agp1, agp2 = pair_cols(Ag)
    aup1, aup2 = pair_cols(Au)

    def pair_rows(Bm):
        BT = Bm.transpose(0, 2, 1)
        out = np.empty((FC, E, 32, 128), np.float32)
        for tt in range(E):
            stack = np.concatenate([BT[(tt - 1) % E], BT[tt]], axis=0)
            out[:, tt, :, :] = stack.reshape(32, FC, 128).transpose(1, 0, 2)
        return co(out.transpose(0, 2, 1, 3))
    bgp = pair_rows(Bg)
    bup = pair_rows(Bu)

    AdT = Ad.transpose(2, 0, 1).reshape(FC, 128, E, RK)
    adS = co(AdT.transpose(1, 0, 2, 3))
    bdS = np.zeros((128, 2, D), np.float32)
    for e_i in range(E):
        for i in range(RK):
            bdS[32 * (e_i // 2) + i, e_i % 2, :] = Bd[e_i, :, i]
    bdS = co(bdS)

    pm2c = co(np.where((np.arange(128) % 32) < 16, -2.0, 2.0).astype(np.float32).reshape(128, 1))
    in_maps = []
    for cid in range(8):
        b, q = cid // 4, cid % 4
        heads = list(range(4 * q, 4 * q + 4))
        rows = []
        for sec in range(3):
            for hh in heads:
                rows.extend(range(sec * D + hh * DH, sec * D + (hh + 1) * DH))
        wq = mblocks(co(wqkv[rows, :]).T, 12)
        wqkv_T = co(wq.transpose(1, 0, 2, 3))
        modidx = np.array([[2 * j + b] for j in range(8)], np.uint32)
        pidx = np.arange(128, dtype=np.uint32)[:, None]
        kidx = np.arange(KC, dtype=np.uint32)[None, :]
        aidx = (4 * (128 * kidx + pidx) + q).astype(np.uint32)
        im = {
            "x_b": co(x[b]),
            "x_m": co(x[b, NT * q:NT * (q + 1), :]),
            "xT_m": co(x[b, NT * q:NT * (q + 1), :].T),
            "c2": co(c.T.reshape(KC, 128, B).transpose(1, 0, 2)),
            "wmod_sl": co(wmod[:, 1536 * cid:1536 * (cid + 1)].reshape(KC, 128, 1536).transpose(1, 0, 2)),
            "modidx": modidx,
            "wqkv_T": wqkv_T,
            "slopes": co(slope[heads].reshape(1, HPC)),
            "wg_T": wg_T, "wo_T": wo_T, "nw": nw_cols, "aidx": aidx,
            "Wg_T": Wg_T, "Wu_T": Wu_T, "Wd_T": Wd_T, "wr_T": wr_T,
            "agp1": agp1, "agp2": agp2, "aup1": aup1, "aup2": aup2,
            "bgp": bgp, "bup": bup, "adS": adS, "bdS": bdS,
            "pm2": pm2c,
        }
        in_maps.append(im)
    return in_maps


_NC_CACHE = {}


def kernel(**inputs):
    in_maps = host_prep(inputs)
    if "main" not in _NC_CACHE:
        _NC_CACHE["main"] = build_program(taps=DEBUG_TAPS)
    nc = _NC_CACHE["main"]
    res = run_bass_kernel_spmd(nc, in_maps, core_ids=list(range(8)))
    out = np.empty((B, S, D), np.float32)
    for cid in range(8):
        b, q = cid // 4, cid % 4
        out[b, NT * q:NT * (q + 1), :] = res.results[cid]["y"].T
    return out


# version 05
# speedup vs baseline: 1.0657x; 1.0657x over previous
import numpy as np
import ml_dtypes
import concourse.bass as bass
import concourse.mybir as mybir
from concourse.bass_utils import run_bass_kernel_spmd

F32 = mybir.dt.float32
BF16 = mybir.dt.bfloat16
U32 = mybir.dt.uint32
U8 = mybir.dt.uint8
AF = mybir.ActivationFunctionType
ALU = mybir.AluOpType
AX = mybir.AxisListType

B, S, D, H, DH = 2, 4096, 2048, 16, 128
HPC = 4
NT = 1024
DFF, E, RK = 5632, 8, 16
KC = D // 128
FC = DFF // 128
TBLK = 512
NTB = S // TBLK
SBLK = 256
EPS_LN = 1e-5
EPS_RMS = 1e-6

DEBUG_TAPS = ()
STOP_AFTER = None


EVENTS = {}


class MK:
    def __init__(self, eng_name, handles, sems):
        self.me = eng_name
        self.h = handles
        self.sems = sems
        self.cnt = {k: 0 for k in sems}
        self.ev = []
        EVENTS[eng_name] = self.ev

    def emit(self, eng, fn, inc=None):
        if self.me == eng:
            ins = fn(self.h[eng])
            if inc is not None and ins is not None:
                ins.then_inc(self.sems[inc[0]], inc[1])
            if inc is not None:
                self.ev.append(("inc", inc[0], inc[1]))
        if inc is not None:
            self.cnt[inc[0]] += inc[1]

    def wait(self, eng, sem, val=None):
        v = self.cnt[sem] if val is None else val
        if self.me == eng and v > 0:
            self.h[eng].wait_ge(self.sems[sem], v)
            self.ev.append(("wait", sem, v))

    def drain(self, eng):
        if self.me == eng:
            self.h[eng].drain()


def build_program(taps=()):
    nc = bass.Bass("TRN2", target_bir_lowering=False)
    d = {}

    def din(name, shape, dt=F32):
        d[name] = nc.dram_tensor(name, shape, dt, kind="ExternalInput").ap()

    def dint(name, shape, dt=F32):
        d[name] = nc.dram_tensor(name, shape, dt).ap()

    din("x_b", [S, D]);  din("x_m", [NT, D]);  din("xT_m", [D, NT])
    din("c2", [128, KC, B]);   din("wmod_sl", [128, KC, 1536]);  din("modidx", [8, 1], U32)
    din("wqkv_T", [128, 12, KC, 128]);  din("slopes", [1, HPC])
    din("wg_T", [KC, 128, KC, 128]);    din("wo_T", [KC, 128, KC, 128])
    din("nw", [128, KC]);         din("aidx", [128, KC], U32)
    din("Wg_T", [FC, 128, KC, 128]);    din("Wu_T", [FC, 128, KC, 128]);  din("Wd_T", [KC, 128, FC, 128])
    din("wr_T", [128, KC, E])
    din("agp1", [128, KC, 128]); din("agp2", [128, KC, 128]); din("aup1", [128, KC, 128]); din("aup2", [128, KC, 128])
    din("bgp", [FC, 32, E, 128]);  din("bup", [FC, 32, E, 128])
    din("adS", [128, FC, E, 16]);  din("bdS", [128, 2, D]);  din("pm2", [128, 1])
    d["y"] = nc.dram_tensor("y", [D, NT], F32, kind="ExternalOutput").ap()

    dint("xnb_park", [S, D], BF16)
    dint("xnm_park", [NT, D], BF16)
    dint("xp_park", [D, NT])
    dint("mod_gi", [B, 1536]);  dint("mod_go", [16, 1536])
    dint("att_gi", [HPC * DH, S], BF16);  dint("att_go", [D, S], BF16)

    tapd = {}
    for tname, shape, dt in taps:
        tapd[tname] = nc.dram_tensor("tap_" + tname, shape, dt, kind="ExternalOutput").ap()

    guards = []
    arena_cm = nc.sbuf_tensor("arena", [128, 212800], U8)
    arena_cm.__enter__(); guards.append(arena_cm)
    BASE = 16512
    RA = BASE + 0
    RB = BASE + 49152
    RC = BASE + 81920
    RD = BASE + 116736
    RE = BASE + 141312
    RF = BASE + 157696

    T = {}

    def at(name, shape, dt, off):
        T[name] = nc.alloc_sbuf_tensor_at(name, shape, dt, offset=off)

    at("wmod_sb", [128, KC, 1536], BF16, RA)
    at("g8", [8, 1536], F32, RA)
    at("wqkv_sb", [128, 12, KC, 128], BF16, RA)
    at("attnz", [128, KC, NT], BF16, RA)
    at("rstat", [4, 1024], F32, RA + 40960)
    at("lrow", [1, 2048], F32, RA + 40960)
    at("rl8", [8, NT], F32, RA)
    at("wef", [8, NT], BF16, RA + 8192)
    at("hbase", [128, FC, TBLK], BF16, RA)

    at("gate_h2", [128, KC, NT], BF16, RB)
    at("modrows", [2, 1536], F32, RB)
    at("kv", [128, HPC, 128], F32, RB + 8192)
    at("kvb", [128, HPC, 128], BF16, RB + 10240)
    at("dtt", [128, HPC, 2, SBLK], BF16, RB + 11264)
    at("qdec", [128, HPC, SBLK], BF16, RB + 15360)
    at("kdec", [128, HPC, 2], F32, RB + 17408)

    at("qkvblk", [128, 2, 12, TBLK], BF16, RC)
    at("scanbuf", [128, HPC, 5, SBLK], BF16, RC + 24576)
    at("h1m", [128, KC, NT], BF16, RC)
    at("agp_sb", [128, 4, KC, 128], BF16, RC)
    at("wr_sb", [128, KC, E], BF16, RC + 16384)
    at("xagP", [128, 2, NT], BF16, RC + 16672)
    at("xauP", [128, 2, NT], BF16, RC + 20768)
    at("hexp", [128, 2, TBLK], BF16, RC + 24864)
    at("sgb", [128, 2, TBLK], BF16, RC + 26912)
    at("sgbb", [128, TBLK], BF16, RC + 28960)
    at("tadd", [128, 2, TBLK], BF16, RC + 29984)
    at("usb", [128, 2, TBLK], BF16, RC + 32032)
    at("wexpE", [128, NT], BF16, RA + 47104)
    at("gbb", [128, TBLK], BF16, RC)
    at("hAwE", [128, TBLK], BF16, RA + 45056)
    at("hAwO", [128, TBLK], BF16, RA + 46080)

    at("xstage", [128, 3, D], F32, RD)
    at("wcolA", [128, 2, KC, 128], BF16, RD)
    at("wcolB", [128, 2, KC, 128], BF16, RD + 8192)
    at("bgp_sb", [32, 2, E, 128], BF16, RD + 16384)
    at("bup_sb", [32, 2, E, 128], BF16, RD + 20480)
    at("wdcol", [128, 2, FC, 128], BF16, RD)

    at("hblk", [128, 2, KC, TBLK], BF16, RE)
    at("xat", [32, 2 * E, TBLK], BF16, RE)

    off = RF
    def fx(name, shape, dt):
        nonlocal off
        n = int(np.prod(shape[1:]))
        sz = n * (4 if dt == F32 else (2 if dt in (BF16,) else 4))
        at(name, shape, dt, off)
        off += (sz + 31) // 32 * 32

    fx("resid", [128, 2, NT], F32)
    fx("smallb", [128, 3072], BF16)
    fx("small", [128, 784], F32)
    fx("rbc", [128, NT], BF16)
    fx("adS_sb", [128, FC, E, 16], BF16)
    fx("bdS_sb", [128, 2, D], BF16)
    fx("c2_sb", [128, KC, B], F32)
    fx("c2sil", [128, KC, B], BF16)
    fx("modc", [128, 96], F32)
    fx("modd", [128, 96], F32)
    fx("stats", [128, 64], F32)
    fx("statm", [128, 16], F32)
    fx("stat6", [128, 48], F32)
    fx("isc", [128, 64], F32)
    fx("iscm", [128, 16], F32)
    fx("wexpO", [128, NT], BF16)
    fx("ubb", [128, TBLK], BF16)
    fx("consts", [128, 8], F32)
    fx("nw_sb", [128, KC], F32)
    fx("rt", [128, 8, E], F32)
    fx("rsc", [128, 8, E], F32)
    fx("rcol", [128, 8, 4], F32)
    fx("ident", [128, 128], BF16)
    fx("identf", [128, 128], F32)
    fx("onesb", [1, 128], BF16)
    fx("onecol", [128, 1], BF16)
    fx("slop", [1, 8], F32)
    fx("pm2c", [128, 1], F32)
    fx("slopc", [128, 8], F32)
    fx("idx_m", [8, 1], U32)
    fx("idx_a", [128, KC], U32)
    fx("exp16", [8, 2, 128], BF16)
    assert off <= 229344, off

    sm = T["small"]

    psl = []
    for i in range(8):
        cm = nc.psum_tensor("psb%d" % i, [128, 512], F32)
        psl.append(cm.__enter__()); guards.append(cm)

    semnames = ["Wld", "Xld", "Park", "Trp", "Bnc", "Gth", "CC", "Tmm", "Sev",
                "Vev", "Gev", "Out", "Hfree", "Qfree", "PkA", "PkB", "WlA", "WlB",
                "WdA", "WdB", "XsA", "XsB", "XlA", "XlB", "TrA", "TrB", "Wq", "Wst"]
    sems = {}
    for name in semnames:
        cm = nc.semaphore(name)
        sems[name] = cm.__enter__(); guards.append(cm)

    def walk(mk: MK):
        emit, wait, drain = mk.emit, mk.wait, mk.drain
        P = psl
        pstr = psl[6][:, :].bitcast(BF16)[:, 0:512]

        def tap(name, src_fn, after=None):
            if name in tapd:
                if after:
                    wait("sync", after[0], after[1])
                emit("sync", lambda e: e.dma_start(out=tapd[name], in_=src_fn()), inc=("Out", 16))

        emit("gpsimd", lambda g: g.memset(T["consts"][:, 0:1], EPS_LN))
        emit("gpsimd", lambda g: g.memset(T["consts"][:, 1:2], -0.5))
        emit("gpsimd", lambda g: g.memset(T["consts"][:, 2:3], EPS_RMS))
        emit("gpsimd", lambda g: g.memset(T["consts"][:, 3:4], 1.0 / 2048.0))
        emit("gpsimd", lambda g: g.memset(T["onesb"][:, :], 1.0))
        emit("gpsimd", lambda g: g.memset(T["onecol"][:, :], 1.0))
        emit("gpsimd", lambda g: g.memset(T["ident"][:, :], 1.0))
        drain("gpsimd")
        emit("gpsimd", lambda g: g.affine_select(
            T["ident"][:, :], T["ident"][:, :], compare_op=ALU.is_equal, fill=0.0,
            pattern=[[1, 128]], base=0, channel_multiplier=-1))
        emit("gpsimd", lambda g: g.memset(T["identf"][:, :], 1.0))
        drain("gpsimd")
        emit("gpsimd", lambda g: g.affine_select(
            T["identf"][:, :], T["identf"][:, :], compare_op=ALU.is_equal, fill=0.0,
            pattern=[[1, 128]], base=0, channel_multiplier=-1))
        emit("gpsimd", lambda g: g.memset(T["exp16"][:, :, :], 2.0))
        drain("gpsimd")
        emit("gpsimd", lambda g: g.affine_select(
            T["exp16"][:, 0, :], T["exp16"][:, 0, :], compare_op=ALU.is_equal, fill=0.0,
            pattern=[[2, 4], [0, 32]], base=0, channel_multiplier=-1))
        drain("gpsimd")
        emit("gpsimd", lambda g: g.affine_select(
            T["exp16"][:, 1, :], T["exp16"][:, 1, :], compare_op=ALU.is_equal, fill=0.0,
            pattern=[[2, 4], [0, 32]], base=1, channel_multiplier=-1))
        drain("gpsimd")
        emit("gpsimd", lambda g: g.nop(), inc=("Gev", 1))

        emit("sync", lambda e: e.dma_start(out=T["slop"][:, 0:HPC], in_=d["slopes"]), inc=("Xld", 16))
        emit("sync", lambda e: e.dma_start(out=T["pm2c"][:, :], in_=d["pm2"]), inc=("Xld", 16))
        emit("sync", lambda e: e.dma_start(out=T["idx_m"][:, :], in_=d["modidx"]), inc=("Xld", 16))
        emit("sync", lambda e: e.dma_start(out=T["idx_a"][:, :], in_=d["aidx"]), inc=("Xld", 16))
        emit("sync", lambda e: e.dma_start(out=T["nw_sb"][:, :], in_=d["nw"]), inc=("Xld", 16))
        emit("sync", lambda e: e.dma_start(
            out=T["c2_sb"][:, :, :], in_=d["c2"]), inc=("Xld", 16))
        XLD0 = mk.cnt["Xld"]
        emit("gpsimd", lambda g: g.dma_start(
            out=T["wmod_sb"][:, :, :], in_=d["wmod_sl"]),
            inc=("Wst", 16))
        WMOD_W = mk.cnt["Wst"]

        wait("scalar", "Xld", XLD0)
        emit("scalar", lambda s: s.activation(T["c2sil"][:, :, :], T["c2_sb"][:, :, :], AF.Silu),
             inc=("Sev", 1))
        C2S_S = mk.cnt["Sev"]
        wait("tensor", "Sev", C2S_S)
        wait("tensor", "Wst", WMOD_W)
        for n0 in range(3):
            def modmm(e, n0=n0):
                for k in range(KC):
                    last = e.matmul(P[n0][0:B, :], T["c2sil"][:, k, :],
                             T["wmod_sb"][:, k, n0 * 512:(n0 + 1) * 512],
                             start=(k == 0), stop=(k == KC - 1))
                return last
            emit("tensor", modmm, inc=("Tmm", 1))
        wait("vector", "Tmm", mk.cnt["Tmm"])
        for n0 in range(3):
            emit("vector", lambda v, n0=n0: v.tensor_copy(
                T["modrows"][0:B, n0 * 512:(n0 + 1) * 512], P[n0][0:B, :]))
        drain("vector")
        emit("vector", lambda v: v.nop(), inc=("Vev", 1))
        wait("sync", "Vev", mk.cnt["Vev"])
        emit("sync", lambda e: e.dma_start(out=d["mod_gi"], in_=T["modrows"][0:B, 0:1536]), inc=("Park", 16))
        MODGI_P = mk.cnt["Park"]
        wait("gpsimd", "Park", MODGI_P)
        emit("gpsimd", lambda g: g.collective_compute(
            "AllGather", ALU.bypass, replica_groups=[list(range(8))],
            ins=[d["mod_gi"].opt()], outs=[d["mod_go"].opt()]), inc=("CC", 1))
        wait("gpsimd", "CC", 1)
        wait("gpsimd", "Xld", XLD0)
        emit("gpsimd", lambda g: g.indirect_dma_start(
            out=T["g8"][:, :], out_offset=None, in_=d["mod_go"],
            in_offset=bass.IndirectOffsetOnAxis(ap=T["idx_m"][:, :], axis=0)), inc=("Gth", 16))
        MODG_G = mk.cnt["Gth"]
        wait("tensor", "Gth", MODG_G)
        for j in range(12):
            emit("tensor", lambda e, j=j: e.transpose(
                P[1][:, j * 8:(j + 1) * 8], T["g8"][:, j * 128:(j + 1) * 128],
                T["identf"][0:8, 0:8]), inc=("Tmm", 1))
        MODTR_T = mk.cnt["Tmm"]
        wait("vector", "Tmm", MODTR_T)
        for j in range(12):
            emit("vector", lambda v, j=j: v.tensor_copy(
                T["modc"][:, j:96:12], P[1][:, j * 8:(j + 1) * 8]))
        drain("vector")
        emit("vector", lambda v: v.tensor_copy(T["modd"][:, 0:16], T["modc"][:, 0:16]))
        emit("vector", lambda v: v.tensor_scalar(T["modd"][:, 16:32], T["modc"][:, 16:32],
                                                 1.0, None, ALU.add))
        emit("vector", lambda v: v.tensor_copy(T["modd"][:, 48:64], T["modc"][:, 48:64]))
        emit("vector", lambda v: v.tensor_scalar(T["modd"][:, 64:80], T["modc"][:, 64:80],
                                                 1.0, None, ALU.add))
        drain("vector")
        emit("vector", lambda v: v.nop(), inc=("Vev", 1))
        MODC_V = mk.cnt["Vev"]
        wait("scalar", "Vev", MODC_V)
        emit("scalar", lambda s: s.sign(T["isc"][:, 0:16], T["modc"][:, 32:48]))
        emit("scalar", lambda s: s.sign(T["isc"][:, 16:32], T["modc"][:, 80:96]))
        drain("scalar")
        emit("scalar", lambda s: s.nop(), inc=("Sev", 1))
        wait("vector", "Sev", mk.cnt["Sev"])
        emit("vector", lambda v: v.tensor_tensor(T["isc"][:, 32:48], T["modc"][:, 32:48],
                                                 T["isc"][:, 0:16], ALU.mult))
        emit("vector", lambda v: v.tensor_tensor(T["isc"][:, 48:64], T["modc"][:, 80:96],
                                                 T["isc"][:, 16:32], ALU.mult))
        drain("vector")
        emit("vector", lambda v: v.tensor_scalar(T["isc"][:, 32:48], T["isc"][:, 32:48],
                                                 0.25, None, ALU.max))
        emit("vector", lambda v: v.tensor_scalar(T["isc"][:, 48:64], T["isc"][:, 48:64],
                                                 0.25, None, ALU.max))
        drain("vector")
        emit("vector", lambda v: v.tensor_tensor(T["modd"][:, 32:48], T["isc"][:, 32:48],
                                                 T["isc"][:, 0:16], ALU.mult))
        emit("vector", lambda v: v.tensor_tensor(T["modd"][:, 80:96], T["isc"][:, 48:64],
                                                 T["isc"][:, 16:32], ALU.mult))
        drain("vector")
        emit("vector", lambda v: v.nop(), inc=("Vev", 1))
        MODD_V = mk.cnt["Vev"]
        tap("modd", lambda: T["modd"][:, :], after=("Vev" if mk.me == "sync" else "Vev", MODD_V))

        if STOP_AFTER == "P1":
            return
        wait("vector", "Xld", XLD0)
        emit("vector", lambda v: v.tensor_scalar(T["slop"][:, 4:8], T["slop"][:, 0:4],
                                                 -1.0, None, ALU.mult))
        drain("vector")
        emit("vector", lambda v: v.nop(), inc=("Vev", 1))
        NEGS_V = mk.cnt["Vev"]
        wait("scalar", "Vev", NEGS_V)
        emit("scalar", lambda s: s.copy(T["smallb"][0:1, 0:8], T["slop"][0:1, 0:8]))
        drain("scalar")
        emit("scalar", lambda s: s.nop(), inc=("Sev", 1))
        wait("tensor", "Sev", mk.cnt["Sev"])
        wait("tensor", "Gev", 1)
        emit("tensor", lambda e: e.matmul(P[3][:, 0:8], T["onesb"][0:1, :],
                                          T["smallb"][0:1, 0:8], start=True, stop=True),
             inc=("Tmm", 1))
        wait("vector", "Tmm", mk.cnt["Tmm"])
        emit("vector", lambda v: v.tensor_copy(T["slopc"][:, :], P[3][:, 0:8]))
        drain("vector")
        emit("vector", lambda v: v.nop(), inc=("Vev", 1))
        SLOPC_V = mk.cnt["Vev"]
        if STOP_AFTER == "P2a":
            return
        wait("gpsimd", "Park", MODGI_P)
        emit("gpsimd", lambda g: g.iota(sm[0:1, 0:SBLK], pattern=[[1, SBLK]], base=1,
                                        channel_multiplier=0, allow_small_or_imprecise_dtypes=True))
        emit("gpsimd", lambda g: g.iota(sm[:, SBLK:SBLK + 2], pattern=[[-128, 2]], base=255,
                                        channel_multiplier=-1, allow_small_or_imprecise_dtypes=True))
        drain("gpsimd")
        emit("gpsimd", lambda g: g.nop(), inc=("Gev", 1))
        IOTA1_G = mk.cnt["Gev"]
        wait("scalar", "Gev", IOTA1_G)
        wait("scalar", "Vev", SLOPC_V)
        for hh in range(HPC):
            emit("scalar", lambda s, hh=hh: s.activation(
                T["smallb"][0:1, hh * SBLK:(hh + 1) * SBLK], sm[0:1, 0:SBLK],
                AF.Exp, bias=0.0, scale=T["slop"][0:1, 4 + hh:5 + hh]))
            emit("scalar", lambda s, hh=hh: s.activation(
                T["kdec"][:, hh, :], sm[:, SBLK:SBLK + 2],
                AF.Exp, bias=0.0, scale=T["slopc"][:, 4 + hh:5 + hh]))
        drain("scalar")
        emit("scalar", lambda s: s.nop(), inc=("Sev", 1))
        QDR_S = mk.cnt["Sev"]
        if STOP_AFTER == "P2b":
            return
        wait("tensor", "Sev", QDR_S)
        wait("tensor", "Gev", 1)
        QDEC_BASE_V = mk.cnt["Vev"]
        for hh in range(HPC):
            if hh >= 1:
                wait("tensor", "Vev", QDEC_BASE_V + hh)
            emit("tensor", lambda e, hh=hh: e.matmul(
                P[2][:, (hh % 2) * SBLK:(hh % 2 + 1) * SBLK], T["onesb"][0:1, :],
                T["smallb"][0:1, hh * SBLK:(hh + 1) * SBLK], start=True, stop=True),
                inc=("Tmm", 1))
            wait("vector", "Tmm", mk.cnt["Tmm"])
            emit("vector", lambda v, hh=hh: v.tensor_copy(
                T["qdec"][:, hh, :], P[2][:, (hh % 2) * SBLK:(hh % 2 + 1) * SBLK]))
            drain("vector")
            emit("vector", lambda v: v.nop(), inc=("Vev", 1))
        QDEC_V = mk.cnt["Vev"]
        if STOP_AFTER == "P2c":
            return
        for half in range(2):
            wait("gpsimd", "Sev", mk.cnt["Sev"])
            emit("gpsimd", lambda g, half=half: g.iota(
                sm[:, 0:SBLK], pattern=[[1, SBLK]], base=-128 * half,
                channel_multiplier=-1, allow_small_or_imprecise_dtypes=True))
            drain("gpsimd")
            emit("gpsimd", lambda g: g.nop(), inc=("Gev", 1))
            iota_g = mk.cnt["Gev"]
            for hh in range(HPC):
                wait("scalar", "Gev", iota_g)
                wait("scalar", "Gev", mk.cnt["Gev"])
                emit("scalar", lambda s, hh=hh: s.activation(
                    sm[:, 512:512 + SBLK], sm[:, 0:SBLK], AF.Exp, bias=0.0,
                    scale=T["slopc"][:, 4 + hh:5 + hh]))
                drain("scalar")
                emit("scalar", lambda s: s.nop(), inc=("Sev", 1))
                wait("gpsimd", "Sev", mk.cnt["Sev"])
                emit("gpsimd", lambda g, hh=hh, half=half: g.affine_select(
                    T["dtt"][:, hh, half, :], sm[:, 512:512 + SBLK],
                    compare_op=ALU.is_ge, fill=0.0, pattern=[[1, SBLK]],
                    base=-128 * half, channel_multiplier=-1))
                drain("gpsimd")
                emit("gpsimd", lambda g: g.nop(), inc=("Gev", 1))
        DTT_G = mk.cnt["Gev"]
        if STOP_AFTER == "P2d":
            return
        emit("scalar", lambda s: s.activation(sm[0:1, 776:780], T["slop"][0:1, 4:8],
                                              AF.Exp, bias=0.0, scale=256.0))
        drain("scalar")
        emit("scalar", lambda s: s.copy(T["smallb"][0:1, 1024:1028], sm[0:1, 776:780]))
        drain("scalar")
        emit("scalar", lambda s: s.nop(), inc=("Sev", 1))
        wait("tensor", "Sev", mk.cnt["Sev"])
        wait("tensor", "Vev", QDEC_V)
        emit("tensor", lambda e: e.matmul(P[2][:, 0:4], T["onesb"][0:1, :],
                                          T["smallb"][0:1, 1024:1028], start=True, stop=True),
             inc=("Tmm", 1))
        wait("vector", "Tmm", mk.cnt["Tmm"])
        emit("vector", lambda v: v.tensor_copy(sm[:, 772:776], P[2][:, 0:4]))
        drain("vector")
        emit("vector", lambda v: v.memset(T["kv"][:, :, :], 0.0))
        emit("vector", lambda v: v.memset(T["kvb"][:, :, :], 0.0))
        drain("vector")
        emit("vector", lambda v: v.nop(), inc=("Vev", 1))
        BD_V = mk.cnt["Vev"]

        if STOP_AFTER == "P2":
            return
        wait("gpsimd", "Tmm", MODTR_T)
        emit("gpsimd", lambda g: g.dma_start(
            out=T["wqkv_sb"][:, :, :, :],
            in_=d["wqkv_T"]), inc=("Wq", 16))
        WQKV_W = mk.cnt["Wq"]

        xn_sev = []
        xs_sems = ["XsA", "XsB", "TrA"]
        stat_v = []
        all_tiles = [(d["x_b"], tt, T["stats"]) for tt in range(32)] + \
                    [(d["x_m"], tt, T["statm"]) for tt in range(8)]
        for gi, (src_t, tt, statcol) in enumerate(all_tiles):
            buf = gi % 3
            xsem = xs_sems[buf]
            if gi >= 3:
                wait("sync", "Vev", stat_v[gi - 3])
            emit("sync", lambda e, src_t=src_t, tt=tt, buf=buf: e.dma_start(
                out=T["xstage"][:, buf, :], in_=src_t[tt * 128:(tt + 1) * 128, :]),
                inc=(xsem, 16))
            wait("vector", xsem, mk.cnt[xsem])
            s6 = (gi % 2) * 24
            for pc in range(4):
                emit("vector", lambda v, buf=buf, pc=pc, s6=s6: v.bn_stats(
                    T["stat6"][:, s6 + pc * 6:s6 + (pc + 1) * 6],
                    T["xstage"][:, buf, pc * 512:(pc + 1) * 512]))
            drain("vector")
            emit("vector", lambda v, tt=tt, statcol=statcol, s6=s6: v.bn_aggr(
                statcol[:, 2 * tt:2 * tt + 2], T["stat6"][:, s6:s6 + 24]), inc=("Vev", 1))
            stat_v.append(mk.cnt["Vev"])
        STATS_V = mk.cnt["Vev"]
        wait("scalar", "Vev", STATS_V)
        wait("scalar", "Gev", 1)
        for statcol, n in ((T["stats"], 32), (T["statm"], 8)):
            dst = T["isc"] if n == 32 else T["iscm"]
            emit("scalar", lambda s, statcol=statcol, dst=dst, n=n: s.activation(
                dst[:, 0:2 * n:2], statcol[:, 1:2 * n:2],
                AF.Ln, bias=T["consts"][:, 0:1], scale=1.0))
            drain("scalar")
            emit("scalar", lambda s, dst=dst, n=n: s.activation(
                dst[:, 0:2 * n:2], dst[:, 0:2 * n:2],
                AF.Exp, bias=0.0, scale=T["consts"][:, 1:2]))
            drain("scalar")
        emit("scalar", lambda s: s.nop(), inc=("Sev", 1))
        wait("vector", "Sev", mk.cnt["Sev"])
        for statcol, n in ((T["stats"], 32), (T["statm"], 8)):
            dst = T["isc"] if n == 32 else T["iscm"]
            emit("vector", lambda v, statcol=statcol, dst=dst, n=n: v.tensor_tensor(
                dst[:, 1:2 * n:2], statcol[:, 0:2 * n:2], dst[:, 0:2 * n:2], ALU.mult))
            drain("vector")
            emit("vector", lambda v, dst=dst, n=n: v.tensor_scalar(
                dst[:, 1:2 * n:2], dst[:, 1:2 * n:2], -1.0, None, ALU.mult))
            drain("vector")
        emit("vector", lambda v: v.nop(), inc=("Vev", 1))
        ISTD_V = mk.cnt["Vev"]
        wait("scalar", "Vev", ISTD_V)
        for gi, (src_t, tt, statcol) in enumerate(all_tiles):
            buf = gi % 3
            xsem = xs_sems[buf]
            if gi >= 3:
                wait("sync", "Sev", xn_sev[gi - 3])
            else:
                wait("sync", "Vev", STATS_V)
            emit("sync", lambda e, src_t=src_t, tt=tt, buf=buf: e.dma_start(
                out=T["xstage"][:, buf, :], in_=src_t[tt * 128:(tt + 1) * 128, :]),
                inc=(xsem, 16))
            isc = T["isc"] if gi < 32 else T["iscm"]
            hb = gi % 2
            pk_sem = "PkA" if hb == 0 else "PkB"
            wait("scalar", xsem, mk.cnt[xsem])
            if gi >= 2:
                wait("scalar", pk_sem, mk.cnt[pk_sem])
            emit("scalar", lambda s, tt=tt, buf=buf, hb=hb, isc=isc: s.activation(
                T["hblk"][:, hb, :, :].rearrange("p a b -> p (a b)")[:, 0:D],
                T["xstage"][:, buf, :], AF.Identity,
                bias=isc[:, 2 * tt + 1:2 * tt + 2],
                scale=isc[:, 2 * tt:2 * tt + 1]), inc=("Sev", 1))
            xn_sev.append(mk.cnt["Sev"])
            park = d["xnb_park"] if gi < 32 else d["xnm_park"]
            wait("sync", "Sev", mk.cnt["Sev"])
            emit("sync", lambda e, tt=tt, hb=hb, park=park: e.dma_start(
                out=park[tt * 128:(tt + 1) * 128, :],
                in_=T["hblk"][:, hb, :, :].rearrange("p a b -> p (a b)")[:, 0:D]),
                inc=(pk_sem, 16))
        PKA_ALL = mk.cnt["PkA"]
        PKB_ALL = mk.cnt["PkB"]

        if STOP_AFTER == "P3":
            return
        hv = [0, 0]
        qkv_sev = [0] * NTB
        scan_state = dict(tr_evac_v=0, mask_v=0, kvupd_v=0)
        att_evac = {3: 0, 4: 0}
        kvb_sev = [0] * HPC
        bank_evac_s = [0, 0]

        for tb in range(NTB):
            buf = tb % 2
            tr_sem = "TrA" if buf == 0 else "TrB"
            wait("sync", "PkA", PKA_ALL)
            wait("sync", "PkB", PKB_ALL)
            if tb >= 2:
                wait("sync", "Hfree", tb - 1)
            wait("sync", "Bnc", mk.cnt["Bnc"])
            emit("sync", lambda e, tb=tb, buf=buf: e.dma_start_transpose(
                T["hblk"][:, buf, :, :], d["xnb_park"][tb * TBLK:(tb + 1) * TBLK, :]),
                inc=(tr_sem, 16))
            wait("vector", tr_sem, mk.cnt[tr_sem])
            wait("vector", "Vev", MODD_V)
            for k in range(KC):
                emit("vector", lambda v, buf=buf, k=k: v.tensor_scalar(
                    T["hblk"][:, buf, k, :], T["hblk"][:, buf, k, :],
                    T["modd"][:, 16 + k:17 + k], T["modd"][:, 0 + k:1 + k], ALU.mult, ALU.add),
                    inc=("Vev", 1) if k == KC - 1 else None)
            hv[buf] = mk.cnt["Vev"]

            wait("tensor", "Vev", hv[buf])
            wait("tensor", "Wq", WQKV_W)
            qbuf = tb % 2
            if tb >= 2:
                wait("tensor", "Qfree", tb - 1)
            for m in range(12):
                bk = m % 2
                wait("tensor", "Sev", bank_evac_s[bk])
                def qmm(e, m=m, buf=buf, bk=bk):
                    for k in range(KC):
                        last = e.matmul(P[bk][:, :], T["wqkv_sb"][:, m, k, :],
                                 T["hblk"][:, buf, k, :], start=(k == 0), stop=(k == KC - 1))
                    return last
                emit("tensor", qmm, inc=("Tmm", 1))
                wait("scalar", "Tmm", mk.cnt["Tmm"])
                emit("scalar", lambda s, m=m, qbuf=qbuf, bk=bk: s.activation(
                    T["qkvblk"][:, qbuf, m, :], P[bk][:, :], AF.Silu), inc=("Sev", 1))
                bank_evac_s[bk] = mk.cnt["Sev"]
            qkv_sev[tb] = mk.cnt["Sev"]
            emit("tensor", lambda e: e.nop(), inc=("Hfree", 1))

            for jj in range(2):
                j = 2 * tb + jj
                for hh in range(HPC):
                    wait("tensor", "Sev", qkv_sev[tb])
                    wait("tensor", "Vev", scan_state["tr_evac_v"])
                    def tr4(e, hh=hh, jj=jj, qbuf=qbuf):
                        for half in range(2):
                            last = e.transpose(pstr[:, half * 128:(half + 1) * 128],
                                        T["qkvblk"][:, qbuf, 4 + hh,
                                                    jj * SBLK + half * 128:jj * SBLK + (half + 1) * 128],
                                        T["ident"][:, :])
                        for half in range(2):
                            last = e.transpose(pstr[:, 256 + half * 128:256 + (half + 1) * 128],
                                        T["qkvblk"][:, qbuf, 8 + hh,
                                                    jj * SBLK + half * 128:jj * SBLK + (half + 1) * 128],
                                        T["ident"][:, :])
                        return last
                    emit("tensor", tr4, inc=("Tmm", 1))
                    wait("vector", "Tmm", mk.cnt["Tmm"])
                    wait("vector", "Vev", QDEC_V)
                    for half in range(2):
                        emit("vector", lambda v, hh=hh, half=half: v.tensor_scalar(
                            T["scanbuf"][:, hh, 2, half * 128:(half + 1) * 128],
                            pstr[:, half * 128:(half + 1) * 128],
                            T["kdec"][:, hh, half:half + 1], None, ALU.mult))
                    for half in range(2):
                        emit("vector", lambda v, hh=hh, half=half: v.tensor_copy(
                            T["scanbuf"][:, hh, 3, half * 128:(half + 1) * 128],
                            pstr[:, 256 + half * 128:256 + (half + 1) * 128]))
                    emit("vector", lambda v, hh=hh, jj=jj, qbuf=qbuf: v.tensor_tensor(
                        T["scanbuf"][:, hh, 4, :],
                        T["qkvblk"][:, qbuf, hh, jj * SBLK:(jj + 1) * SBLK],
                        T["qdec"][:, hh, :], ALU.mult), inc=("Vev", 1))
                    scan_state["tr_evac_v"] = mk.cnt["Vev"]
                    wait("tensor", "Vev", scan_state["tr_evac_v"])
                    wait("tensor", "Vev", scan_state["mask_v"])
                    def smm(e, hh=hh, jj=jj, qbuf=qbuf):
                        for half in range(2):
                            last = e.matmul(P[2][:, half * SBLK:(half + 1) * SBLK],
                                     T["qkvblk"][:, qbuf, 4 + hh,
                                                 jj * SBLK + half * 128:jj * SBLK + (half + 1) * 128],
                                     T["qkvblk"][:, qbuf, hh, jj * SBLK:(jj + 1) * SBLK],
                                     start=True, stop=True)
                        return last
                    emit("tensor", smm, inc=("Tmm", 1))
                    wait("vector", "Tmm", mk.cnt["Tmm"])
                    wait("vector", "Gev", DTT_G)
                    for half in range(2):
                        ii = emit("vector", lambda v, hh=hh, half=half: v.tensor_tensor(
                            T["scanbuf"][:, hh, half, :], P[2][:, half * SBLK:(half + 1) * SBLK],
                            T["dtt"][:, hh, half, :], ALU.mult),
                            inc=("Vev", 1) if half == 1 else None)
                    scan_state["mask_v"] = mk.cnt["Vev"]
                    abank = 3 + (hh % 2)
                    wait("tensor", "Vev", scan_state["mask_v"])
                    wait("tensor", "Sev", kvb_sev[hh])
                    wait("tensor", "Sev", att_evac[abank])
                    def amm(e, hh=hh, abank=abank):
                        last = e.matmul(P[abank][:, 0:SBLK], T["kvb"][:, hh, :],
                                 T["scanbuf"][:, hh, 4, :], start=True, stop=False)
                        for half in range(2):
                            last = e.matmul(P[abank][:, 0:SBLK],
                                     T["scanbuf"][:, hh, 3, half * 128:(half + 1) * 128],
                                     T["scanbuf"][:, hh, half, :],
                                     start=False, stop=(half == 1))
                        return last
                    emit("tensor", amm, inc=("Tmm", 1))
                    att_t = mk.cnt["Tmm"]
                    wait("tensor", "Vev", scan_state["kvupd_v"])
                    def kmm(e, hh=hh):
                        for half in range(2):
                            last = e.matmul(P[5][:, 0:128],
                                     T["scanbuf"][:, hh, 2, half * 128:(half + 1) * 128],
                                     T["scanbuf"][:, hh, 3, half * 128:(half + 1) * 128],
                                     start=(half == 0), stop=(half == 1))
                        return last
                    emit("tensor", kmm, inc=("Tmm", 1))
                    kv_t = mk.cnt["Tmm"]
                    wait("scalar", "Tmm", att_t)
                    wait("scalar", "Bnc", mk.cnt["Bnc"])
                    emit("scalar", lambda s, hh=hh, abank=abank: s.activation(
                        T["smallb"][:, 2048 + hh * SBLK:2048 + (hh + 1) * SBLK],
                        P[abank][:, 0:SBLK], AF.Copy), inc=("Sev", 1))
                    att_evac[abank] = mk.cnt["Sev"]
                    wait("sync", "Sev", att_evac[abank])
                    emit("sync", lambda e, hh=hh, j=j: e.dma_start(
                        out=d["att_gi"][hh * 128:(hh + 1) * 128, j * SBLK:(j + 1) * SBLK],
                        in_=T["smallb"][:, 2048 + hh * SBLK:2048 + (hh + 1) * SBLK]),
                        inc=("Bnc", 16))
                    wait("vector", "Tmm", kv_t)
                    wait("vector", "Vev", BD_V)
                    emit("vector", lambda v, hh=hh: v.scalar_tensor_tensor(
                        T["kv"][:, hh, :], T["kv"][:, hh, :], sm[:, 772 + hh:773 + hh],
                        P[5][:, 0:128], ALU.mult, ALU.add), inc=("Vev", 1))
                    scan_state["kvupd_v"] = mk.cnt["Vev"]
                    wait("scalar", "Vev", scan_state["kvupd_v"])
                    emit("scalar", lambda s, hh=hh: s.copy(T["kvb"][:, hh, :], T["kv"][:, hh, :]),
                         inc=("Sev", 1))
                    kvb_sev[hh] = mk.cnt["Sev"]
            emit("vector", lambda v: v.nop(), inc=("Qfree", 1))
        BNC_ALL = mk.cnt["Bnc"]

        if STOP_AFTER == "P4":
            return
        wait("sync", "Tmm", mk.cnt["Tmm"])
        wait("sync", "Vev", mk.cnt["Vev"])
        wait("sync", "Sev", mk.cnt["Sev"])
        wait("sync", "Bnc", mk.cnt["Bnc"])
        for half in range(2):
            emit("sync", lambda e, half=half: e.dma_start_transpose(
                T["h1m"][:, :, half * TBLK:(half + 1) * TBLK],
                d["xnm_park"][half * TBLK:(half + 1) * TBLK, :]), inc=("Trp", 16))
        wait("vector", "Trp", mk.cnt["Trp"])
        for k in range(KC):
            emit("vector", lambda v, k=k: v.tensor_scalar(
                T["h1m"][:, k, :], T["h1m"][:, k, :],
                T["modd"][:, 16 + k:17 + k], T["modd"][:, 0 + k:1 + k], ALU.mult, ALU.add),
                inc=("Vev", 1) if k == KC - 1 else None)
        H1M_V = mk.cnt["Vev"]

        wait("gpsimd", "Bnc", BNC_ALL)
        emit("gpsimd", lambda g: g.collective_compute(
            "AllGather", ALU.bypass, replica_groups=[[0, 1, 2, 3], [4, 5, 6, 7]],
            ins=[d["att_gi"].opt()], outs=[d["att_go"].opt()]), inc=("CC", 1))
        ATTCC = mk.cnt["CC"]

        wait("gpsimd", "Sev", xn_sev[-1])
        emit("gpsimd", lambda g: g.dma_start(
            out=T["wcolA"][:, 0, :, :], in_=d["wg_T"][0]),
            inc=("WlA", 16))
        gate_evac = [0, 0]
        gate_t = []
        for m in range(KC):
            if m + 1 < KC:
                if m >= 1:
                    wait("gpsimd", "Tmm", gate_t[m - 1])
                wl = "WlA" if (m + 1) % 2 == 0 else "WlB"
                emit("gpsimd", lambda g, m=m: g.dma_start(
                    out=T["wcolA"][:, (m + 1) % 2, :, :],
                    in_=d["wg_T"][m + 1]), inc=(wl, 16))
            wl_m = "WlA" if m % 2 == 0 else "WlB"
            wait("tensor", wl_m, mk.cnt[wl_m])
            wait("tensor", "Vev", H1M_V)
            wait("tensor", "Sev", gate_evac[m % 2])
            def gmm(e, m=m):
                for half in range(2):
                    for k in range(KC):
                        last = e.matmul(P[2 * (m % 2) + half][:, :], T["wcolA"][:, m % 2, k, :],
                                 T["h1m"][:, k, half * TBLK:(half + 1) * TBLK],
                                 start=(k == 0), stop=(k == KC - 1))
                return last
            emit("tensor", gmm, inc=("Tmm", 1))
            gate_t.append(mk.cnt["Tmm"])
            wait("scalar", "Tmm", mk.cnt["Tmm"])
            for half in range(2):
                emit("scalar", lambda s, m=m, half=half: s.activation(
                    T["gate_h2"][:, m, half * TBLK:(half + 1) * TBLK],
                    P[2 * (m % 2) + half][:, :], AF.Sigmoid))
            drain("scalar")
            emit("scalar", lambda s: s.nop(), inc=("Sev", 1))
            gate_evac[m % 2] = mk.cnt["Sev"]
        GATE_S = mk.cnt["Sev"]
        tap("gate", lambda: T["gate_h2"][:, :, :], after=("Sev", GATE_S))

        if STOP_AFTER == "P5":
            return
        wait("gpsimd", "CC", ATTCC)
        for k in range(KC):
            emit("gpsimd", lambda g, k=k: g.indirect_dma_start(
                out=T["attnz"][:, k, :], out_offset=None,
                in_=d["att_go"].rearrange("a (q c) -> (a q) c", c=NT),
                in_offset=bass.IndirectOffsetOnAxis(ap=T["idx_a"][:, k:k + 1], axis=0)),
                inc=("Gth", 16))
        GTH_A = mk.cnt["Gth"]
        tap("attng", lambda: T["attnz"][:, :, :], after=("Gth", GTH_A))

        rms_t = 0
        for k in range(KC):
            wait("vector", "Gth", GTH_A)
            if k > 0:
                wait("vector", "Tmm", rms_t)
            emit("vector", lambda v, k=k: v.tensor_tensor(
                T["smallb"][:, 0:NT], T["attnz"][:, k, :], T["attnz"][:, k, :], ALU.mult),
                inc=("Vev", 1))
            wait("tensor", "Vev", mk.cnt["Vev"])
            def rmm(e, k=k):
                last = e.matmul(P[4][0:1, :], T["onecol"][:, 0:1], T["smallb"][:, 0:512],
                         start=(k == 0), stop=(k == KC - 1))
                last = e.matmul(P[5][0:1, :], T["onecol"][:, 0:1], T["smallb"][:, 512:1024],
                         start=(k == 0), stop=(k == KC - 1))
                return last
            emit("tensor", rmm, inc=("Tmm", 1))
            rms_t = mk.cnt["Tmm"]
        wait("scalar", "Tmm", rms_t)
        for half, bank in ((0, 4), (1, 5)):
            emit("scalar", lambda s, half=half, bank=bank: s.activation(
                T["rstat"][0:1, half * 512:(half + 1) * 512], P[bank][0:1, :],
                AF.Ln, bias=T["consts"][0:1, 2:3], scale=T["consts"][0:1, 3:4]))
        drain("scalar")
        for half in range(2):
            emit("scalar", lambda s, half=half: s.activation(
                T["rstat"][0:1, half * 512:(half + 1) * 512],
                T["rstat"][0:1, half * 512:(half + 1) * 512],
                AF.Exp, bias=0.0, scale=T["consts"][0:1, 1:2]))
        drain("scalar")
        emit("scalar", lambda s: s.copy(T["smallb"][0:1, 0:NT], T["rstat"][0:1, 0:NT]))
        drain("scalar")
        emit("scalar", lambda s: s.nop(), inc=("Sev", 1))
        wait("tensor", "Sev", mk.cnt["Sev"])
        for half in range(2):
            if half == 1:
                wait("tensor", "Vev", mk.cnt["Vev"])
            emit("tensor", lambda e, half=half: e.matmul(
                P[6][:, :], T["onesb"][0:1, :], T["smallb"][0:1, half * 512:(half + 1) * 512],
                start=True, stop=True), inc=("Tmm", 1))
            wait("vector", "Tmm", mk.cnt["Tmm"])
            emit("vector", lambda v, half=half: v.tensor_copy(
                T["rbc"][:, half * 512:(half + 1) * 512], P[6][:, :]))
            drain("vector")
            emit("vector", lambda v: v.nop(), inc=("Vev", 1))
        RBC_V = mk.cnt["Vev"]

        wait("vector", "Sev", GATE_S)
        wait("vector", "Vev", RBC_V)
        for k in range(KC):
            emit("vector", lambda v, k=k: v.scalar_tensor_tensor(
                T["attnz"][:, k, :], T["attnz"][:, k, :], T["nw_sb"][:, k:k + 1],
                T["gate_h2"][:, k, :], ALU.mult, ALU.mult),
                inc=("Vev", 1) if k == KC - 1 else None)
        wait("vector", "Vev", mk.cnt["Vev"])
        for k in range(KC):
            emit("vector", lambda v, k=k: v.tensor_tensor(
                T["attnz"][:, k, :], T["attnz"][:, k, :], T["rbc"][:, :], ALU.mult),
                inc=("Vev", 1) if k == KC - 1 else None)
        Z_V = mk.cnt["Vev"]
        tap("z", lambda: T["attnz"][:, :, :], after=("Vev", Z_V))

        wait("gpsimd", "Tmm", gate_t[-1])
        emit("gpsimd", lambda g: g.dma_start(
            out=T["wcolA"][:, 0, :, :], in_=d["wo_T"][0]),
            inc=("WlA", 16))
        xp_evac = [0, 0]
        xp_done = [0, 0]
        op_t = []
        ln2_t = 0
        for m in range(KC):
            if m + 1 < KC:
                if m >= 1:
                    wait("gpsimd", "Tmm", op_t[m - 1])
                wl = "WlA" if (m + 1) % 2 == 0 else "WlB"
                emit("gpsimd", lambda g, m=m: g.dma_start(
                    out=T["wcolA"][:, (m + 1) % 2, :, :],
                    in_=d["wo_T"][m + 1]), inc=(wl, 16))
            xl_m = "XlA" if m % 2 == 0 else "XlB"
            if m >= 2:
                wait("sync", "Vev", xp_done[m % 2])
                wait("sync", "Park", mk.cnt["Park"])
            emit("sync", lambda e, m=m: e.dma_start(
                out=T["resid"][:, m % 2, :], in_=d["xT_m"][m * 128:(m + 1) * 128, :]),
                inc=(xl_m, 16))
            xld_m = mk.cnt[xl_m]
            wl_m = "WlA" if m % 2 == 0 else "WlB"
            wait("tensor", wl_m, mk.cnt[wl_m])
            wait("tensor", "Vev", Z_V)
            for half in range(2):
                wait("tensor", "Vev", xp_evac[half])
                def omm(e, m=m, half=half):
                    for k in range(KC):
                        last = e.matmul(P[half][:, :], T["wcolA"][:, m % 2, k, :],
                                 T["attnz"][:, k, half * TBLK:(half + 1) * TBLK],
                                 start=(k == 0), stop=(k == KC - 1))
                    return last
                emit("tensor", omm, inc=("Tmm", 1))
                omm_t = mk.cnt["Tmm"]
                wait("vector", "Tmm", omm_t)
                wait("vector", xl_m, xld_m)
                emit("vector", lambda v, m=m, half=half: v.scalar_tensor_tensor(
                    T["resid"][:, m % 2, half * TBLK:(half + 1) * TBLK],
                    P[half][:, :], T["modd"][:, 32 + m:33 + m],
                    T["resid"][:, m % 2, half * TBLK:(half + 1) * TBLK], ALU.mult, ALU.add))
                drain("vector")
                emit("vector", lambda v: v.nop(), inc=("Vev", 1))
                xp_evac[half] = mk.cnt["Vev"]
            emit("vector", lambda v, m=m: v.tensor_copy(
                T["smallb"][:, 0:NT], T["resid"][:, m % 2, :]))
            emit("vector", lambda v, m=m: v.tensor_tensor(
                T["smallb"][:, NT:2 * NT], T["resid"][:, m % 2, :], T["resid"][:, m % 2, :],
                ALU.mult), inc=("Vev", 1))
            xp_done[m % 2] = mk.cnt["Vev"]
            wait("sync", "Vev", xp_done[m % 2])
            emit("sync", lambda e, m=m: e.dma_start(
                out=d["xp_park"][m * 128:(m + 1) * 128, :], in_=T["resid"][:, m % 2, :]),
                inc=("Park", 16))
            wait("tensor", "Vev", xp_done[m % 2])
            def lsmm(e, m=m):
                last = e.matmul(P[2][0:1, :], T["onecol"][:, 0:1], T["smallb"][:, 0:512],
                         start=(m == 0), stop=(m == KC - 1))
                last = e.matmul(P[3][0:1, :], T["onecol"][:, 0:1], T["smallb"][:, 512:1024],
                         start=(m == 0), stop=(m == KC - 1))
                last = e.matmul(P[6][0:1, :], T["onecol"][:, 0:1], T["smallb"][:, NT:NT + 512],
                         start=(m == 0), stop=(m == KC - 1))
                last = e.matmul(P[4][0:1, :], T["onecol"][:, 0:1], T["smallb"][:, NT + 512:2 * NT],
                         start=(m == 0), stop=(m == KC - 1))
                return last
            emit("tensor", lsmm, inc=("Tmm", 1))
            ln2_t = mk.cnt["Tmm"]
            op_t.append(mk.cnt["Tmm"])
        XPARK_ALL = mk.cnt["Park"]

        wait("vector", "Tmm", ln2_t)
        emit("vector", lambda v: v.tensor_scalar(T["lrow"][0:1, 0:512], P[2][0:1, :],
                                                 1.0 / 2048.0, None, ALU.mult))
        emit("vector", lambda v: v.tensor_scalar(T["lrow"][0:1, 512:1024], P[3][0:1, :],
                                                 1.0 / 2048.0, None, ALU.mult))
        emit("vector", lambda v: v.tensor_scalar(T["lrow"][0:1, 1024:1536], P[6][0:1, :],
                                                 1.0 / 2048.0, None, ALU.mult))
        emit("vector", lambda v: v.tensor_scalar(T["lrow"][0:1, 1536:2048], P[4][0:1, :],
                                                 1.0 / 2048.0, None, ALU.mult))
        drain("vector")
        emit("vector", lambda v: v.tensor_tensor(sm[0:1, 0:512], T["lrow"][0:1, 0:512],
                                                 T["lrow"][0:1, 0:512], ALU.mult))
        drain("vector")
        emit("vector", lambda v: v.tensor_tensor(T["lrow"][0:1, 1024:1536],
                                                 T["lrow"][0:1, 1024:1536],
                                                 sm[0:1, 0:512], ALU.subtract))
        drain("vector")
        emit("vector", lambda v: v.tensor_tensor(sm[0:1, 0:512], T["lrow"][0:1, 512:1024],
                                                 T["lrow"][0:1, 512:1024], ALU.mult))
        drain("vector")
        emit("vector", lambda v: v.tensor_tensor(T["lrow"][0:1, 1536:2048],
                                                 T["lrow"][0:1, 1536:2048],
                                                 sm[0:1, 0:512], ALU.subtract))
        drain("vector")
        emit("vector", lambda v: v.nop(), inc=("Vev", 1))
        wait("scalar", "Vev", mk.cnt["Vev"])
        emit("scalar", lambda s: s.activation(T["lrow"][0:1, 1024:2048], T["lrow"][0:1, 1024:2048],
                                              AF.Ln, bias=T["consts"][0:1, 0:1], scale=1.0))
        drain("scalar")
        emit("scalar", lambda s: s.activation(T["lrow"][0:1, 1024:2048], T["lrow"][0:1, 1024:2048],
                                              AF.Exp, bias=0.0, scale=T["consts"][0:1, 1:2]))
        drain("scalar")
        emit("scalar", lambda s: s.nop(), inc=("Sev", 1))
        wait("vector", "Sev", mk.cnt["Sev"])
        emit("vector", lambda v: v.tensor_tensor(T["lrow"][0:1, 0:1024], T["lrow"][0:1, 0:1024],
                                                 T["lrow"][0:1, 1024:2048], ALU.mult))
        drain("vector")
        emit("vector", lambda v: v.tensor_scalar(T["lrow"][0:1, 0:1024], T["lrow"][0:1, 0:1024],
                                                 -1.0, None, ALU.mult))
        drain("vector")
        emit("vector", lambda v: v.tensor_copy(T["smallb"][0:1, 0:NT], T["lrow"][0:1, 1024:2048]))
        emit("vector", lambda v: v.tensor_copy(T["smallb"][0:1, NT:2 * NT], T["lrow"][0:1, 0:1024]))
        drain("vector")
        emit("vector", lambda v: v.nop(), inc=("Vev", 1))
        wait("tensor", "Vev", mk.cnt["Vev"])
        for half in range(2):
            if half == 1:
                wait("tensor", "Vev", mk.cnt["Vev"])
            emit("tensor", lambda e, half=half: e.matmul(
                P[5][:, :], T["onesb"][0:1, :], T["smallb"][0:1, half * 512:(half + 1) * 512],
                start=True, stop=True), inc=("Tmm", 1))
            wait("vector", "Tmm", mk.cnt["Tmm"])
            emit("vector", lambda v, half=half: v.tensor_copy(
                T["rbc"][:, half * 512:(half + 1) * 512], P[5][:, :]))
            drain("vector")
            emit("vector", lambda v: v.nop(), inc=("Vev", 1))
            wait("tensor", "Vev", mk.cnt["Vev"])
            emit("tensor", lambda e, half=half: e.matmul(
                P[5][:, :], T["onesb"][0:1, :],
                T["smallb"][0:1, NT + half * 512:NT + (half + 1) * 512],
                start=True, stop=True), inc=("Tmm", 1))
            wait("vector", "Tmm", mk.cnt["Tmm"])
            emit("vector", lambda v, half=half: v.tensor_copy(
                T["smallb"][:, 2048 + half * 512:2048 + (half + 1) * 512], P[5][:, :]))
            drain("vector")
            emit("vector", lambda v: v.nop(), inc=("Vev", 1))
        IBC_V = mk.cnt["Vev"]

        for m in range(KC):
            xl_m = "XlA" if m % 2 == 0 else "XlB"
            wait("sync", "Park", XPARK_ALL)
            if m >= 2:
                wait("sync", "Vev", mk.cnt["Vev"])
            emit("sync", lambda e, m=m: e.dma_start(
                out=T["resid"][:, m % 2, :], in_=d["xp_park"][m * 128:(m + 1) * 128, :]),
                inc=(xl_m, 16))
            wait("vector", xl_m, mk.cnt[xl_m])
            wait("vector", "Vev", IBC_V)
            emit("vector", lambda v, m=m: v.tensor_tensor(
                T["gate_h2"][:, m, :], T["resid"][:, m % 2, :], T["rbc"][:, :], ALU.mult),
                inc=("Vev", 1))
            wait("vector", "Vev", mk.cnt["Vev"])
            emit("vector", lambda v, m=m: v.tensor_tensor(
                T["gate_h2"][:, m, :], T["gate_h2"][:, m, :],
                T["smallb"][:, 2048:3072], ALU.add), inc=("Vev", 1))
            wait("vector", "Vev", mk.cnt["Vev"])
            emit("vector", lambda v, m=m: v.tensor_scalar(
                T["gate_h2"][:, m, :], T["gate_h2"][:, m, :],
                T["modd"][:, 64 + m:65 + m], T["modd"][:, 48 + m:49 + m], ALU.mult, ALU.add),
                inc=("Vev", 1))
        H2_V = mk.cnt["Vev"]
        tap("h2", lambda: T["gate_h2"][:, :, :], after=("Vev", H2_V))

        if STOP_AFTER == "P6":
            return
        wait("gpsimd", "Sev", GATE_S)
        emit("gpsimd", lambda g: g.dma_start(
            out=T["wr_sb"][:, :, :], in_=d["wr_T"]),
            inc=("Wst", 16))
        for i, nm in enumerate(["agp1", "agp2", "aup1", "aup2"]):
            emit("gpsimd", lambda g, i=i, nm=nm: g.dma_start(
                out=T["agp_sb"][:, i, :, :], in_=d[nm]),
                inc=("Wst", 16))
        emit("gpsimd", lambda g: g.dma_start(out=T["adS_sb"][:, :, :, :],
                                             in_=d["adS"]), inc=("Wst", 16))
        emit("gpsimd", lambda g: g.dma_start(out=T["bdS_sb"][:, :, :], in_=d["bdS"]), inc=("Wst", 16))
        WFFN_W = mk.cnt["Wst"]

        wait("tensor", "Wst", WFFN_W)
        wait("tensor", "Vev", H2_V)
        def rlmm2(e):
            for half in range(2):
                bank = P[4] if half == 0 else P[5]
                for k in range(KC):
                    last = e.matmul(bank[0:E, :], T["wr_sb"][:, k, :],
                             T["gate_h2"][:, k, half * TBLK:(half + 1) * TBLK],
                             start=(k == 0), stop=(k == KC - 1))
            return last
        emit("tensor", rlmm2, inc=("Tmm", 1))
        wait("vector", "Tmm", mk.cnt["Tmm"])
        emit("vector", lambda v: v.tensor_copy(T["rl8"][:, 0:512], P[4][0:E, :]))
        emit("vector", lambda v: v.tensor_copy(T["rl8"][:, 512:1024], P[5][0:E, :]))
        drain("vector")
        emit("vector", lambda v: v.nop(), inc=("Vev", 1))
        wait("tensor", "Vev", mk.cnt["Vev"])
        for c in range(8):
            emit("tensor", lambda e, c=c: e.transpose(
                P[4][:, c * 8:(c + 1) * 8], T["rl8"][:, c * 128:(c + 1) * 128],
                T["identf"][0:8, 0:8]), inc=("Tmm", 1))
        wait("vector", "Tmm", mk.cnt["Tmm"])
        for c in range(8):
            emit("vector", lambda v, c=c: v.tensor_copy(T["rt"][:, c, :], P[4][:, c * 8:(c + 1) * 8]))
        drain("vector")
        for c in range(8):
            emit("vector", lambda v, c=c: v.reduce_max(T["rcol"][:, c, 0:1], T["rt"][:, c, :], AX.X))
        drain("vector")
        for c in range(8):
            emit("vector", lambda v, c=c: v.tensor_scalar(T["rcol"][:, c, 1:2], T["rcol"][:, c, 0:1],
                                                          -1.0, None, ALU.mult))
        drain("vector")
        emit("vector", lambda v: v.nop(), inc=("Vev", 1))
        wait("scalar", "Vev", mk.cnt["Vev"])
        for c in range(8):
            emit("scalar", lambda s, c=c: s.activation(
                T["rsc"][:, c, :], T["rt"][:, c, :], AF.Exp, bias=T["rcol"][:, c, 1:2], scale=1.0))
        drain("scalar")
        emit("scalar", lambda s: s.nop(), inc=("Sev", 1))
        wait("vector", "Sev", mk.cnt["Sev"])
        for c in range(8):
            emit("vector", lambda v, c=c: v.tensor_reduce(T["rcol"][:, c, 0:1], T["rsc"][:, c, :],
                                                          AX.X, ALU.add))
        drain("vector")
        for c in range(8):
            emit("vector", lambda v, c=c: v.reciprocal(T["rcol"][:, c, 1:2], T["rcol"][:, c, 0:1]))
        drain("vector")
        for c in range(8):
            emit("vector", lambda v, c=c: v.tensor_scalar(T["rt"][:, c, :], T["rsc"][:, c, :],
                                                          T["rcol"][:, c, 1:2], None, ALU.mult))
        drain("vector")
        for c in range(8):
            emit("vector", lambda v, c=c: v.reduce_max(T["rcol"][:, c, 0:1], T["rt"][:, c, :], AX.X))
        drain("vector")
        for c in range(8):
            emit("vector", lambda v, c=c: v.tensor_scalar(T["rsc"][:, c, :], T["rt"][:, c, :],
                                                          T["rcol"][:, c, 0:1], None, ALU.is_equal))
        drain("vector")
        for c in range(8):
            emit("vector", lambda v, c=c: v.scalar_tensor_tensor(
                T["rsc"][:, c, :], T["rsc"][:, c, :], -1e30, T["rt"][:, c, :], ALU.mult, ALU.add))
        drain("vector")
        for c in range(8):
            emit("vector", lambda v, c=c: v.reduce_max(T["rcol"][:, c, 2:3], T["rsc"][:, c, :], AX.X))
        drain("vector")
        for c in range(8):
            emit("vector", lambda v, c=c: v.tensor_scalar(T["rsc"][:, c, :], T["rt"][:, c, :],
                                                          T["rcol"][:, c, 2:3], None, ALU.is_ge))
        drain("vector")
        for c in range(8):
            emit("vector", lambda v, c=c: v.tensor_tensor(T["rt"][:, c, :], T["rt"][:, c, :],
                                                          T["rsc"][:, c, :], ALU.mult))
        drain("vector")
        emit("vector", lambda v: v.nop(), inc=("Vev", 1))
        TOPW_V = mk.cnt["Vev"]
        tap("topw", lambda: T["rt"][:, :, :], after=("Vev", TOPW_V))
        wait("tensor", "Vev", TOPW_V)
        for c in range(8):
            bank = P[4] if c < 4 else P[5]
            emit("tensor", lambda e, c=c, bank=bank: e.transpose(
                bank[0:8, (c % 4) * 128:(c % 4 + 1) * 128], T["rt"][:, c, :], T["identf"][:, :]),
                inc=("Tmm", 1))
        wait("vector", "Tmm", mk.cnt["Tmm"])
        emit("vector", lambda v: v.tensor_copy(T["wef"][:, 0:512], P[4][0:8, 0:512]))
        emit("vector", lambda v: v.tensor_copy(T["wef"][:, 512:1024], P[5][0:8, 0:512]))
        drain("vector")
        emit("vector", lambda v: v.nop(), inc=("Vev", 1))
        wait("tensor", "Vev", mk.cnt["Vev"])
        for par, dst in ((0, "wexpE"), (1, "wexpO")):
            for half in range(2):
                if not (par == 0 and half == 0):
                    wait("tensor", "Vev", mk.cnt["Vev"])
                emit("tensor", lambda e, par=par, half=half: e.matmul(
                    P[6][:, :], T["exp16"][:, par, :], T["wef"][:, half * 512:(half + 1) * 512],
                    start=True, stop=True), inc=("Tmm", 1))
                wait("vector", "Tmm", mk.cnt["Tmm"])
                emit("vector", lambda v, dst=dst, half=half: v.tensor_copy(
                    T[dst][:, half * 512:(half + 1) * 512], P[6][:, :]))
                drain("vector")
                emit("vector", lambda v: v.nop(), inc=("Vev", 1))
        WEXP_V = mk.cnt["Vev"]

        xa_dst = [("xagP", 0), ("xagP", 1), ("xauP", 0), ("xauP", 1)]
        for si in range(4):
            dstn, dsti = xa_dst[si]
            for half in range(2):
                if not (si == 0 and half == 0):
                    wait("tensor", "Vev", mk.cnt["Vev"])
                def xamm(e, si=si, half=half):
                    for k in range(KC):
                        last = e.matmul(P[3][:, :], T["agp_sb"][:, si, k, :],
                                 T["gate_h2"][:, k, half * TBLK:(half + 1) * TBLK],
                                 start=(k == 0), stop=(k == KC - 1))
                    return last
                emit("tensor", xamm, inc=("Tmm", 1))
                wait("vector", "Tmm", mk.cnt["Tmm"])
                wait("vector", "Xld", XLD0)
                emit("vector", lambda v, dstn=dstn, dsti=dsti, half=half: v.tensor_scalar(
                    T[dstn][:, dsti, half * TBLK:(half + 1) * TBLK], P[3][:, :],
                    T["pm2c"][:, 0:1], None, ALU.mult))
                drain("vector")
                emit("vector", lambda v: v.nop(), inc=("Vev", 1))
        XA_V = mk.cnt["Vev"]
        tap("xa", lambda: T["xagP"][:, :, :], after=("Vev", XA_V))

        if STOP_AFTER == "P7":
            return
        xat_gth = 0
        for half in range(2):
            wait("gpsimd", "Vev", XA_V)
            if half == 1:
                wait("gpsimd", "Tmm", mk.cnt["Tmm"])
            for tg in range(E):
                srcn = "xagP"; srci = 0 if tg < 4 else 1
                emit("gpsimd", lambda g, tg=tg, srcn=srcn, srci=srci, half=half: g.dma_start(
                    out=T["xat"][:, tg, :],
                    in_=T[srcn][32 * (tg % 4):32 * (tg % 4) + 32, srci,
                                half * TBLK:(half + 1) * TBLK]), inc=("Gth", 16))
            for tg in range(E):
                srcn = "xauP"; srci = 0 if tg < 4 else 1
                emit("gpsimd", lambda g, tg=tg, srcn=srcn, srci=srci, half=half: g.dma_start(
                    out=T["xat"][:, E + tg, :],
                    in_=T[srcn][32 * (tg % 4):32 * (tg % 4) + 32, srci,
                                half * TBLK:(half + 1) * TBLK]), inc=("Gth", 16))
            xat_gth = mk.cnt["Gth"]

            if half == 1:
                wait("gpsimd", "Tmm", mk.cnt["Tmm"])
            wait("gpsimd", "Tmm", op_t[-1])
            emit("gpsimd", lambda g: g.dma_start(
                out=T["wcolA"][:, 0, :, :], in_=d["Wg_T"][0]),
                inc=("WlA", 16))
            emit("gpsimd", lambda g: g.dma_start(
                out=T["wcolB"][:, 0, :, :], in_=d["Wu_T"][0]),
                inc=("WlA", 16))
            emit("gpsimd", lambda g: g.dma_start(
                out=T["bgp_sb"][:, 0, :, :], in_=d["bgp"][0]),
                inc=("WdA", 16))
            emit("gpsimd", lambda g: g.dma_start(
                out=T["bup_sb"][:, 0, :, :], in_=d["bup"][0]),
                inc=("WdA", 16))
            wait("vector", "Tmm", mk.cnt["Tmm"])
            emit("vector", lambda v: v.memset(P[6][:, :], 0.0))
            emit("vector", lambda v: v.memset(P[7][:, :], 0.0))
            drain("vector")
            emit("vector", lambda v: v.nop(), inc=("Vev", 1))
            HAZ_V = mk.cnt["Vev"]
            ffn_t = []

            chunk_t = [0] * (FC + 1)
            base_done_t = [0] * (FC + 1)
            silu_base_s = [0] * FC
            mult_base_v = [0] * FC
            gub_read_v = mk.cnt["Vev"]
            gub_read_t = mk.cnt["Tmm"]
            prev_silu = [0] * E
            prev_uadd = [0] * E
            prev_mult = [0] * E
            for i in range(FC + 1):
                if i + 1 < FC:
                    if i >= 1:
                        wait("gpsimd", "Tmm", base_done_t[i - 1])
                    wl = "WlA" if (i + 1) % 2 == 0 else "WlB"
                    for nm, dst in (("Wg_T", "wcolA"), ("Wu_T", "wcolB")):
                        emit("gpsimd", lambda g, i=i, nm=nm, dst=dst: g.dma_start(
                            out=T[dst][:, (i + 1) % 2, :, :], in_=d[nm][i + 1]), inc=(wl, 16))
                silu_s = [0] * E
                gadd_v = [0] * E
                uadd_v = [0] * E
                mult_v = [0] * E
                ad_t = [0] * E
                if i >= 1:
                    if i >= 2:
                        wait("scalar", "Vev", mult_base_v[i - 2])
                        wait("scalar", "Vev", gub_read_v)
                        wait("scalar", "Tmm", gub_read_t)
                    wait("scalar", "Tmm", base_done_t[i - 1])
                    emit("scalar", lambda s: s.activation(
                        T["sgbb"][:, :], P[0][:, :], AF.Silu))
                    emit("scalar", lambda s: s.copy(T["gbb"][:, :], P[0][:, :]))
                    emit("scalar", lambda s: s.copy(T["ubb"][:, :], P[1][:, :]), inc=("Sev", 1))
                    silu_base_s[i - 1] = mk.cnt["Sev"]
                    wait("vector", "Tmm", base_done_t[i - 1])
                    wait("vector", "Sev", silu_base_s[i - 1])
                    wait("vector", "Sev", silu_base_s[i - 1])
                    emit("vector", lambda v, i=i: v.tensor_tensor(
                        T["hbase"][:, i - 1, :], T["sgbb"][:, :], T["ubb"][:, :],
                        ALU.mult), inc=("Vev", 1))
                    mult_base_v[i - 1] = mk.cnt["Vev"]
                if i < FC:
                    wl_fc = "WlA" if i % 2 == 0 else "WlB"
                    wait("tensor", wl_fc, mk.cnt[wl_fc])
                    wait("tensor", "Vev", H2_V)
                    if i >= 1:
                        wait("tensor", "Sev", silu_base_s[i - 1])
                    def bmm(e, i=i, half=half):
                        for k in range(KC):
                            e.matmul(P[0][:, :], T["wcolA"][:, i % 2, k, :],
                                     T["gate_h2"][:, k, half * TBLK:(half + 1) * TBLK],
                                     start=(k == 0), stop=(k == KC - 1))
                        for k in range(KC):
                            last = e.matmul(P[1][:, :], T["wcolB"][:, i % 2, k, :],
                                            T["gate_h2"][:, k, half * TBLK:(half + 1) * TBLK],
                                            start=(k == 0), stop=(k == KC - 1))
                        return last
                    emit("tensor", bmm, inc=("Tmm", 1))
                    base_done_t[i] = mk.cnt["Tmm"]
                if i >= 1:
                    wdp = "WdA" if (i - 1) % 2 == 0 else "WdB"
                    wait("tensor", wdp, mk.cnt[wdp])
                    wait("tensor", "Gth", xat_gth)
                    wait("tensor", "Vev", HAZ_V)
                    wait("tensor", "Sev", silu_base_s[i - 1])
                    lora_last = 0
                    for e_i in range(E):
                        lgb = 2 + (e_i % 2)
                        lub = 4 + (e_i % 2)
                        if e_i >= 2:
                            wait("tensor", "Sev", silu_s[e_i - 2])
                            wait("tensor", "Vev", uadd_v[e_i - 2])
                        else:
                            wait("tensor", "Sev", prev_silu[e_i - 2 + E])
                            wait("tensor", "Vev", prev_uadd[e_i - 2 + E])
                        def lmm(e, i=i, e_i=e_i, lgb=lgb, lub=lub):
                            e.matmul(P[lgb][:, :], T["ident"][:, :], T["gbb"][:, :],
                                     start=True, stop=False, skip_group_check=True)
                            e.matmul(P[lgb][:, :], T["bgp_sb"][:, (i - 1) % 2, e_i, :],
                                     T["xat"][:, e_i, :],
                                     start=False, stop=True, skip_group_check=True)
                            last = e.matmul(P[lub][:, :], T["bup_sb"][:, (i - 1) % 2, e_i, :],
                                            T["xat"][:, E + e_i, :],
                                            start=True, stop=True, skip_group_check=True)
                            return last
                        emit("tensor", lmm, inc=("Tmm", 1))
                        lora_t = mk.cnt["Tmm"]
                        lora_last = lora_t
                        if e_i >= 1:
                            wait("tensor", "Vev", mult_v[e_i - 1])
                            emit("tensor", lambda e, i=i, e_i=e_i: e.matmul(
                                P[6 + ((e_i - 1) % 2)][32 * ((e_i - 1) // 2):32 * ((e_i - 1) // 2) + 16, :],
                                T["adS_sb"][:, i - 1, e_i - 1, :], T["hexp"][:, (e_i - 1) % 2, :],
                                start=False, stop=True, skip_group_check=True,
                                tile_position=(0, 32 * ((e_i - 1) // 2))), inc=("Tmm", 1))
                            ad_t[e_i - 1] = mk.cnt["Tmm"]
                        wait("vector", "Tmm", lora_t)
                        if e_i >= 2:
                            wait("vector", "Vev", mult_v[e_i - 2])
                        emit("vector", lambda v, e_i=e_i, lub=lub: v.tensor_tensor(
                            T["usb"][:, e_i % 2, :], P[lub][:, :], T["ubb"][:, :], ALU.add),
                            inc=("Vev", 1))
                        uadd_v[e_i] = mk.cnt["Vev"]
                        wait("scalar", "Tmm", lora_t)
                        if e_i >= 2:
                            wait("scalar", "Vev", mult_v[e_i - 2])
                        else:
                            wait("scalar", "Vev", prev_mult[e_i - 2 + E])
                        emit("scalar", lambda s, e_i=e_i, lgb=lgb: s.activation(
                            T["sgb"][:, e_i % 2, :], P[lgb][:, :], AF.Silu),
                            inc=("Sev", 1))
                        silu_s[e_i] = mk.cnt["Sev"]
                        wait("vector", "Vev", uadd_v[e_i])
                        wait("vector", "Sev", silu_s[e_i])
                        if e_i >= 2:
                            wait("vector", "Tmm", ad_t[e_i - 2])
                        emit("vector", lambda v, e_i=e_i: v.tensor_tensor(
                            T["hexp"][:, e_i % 2, :], T["sgb"][:, e_i % 2, :],
                            T["usb"][:, e_i % 2, :], ALU.mult), inc=("Vev", 1))
                        mult_v[e_i] = mk.cnt["Vev"]
                    wait("tensor", "Vev", mult_v[E - 1])
                    emit("tensor", lambda e, i=i: e.matmul(
                        P[6 + ((E - 1) % 2)][32 * ((E - 1) // 2):32 * ((E - 1) // 2) + 16, :],
                        T["adS_sb"][:, i - 1, E - 1, :], T["hexp"][:, (E - 1) % 2, :],
                        start=False, stop=True, skip_group_check=True,
                        tile_position=(0, 32 * ((E - 1) // 2))), inc=("Tmm", 1))
                    ad_t[E - 1] = mk.cnt["Tmm"]
                    gub_read_v = uadd_v[E - 1]
                    gub_read_t = lora_last
                    prev_silu = list(silu_s)
                    prev_uadd = list(uadd_v)
                    prev_mult = list(mult_v)
                chunk_t[i] = mk.cnt["Tmm"]
                if i + 1 < FC:
                    wait("gpsimd", "Tmm", chunk_t[i])
                    wlp = "WdA" if (i + 1) % 2 == 0 else "WdB"
                    for nm, dst in (("bgp", "bgp_sb"), ("bup", "bup_sb")):
                        emit("gpsimd", lambda g, i=i, nm=nm, dst=dst: g.dma_start(
                            out=T[dst][:, (i + 1) % 2, :, :], in_=d[nm][i + 1]), inc=(wlp, 16))
            AD_T = mk.cnt["Tmm"]
            wait("vector", "Tmm", AD_T)
            wait("vector", "Vev", WEXP_V)
            emit("vector", lambda v, half=half: v.tensor_tensor(
                T["hAwE"][:, :], P[6][:, :], T["wexpE"][:, half * TBLK:(half + 1) * TBLK], ALU.mult))
            emit("vector", lambda v, half=half: v.tensor_tensor(
                T["hAwO"][:, :], P[7][:, :], T["wexpO"][:, half * TBLK:(half + 1) * TBLK], ALU.mult))
            drain("vector")
            emit("vector", lambda v: v.nop(), inc=("Vev", 1))
            HAW_V = mk.cnt["Vev"]

            wait("gpsimd", "Tmm", AD_T)
            emit("gpsimd", lambda g: g.dma_start(
                out=T["wdcol"][:, 0, :, :], in_=d["Wd_T"][0]),
                inc=("WdA", 16))
            out_evac = [0, 0]
            dm_t = []
            for mo in range(KC):
                if mo + 1 < KC:
                    if mo >= 1:
                        wait("gpsimd", "Tmm", dm_t[mo - 1])
                    wd = "WdA" if (mo + 1) % 2 == 0 else "WdB"
                    emit("gpsimd", lambda g, mo=mo: g.dma_start(
                        out=T["wdcol"][:, (mo + 1) % 2, :, :],
                        in_=d["Wd_T"][mo + 1]), inc=(wd, 16))
                xl_mo = "XlA" if mo % 2 == 0 else "XlB"
                if mo >= 2:
                    wait("sync", "Vev", out_evac[mo % 2])
                else:
                    wait("sync", "Vev", H2_V)
                    if half == 1:
                        wait("sync", "Vev", out_evac[mo % 2])
                wait("sync", "Out", mk.cnt["Out"])
                emit("sync", lambda e, mo=mo, half=half: e.dma_start(
                    out=T["resid"][:, mo % 2, 0:TBLK],
                    in_=d["xp_park"][mo * 128:(mo + 1) * 128, half * TBLK:(half + 1) * TBLK]),
                    inc=(xl_mo, 16))
                wd_mo = "WdA" if mo % 2 == 0 else "WdB"
                wait("tensor", wd_mo, mk.cnt[wd_mo])
                wait("tensor", "Vev", HAW_V)
                if mo >= 2:
                    wait("tensor", "Vev", out_evac[mo % 2])
                def dmm(e, mo=mo):
                    for k in range(FC):
                        last = e.matmul(P[4 + (mo % 2)][:, :], T["wdcol"][:, mo % 2, k, :],
                                 T["hbase"][:, k, :], start=(k == 0), stop=False)
                    last = e.matmul(P[4 + (mo % 2)][:, :], T["bdS_sb"][:, 0, mo * 128:(mo + 1) * 128],
                             T["hAwE"][:, :], start=False, stop=False)
                    last = e.matmul(P[4 + (mo % 2)][:, :], T["bdS_sb"][:, 1, mo * 128:(mo + 1) * 128],
                             T["hAwO"][:, :], start=False, stop=True)
                    return last
                emit("tensor", dmm, inc=("Tmm", 1))
                dm_t.append(mk.cnt["Tmm"])
                wait("vector", "Tmm", mk.cnt["Tmm"])
                wait("vector", xl_mo, mk.cnt[xl_mo])
                emit("vector", lambda v, mo=mo: v.scalar_tensor_tensor(
                    T["resid"][:, mo % 2, 0:TBLK], P[4 + (mo % 2)][:, :],
                    T["modd"][:, 80 + mo:81 + mo], T["resid"][:, mo % 2, 0:TBLK],
                    ALU.mult, ALU.add))
                drain("vector")
                emit("vector", lambda v: v.nop(), inc=("Vev", 1))
                out_evac[mo % 2] = mk.cnt["Vev"]
                wait("sync", "Vev", out_evac[mo % 2])
                emit("sync", lambda e, mo=mo, half=half: e.dma_start(
                    out=d["y"][mo * 128:(mo + 1) * 128, half * TBLK:(half + 1) * TBLK],
                    in_=T["resid"][:, mo % 2, 0:TBLK]), inc=("Out", 16))
        wait("sync", "Out", mk.cnt["Out"])
        wait("gpsimd", "Out", mk.cnt["Out"])
        wait("scalar", "Out", mk.cnt["Out"])
        wait("vector", "Out", mk.cnt["Out"])
        wait("tensor", "Out", mk.cnt["Out"])

    handles = {}
    with nc.Block() as block:
        def runner(eng):
            def f(h):
                hs = {eng: h}
                mk = MK(eng, hs, sems)
                walk(mk)
            return f
        block.gpsimd(runner("gpsimd"))
        block.sync(runner("sync"))
        block.scalar(runner("scalar"))
        block.vector(runner("vector"))
        block.tensor(runner("tensor"))

    for cm in reversed(guards):
        cm.__exit__(None, None, None)
    return nc



def host_prep(inp):
    x = np.asarray(inp["L0_attn_hidden_states"], np.float32)
    c = np.asarray(inp["L0_c"], np.float32)
    wmod = np.asarray(inp["L0_wmod"], np.float32)
    wqkv = np.asarray(inp["L0_attn_w_qkv"], np.float32)
    slope = np.asarray(inp["L0_attn_slope_rate"], np.float32)
    wgate = np.asarray(inp["L0_attn_w_gate"], np.float32)
    wout = np.asarray(inp["L0_attn_w_out"], np.float32)
    normw = np.asarray(inp["L0_attn_norm_w"], np.float32)
    Wg = np.asarray(inp["L0_ffn_Wg"], np.float32)
    Wu = np.asarray(inp["L0_ffn_Wu"], np.float32)
    Wd = np.asarray(inp["L0_ffn_Wd"], np.float32)
    Wr = np.asarray(inp["L0_ffn_Wr"], np.float32)
    Ag = np.asarray(inp["L0_ffn_Ag"], np.float32)
    Bg = np.asarray(inp["L0_ffn_Bg"], np.float32)
    Au = np.asarray(inp["L0_ffn_Au"], np.float32)
    Bu = np.asarray(inp["L0_ffn_Bu"], np.float32)
    Ad = np.asarray(inp["L0_ffn_Ad"], np.float32)
    Bd = np.asarray(inp["L0_ffn_Bd"], np.float32)

    co = np.ascontiguousarray

    def mblocks(wT, nblk):
        K, N = wT.shape
        kc = K // 128
        return co(wT.reshape(kc, 128, nblk, 128).transpose(2, 1, 0, 3))

    wg_T = mblocks(wgate.T, KC)
    wo_T = mblocks(wout.T, KC)
    Wg_T = mblocks(Wg.T, FC)
    Wu_T = mblocks(Wu.T, FC)
    Wd_T = mblocks(Wd.T, KC)
    wr_T = co(Wr.T.reshape(KC, 128, E).transpose(1, 0, 2))
    nw_cols = co(normw.reshape(KC, 128).T)

    def pair_cols(A):
        AT = A.transpose(2, 0, 1)
        def mk(tt0):
            cols = []
            for tt in range(tt0, tt0 + 4):
                cols.append(AT[:, (tt - 1) % E, :])
                cols.append(AT[:, tt, :])
            m = np.concatenate(cols, axis=1)
            return co(m.reshape(KC, 128, 128).transpose(1, 0, 2))
        return mk(0), mk(4)
    agp1, agp2 = pair_cols(Ag)
    aup1, aup2 = pair_cols(Au)

    def pair_rows(Bm):
        BT = Bm.transpose(0, 2, 1)
        out = np.empty((FC, E, 32, 128), np.float32)
        for tt in range(E):
            stack = np.concatenate([BT[(tt - 1) % E], BT[tt]], axis=0)
            out[:, tt, :, :] = stack.reshape(32, FC, 128).transpose(1, 0, 2)
        return co(out.transpose(0, 2, 1, 3))
    bgp = pair_rows(Bg)
    bup = pair_rows(Bu)

    AdT = Ad.transpose(2, 0, 1).reshape(FC, 128, E, RK)
    adS = co(AdT.transpose(1, 0, 2, 3))
    bdS = np.zeros((128, 2, D), np.float32)
    for e_i in range(E):
        for i in range(RK):
            bdS[32 * (e_i // 2) + i, e_i % 2, :] = Bd[e_i, :, i]
    bdS = co(bdS)

    pm2c = co(np.where((np.arange(128) % 32) < 16, -2.0, 2.0).astype(np.float32).reshape(128, 1))
    in_maps = []
    for cid in range(8):
        b, q = cid // 4, cid % 4
        heads = list(range(4 * q, 4 * q + 4))
        rows = []
        for sec in range(3):
            for hh in heads:
                rows.extend(range(sec * D + hh * DH, sec * D + (hh + 1) * DH))
        wq = mblocks(co(wqkv[rows, :]).T, 12)
        wqkv_T = co(wq.transpose(1, 0, 2, 3))
        modidx = np.array([[2 * j + b] for j in range(8)], np.uint32)
        pidx = np.arange(128, dtype=np.uint32)[:, None]
        kidx = np.arange(KC, dtype=np.uint32)[None, :]
        aidx = (4 * (128 * kidx + pidx) + q).astype(np.uint32)
        im = {
            "x_b": co(x[b]),
            "x_m": co(x[b, NT * q:NT * (q + 1), :]),
            "xT_m": co(x[b, NT * q:NT * (q + 1), :].T),
            "c2": co(c.T.reshape(KC, 128, B).transpose(1, 0, 2)),
            "wmod_sl": co(wmod[:, 1536 * cid:1536 * (cid + 1)].reshape(KC, 128, 1536).transpose(1, 0, 2)),
            "modidx": modidx,
            "wqkv_T": wqkv_T,
            "slopes": co(slope[heads].reshape(1, HPC)),
            "wg_T": wg_T, "wo_T": wo_T, "nw": nw_cols, "aidx": aidx,
            "Wg_T": Wg_T, "Wu_T": Wu_T, "Wd_T": Wd_T, "wr_T": wr_T,
            "agp1": agp1, "agp2": agp2, "aup1": aup1, "aup2": aup2,
            "bgp": bgp, "bup": bup, "adS": adS, "bdS": bdS,
            "pm2": pm2c,
        }
        in_maps.append(im)
    return in_maps


_NC_CACHE = {}


def kernel(**inputs):
    in_maps = host_prep(inputs)
    if "main" not in _NC_CACHE:
        _NC_CACHE["main"] = build_program(taps=DEBUG_TAPS)
    nc = _NC_CACHE["main"]
    res = run_bass_kernel_spmd(nc, in_maps, core_ids=list(range(8)))
    out = np.empty((B, S, D), np.float32)
    for cid in range(8):
        b, q = cid // 4, cid % 4
        out[b, NT * q:NT * (q + 1), :] = res.results[cid]["y"].T
    return out


# version 09
# speedup vs baseline: 1.2753x; 1.1967x over previous
import numpy as np
import ml_dtypes
import concourse.bass as bass
import concourse.mybir as mybir
from concourse.bass_utils import run_bass_kernel_spmd

F32 = mybir.dt.float32
BF16 = mybir.dt.bfloat16
U32 = mybir.dt.uint32
U8 = mybir.dt.uint8
AF = mybir.ActivationFunctionType
ALU = mybir.AluOpType
AX = mybir.AxisListType

B, S, D, H, DH = 2, 4096, 2048, 16, 128
HPC = 4
NT = 1024
DFF, E, RK = 5632, 8, 16
KC = D // 128
FC = DFF // 128
TBLK = 512
NTB = S // TBLK
SBLK = 256
EPS_LN = 1e-5
EPS_RMS = 1e-6

DEBUG_TAPS = ()
STOP_AFTER = None


EVENTS = {}


class MK:
    def __init__(self, eng_name, handles, sems):
        self.me = eng_name
        self.h = handles
        self.sems = sems
        self.cnt = {k: 0 for k in sems}
        self.ev = []
        EVENTS[eng_name] = self.ev

    def emit(self, eng, fn, inc=None):
        if self.me == eng:
            ins = fn(self.h[eng])
            if inc is not None and ins is not None:
                ins.then_inc(self.sems[inc[0]], inc[1])
            if inc is not None:
                self.ev.append(("inc", inc[0], inc[1]))
        if inc is not None:
            self.cnt[inc[0]] += inc[1]

    def wait(self, eng, sem, val=None):
        v = self.cnt[sem] if val is None else val
        if self.me == eng and v > 0:
            self.h[eng].wait_ge(self.sems[sem], v)
            self.ev.append(("wait", sem, v))

    def drain(self, eng):
        if self.me == eng:
            self.h[eng].drain()


def build_program(taps=()):
    nc = bass.Bass("TRN2", target_bir_lowering=False)
    d = {}

    def din(name, shape, dt=F32):
        d[name] = nc.dram_tensor(name, shape, dt, kind="ExternalInput").ap()

    def dint(name, shape, dt=F32):
        d[name] = nc.dram_tensor(name, shape, dt).ap()

    din("x_b", [S, D]);  din("x_m", [NT, D]);  din("xT_m", [D, NT])
    din("c2", [128, KC, B]);   din("wmod_sl", [128, KC, 1536]);  din("modidx", [8, 1], U32)
    din("wqkv_T", [128, 12, KC, 128]);  din("slopes", [1, HPC])
    din("wg_T", [KC, 128, KC, 128]);    din("wo_T", [KC, 128, KC, 128])
    din("nw", [128, KC]);         din("aidx", [128, KC], U32);  din("pidx", [128, 32], U32)
    din("Wg_T", [FC, 128, KC, 128]);    din("Wu_T", [FC, 128, KC, 128]);  din("Wd_T", [KC, 128, FC, 128])
    din("wr_T", [128, KC, E])
    din("agp1", [128, KC, 128]); din("agp2", [128, KC, 128]); din("aup1", [128, KC, 128]); din("aup2", [128, KC, 128])
    din("bgp", [FC, 32, E, 128]);  din("bup", [FC, 32, E, 128])
    din("adS", [128, FC, E, 16]);  din("bdS", [128, 2, D]);  din("pm2", [128, 1])
    d["y"] = nc.dram_tensor("y", [D, NT], F32, kind="ExternalOutput").ap()

    dint("xnb_park", [S, D], BF16)
    dint("xnm_park", [NT, D], BF16)
    dint("xp_park", [D, NT])
    dint("mod_gi", [B, 1536]);  dint("mod_go", [16, 1536])
    dint("att_gi", [HPC * DH, S], BF16);  dint("att_go", [D, S], BF16)

    tapd = {}
    for tname, shape, dt in taps:
        tapd[tname] = nc.dram_tensor("tap_" + tname, shape, dt, kind="ExternalOutput").ap()

    guards = []
    arena_cm = nc.sbuf_tensor("arena", [128, 212800], U8)
    arena_cm.__enter__(); guards.append(arena_cm)
    BASE = 16512
    RA = BASE + 0
    RB = BASE + 49152
    RC = BASE + 81920
    RD = BASE + 116736
    RE = BASE + 141312
    RF = BASE + 157696

    T = {}

    def at(name, shape, dt, off):
        T[name] = nc.alloc_sbuf_tensor_at(name, shape, dt, offset=off)

    at("wmod_sb", [128, KC, 1536], BF16, RA)
    at("g8", [8, 1536], F32, RA)
    at("wqkv_sb", [128, 12, KC, 128], BF16, RA)
    at("attnz", [128, KC, NT], BF16, RA)
    at("rstat", [4, 1024], F32, RA + 40960)
    at("lrow", [1, 2048], F32, RA + 40960)
    at("rl8", [8, NT], F32, RA)
    at("wef", [8, NT], BF16, RA + 8192)
    at("hbase", [128, FC, TBLK], BF16, RA)

    at("gate_h2", [128, KC, NT], BF16, RB)
    at("modrows", [2, 1536], F32, RB)
    at("kv", [128, HPC, 128], F32, RB + 8192)
    at("kvb", [128, HPC, 128], BF16, RB + 10240)
    at("dtt", [128, HPC, 2, SBLK], BF16, RB + 11264)
    at("qdec", [128, HPC, SBLK], BF16, RB + 15360)
    at("kdec", [128, HPC, 2], F32, RB + 17408)
    at("xaff", [128, 3, D], BF16, RB + 17472)
    at("bnc_st", [128, HPC, SBLK], BF16, RB + 29760)

    at("qkvblk", [128, 2, 12, TBLK], BF16, RC)
    at("scanbuf", [128, HPC, 5, SBLK], BF16, RC + 24576)
    at("h1m", [128, KC, NT], BF16, RC)
    at("agp_sb", [128, 4, KC, 128], BF16, RC)
    at("wr_sb", [128, KC, E], BF16, RC + 16384)
    at("xagP", [128, 2, NT], BF16, RC + 16672)
    at("xauP", [128, 2, NT], BF16, RC + 20768)
    at("hexp", [128, 2, TBLK], BF16, RC + 24864)
    at("sgb", [128, 2, TBLK], BF16, RC + 26912)
    at("tadd", [128, 2, TBLK], BF16, RC + 29984)
    at("usb", [128, 2, TBLK], BF16, RC + 32032)
    at("wexpE", [128, NT], BF16, RA + 47104)
    at("gbb", [128, 2, TBLK], BF16, RC)
    at("sgbb", [128, 2, TBLK], BF16, RC + 2048)
    at("ubb", [128, 2, TBLK], BF16, RC + 4096)
    at("hAwE", [128, TBLK], BF16, RA + 45056)
    at("hAwO", [128, TBLK], BF16, RA + 46080)

    at("xstage", [128, 3, D], F32, RD)
    at("wcolA", [128, 2, KC, 128], BF16, RD)
    at("wcolB", [128, 2, KC, 128], BF16, RD + 8192)
    at("bgp_sb", [32, 2, E, 128], BF16, RD + 16384)
    at("bup_sb", [32, 2, E, 128], BF16, RD + 20480)
    at("wdcol", [128, 2, FC, 128], BF16, RD)

    at("hblk", [128, 2, KC, TBLK], BF16, RE)
    at("xat", [32, 2 * E, TBLK], BF16, RE)

    off = RF
    def fx(name, shape, dt):
        nonlocal off
        n = int(np.prod(shape[1:]))
        sz = n * (4 if dt == F32 else (2 if dt in (BF16,) else 4))
        at(name, shape, dt, off)
        off += (sz + 31) // 32 * 32

    fx("resid", [128, 2, NT], F32)
    fx("smallb", [128, 3072], BF16)
    fx("small", [128, 784], F32)
    fx("rbc", [128, NT], BF16)
    fx("adS_sb", [128, FC, E, 16], BF16)
    fx("bdS_sb", [128, 2, D], BF16)
    fx("c2_sb", [128, KC, B], F32)
    fx("c2sil", [128, KC, B], BF16)
    fx("modc", [128, 96], F32)
    fx("modd", [128, 96], F32)
    fx("stats", [128, 64], F32)
    fx("statm", [128, 16], F32)
    fx("stat6", [128, 48], F32)
    fx("isc", [128, 64], F32)
    fx("iscm", [128, 16], F32)
    fx("wexpO", [128, NT], BF16)
    fx("consts", [128, 8], F32)
    fx("nw_sb", [128, KC], F32)
    fx("rt", [128, 8, E], F32)
    fx("rsc", [128, 8, E], F32)
    fx("rcol", [128, 8, 4], F32)
    fx("ident", [128, 128], BF16)
    fx("identf", [128, 128], F32)
    fx("onesb", [1, 128], BF16)
    fx("onecol", [128, 1], BF16)
    fx("slop", [1, 8], F32)
    fx("pm2c", [128, 1], F32)
    fx("slopc", [128, 8], F32)
    fx("idx_m", [8, 1], U32)
    fx("idx_a", [128, KC], U32)
    fx("idx_p", [128, 32], U32)
    fx("exp16", [8, 2, 128], BF16)
    assert off <= 229344, off

    sm = T["small"]

    psl = []
    for i in range(8):
        cm = nc.psum_tensor("psb%d" % i, [128, 512], F32)
        psl.append(cm.__enter__()); guards.append(cm)

    semnames = ["Wld", "Xld", "Park", "Trp", "Bnc", "Gth", "CC", "Tmm", "Sev",
                "Vev", "Gev", "Out", "Hfree", "Qfree", "PkA", "PkB", "WlA", "WlB",
                "WdA", "WdB", "XsA", "XsB", "XlA", "XlB", "TrA", "TrB", "Wq", "Wst"]
    sems = {}
    for name in semnames:
        cm = nc.semaphore(name)
        sems[name] = cm.__enter__(); guards.append(cm)

    def walk(mk: MK):
        emit, wait, drain = mk.emit, mk.wait, mk.drain
        P = psl
        pstr = psl[6][:, :].bitcast(BF16)[:, 0:512]

        def tap(name, src_fn, after=None):
            if name in tapd:
                if after:
                    wait("sync", after[0], after[1])
                emit("sync", lambda e: e.dma_start(out=tapd[name], in_=src_fn()), inc=("Out", 16))

        emit("gpsimd", lambda g: g.memset(T["consts"][:, 0:1], EPS_LN))
        emit("gpsimd", lambda g: g.memset(T["consts"][:, 1:2], -0.5))
        emit("gpsimd", lambda g: g.memset(T["consts"][:, 2:3], EPS_RMS))
        emit("gpsimd", lambda g: g.memset(T["consts"][:, 3:4], 1.0 / 2048.0))
        emit("gpsimd", lambda g: g.memset(T["onesb"][:, :], 1.0))
        emit("gpsimd", lambda g: g.memset(T["onecol"][:, :], 1.0))
        emit("gpsimd", lambda g: g.memset(T["ident"][:, :], 1.0))
        drain("gpsimd")
        emit("gpsimd", lambda g: g.affine_select(
            T["ident"][:, :], T["ident"][:, :], compare_op=ALU.is_equal, fill=0.0,
            pattern=[[1, 128]], base=0, channel_multiplier=-1))
        emit("gpsimd", lambda g: g.memset(T["identf"][:, :], 1.0))
        drain("gpsimd")
        emit("gpsimd", lambda g: g.affine_select(
            T["identf"][:, :], T["identf"][:, :], compare_op=ALU.is_equal, fill=0.0,
            pattern=[[1, 128]], base=0, channel_multiplier=-1))
        emit("gpsimd", lambda g: g.memset(T["exp16"][:, :, :], 2.0))
        drain("gpsimd")
        emit("gpsimd", lambda g: g.affine_select(
            T["exp16"][:, 0, :], T["exp16"][:, 0, :], compare_op=ALU.is_equal, fill=0.0,
            pattern=[[2, 4], [0, 32]], base=0, channel_multiplier=-1))
        drain("gpsimd")
        emit("gpsimd", lambda g: g.affine_select(
            T["exp16"][:, 1, :], T["exp16"][:, 1, :], compare_op=ALU.is_equal, fill=0.0,
            pattern=[[2, 4], [0, 32]], base=1, channel_multiplier=-1))
        drain("gpsimd")
        emit("gpsimd", lambda g: g.nop(), inc=("Gev", 1))

        emit("sync", lambda e: e.dma_start(out=T["slop"][:, 0:HPC], in_=d["slopes"]), inc=("Xld", 16))
        emit("sync", lambda e: e.dma_start(out=T["pm2c"][:, :], in_=d["pm2"]), inc=("Xld", 16))
        emit("sync", lambda e: e.dma_start(out=T["idx_m"][:, :], in_=d["modidx"]), inc=("Xld", 16))
        emit("sync", lambda e: e.dma_start(out=T["idx_a"][:, :], in_=d["aidx"]), inc=("Xld", 16))
        emit("sync", lambda e: e.dma_start(out=T["idx_p"][:, :], in_=d["pidx"]), inc=("Xld", 16))
        emit("sync", lambda e: e.dma_start(out=T["nw_sb"][:, :], in_=d["nw"]), inc=("Xld", 16))
        emit("sync", lambda e: e.dma_start(
            out=T["c2_sb"][:, :, :], in_=d["c2"]), inc=("Xld", 16))
        XLD0 = mk.cnt["Xld"]
        emit("gpsimd", lambda g: g.dma_start(
            out=T["wmod_sb"][:, :, :], in_=d["wmod_sl"]),
            inc=("Wst", 16))
        WMOD_W = mk.cnt["Wst"]

        wait("scalar", "Xld", XLD0)
        emit("scalar", lambda s: s.activation(T["c2sil"][:, :, :], T["c2_sb"][:, :, :], AF.Silu),
             inc=("Sev", 1))
        C2S_S = mk.cnt["Sev"]
        wait("tensor", "Sev", C2S_S)
        wait("tensor", "Wst", WMOD_W)
        for n0 in range(3):
            def modmm(e, n0=n0):
                for k in range(KC):
                    last = e.matmul(P[n0][0:B, :], T["c2sil"][:, k, :],
                             T["wmod_sb"][:, k, n0 * 512:(n0 + 1) * 512],
                             start=(k == 0), stop=(k == KC - 1))
                return last
            emit("tensor", modmm, inc=("Tmm", 1))
        wait("vector", "Tmm", mk.cnt["Tmm"])
        for n0 in range(3):
            emit("vector", lambda v, n0=n0: v.tensor_copy(
                T["modrows"][0:B, n0 * 512:(n0 + 1) * 512], P[n0][0:B, :]))
        drain("vector")
        emit("vector", lambda v: v.nop(), inc=("Vev", 1))
        wait("sync", "Vev", mk.cnt["Vev"])
        emit("sync", lambda e: e.dma_start(out=d["mod_gi"], in_=T["modrows"][0:B, 0:1536]), inc=("Park", 16))
        MODGI_P = mk.cnt["Park"]
        wait("gpsimd", "Park", MODGI_P)
        emit("gpsimd", lambda g: g.collective_compute(
            "AllGather", ALU.bypass, replica_groups=[list(range(8))],
            ins=[d["mod_gi"].opt()], outs=[d["mod_go"].opt()]), inc=("CC", 1))
        wait("gpsimd", "CC", 1)
        wait("gpsimd", "Xld", XLD0)
        emit("gpsimd", lambda g: g.indirect_dma_start(
            out=T["g8"][:, :], out_offset=None, in_=d["mod_go"],
            in_offset=bass.IndirectOffsetOnAxis(ap=T["idx_m"][:, :], axis=0)), inc=("Gth", 16))
        MODG_G = mk.cnt["Gth"]
        wait("tensor", "Gth", MODG_G)
        for j in range(12):
            emit("tensor", lambda e, j=j: e.transpose(
                P[1][:, j * 8:(j + 1) * 8], T["g8"][:, j * 128:(j + 1) * 128],
                T["identf"][0:8, 0:8]), inc=("Tmm", 1))
        MODTR_T = mk.cnt["Tmm"]
        wait("vector", "Tmm", MODTR_T)
        for j in range(12):
            emit("vector", lambda v, j=j: v.tensor_copy(
                T["modc"][:, j:96:12], P[1][:, j * 8:(j + 1) * 8]))
        drain("vector")
        emit("vector", lambda v: v.tensor_copy(T["modd"][:, 0:16], T["modc"][:, 0:16]))
        emit("vector", lambda v: v.tensor_scalar(T["modd"][:, 16:32], T["modc"][:, 16:32],
                                                 1.0, None, ALU.add))
        emit("vector", lambda v: v.tensor_copy(T["modd"][:, 48:64], T["modc"][:, 48:64]))
        emit("vector", lambda v: v.tensor_scalar(T["modd"][:, 64:80], T["modc"][:, 64:80],
                                                 1.0, None, ALU.add))
        drain("vector")
        emit("vector", lambda v: v.nop(), inc=("Vev", 1))
        MODC_V = mk.cnt["Vev"]
        wait("scalar", "Vev", MODC_V)
        emit("scalar", lambda s: s.sign(T["isc"][:, 0:16], T["modc"][:, 32:48]))
        emit("scalar", lambda s: s.sign(T["isc"][:, 16:32], T["modc"][:, 80:96]))
        drain("scalar")
        emit("scalar", lambda s: s.nop(), inc=("Sev", 1))
        wait("vector", "Sev", mk.cnt["Sev"])
        emit("vector", lambda v: v.tensor_tensor(T["isc"][:, 32:48], T["modc"][:, 32:48],
                                                 T["isc"][:, 0:16], ALU.mult))
        emit("vector", lambda v: v.tensor_tensor(T["isc"][:, 48:64], T["modc"][:, 80:96],
                                                 T["isc"][:, 16:32], ALU.mult))
        drain("vector")
        emit("vector", lambda v: v.tensor_scalar(T["isc"][:, 32:48], T["isc"][:, 32:48],
                                                 0.25, None, ALU.max))
        emit("vector", lambda v: v.tensor_scalar(T["isc"][:, 48:64], T["isc"][:, 48:64],
                                                 0.25, None, ALU.max))
        drain("vector")
        emit("vector", lambda v: v.tensor_tensor(T["modd"][:, 32:48], T["isc"][:, 32:48],
                                                 T["isc"][:, 0:16], ALU.mult))
        emit("vector", lambda v: v.tensor_tensor(T["modd"][:, 80:96], T["isc"][:, 48:64],
                                                 T["isc"][:, 16:32], ALU.mult))
        drain("vector")
        emit("vector", lambda v: v.nop(), inc=("Vev", 1))
        MODD_V = mk.cnt["Vev"]
        tap("modd", lambda: T["modd"][:, :], after=("Vev" if mk.me == "sync" else "Vev", MODD_V))

        if STOP_AFTER == "P1":
            return
        wait("vector", "Xld", XLD0)
        emit("vector", lambda v: v.tensor_scalar(T["slop"][:, 4:8], T["slop"][:, 0:4],
                                                 -1.0, None, ALU.mult))
        drain("vector")
        emit("vector", lambda v: v.nop(), inc=("Vev", 1))
        NEGS_V = mk.cnt["Vev"]
        wait("scalar", "Vev", NEGS_V)
        emit("scalar", lambda s: s.copy(T["smallb"][0:1, 0:8], T["slop"][0:1, 0:8]))
        drain("scalar")
        emit("scalar", lambda s: s.nop(), inc=("Sev", 1))
        wait("tensor", "Sev", mk.cnt["Sev"])
        wait("tensor", "Gev", 1)
        emit("tensor", lambda e: e.matmul(P[3][:, 0:8], T["onesb"][0:1, :],
                                          T["smallb"][0:1, 0:8], start=True, stop=True),
             inc=("Tmm", 1))
        wait("vector", "Tmm", mk.cnt["Tmm"])
        emit("vector", lambda v: v.tensor_copy(T["slopc"][:, :], P[3][:, 0:8]))
        drain("vector")
        emit("vector", lambda v: v.nop(), inc=("Vev", 1))
        SLOPC_V = mk.cnt["Vev"]
        if STOP_AFTER == "P2a":
            return
        wait("gpsimd", "Park", MODGI_P)
        emit("gpsimd", lambda g: g.iota(sm[0:1, 0:SBLK], pattern=[[1, SBLK]], base=1,
                                        channel_multiplier=0, allow_small_or_imprecise_dtypes=True))
        emit("gpsimd", lambda g: g.iota(sm[:, SBLK:SBLK + 2], pattern=[[-128, 2]], base=255,
                                        channel_multiplier=-1, allow_small_or_imprecise_dtypes=True))
        drain("gpsimd")
        emit("gpsimd", lambda g: g.nop(), inc=("Gev", 1))
        IOTA1_G = mk.cnt["Gev"]
        wait("scalar", "Gev", IOTA1_G)
        wait("scalar", "Vev", SLOPC_V)
        for hh in range(HPC):
            emit("scalar", lambda s, hh=hh: s.activation(
                T["smallb"][0:1, hh * SBLK:(hh + 1) * SBLK], sm[0:1, 0:SBLK],
                AF.Exp, bias=0.0, scale=T["slop"][0:1, 4 + hh:5 + hh]))
            emit("scalar", lambda s, hh=hh: s.activation(
                T["kdec"][:, hh, :], sm[:, SBLK:SBLK + 2],
                AF.Exp, bias=0.0, scale=T["slopc"][:, 4 + hh:5 + hh]))
        drain("scalar")
        emit("scalar", lambda s: s.nop(), inc=("Sev", 1))
        QDR_S = mk.cnt["Sev"]
        if STOP_AFTER == "P2b":
            return
        wait("tensor", "Sev", QDR_S)
        wait("tensor", "Gev", 1)
        QDEC_BASE_V = mk.cnt["Vev"]
        for hh in range(HPC):
            if hh >= 1:
                wait("tensor", "Vev", QDEC_BASE_V + hh)
            emit("tensor", lambda e, hh=hh: e.matmul(
                P[2][:, (hh % 2) * SBLK:(hh % 2 + 1) * SBLK], T["onesb"][0:1, :],
                T["smallb"][0:1, hh * SBLK:(hh + 1) * SBLK], start=True, stop=True),
                inc=("Tmm", 1))
            wait("vector", "Tmm", mk.cnt["Tmm"])
            emit("vector", lambda v, hh=hh: v.tensor_copy(
                T["qdec"][:, hh, :], P[2][:, (hh % 2) * SBLK:(hh % 2 + 1) * SBLK]))
            drain("vector")
            emit("vector", lambda v: v.nop(), inc=("Vev", 1))
        QDEC_V = mk.cnt["Vev"]
        if STOP_AFTER == "P2c":
            return
        for half in range(2):
            wait("gpsimd", "Sev", mk.cnt["Sev"])
            emit("gpsimd", lambda g, half=half: g.iota(
                sm[:, 0:SBLK], pattern=[[1, SBLK]], base=-128 * half,
                channel_multiplier=-1, allow_small_or_imprecise_dtypes=True))
            drain("gpsimd")
            emit("gpsimd", lambda g: g.nop(), inc=("Gev", 1))
            iota_g = mk.cnt["Gev"]
            for hh in range(HPC):
                wait("scalar", "Gev", iota_g)
                wait("scalar", "Gev", mk.cnt["Gev"])
                emit("scalar", lambda s, hh=hh: s.activation(
                    sm[:, 512:512 + SBLK], sm[:, 0:SBLK], AF.Exp, bias=0.0,
                    scale=T["slopc"][:, 4 + hh:5 + hh]))
                drain("scalar")
                emit("scalar", lambda s: s.nop(), inc=("Sev", 1))
                wait("gpsimd", "Sev", mk.cnt["Sev"])
                emit("gpsimd", lambda g, hh=hh, half=half: g.affine_select(
                    T["dtt"][:, hh, half, :], sm[:, 512:512 + SBLK],
                    compare_op=ALU.is_ge, fill=0.0, pattern=[[1, SBLK]],
                    base=-128 * half, channel_multiplier=-1))
                drain("gpsimd")
                emit("gpsimd", lambda g: g.nop(), inc=("Gev", 1))
        DTT_G = mk.cnt["Gev"]
        if STOP_AFTER == "P2d":
            return
        emit("scalar", lambda s: s.activation(sm[0:1, 776:780], T["slop"][0:1, 4:8],
                                              AF.Exp, bias=0.0, scale=256.0))
        drain("scalar")
        emit("scalar", lambda s: s.copy(T["smallb"][0:1, 1024:1028], sm[0:1, 776:780]))
        drain("scalar")
        emit("scalar", lambda s: s.nop(), inc=("Sev", 1))
        wait("tensor", "Sev", mk.cnt["Sev"])
        wait("tensor", "Vev", QDEC_V)
        emit("tensor", lambda e: e.matmul(P[2][:, 0:4], T["onesb"][0:1, :],
                                          T["smallb"][0:1, 1024:1028], start=True, stop=True),
             inc=("Tmm", 1))
        wait("vector", "Tmm", mk.cnt["Tmm"])
        emit("vector", lambda v: v.tensor_copy(sm[:, 772:776], P[2][:, 0:4]))
        drain("vector")
        emit("vector", lambda v: v.memset(T["kv"][:, :, :], 0.0))
        emit("vector", lambda v: v.memset(T["kvb"][:, :, :], 0.0))
        drain("vector")
        emit("vector", lambda v: v.nop(), inc=("Vev", 1))
        BD_V = mk.cnt["Vev"]

        if STOP_AFTER == "P2":
            return
        wait("gpsimd", "Tmm", MODTR_T)
        emit("gpsimd", lambda g: g.dma_start(
            out=T["wqkv_sb"][:, :, :, :],
            in_=d["wqkv_T"]), inc=("Wq", 16))
        WQKV_W = mk.cnt["Wq"]

        p7b = psl[7][:, :].bitcast(BF16)[:, 0:512]
        xs_sems = ["XsA", "XsB", "TrA"]
        dma_sem = {}
        park_p = {}
        stat_vv = {}
        bias_v = {}
        aff_v = {}
        xa_read_t = {}
        evac_vv = {}
        qmm_last = {}

        def ln_dma(g):
            sl = g % 3
            xsem = xs_sems[sl]
            if g >= 3:
                wait("sync", "Vev", aff_v[g - 3])
            emit("sync", lambda e, g=g, sl=sl: e.dma_start(
                out=T["xstage"][:, sl, :], in_=d["x_b"][g * 128:(g + 1) * 128, :]),
                inc=(xsem, 16))
            dma_sem[g] = (xsem, mk.cnt[xsem])

        def ln_stats(g):
            sl = g % 3
            xsem, cnt = dma_sem[g]
            wait("vector", xsem, cnt)
            s6 = (g % 2) * 24
            for pc in range(4):
                emit("vector", lambda v, sl=sl, pc=pc, s6=s6: v.bn_stats(
                    T["stat6"][:, s6 + pc * 6:s6 + (pc + 1) * 6],
                    T["xstage"][:, sl, pc * 512:(pc + 1) * 512]))
            drain("vector")
            emit("vector", lambda v, g=g, s6=s6: v.bn_aggr(
                T["stats"][:, 2 * (g % 16):2 * (g % 16) + 2], T["stat6"][:, s6:s6 + 24]),
                inc=("Vev", 1))
            stat_vv[g] = mk.cnt["Vev"]

        def ln_istd_pair(g):
            c = 2 * (g % 16)
            wait("vector", "Vev", stat_vv[g])
            emit("vector", lambda v, c=c: v.tensor_scalar(
                T["isc"][:, c - 2:c + 2:2], T["stats"][:, c - 1:c + 2:2],
                1e-5, None, ALU.add))
            drain("vector")
            emit("vector", lambda v, c=c: v.reciprocal(
                T["isc"][:, c - 2:c + 2:2], T["isc"][:, c - 2:c + 2:2]),
                inc=("Vev", 1))
            rv = mk.cnt["Vev"]
            wait("scalar", "Vev", rv)
            emit("scalar", lambda s, c=c: s.activation(
                T["isc"][:, c - 2:c + 2:2], T["isc"][:, c - 2:c + 2:2],
                AF.Sqrt), inc=("Sev", 1))
            rs = mk.cnt["Sev"]
            wait("vector", "Sev", rs)
            emit("vector", lambda v, c=c: v.scalar_tensor_tensor(
                T["isc"][:, c - 1:c + 2:2], T["stats"][:, c - 2:c + 2:2],
                -1.0, T["isc"][:, c - 2:c + 2:2], ALU.mult, ALU.mult),
                inc=("Vev", 1))
            bias_v[g] = mk.cnt["Vev"]

        def ln_affine(g):
            sl = g % 3
            c = 2 * (g % 16)
            wait("vector", "Vev", bias_v[g | 1])
            if g >= 3:
                wait("vector", "Tmm", xa_read_t[g - 3])
                wait("vector", ["PkA", "PkB", "TrB"][(g - 3) % 3], park_p[g - 3])
            emit("vector", lambda v, g=g, sl=sl, c=c: v.tensor_scalar(
                T["xaff"][:, g % 3, :], T["xstage"][:, sl, :],
                T["isc"][:, c:c + 1], T["isc"][:, c + 1:c + 2],
                ALU.mult, ALU.add), inc=("Vev", 1))
            aff_v[g] = mk.cnt["Vev"]
            wait("gpsimd", "Vev", aff_v[g])
            wait("gpsimd", "Xld", XLD0)
            pk = ["PkA", "PkB", "TrB"][g % 3]
            emit("gpsimd", lambda gg, g=g: gg.indirect_dma_start(
                out=d["xnm_park"],
                out_offset=bass.IndirectOffsetOnAxis(ap=T["idx_p"][:, g:g + 1], axis=0),
                in_=T["xaff"][:, g % 3, :], in_offset=None,
                bounds_check=NT, oob_is_err=False), inc=(pk, 16))
            park_p[g] = mk.cnt[pk]

        def xn_pass(blk, pi):
            jj, p = pi // 4, pi % 4
            g = 4 * blk + jj
            wait("tensor", "Vev", aff_v[g])
            prev = evac_vv.get((blk, pi - 1), evac_vv.get((blk - 1, 15), 0))
            if prev:
                wait("tensor", "Vev", prev)
            def trp(e, g=g, p=p):
                for i, k in enumerate(range(4 * p, 4 * p + 4)):
                    last = e.transpose(p7b[:, 128 * i:128 * (i + 1)],
                                       T["xaff"][:, g % 3, 128 * k:128 * (k + 1)],
                                       T["ident"][:, :])
                return last
            emit("tensor", trp, inc=("Tmm", 1))
            if p == 3:
                xa_read_t[g] = mk.cnt["Tmm"]
            wait("vector", "Tmm", mk.cnt["Tmm"])
            if pi == 0 and blk >= 2:
                wait("vector", "Tmm", qmm_last[blk - 2])
            emit("vector", lambda v, blk=blk, jj=jj, p=p: v.tensor_copy(
                T["hblk"][:, blk % 2, 4 * p:4 * p + 4, 128 * jj:128 * (jj + 1)],
                p7b[:, 0:512]), inc=("Vev", 1))
            evac_vv[(blk, pi)] = mk.cnt["Vev"]

        for g in range(3):
            ln_dma(g)
        ln_stats(0)
        ln_stats(1)
        ln_istd_pair(1)
        ln_affine(0)
        ln_affine(1)
        ln_dma(3)
        ln_stats(2)
        ln_stats(3)
        ln_istd_pair(3)
        ln_affine(2)
        for pi in range(4):
            xn_pass(0, pi)
        ln_affine(3)
        for pi in range(4, 16):
            xn_pass(0, pi)

        if STOP_AFTER == "P3":
            return
        hv = [0, 0]
        qkv_sev = [0] * NTB
        scan_state = dict(tr_evac_v=0, mask_v=0, kvupd_v=0)
        att_evac = {3: 0, 4: 0}
        kvb_sev = [0] * HPC
        bank_evac_s = [0, 0]

        for tb in range(NTB):
            buf = tb % 2
            blk = tb + 1 if tb + 1 < NTB else None
            if blk is not None:
                for jj in range(3):
                    ln_dma(4 * blk + jj)
            wait("vector", "Vev", MODD_V)
            wait("vector", "Vev", evac_vv[(tb, 15)])
            for k in range(KC):
                emit("vector", lambda v, buf=buf, k=k: v.tensor_scalar(
                    T["hblk"][:, buf, k, :], T["hblk"][:, buf, k, :],
                    T["modd"][:, 16 + k:17 + k], T["modd"][:, 0 + k:1 + k], ALU.mult, ALU.add),
                    inc=("Vev", 1) if k == KC - 1 else None)
            hv[buf] = mk.cnt["Vev"]

            wait("tensor", "Vev", hv[buf])
            wait("tensor", "Wq", WQKV_W)
            qbuf = tb % 2
            if tb >= 2:
                wait("tensor", "Qfree", tb - 1)
            for m in range(12):
                bk = m % 2
                wait("tensor", "Sev", bank_evac_s[bk])
                def qmm(e, m=m, buf=buf, bk=bk):
                    for k in range(KC):
                        last = e.matmul(P[bk][:, :], T["wqkv_sb"][:, m, k, :],
                                 T["hblk"][:, buf, k, :], start=(k == 0), stop=(k == KC - 1))
                    return last
                emit("tensor", qmm, inc=("Tmm", 1))
                wait("scalar", "Tmm", mk.cnt["Tmm"])
                emit("scalar", lambda s, m=m, qbuf=qbuf, bk=bk: s.activation(
                    T["qkvblk"][:, qbuf, m, :], P[bk][:, :], AF.Silu), inc=("Sev", 1))
                bank_evac_s[bk] = mk.cnt["Sev"]
            qkv_sev[tb] = mk.cnt["Sev"]
            qmm_last[tb] = mk.cnt["Tmm"]

            for jj in range(2):
                j = 2 * tb + jj
                for hh in range(HPC):
                    st = 4 * jj + hh
                    wait("tensor", "Sev", qkv_sev[tb])
                    wait("tensor", "Vev", scan_state["tr_evac_v"])
                    def tr4(e, hh=hh, jj=jj, qbuf=qbuf):
                        for half in range(2):
                            last = e.transpose(pstr[:, half * 128:(half + 1) * 128],
                                        T["qkvblk"][:, qbuf, 4 + hh,
                                                    jj * SBLK + half * 128:jj * SBLK + (half + 1) * 128],
                                        T["ident"][:, :])
                        for half in range(2):
                            last = e.transpose(pstr[:, 256 + half * 128:256 + (half + 1) * 128],
                                        T["qkvblk"][:, qbuf, 8 + hh,
                                                    jj * SBLK + half * 128:jj * SBLK + (half + 1) * 128],
                                        T["ident"][:, :])
                        return last
                    emit("tensor", tr4, inc=("Tmm", 1))
                    wait("vector", "Tmm", mk.cnt["Tmm"])
                    wait("vector", "Vev", QDEC_V)
                    for half in range(2):
                        emit("vector", lambda v, hh=hh, half=half: v.tensor_scalar(
                            T["scanbuf"][:, hh, 2, half * 128:(half + 1) * 128],
                            pstr[:, half * 128:(half + 1) * 128],
                            T["kdec"][:, hh, half:half + 1], None, ALU.mult))
                    for half in range(2):
                        emit("vector", lambda v, hh=hh, half=half: v.tensor_copy(
                            T["scanbuf"][:, hh, 3, half * 128:(half + 1) * 128],
                            pstr[:, 256 + half * 128:256 + (half + 1) * 128]))
                    emit("vector", lambda v, hh=hh, jj=jj, qbuf=qbuf: v.tensor_tensor(
                        T["scanbuf"][:, hh, 4, :],
                        T["qkvblk"][:, qbuf, hh, jj * SBLK:(jj + 1) * SBLK],
                        T["qdec"][:, hh, :], ALU.mult), inc=("Vev", 1))
                    scan_state["tr_evac_v"] = mk.cnt["Vev"]
                    wait("tensor", "Vev", scan_state["tr_evac_v"])
                    wait("tensor", "Vev", scan_state["mask_v"])
                    def smm(e, hh=hh, jj=jj, qbuf=qbuf):
                        for half in range(2):
                            last = e.matmul(P[2][:, half * SBLK:(half + 1) * SBLK],
                                     T["qkvblk"][:, qbuf, 4 + hh,
                                                 jj * SBLK + half * 128:jj * SBLK + (half + 1) * 128],
                                     T["qkvblk"][:, qbuf, hh, jj * SBLK:(jj + 1) * SBLK],
                                     start=True, stop=True)
                        return last
                    emit("tensor", smm, inc=("Tmm", 1))
                    wait("vector", "Tmm", mk.cnt["Tmm"])
                    wait("vector", "Gev", DTT_G)
                    for half in range(2):
                        ii = emit("vector", lambda v, hh=hh, half=half: v.tensor_tensor(
                            T["scanbuf"][:, hh, half, :], P[2][:, half * SBLK:(half + 1) * SBLK],
                            T["dtt"][:, hh, half, :], ALU.mult),
                            inc=("Vev", 1) if half == 1 else None)
                    scan_state["mask_v"] = mk.cnt["Vev"]
                    abank = 3 + (hh % 2)
                    wait("tensor", "Vev", scan_state["mask_v"])
                    wait("tensor", "Sev", kvb_sev[hh])
                    wait("tensor", "Sev", att_evac[abank])
                    def amm(e, hh=hh, abank=abank):
                        last = e.matmul(P[abank][:, 0:SBLK], T["kvb"][:, hh, :],
                                 T["scanbuf"][:, hh, 4, :], start=True, stop=False)
                        for half in range(2):
                            last = e.matmul(P[abank][:, 0:SBLK],
                                     T["scanbuf"][:, hh, 3, half * 128:(half + 1) * 128],
                                     T["scanbuf"][:, hh, half, :],
                                     start=False, stop=(half == 1))
                        return last
                    emit("tensor", amm, inc=("Tmm", 1))
                    att_t = mk.cnt["Tmm"]
                    wait("tensor", "Vev", scan_state["kvupd_v"])
                    def kmm(e, hh=hh):
                        for half in range(2):
                            last = e.matmul(P[5][:, 0:128],
                                     T["scanbuf"][:, hh, 2, half * 128:(half + 1) * 128],
                                     T["scanbuf"][:, hh, 3, half * 128:(half + 1) * 128],
                                     start=(half == 0), stop=(half == 1))
                        return last
                    emit("tensor", kmm, inc=("Tmm", 1))
                    kv_t = mk.cnt["Tmm"]
                    if blk is not None:
                        if st == 4:
                            for pi in range(4):
                                xn_pass(blk, pi)
                        elif st == 5:
                            for pi in range(4, 8):
                                xn_pass(blk, pi)
                    wait("scalar", "Tmm", att_t)
                    wait("scalar", "Bnc", mk.cnt["Bnc"])
                    emit("scalar", lambda s, hh=hh, abank=abank: s.activation(
                        T["bnc_st"][:, hh, :],
                        P[abank][:, 0:SBLK], AF.Copy), inc=("Sev", 1))
                    att_evac[abank] = mk.cnt["Sev"]
                    wait("sync", "Sev", att_evac[abank])
                    emit("sync", lambda e, hh=hh, j=j: e.dma_start(
                        out=d["att_gi"][hh * 128:(hh + 1) * 128, j * SBLK:(j + 1) * SBLK],
                        in_=T["bnc_st"][:, hh, :]),
                        inc=("Bnc", 16))
                    wait("vector", "Tmm", kv_t)
                    wait("vector", "Vev", BD_V)
                    emit("vector", lambda v, hh=hh: v.scalar_tensor_tensor(
                        T["kv"][:, hh, :], T["kv"][:, hh, :], sm[:, 772 + hh:773 + hh],
                        P[5][:, 0:128], ALU.mult, ALU.add), inc=("Vev", 1))
                    scan_state["kvupd_v"] = mk.cnt["Vev"]
                    wait("scalar", "Vev", scan_state["kvupd_v"])
                    emit("scalar", lambda s, hh=hh: s.copy(T["kvb"][:, hh, :], T["kv"][:, hh, :]),
                         inc=("Sev", 1))
                    kvb_sev[hh] = mk.cnt["Sev"]
                    if blk is not None:
                        g0 = 4 * blk
                        if st == 0:
                            ln_stats(g0)
                        elif st == 1:
                            ln_stats(g0 + 1)
                            ln_istd_pair(g0 + 1)
                        elif st == 2:
                            ln_affine(g0)
                            ln_affine(g0 + 1)
                            ln_dma(g0 + 3)
                        elif st == 3:
                            ln_stats(g0 + 2)
                        elif st == 6:
                            ln_stats(g0 + 3)
                            ln_istd_pair(g0 + 3)
                        elif st == 7:
                            ln_affine(g0 + 2)
            if blk is not None:
                for pi in range(8, 12):
                    xn_pass(blk, pi)
                ln_affine(4 * blk + 3)
                for pi in range(12, 16):
                    xn_pass(blk, pi)
            emit("vector", lambda v: v.nop(), inc=("Qfree", 1))
        SCAN_T = mk.cnt["Tmm"]

        PARK_ALL = (mk.cnt["PkA"], mk.cnt["PkB"], mk.cnt["TrB"])
        BNC_ALL = mk.cnt["Bnc"]

        if STOP_AFTER == "P4":
            return
        wait("sync", "Tmm", mk.cnt["Tmm"])
        wait("sync", "Vev", mk.cnt["Vev"])
        wait("sync", "Sev", mk.cnt["Sev"])
        wait("sync", "Bnc", mk.cnt["Bnc"])
        for pk_i, pk_n in enumerate(["PkA", "PkB", "TrB"]):
            wait("sync", pk_n, PARK_ALL[pk_i])
        for half in range(2):
            emit("sync", lambda e, half=half: e.dma_start_transpose(
                T["h1m"][:, :, half * TBLK:(half + 1) * TBLK],
                d["xnm_park"][half * TBLK:(half + 1) * TBLK, :]), inc=("Trp", 16))
        wait("vector", "Trp", mk.cnt["Trp"])
        for k in range(KC):
            emit("vector", lambda v, k=k: v.tensor_scalar(
                T["h1m"][:, k, :], T["h1m"][:, k, :],
                T["modd"][:, 16 + k:17 + k], T["modd"][:, 0 + k:1 + k], ALU.mult, ALU.add),
                inc=("Vev", 1) if k == KC - 1 else None)
        H1M_V = mk.cnt["Vev"]

        wait("gpsimd", "Bnc", BNC_ALL)
        emit("gpsimd", lambda g: g.collective_compute(
            "AllGather", ALU.bypass, replica_groups=[[0, 1, 2, 3], [4, 5, 6, 7]],
            ins=[d["att_gi"].opt()], outs=[d["att_go"].opt()]), inc=("CC", 1))
        ATTCC = mk.cnt["CC"]

        wait("gpsimd", "Vev", aff_v[31])
        emit("gpsimd", lambda g: g.dma_start(
            out=T["wcolA"][:, 0, :, :], in_=d["wg_T"][0]),
            inc=("WlA", 16))
        gate_evac = [0, 0]
        gate_t = []
        for m in range(KC):
            if m + 1 < KC:
                if m >= 1:
                    wait("gpsimd", "Tmm", gate_t[m - 1])
                wl = "WlA" if (m + 1) % 2 == 0 else "WlB"
                emit("gpsimd", lambda g, m=m: g.dma_start(
                    out=T["wcolA"][:, (m + 1) % 2, :, :],
                    in_=d["wg_T"][m + 1]), inc=(wl, 16))
            wl_m = "WlA" if m % 2 == 0 else "WlB"
            wait("tensor", wl_m, mk.cnt[wl_m])
            wait("tensor", "Vev", H1M_V)
            wait("tensor", "Sev", gate_evac[m % 2])
            def gmm(e, m=m):
                for half in range(2):
                    for k in range(KC):
                        last = e.matmul(P[2 * (m % 2) + half][:, :], T["wcolA"][:, m % 2, k, :],
                                 T["h1m"][:, k, half * TBLK:(half + 1) * TBLK],
                                 start=(k == 0), stop=(k == KC - 1))
                return last
            emit("tensor", gmm, inc=("Tmm", 1))
            gate_t.append(mk.cnt["Tmm"])
            wait("scalar", "Tmm", mk.cnt["Tmm"])
            for half in range(2):
                emit("scalar", lambda s, m=m, half=half: s.activation(
                    T["gate_h2"][:, m, half * TBLK:(half + 1) * TBLK],
                    P[2 * (m % 2) + half][:, :], AF.Sigmoid))
            drain("scalar")
            emit("scalar", lambda s: s.nop(), inc=("Sev", 1))
            gate_evac[m % 2] = mk.cnt["Sev"]
        GATE_S = mk.cnt["Sev"]
        tap("gate", lambda: T["gate_h2"][:, :, :], after=("Sev", GATE_S))

        if STOP_AFTER == "P5":
            return
        wait("gpsimd", "CC", ATTCC)
        for k in range(KC):
            emit("gpsimd", lambda g, k=k: g.indirect_dma_start(
                out=T["attnz"][:, k, :], out_offset=None,
                in_=d["att_go"].rearrange("a (q c) -> (a q) c", c=NT),
                in_offset=bass.IndirectOffsetOnAxis(ap=T["idx_a"][:, k:k + 1], axis=0)),
                inc=("Gth", 16))
        GTH_A = mk.cnt["Gth"]
        tap("attng", lambda: T["attnz"][:, :, :], after=("Gth", GTH_A))

        rms_t = 0
        for k in range(KC):
            wait("vector", "Gth", GTH_A)
            if k > 0:
                wait("vector", "Tmm", rms_t)
            emit("vector", lambda v, k=k: v.tensor_tensor(
                T["smallb"][:, 0:NT], T["attnz"][:, k, :], T["attnz"][:, k, :], ALU.mult),
                inc=("Vev", 1))
            wait("tensor", "Vev", mk.cnt["Vev"])
            def rmm(e, k=k):
                last = e.matmul(P[4][0:1, :], T["onecol"][:, 0:1], T["smallb"][:, 0:512],
                         start=(k == 0), stop=(k == KC - 1))
                last = e.matmul(P[5][0:1, :], T["onecol"][:, 0:1], T["smallb"][:, 512:1024],
                         start=(k == 0), stop=(k == KC - 1))
                return last
            emit("tensor", rmm, inc=("Tmm", 1))
            rms_t = mk.cnt["Tmm"]
        wait("scalar", "Tmm", rms_t)
        for half, bank in ((0, 4), (1, 5)):
            emit("scalar", lambda s, half=half, bank=bank: s.activation(
                T["rstat"][0:1, half * 512:(half + 1) * 512], P[bank][0:1, :],
                AF.Ln, bias=T["consts"][0:1, 2:3], scale=T["consts"][0:1, 3:4]))
        drain("scalar")
        for half in range(2):
            emit("scalar", lambda s, half=half: s.activation(
                T["rstat"][0:1, half * 512:(half + 1) * 512],
                T["rstat"][0:1, half * 512:(half + 1) * 512],
                AF.Exp, bias=0.0, scale=T["consts"][0:1, 1:2]))
        drain("scalar")
        emit("scalar", lambda s: s.copy(T["smallb"][0:1, 0:NT], T["rstat"][0:1, 0:NT]))
        drain("scalar")
        emit("scalar", lambda s: s.nop(), inc=("Sev", 1))
        wait("tensor", "Sev", mk.cnt["Sev"])
        for half in range(2):
            if half == 1:
                wait("tensor", "Vev", mk.cnt["Vev"])
            emit("tensor", lambda e, half=half: e.matmul(
                P[6][:, :], T["onesb"][0:1, :], T["smallb"][0:1, half * 512:(half + 1) * 512],
                start=True, stop=True), inc=("Tmm", 1))
            wait("vector", "Tmm", mk.cnt["Tmm"])
            emit("vector", lambda v, half=half: v.tensor_copy(
                T["rbc"][:, half * 512:(half + 1) * 512], P[6][:, :]))
            drain("vector")
            emit("vector", lambda v: v.nop(), inc=("Vev", 1))
        RBC_V = mk.cnt["Vev"]

        wait("vector", "Sev", GATE_S)
        wait("vector", "Vev", RBC_V)
        for k in range(KC):
            emit("vector", lambda v, k=k: v.scalar_tensor_tensor(
                T["attnz"][:, k, :], T["attnz"][:, k, :], T["nw_sb"][:, k:k + 1],
                T["gate_h2"][:, k, :], ALU.mult, ALU.mult),
                inc=("Vev", 1) if k == KC - 1 else None)
        wait("vector", "Vev", mk.cnt["Vev"])
        for k in range(KC):
            emit("vector", lambda v, k=k: v.tensor_tensor(
                T["attnz"][:, k, :], T["attnz"][:, k, :], T["rbc"][:, :], ALU.mult),
                inc=("Vev", 1) if k == KC - 1 else None)
        Z_V = mk.cnt["Vev"]
        tap("z", lambda: T["attnz"][:, :, :], after=("Vev", Z_V))

        wait("gpsimd", "Tmm", gate_t[-1])
        emit("gpsimd", lambda g: g.dma_start(
            out=T["wcolA"][:, 0, :, :], in_=d["wo_T"][0]),
            inc=("WlA", 16))
        xp_evac = [0, 0]
        xp_done = [0, 0]
        op_t = []
        ln2_t = 0
        for m in range(KC):
            if m + 1 < KC:
                if m >= 1:
                    wait("gpsimd", "Tmm", op_t[m - 1])
                wl = "WlA" if (m + 1) % 2 == 0 else "WlB"
                emit("gpsimd", lambda g, m=m: g.dma_start(
                    out=T["wcolA"][:, (m + 1) % 2, :, :],
                    in_=d["wo_T"][m + 1]), inc=(wl, 16))
            xl_m = "XlA" if m % 2 == 0 else "XlB"
            if m >= 2:
                wait("sync", "Vev", xp_done[m % 2])
                wait("sync", "Park", mk.cnt["Park"])
            emit("sync", lambda e, m=m: e.dma_start(
                out=T["resid"][:, m % 2, :], in_=d["xT_m"][m * 128:(m + 1) * 128, :]),
                inc=(xl_m, 16))
            xld_m = mk.cnt[xl_m]
            wl_m = "WlA" if m % 2 == 0 else "WlB"
            wait("tensor", wl_m, mk.cnt[wl_m])
            wait("tensor", "Vev", Z_V)
            for half in range(2):
                wait("tensor", "Vev", xp_evac[half])
                def omm(e, m=m, half=half):
                    for k in range(KC):
                        last = e.matmul(P[half][:, :], T["wcolA"][:, m % 2, k, :],
                                 T["attnz"][:, k, half * TBLK:(half + 1) * TBLK],
                                 start=(k == 0), stop=(k == KC - 1))
                    return last
                emit("tensor", omm, inc=("Tmm", 1))
                omm_t = mk.cnt["Tmm"]
                wait("vector", "Tmm", omm_t)
                wait("vector", xl_m, xld_m)
                emit("vector", lambda v, m=m, half=half: v.scalar_tensor_tensor(
                    T["resid"][:, m % 2, half * TBLK:(half + 1) * TBLK],
                    P[half][:, :], T["modd"][:, 32 + m:33 + m],
                    T["resid"][:, m % 2, half * TBLK:(half + 1) * TBLK], ALU.mult, ALU.add))
                drain("vector")
                emit("vector", lambda v: v.nop(), inc=("Vev", 1))
                xp_evac[half] = mk.cnt["Vev"]
            emit("vector", lambda v, m=m: v.tensor_copy(
                T["smallb"][:, 0:NT], T["resid"][:, m % 2, :]))
            emit("vector", lambda v, m=m: v.tensor_tensor(
                T["smallb"][:, NT:2 * NT], T["resid"][:, m % 2, :], T["resid"][:, m % 2, :],
                ALU.mult), inc=("Vev", 1))
            xp_done[m % 2] = mk.cnt["Vev"]
            wait("sync", "Vev", xp_done[m % 2])
            emit("sync", lambda e, m=m: e.dma_start(
                out=d["xp_park"][m * 128:(m + 1) * 128, :], in_=T["resid"][:, m % 2, :]),
                inc=("Park", 16))
            wait("tensor", "Vev", xp_done[m % 2])
            def lsmm(e, m=m):
                last = e.matmul(P[2][0:1, :], T["onecol"][:, 0:1], T["smallb"][:, 0:512],
                         start=(m == 0), stop=(m == KC - 1))
                last = e.matmul(P[3][0:1, :], T["onecol"][:, 0:1], T["smallb"][:, 512:1024],
                         start=(m == 0), stop=(m == KC - 1))
                last = e.matmul(P[6][0:1, :], T["onecol"][:, 0:1], T["smallb"][:, NT:NT + 512],
                         start=(m == 0), stop=(m == KC - 1))
                last = e.matmul(P[4][0:1, :], T["onecol"][:, 0:1], T["smallb"][:, NT + 512:2 * NT],
                         start=(m == 0), stop=(m == KC - 1))
                return last
            emit("tensor", lsmm, inc=("Tmm", 1))
            ln2_t = mk.cnt["Tmm"]
            op_t.append(mk.cnt["Tmm"])
        XPARK_ALL = mk.cnt["Park"]

        wait("vector", "Tmm", ln2_t)
        emit("vector", lambda v: v.tensor_scalar(T["lrow"][0:1, 0:512], P[2][0:1, :],
                                                 1.0 / 2048.0, None, ALU.mult))
        emit("vector", lambda v: v.tensor_scalar(T["lrow"][0:1, 512:1024], P[3][0:1, :],
                                                 1.0 / 2048.0, None, ALU.mult))
        emit("vector", lambda v: v.tensor_scalar(T["lrow"][0:1, 1024:1536], P[6][0:1, :],
                                                 1.0 / 2048.0, None, ALU.mult))
        emit("vector", lambda v: v.tensor_scalar(T["lrow"][0:1, 1536:2048], P[4][0:1, :],
                                                 1.0 / 2048.0, None, ALU.mult))
        drain("vector")
        emit("vector", lambda v: v.tensor_tensor(sm[0:1, 0:512], T["lrow"][0:1, 0:512],
                                                 T["lrow"][0:1, 0:512], ALU.mult))
        drain("vector")
        emit("vector", lambda v: v.tensor_tensor(T["lrow"][0:1, 1024:1536],
                                                 T["lrow"][0:1, 1024:1536],
                                                 sm[0:1, 0:512], ALU.subtract))
        drain("vector")
        emit("vector", lambda v: v.tensor_tensor(sm[0:1, 0:512], T["lrow"][0:1, 512:1024],
                                                 T["lrow"][0:1, 512:1024], ALU.mult))
        drain("vector")
        emit("vector", lambda v: v.tensor_tensor(T["lrow"][0:1, 1536:2048],
                                                 T["lrow"][0:1, 1536:2048],
                                                 sm[0:1, 0:512], ALU.subtract))
        drain("vector")
        emit("vector", lambda v: v.nop(), inc=("Vev", 1))
        wait("scalar", "Vev", mk.cnt["Vev"])
        emit("scalar", lambda s: s.activation(T["lrow"][0:1, 1024:2048], T["lrow"][0:1, 1024:2048],
                                              AF.Ln, bias=T["consts"][0:1, 0:1], scale=1.0))
        drain("scalar")
        emit("scalar", lambda s: s.activation(T["lrow"][0:1, 1024:2048], T["lrow"][0:1, 1024:2048],
                                              AF.Exp, bias=0.0, scale=T["consts"][0:1, 1:2]))
        drain("scalar")
        emit("scalar", lambda s: s.nop(), inc=("Sev", 1))
        wait("vector", "Sev", mk.cnt["Sev"])
        emit("vector", lambda v: v.tensor_tensor(T["lrow"][0:1, 0:1024], T["lrow"][0:1, 0:1024],
                                                 T["lrow"][0:1, 1024:2048], ALU.mult))
        drain("vector")
        emit("vector", lambda v: v.tensor_scalar(T["lrow"][0:1, 0:1024], T["lrow"][0:1, 0:1024],
                                                 -1.0, None, ALU.mult))
        drain("vector")
        emit("vector", lambda v: v.tensor_copy(T["smallb"][0:1, 0:NT], T["lrow"][0:1, 1024:2048]))
        emit("vector", lambda v: v.tensor_copy(T["smallb"][0:1, NT:2 * NT], T["lrow"][0:1, 0:1024]))
        drain("vector")
        emit("vector", lambda v: v.nop(), inc=("Vev", 1))
        wait("tensor", "Vev", mk.cnt["Vev"])
        for half in range(2):
            if half == 1:
                wait("tensor", "Vev", mk.cnt["Vev"])
            emit("tensor", lambda e, half=half: e.matmul(
                P[5][:, :], T["onesb"][0:1, :], T["smallb"][0:1, half * 512:(half + 1) * 512],
                start=True, stop=True), inc=("Tmm", 1))
            wait("vector", "Tmm", mk.cnt["Tmm"])
            emit("vector", lambda v, half=half: v.tensor_copy(
                T["rbc"][:, half * 512:(half + 1) * 512], P[5][:, :]))
            drain("vector")
            emit("vector", lambda v: v.nop(), inc=("Vev", 1))
            wait("tensor", "Vev", mk.cnt["Vev"])
            emit("tensor", lambda e, half=half: e.matmul(
                P[5][:, :], T["onesb"][0:1, :],
                T["smallb"][0:1, NT + half * 512:NT + (half + 1) * 512],
                start=True, stop=True), inc=("Tmm", 1))
            wait("vector", "Tmm", mk.cnt["Tmm"])
            emit("vector", lambda v, half=half: v.tensor_copy(
                T["smallb"][:, 2048 + half * 512:2048 + (half + 1) * 512], P[5][:, :]))
            drain("vector")
            emit("vector", lambda v: v.nop(), inc=("Vev", 1))
        IBC_V = mk.cnt["Vev"]

        for m in range(KC):
            xl_m = "XlA" if m % 2 == 0 else "XlB"
            wait("sync", "Park", XPARK_ALL)
            if m >= 2:
                wait("sync", "Vev", mk.cnt["Vev"])
            emit("sync", lambda e, m=m: e.dma_start(
                out=T["resid"][:, m % 2, :], in_=d["xp_park"][m * 128:(m + 1) * 128, :]),
                inc=(xl_m, 16))
            wait("vector", xl_m, mk.cnt[xl_m])
            wait("vector", "Vev", IBC_V)
            emit("vector", lambda v, m=m: v.tensor_tensor(
                T["gate_h2"][:, m, :], T["resid"][:, m % 2, :], T["rbc"][:, :], ALU.mult),
                inc=("Vev", 1))
            wait("vector", "Vev", mk.cnt["Vev"])
            emit("vector", lambda v, m=m: v.tensor_tensor(
                T["gate_h2"][:, m, :], T["gate_h2"][:, m, :],
                T["smallb"][:, 2048:3072], ALU.add), inc=("Vev", 1))
            wait("vector", "Vev", mk.cnt["Vev"])
            emit("vector", lambda v, m=m: v.tensor_scalar(
                T["gate_h2"][:, m, :], T["gate_h2"][:, m, :],
                T["modd"][:, 64 + m:65 + m], T["modd"][:, 48 + m:49 + m], ALU.mult, ALU.add),
                inc=("Vev", 1))
        H2_V = mk.cnt["Vev"]
        tap("h2", lambda: T["gate_h2"][:, :, :], after=("Vev", H2_V))

        if STOP_AFTER == "P6":
            return
        wait("gpsimd", "Sev", GATE_S)
        emit("gpsimd", lambda g: g.dma_start(
            out=T["wr_sb"][:, :, :], in_=d["wr_T"]),
            inc=("Wst", 16))
        for i, nm in enumerate(["agp1", "agp2", "aup1", "aup2"]):
            emit("gpsimd", lambda g, i=i, nm=nm: g.dma_start(
                out=T["agp_sb"][:, i, :, :], in_=d[nm]),
                inc=("Wst", 16))
        emit("gpsimd", lambda g: g.dma_start(out=T["adS_sb"][:, :, :, :],
                                             in_=d["adS"]), inc=("Wst", 16))
        emit("gpsimd", lambda g: g.dma_start(out=T["bdS_sb"][:, :, :], in_=d["bdS"]), inc=("Wst", 16))
        WFFN_W = mk.cnt["Wst"]

        wait("tensor", "Wst", WFFN_W)
        wait("tensor", "Vev", H2_V)
        def rlmm2(e):
            for half in range(2):
                bank = P[4] if half == 0 else P[5]
                for k in range(KC):
                    last = e.matmul(bank[0:E, :], T["wr_sb"][:, k, :],
                             T["gate_h2"][:, k, half * TBLK:(half + 1) * TBLK],
                             start=(k == 0), stop=(k == KC - 1))
            return last
        emit("tensor", rlmm2, inc=("Tmm", 1))
        wait("vector", "Tmm", mk.cnt["Tmm"])
        emit("vector", lambda v: v.tensor_copy(T["rl8"][:, 0:512], P[4][0:E, :]))
        emit("vector", lambda v: v.tensor_copy(T["rl8"][:, 512:1024], P[5][0:E, :]))
        drain("vector")
        emit("vector", lambda v: v.nop(), inc=("Vev", 1))
        wait("tensor", "Vev", mk.cnt["Vev"])
        for c in range(8):
            emit("tensor", lambda e, c=c: e.transpose(
                P[4][:, c * 8:(c + 1) * 8], T["rl8"][:, c * 128:(c + 1) * 128],
                T["identf"][0:8, 0:8]), inc=("Tmm", 1))
        wait("vector", "Tmm", mk.cnt["Tmm"])
        for c in range(8):
            emit("vector", lambda v, c=c: v.tensor_copy(T["rt"][:, c, :], P[4][:, c * 8:(c + 1) * 8]))
        drain("vector")
        for c in range(8):
            emit("vector", lambda v, c=c: v.reduce_max(T["rcol"][:, c, 0:1], T["rt"][:, c, :], AX.X))
        drain("vector")
        for c in range(8):
            emit("vector", lambda v, c=c: v.tensor_scalar(T["rcol"][:, c, 1:2], T["rcol"][:, c, 0:1],
                                                          -1.0, None, ALU.mult))
        drain("vector")
        emit("vector", lambda v: v.nop(), inc=("Vev", 1))
        wait("scalar", "Vev", mk.cnt["Vev"])
        for c in range(8):
            emit("scalar", lambda s, c=c: s.activation(
                T["rsc"][:, c, :], T["rt"][:, c, :], AF.Exp, bias=T["rcol"][:, c, 1:2], scale=1.0))
        drain("scalar")
        emit("scalar", lambda s: s.nop(), inc=("Sev", 1))
        wait("vector", "Sev", mk.cnt["Sev"])
        for c in range(8):
            emit("vector", lambda v, c=c: v.tensor_reduce(T["rcol"][:, c, 0:1], T["rsc"][:, c, :],
                                                          AX.X, ALU.add))
        drain("vector")
        for c in range(8):
            emit("vector", lambda v, c=c: v.reciprocal(T["rcol"][:, c, 1:2], T["rcol"][:, c, 0:1]))
        drain("vector")
        for c in range(8):
            emit("vector", lambda v, c=c: v.tensor_scalar(T["rt"][:, c, :], T["rsc"][:, c, :],
                                                          T["rcol"][:, c, 1:2], None, ALU.mult))
        drain("vector")
        for c in range(8):
            emit("vector", lambda v, c=c: v.reduce_max(T["rcol"][:, c, 0:1], T["rt"][:, c, :], AX.X))
        drain("vector")
        for c in range(8):
            emit("vector", lambda v, c=c: v.tensor_scalar(T["rsc"][:, c, :], T["rt"][:, c, :],
                                                          T["rcol"][:, c, 0:1], None, ALU.is_equal))
        drain("vector")
        for c in range(8):
            emit("vector", lambda v, c=c: v.scalar_tensor_tensor(
                T["rsc"][:, c, :], T["rsc"][:, c, :], -1e30, T["rt"][:, c, :], ALU.mult, ALU.add))
        drain("vector")
        for c in range(8):
            emit("vector", lambda v, c=c: v.reduce_max(T["rcol"][:, c, 2:3], T["rsc"][:, c, :], AX.X))
        drain("vector")
        for c in range(8):
            emit("vector", lambda v, c=c: v.tensor_scalar(T["rsc"][:, c, :], T["rt"][:, c, :],
                                                          T["rcol"][:, c, 2:3], None, ALU.is_ge))
        drain("vector")
        for c in range(8):
            emit("vector", lambda v, c=c: v.tensor_tensor(T["rt"][:, c, :], T["rt"][:, c, :],
                                                          T["rsc"][:, c, :], ALU.mult))
        drain("vector")
        emit("vector", lambda v: v.nop(), inc=("Vev", 1))
        TOPW_V = mk.cnt["Vev"]
        tap("topw", lambda: T["rt"][:, :, :], after=("Vev", TOPW_V))
        wait("tensor", "Vev", TOPW_V)
        for c in range(8):
            bank = P[4] if c < 4 else P[5]
            emit("tensor", lambda e, c=c, bank=bank: e.transpose(
                bank[0:8, (c % 4) * 128:(c % 4 + 1) * 128], T["rt"][:, c, :], T["identf"][:, :]),
                inc=("Tmm", 1))
        wait("vector", "Tmm", mk.cnt["Tmm"])
        emit("vector", lambda v: v.tensor_copy(T["wef"][:, 0:512], P[4][0:8, 0:512]))
        emit("vector", lambda v: v.tensor_copy(T["wef"][:, 512:1024], P[5][0:8, 0:512]))
        drain("vector")
        emit("vector", lambda v: v.nop(), inc=("Vev", 1))
        wait("tensor", "Vev", mk.cnt["Vev"])
        for par, dst in ((0, "wexpE"), (1, "wexpO")):
            for half in range(2):
                if not (par == 0 and half == 0):
                    wait("tensor", "Vev", mk.cnt["Vev"])
                emit("tensor", lambda e, par=par, half=half: e.matmul(
                    P[6][:, :], T["exp16"][:, par, :], T["wef"][:, half * 512:(half + 1) * 512],
                    start=True, stop=True), inc=("Tmm", 1))
                wait("vector", "Tmm", mk.cnt["Tmm"])
                emit("vector", lambda v, dst=dst, half=half: v.tensor_copy(
                    T[dst][:, half * 512:(half + 1) * 512], P[6][:, :]))
                drain("vector")
                emit("vector", lambda v: v.nop(), inc=("Vev", 1))
        WEXP_V = mk.cnt["Vev"]

        xa_dst = [("xagP", 0), ("xagP", 1), ("xauP", 0), ("xauP", 1)]
        for si in range(4):
            dstn, dsti = xa_dst[si]
            for half in range(2):
                if not (si == 0 and half == 0):
                    wait("tensor", "Vev", mk.cnt["Vev"])
                def xamm(e, si=si, half=half):
                    for k in range(KC):
                        last = e.matmul(P[3][:, :], T["agp_sb"][:, si, k, :],
                                 T["gate_h2"][:, k, half * TBLK:(half + 1) * TBLK],
                                 start=(k == 0), stop=(k == KC - 1))
                    return last
                emit("tensor", xamm, inc=("Tmm", 1))
                wait("vector", "Tmm", mk.cnt["Tmm"])
                wait("vector", "Xld", XLD0)
                emit("vector", lambda v, dstn=dstn, dsti=dsti, half=half: v.tensor_scalar(
                    T[dstn][:, dsti, half * TBLK:(half + 1) * TBLK], P[3][:, :],
                    T["pm2c"][:, 0:1], None, ALU.mult))
                drain("vector")
                emit("vector", lambda v: v.nop(), inc=("Vev", 1))
        XA_V = mk.cnt["Vev"]
        tap("xa", lambda: T["xagP"][:, :, :], after=("Vev", XA_V))

        if STOP_AFTER == "P7":
            return
        xat_gth = 0
        carry_sev = carry_mbv = carry_uadd = 0
        prev_silu = [0] * E
        prev_uadd = [0] * E
        prev_mult = [0] * E
        for half in range(2):
            wait("gpsimd", "Vev", XA_V)
            if half == 1:
                wait("gpsimd", "Tmm", mk.cnt["Tmm"])
            for tg in range(E):
                srcn = "xagP"; srci = 0 if tg < 4 else 1
                emit("gpsimd", lambda g, tg=tg, srcn=srcn, srci=srci, half=half: g.dma_start(
                    out=T["xat"][:, tg, :],
                    in_=T[srcn][32 * (tg % 4):32 * (tg % 4) + 32, srci,
                                half * TBLK:(half + 1) * TBLK]), inc=("Gth", 16))
            for tg in range(E):
                srcn = "xauP"; srci = 0 if tg < 4 else 1
                emit("gpsimd", lambda g, tg=tg, srcn=srcn, srci=srci, half=half: g.dma_start(
                    out=T["xat"][:, E + tg, :],
                    in_=T[srcn][32 * (tg % 4):32 * (tg % 4) + 32, srci,
                                half * TBLK:(half + 1) * TBLK]), inc=("Gth", 16))
            xat_gth = mk.cnt["Gth"]

            if half == 1:
                wait("gpsimd", "Tmm", mk.cnt["Tmm"])
            wait("gpsimd", "Tmm", op_t[-1])
            emit("gpsimd", lambda g: g.dma_start(
                out=T["wcolA"][:, 0, :, :], in_=d["Wg_T"][0]),
                inc=("WlA", 16))
            emit("gpsimd", lambda g: g.dma_start(
                out=T["wcolB"][:, 0, :, :], in_=d["Wu_T"][0]),
                inc=("WlA", 16))
            emit("gpsimd", lambda g: g.dma_start(
                out=T["bgp_sb"][:, 0, :, :], in_=d["bgp"][0]),
                inc=("WdA", 16))
            emit("gpsimd", lambda g: g.dma_start(
                out=T["bup_sb"][:, 0, :, :], in_=d["bup"][0]),
                inc=("WdA", 16))
            wait("vector", "Tmm", mk.cnt["Tmm"])
            emit("vector", lambda v: v.memset(P[6][:, :], 0.0))
            emit("vector", lambda v: v.memset(P[7][:, :], 0.0))
            drain("vector")
            emit("vector", lambda v: v.nop(), inc=("Vev", 1))
            HAZ_V = mk.cnt["Vev"]
            ffn_t = []

            chunk_t = [0] * (FC + 1)
            base_done_t = [0] * (FC + 1)
            base_g_done_t = [0] * (FC + 1)
            gbe_s = [0] * FC
            ube_s = [0] * FC
            mult_base_v = [0] * FC
            uadd_last = [0] * FC
            for i in range(FC + 1):
                if i + 1 < FC:
                    if i >= 1:
                        wait("gpsimd", "Tmm", base_done_t[i - 1])
                    wl = "WlA" if (i + 1) % 2 == 0 else "WlB"
                    for nm, dst in (("Wg_T", "wcolA"), ("Wu_T", "wcolB")):
                        emit("gpsimd", lambda g, i=i, nm=nm, dst=dst: g.dma_start(
                            out=T[dst][:, (i + 1) % 2, :, :], in_=d[nm][i + 1]), inc=(wl, 16))
                silu_s = [0] * E
                uadd_v = [0] * E
                mult_v = [0] * E
                ad_t = [0] * E
                if i >= 1:
                    pp = (i - 1) % 2
                    wait("scalar", "Tmm", base_done_t[i - 1])
                    if i >= 3:
                        wait("scalar", "Vev", uadd_last[i - 3])
                    elif carry_uadd:
                        wait("scalar", "Vev", carry_uadd)
                    emit("scalar", lambda s, pp=pp: s.copy(
                        T["ubb"][:, pp, :], P[1][:, :]), inc=("Sev", 1))
                    ube_s[i - 1] = mk.cnt["Sev"]
                    wait("vector", "Tmm", base_done_t[i - 1])
                    wait("vector", "Sev", ube_s[i - 1])
                    emit("vector", lambda v, i=i, pp=pp: v.tensor_tensor(
                        T["hbase"][:, i - 1, :], T["sgbb"][:, pp, :], T["ubb"][:, pp, :],
                        ALU.mult), inc=("Vev", 1))
                    mult_base_v[i - 1] = mk.cnt["Vev"]
                def bmmB(e, i=i, q=0, half=half):
                    bank, wc = (0, "wcolA") if q < 4 else (1, "wcolB")
                    qq = q % 4
                    for k in range(4 * qq, 4 * qq + 4):
                        last = e.matmul(P[bank][:, :], T[wc][:, i % 2, k, :],
                                        T["gate_h2"][:, k, half * TBLK:(half + 1) * TBLK],
                                        start=(k == 0), stop=(k == KC - 1),
                                        skip_group_check=True)
                    return last
                if i == 0:
                    wl_fc = "WlA"
                    wait("tensor", wl_fc, mk.cnt[wl_fc])
                    wait("tensor", "Vev", H2_V)
                    if carry_sev:
                        wait("tensor", "Sev", carry_sev)
                    def bmm(e, i=i, half=half):
                        for q in range(2 * 4):
                            last = bmmB(e, i=i, q=q, half=half)
                        return last
                    emit("tensor", bmm, inc=("Tmm", 1))
                    base_g_done_t[i] = base_done_t[i] = mk.cnt["Tmm"]
                    wait("scalar", "Tmm", base_done_t[0])
                    if carry_mbv:
                        wait("scalar", "Vev", carry_mbv)
                    emit("scalar", lambda s: s.activation(
                        T["sgbb"][:, 0, :], P[0][:, :], AF.Silu))
                    emit("scalar", lambda s: s.copy(
                        T["gbb"][:, 0, :], P[0][:, :]), inc=("Sev", 1))
                    gbe_s[0] = mk.cnt["Sev"]
                if i >= 1:
                    pp = (i - 1) % 2
                    wdp = "WdA" if pp == 0 else "WdB"
                    wait("tensor", wdp, mk.cnt[wdp])
                    wait("tensor", "Gth", xat_gth)
                    wait("tensor", "Vev", HAZ_V)
                    wait("tensor", "Sev", gbe_s[i - 1])
                    for e_i in range(E):
                        lgb = 2 + (e_i % 2)
                        lub = 4 + (e_i % 2)
                        if e_i >= 2:
                            wait("tensor", "Sev", silu_s[e_i - 2])
                            wait("tensor", "Vev", uadd_v[e_i - 2])
                        else:
                            wait("tensor", "Sev", prev_silu[e_i - 2 + E])
                            wait("tensor", "Vev", prev_uadd[e_i - 2 + E])
                        def lmm(e, i=i, e_i=e_i, lgb=lgb, lub=lub, pp=pp):
                            e.matmul(P[lgb][:, :], T["ident"][:, :], T["gbb"][:, pp, :],
                                     start=True, stop=False, skip_group_check=True)
                            e.matmul(P[lgb][:, :], T["bgp_sb"][:, pp, e_i, :],
                                     T["xat"][:, e_i, :],
                                     start=False, stop=True, skip_group_check=True)
                            last = e.matmul(P[lub][:, :], T["bup_sb"][:, pp, e_i, :],
                                            T["xat"][:, E + e_i, :],
                                            start=True, stop=True, skip_group_check=True)
                            return last
                        emit("tensor", lmm, inc=("Tmm", 1))
                        lora_t = mk.cnt["Tmm"]
                        if e_i >= 1:
                            wait("tensor", "Vev", mult_v[e_i - 1])
                            emit("tensor", lambda e, i=i, e_i=e_i: e.matmul(
                                P[6 + ((e_i - 1) % 2)][32 * ((e_i - 1) // 2):32 * ((e_i - 1) // 2) + 16, :],
                                T["adS_sb"][:, i - 1, e_i - 1, :], T["hexp"][:, (e_i - 1) % 2, :],
                                start=(i == 1), stop=True, skip_group_check=True,
                                tile_position=(0, 32 * ((e_i - 1) // 2))), inc=("Tmm", 1))
                            ad_t[e_i - 1] = mk.cnt["Tmm"]
                        if i < FC:
                            if e_i == 0:
                                wl_fc = "WlA" if i % 2 == 0 else "WlB"
                                wait("tensor", wl_fc, mk.cnt[wl_fc])
                                wait("tensor", "Sev", gbe_s[i - 1])
                            if e_i == 4:
                                wait("tensor", "Sev", ube_s[i - 1])
                            emit("tensor", lambda e, i=i, e_i=e_i, half=half: bmmB(
                                e, i=i, q=e_i, half=half), inc=("Tmm", 1))
                            if e_i == 3:
                                base_g_done_t[i] = mk.cnt["Tmm"]
                            if e_i == E - 1:
                                base_done_t[i] = mk.cnt["Tmm"]
                        wait("vector", "Tmm", lora_t)
                        if e_i >= 2:
                            wait("vector", "Vev", mult_v[e_i - 2])
                        emit("vector", lambda v, e_i=e_i, lub=lub, pp=pp: v.tensor_tensor(
                            T["usb"][:, e_i % 2, :], P[lub][:, :], T["ubb"][:, pp, :], ALU.add),
                            inc=("Vev", 1))
                        uadd_v[e_i] = mk.cnt["Vev"]
                        wait("scalar", "Tmm", lora_t)
                        if e_i >= 2:
                            wait("scalar", "Vev", mult_v[e_i - 2])
                        else:
                            wait("scalar", "Vev", prev_mult[e_i - 2 + E])
                        emit("scalar", lambda s, e_i=e_i, lgb=lgb: s.activation(
                            T["sgb"][:, e_i % 2, :], P[lgb][:, :], AF.Silu),
                            inc=("Sev", 1))
                        silu_s[e_i] = mk.cnt["Sev"]
                        if i < FC and e_i == 3:
                            wait("scalar", "Tmm", base_g_done_t[i])
                            if i >= 2:
                                wait("scalar", "Vev", mult_base_v[i - 2])
                            elif carry_mbv:
                                wait("scalar", "Vev", carry_mbv)
                            emit("scalar", lambda s, i=i: s.activation(
                                T["sgbb"][:, i % 2, :], P[0][:, :], AF.Silu))
                            emit("scalar", lambda s, i=i: s.copy(
                                T["gbb"][:, i % 2, :], P[0][:, :]), inc=("Sev", 1))
                            gbe_s[i] = mk.cnt["Sev"]
                        wait("vector", "Vev", uadd_v[e_i])
                        wait("vector", "Sev", silu_s[e_i])
                        if e_i >= 2:
                            wait("vector", "Tmm", ad_t[e_i - 2])
                        emit("vector", lambda v, e_i=e_i: v.tensor_tensor(
                            T["hexp"][:, e_i % 2, :], T["sgb"][:, e_i % 2, :],
                            T["usb"][:, e_i % 2, :], ALU.mult), inc=("Vev", 1))
                        mult_v[e_i] = mk.cnt["Vev"]
                    wait("tensor", "Vev", mult_v[E - 1])
                    emit("tensor", lambda e, i=i: e.matmul(
                        P[6 + ((E - 1) % 2)][32 * ((E - 1) // 2):32 * ((E - 1) // 2) + 16, :],
                        T["adS_sb"][:, i - 1, E - 1, :], T["hexp"][:, (E - 1) % 2, :],
                        start=(i == 1), stop=True, skip_group_check=True,
                        tile_position=(0, 32 * ((E - 1) // 2))), inc=("Tmm", 1))
                    ad_t[E - 1] = mk.cnt["Tmm"]
                    uadd_last[i - 1] = uadd_v[E - 1]
                    prev_silu = list(silu_s)
                    prev_uadd = list(uadd_v)
                    prev_mult = list(mult_v)
                chunk_t[i] = mk.cnt["Tmm"]
                if i + 1 < FC:
                    wait("gpsimd", "Tmm", chunk_t[i])
                    wlp = "WdA" if (i + 1) % 2 == 0 else "WdB"
                    for nm, dst in (("bgp", "bgp_sb"), ("bup", "bup_sb")):
                        emit("gpsimd", lambda g, i=i, nm=nm, dst=dst: g.dma_start(
                            out=T[dst][:, (i + 1) % 2, :, :], in_=d[nm][i + 1]), inc=(wlp, 16))
            carry_sev = ube_s[FC - 1]
            carry_mbv = mult_base_v[FC - 1]
            carry_uadd = uadd_last[FC - 1]
            AD_T = mk.cnt["Tmm"]
            wait("vector", "Tmm", AD_T)
            wait("vector", "Vev", WEXP_V)
            emit("vector", lambda v, half=half: v.tensor_tensor(
                T["hAwE"][:, :], P[6][:, :], T["wexpE"][:, half * TBLK:(half + 1) * TBLK], ALU.mult))
            emit("vector", lambda v, half=half: v.tensor_tensor(
                T["hAwO"][:, :], P[7][:, :], T["wexpO"][:, half * TBLK:(half + 1) * TBLK], ALU.mult))
            drain("vector")
            emit("vector", lambda v: v.nop(), inc=("Vev", 1))
            HAW_V = mk.cnt["Vev"]

            wait("gpsimd", "Tmm", AD_T)
            emit("gpsimd", lambda g: g.dma_start(
                out=T["wdcol"][:, 0, :, :], in_=d["Wd_T"][0]),
                inc=("WdA", 16))
            out_evac = [0, 0]
            dm_t = []
            for mo in range(KC):
                if mo + 1 < KC:
                    if mo >= 1:
                        wait("gpsimd", "Tmm", dm_t[mo - 1])
                    wd = "WdA" if (mo + 1) % 2 == 0 else "WdB"
                    emit("gpsimd", lambda g, mo=mo: g.dma_start(
                        out=T["wdcol"][:, (mo + 1) % 2, :, :],
                        in_=d["Wd_T"][mo + 1]), inc=(wd, 16))
                xl_mo = "XlA" if mo % 2 == 0 else "XlB"
                if mo >= 2:
                    wait("sync", "Vev", out_evac[mo % 2])
                else:
                    wait("sync", "Vev", H2_V)
                    if half == 1:
                        wait("sync", "Vev", out_evac[mo % 2])
                wait("sync", "Out", mk.cnt["Out"])
                emit("sync", lambda e, mo=mo, half=half: e.dma_start(
                    out=T["resid"][:, mo % 2, 0:TBLK],
                    in_=d["xp_park"][mo * 128:(mo + 1) * 128, half * TBLK:(half + 1) * TBLK]),
                    inc=(xl_mo, 16))
                wd_mo = "WdA" if mo % 2 == 0 else "WdB"
                wait("tensor", wd_mo, mk.cnt[wd_mo])
                wait("tensor", "Vev", HAW_V)
                if mo >= 2:
                    wait("tensor", "Vev", out_evac[mo % 2])
                def dmm(e, mo=mo):
                    for k in range(FC):
                        last = e.matmul(P[4 + (mo % 2)][:, :], T["wdcol"][:, mo % 2, k, :],
                                 T["hbase"][:, k, :], start=(k == 0), stop=False)
                    last = e.matmul(P[4 + (mo % 2)][:, :], T["bdS_sb"][:, 0, mo * 128:(mo + 1) * 128],
                             T["hAwE"][:, :], start=False, stop=False)
                    last = e.matmul(P[4 + (mo % 2)][:, :], T["bdS_sb"][:, 1, mo * 128:(mo + 1) * 128],
                             T["hAwO"][:, :], start=False, stop=True)
                    return last
                emit("tensor", dmm, inc=("Tmm", 1))
                dm_t.append(mk.cnt["Tmm"])
                wait("vector", "Tmm", mk.cnt["Tmm"])
                wait("vector", xl_mo, mk.cnt[xl_mo])
                emit("vector", lambda v, mo=mo: v.scalar_tensor_tensor(
                    T["resid"][:, mo % 2, 0:TBLK], P[4 + (mo % 2)][:, :],
                    T["modd"][:, 80 + mo:81 + mo], T["resid"][:, mo % 2, 0:TBLK],
                    ALU.mult, ALU.add))
                drain("vector")
                emit("vector", lambda v: v.nop(), inc=("Vev", 1))
                out_evac[mo % 2] = mk.cnt["Vev"]
                wait("sync", "Vev", out_evac[mo % 2])
                emit("sync", lambda e, mo=mo, half=half: e.dma_start(
                    out=d["y"][mo * 128:(mo + 1) * 128, half * TBLK:(half + 1) * TBLK],
                    in_=T["resid"][:, mo % 2, 0:TBLK]), inc=("Out", 16))
        wait("sync", "Out", mk.cnt["Out"])
        wait("gpsimd", "Out", mk.cnt["Out"])
        wait("scalar", "Out", mk.cnt["Out"])
        wait("vector", "Out", mk.cnt["Out"])
        wait("tensor", "Out", mk.cnt["Out"])

    handles = {}
    with nc.Block() as block:
        def runner(eng):
            def f(h):
                hs = {eng: h}
                mk = MK(eng, hs, sems)
                walk(mk)
            return f
        block.gpsimd(runner("gpsimd"))
        block.sync(runner("sync"))
        block.scalar(runner("scalar"))
        block.vector(runner("vector"))
        block.tensor(runner("tensor"))

    for cm in reversed(guards):
        cm.__exit__(None, None, None)
    return nc



def host_prep(inp):
    x = np.asarray(inp["L0_attn_hidden_states"], np.float32)
    c = np.asarray(inp["L0_c"], np.float32)
    wmod = np.asarray(inp["L0_wmod"], np.float32)
    wqkv = np.asarray(inp["L0_attn_w_qkv"], np.float32)
    slope = np.asarray(inp["L0_attn_slope_rate"], np.float32)
    wgate = np.asarray(inp["L0_attn_w_gate"], np.float32)
    wout = np.asarray(inp["L0_attn_w_out"], np.float32)
    normw = np.asarray(inp["L0_attn_norm_w"], np.float32)
    Wg = np.asarray(inp["L0_ffn_Wg"], np.float32)
    Wu = np.asarray(inp["L0_ffn_Wu"], np.float32)
    Wd = np.asarray(inp["L0_ffn_Wd"], np.float32)
    Wr = np.asarray(inp["L0_ffn_Wr"], np.float32)
    Ag = np.asarray(inp["L0_ffn_Ag"], np.float32)
    Bg = np.asarray(inp["L0_ffn_Bg"], np.float32)
    Au = np.asarray(inp["L0_ffn_Au"], np.float32)
    Bu = np.asarray(inp["L0_ffn_Bu"], np.float32)
    Ad = np.asarray(inp["L0_ffn_Ad"], np.float32)
    Bd = np.asarray(inp["L0_ffn_Bd"], np.float32)

    co = np.ascontiguousarray

    def mblocks(wT, nblk):
        K, N = wT.shape
        kc = K // 128
        return co(wT.reshape(kc, 128, nblk, 128).transpose(2, 1, 0, 3))

    wg_T = mblocks(wgate.T, KC)
    wo_T = mblocks(wout.T, KC)
    Wg_T = mblocks(Wg.T, FC)
    Wu_T = mblocks(Wu.T, FC)
    Wd_T = mblocks(Wd.T, KC)
    wr_T = co(Wr.T.reshape(KC, 128, E).transpose(1, 0, 2))
    nw_cols = co(normw.reshape(KC, 128).T)

    def pair_cols(A):
        AT = A.transpose(2, 0, 1)
        def mk(tt0):
            cols = []
            for tt in range(tt0, tt0 + 4):
                cols.append(AT[:, (tt - 1) % E, :])
                cols.append(AT[:, tt, :])
            m = np.concatenate(cols, axis=1)
            return co(m.reshape(KC, 128, 128).transpose(1, 0, 2))
        return mk(0), mk(4)
    agp1, agp2 = pair_cols(Ag)
    aup1, aup2 = pair_cols(Au)

    def pair_rows(Bm):
        BT = Bm.transpose(0, 2, 1)
        out = np.empty((FC, E, 32, 128), np.float32)
        for tt in range(E):
            stack = np.concatenate([BT[(tt - 1) % E], BT[tt]], axis=0)
            out[:, tt, :, :] = stack.reshape(32, FC, 128).transpose(1, 0, 2)
        return co(out.transpose(0, 2, 1, 3))
    bgp = pair_rows(Bg)
    bup = pair_rows(Bu)

    AdT = Ad.transpose(2, 0, 1).reshape(FC, 128, E, RK)
    adS = co(AdT.transpose(1, 0, 2, 3))
    bdS = np.zeros((128, 2, D), np.float32)
    for e_i in range(E):
        for i in range(RK):
            bdS[32 * (e_i // 2) + i, e_i % 2, :] = Bd[e_i, :, i]
    bdS = co(bdS)

    pm2c = co(np.where((np.arange(128) % 32) < 16, -2.0, 2.0).astype(np.float32).reshape(128, 1))
    in_maps = []
    for cid in range(8):
        b, q = cid // 4, cid % 4
        heads = list(range(4 * q, 4 * q + 4))
        rows = []
        for sec in range(3):
            for hh in heads:
                rows.extend(range(sec * D + hh * DH, sec * D + (hh + 1) * DH))
        wq = mblocks(co(wqkv[rows, :]).T, 12)
        wqkv_T = co(wq.transpose(1, 0, 2, 3))
        modidx = np.array([[2 * j + b] for j in range(8)], np.uint32)
        parkidx = np.full((128, 32), 10 ** 6, np.uint32)
        for g in range(8 * q, 8 * q + 8):
            parkidx[:, g] = (g - 8 * q) * 128 + np.arange(128)
        pidx = np.arange(128, dtype=np.uint32)[:, None]
        kidx = np.arange(KC, dtype=np.uint32)[None, :]
        aidx = (4 * (128 * kidx + pidx) + q).astype(np.uint32)
        im = {
            "x_b": co(x[b]),
            "x_m": co(x[b, NT * q:NT * (q + 1), :]),
            "xT_m": co(x[b, NT * q:NT * (q + 1), :].T),
            "c2": co(c.T.reshape(KC, 128, B).transpose(1, 0, 2)),
            "wmod_sl": co(wmod[:, 1536 * cid:1536 * (cid + 1)].reshape(KC, 128, 1536).transpose(1, 0, 2)),
            "modidx": modidx,
            "wqkv_T": wqkv_T,
            "slopes": co(slope[heads].reshape(1, HPC)),
            "wg_T": wg_T, "wo_T": wo_T, "nw": nw_cols, "aidx": aidx, "pidx": parkidx,
            "Wg_T": Wg_T, "Wu_T": Wu_T, "Wd_T": Wd_T, "wr_T": wr_T,
            "agp1": agp1, "agp2": agp2, "aup1": aup1, "aup2": aup2,
            "bgp": bgp, "bup": bup, "adS": adS, "bdS": bdS,
            "pm2": pm2c,
        }
        in_maps.append(im)
    return in_maps


_NC_CACHE = {}


def kernel(**inputs):
    in_maps = host_prep(inputs)
    if "main" not in _NC_CACHE:
        _NC_CACHE["main"] = build_program(taps=DEBUG_TAPS)
    nc = _NC_CACHE["main"]
    res = run_bass_kernel_spmd(nc, in_maps, core_ids=list(range(8)))
    out = np.empty((B, S, D), np.float32)
    for cid in range(8):
        b, q = cid // 4, cid % 4
        out[b, NT * q:NT * (q + 1), :] = res.results[cid]["y"].T
    return out


# version 11 (final)
# speedup vs baseline: 1.3165x; 1.0323x over previous
import numpy as np
import ml_dtypes
import concourse.bass as bass
import concourse.mybir as mybir
from concourse.bass_utils import run_bass_kernel_spmd

F32 = mybir.dt.float32
BF16 = mybir.dt.bfloat16
U32 = mybir.dt.uint32
U8 = mybir.dt.uint8
AF = mybir.ActivationFunctionType
ALU = mybir.AluOpType
AX = mybir.AxisListType

B, S, D, H, DH = 2, 4096, 2048, 16, 128
HPC = 4
NT = 1024
DFF, E, RK = 5632, 8, 16
KC = D // 128
FC = DFF // 128
TBLK = 512
NTB = S // TBLK
SBLK = 256
EPS_LN = 1e-5
EPS_RMS = 1e-6

DEBUG_TAPS = ()
STOP_AFTER = None


EVENTS = {}


class MK:
    def __init__(self, eng_name, handles, sems):
        self.me = eng_name
        self.h = handles
        self.sems = sems
        self.cnt = {k: 0 for k in sems}
        self.ev = []
        EVENTS[eng_name] = self.ev

    def emit(self, eng, fn, inc=None):
        if self.me == eng:
            ins = fn(self.h[eng])
            if inc is not None and ins is not None:
                ins.then_inc(self.sems[inc[0]], inc[1])
            if inc is not None:
                self.ev.append(("inc", inc[0], inc[1]))
        if inc is not None:
            self.cnt[inc[0]] += inc[1]

    def wait(self, eng, sem, val=None):
        v = self.cnt[sem] if val is None else val
        if self.me == eng and v > 0:
            self.h[eng].wait_ge(self.sems[sem], v)
            self.ev.append(("wait", sem, v))

    def drain(self, eng):
        if self.me == eng:
            self.h[eng].drain()


def build_program(taps=()):
    nc = bass.Bass("TRN2", target_bir_lowering=False)
    d = {}

    def din(name, shape, dt=F32):
        d[name] = nc.dram_tensor(name, shape, dt, kind="ExternalInput").ap()

    def dint(name, shape, dt=F32):
        d[name] = nc.dram_tensor(name, shape, dt).ap()

    din("x_b", [S, D]);  din("x_m", [NT, D]);  din("xT_m", [D, NT])
    din("c2", [128, KC, B]);   din("wmod_sl", [128, KC, 1536]);  din("modidx", [8, 1], U32)
    din("wqkv_T", [128, 12, KC, 128]);  din("slopes", [1, HPC])
    din("wg_T", [KC, 128, KC, 128]);    din("wo_T", [KC, 128, KC, 128])
    din("nw", [128, KC]);         din("aidx", [128, KC], U32);  din("hidx", [128, NTB], U32)
    din("Wg_T", [FC, 128, KC, 128]);    din("Wu_T", [FC, 128, KC, 128]);  din("Wd_T", [KC, 128, FC, 128])
    din("wr_T", [128, KC, E])
    din("agp1", [128, KC, 128]); din("agp2", [128, KC, 128]); din("aup1", [128, KC, 128]); din("aup2", [128, KC, 128])
    din("bgp", [FC, 32, E, 128]);  din("bup", [FC, 32, E, 128])
    din("adS", [128, FC, E, 16]);  din("bdS", [128, 2, D]);  din("pm2", [128, 1])
    d["y"] = nc.dram_tensor("y", [D, NT], F32, kind="ExternalOutput").ap()

    dint("xnb_park", [S, D], BF16)
    dint("xnm_park", [NT, D], BF16)
    dint("xp_park", [D, NT])
    dint("mod_gi", [B, 1536]);  dint("mod_go", [16, 1536])
    dint("att_gi", [HPC * DH, S], BF16);  dint("att_go", [D, S], BF16)
    dint("h1mp", [256, KC, TBLK], BF16)

    tapd = {}
    for tname, shape, dt in taps:
        tapd[tname] = nc.dram_tensor("tap_" + tname, shape, dt, kind="ExternalOutput").ap()

    guards = []
    arena_cm = nc.sbuf_tensor("arena", [128, 212800], U8)
    arena_cm.__enter__(); guards.append(arena_cm)
    BASE = 16512
    RA = BASE + 0
    RB = BASE + 49152
    RC = BASE + 81920
    RD = BASE + 116736
    RE = BASE + 141312
    RF = BASE + 157696

    T = {}

    def at(name, shape, dt, off):
        T[name] = nc.alloc_sbuf_tensor_at(name, shape, dt, offset=off)

    at("wmod_sb", [128, KC, 1536], BF16, RA)
    at("g8", [8, 1536], F32, RA)
    at("wqkv_sb", [128, 12, KC, 128], BF16, RA)
    at("attnz", [128, KC, NT], BF16, RA)
    at("rstat", [4, 1024], F32, RA + 40960)
    at("lrow", [1, 2048], F32, RA + 40960)
    at("rl8", [8, NT], F32, RA)
    at("wef", [8, NT], BF16, RA + 8192)
    at("hbase", [128, FC, TBLK], BF16, RA)

    at("gate_h2", [128, KC, NT], BF16, RB)
    at("modrows", [2, 1536], F32, RB)
    at("kv", [128, HPC, 128], F32, RB + 8192)
    at("kvb", [128, HPC, 128], BF16, RB + 10240)
    at("dtt", [128, HPC, 2, SBLK], BF16, RB + 11264)
    at("qdec", [128, HPC, SBLK], BF16, RB + 15360)
    at("kdec", [128, HPC, 2], F32, RB + 17408)
    at("xaff", [128, 3, D], BF16, RB + 17472)
    at("bnc_st", [128, HPC, SBLK], BF16, RB + 29760)

    at("qkvblk", [128, 2, 12, TBLK], BF16, RC)
    at("scanbuf", [128, HPC, 5, SBLK], BF16, RC + 24576)
    at("h1m", [128, KC, NT], BF16, RC)
    at("agp_sb", [128, 4, KC, 128], BF16, RC)
    at("wr_sb", [128, KC, E], BF16, RC + 16384)
    at("xagP", [128, 2, NT], BF16, RC + 16672)
    at("xauP", [128, 2, NT], BF16, RC + 20768)
    at("hexp", [128, 2, TBLK], BF16, RC + 24864)
    at("sgb", [128, 2, TBLK], BF16, RC + 26912)
    at("tadd", [128, 2, TBLK], BF16, RC + 29984)
    at("usb", [128, 2, TBLK], BF16, RC + 32032)
    at("wexpE", [128, NT], BF16, RA + 47104)
    at("gbb", [128, 2, TBLK], BF16, RC)
    at("sgbb", [128, 2, TBLK], BF16, RC + 2048)
    at("ubb", [128, 2, TBLK], BF16, RC + 4096)
    at("hAwE", [128, TBLK], BF16, RA + 45056)
    at("hAwO", [128, TBLK], BF16, RA + 46080)

    at("xstage", [128, 3, D], F32, RD)
    at("wcolA", [128, 2, KC, 128], BF16, RD)
    at("wcolB", [128, 2, KC, 128], BF16, RD + 8192)
    at("bgp_sb", [32, 2, E, 128], BF16, RD + 16384)
    at("bup_sb", [32, 2, E, 128], BF16, RD + 20480)
    at("wdcol", [128, 2, FC, 128], BF16, RD)

    at("hblk", [128, 2, KC, TBLK], BF16, RE)
    at("xat", [32, 2 * E, TBLK], BF16, RE)

    off = RF
    def fx(name, shape, dt):
        nonlocal off
        n = int(np.prod(shape[1:]))
        sz = n * (4 if dt == F32 else (2 if dt in (BF16,) else 4))
        at(name, shape, dt, off)
        off += (sz + 31) // 32 * 32

    fx("resid", [128, 2, NT], F32)
    fx("smallb", [128, 3072], BF16)
    fx("small", [128, 784], F32)
    fx("rbc", [128, NT], BF16)
    fx("adS_sb", [128, FC, E, 16], BF16)
    fx("bdS_sb", [128, 2, D], BF16)
    fx("c2_sb", [128, KC, B], F32)
    fx("c2sil", [128, KC, B], BF16)
    fx("modc", [128, 96], F32)
    fx("modd", [128, 96], F32)
    fx("stats", [128, 64], F32)
    fx("statm", [128, 16], F32)
    fx("stat6", [128, 48], F32)
    fx("isc", [128, 64], F32)
    fx("iscm", [128, 16], F32)
    fx("wexpO", [128, NT], BF16)
    fx("consts", [128, 8], F32)
    fx("nw_sb", [128, KC], F32)
    fx("rt", [128, 8, E], F32)
    fx("rsc", [128, 8, E], F32)
    fx("rcol", [128, 8, 4], F32)
    fx("ident", [128, 128], BF16)
    fx("identf", [128, 128], F32)
    fx("onesb", [1, 128], BF16)
    fx("onecol", [128, 1], BF16)
    fx("slop", [1, 8], F32)
    fx("pm2c", [128, 1], F32)
    fx("slopc", [128, 8], F32)
    fx("idx_m", [8, 1], U32)
    fx("idx_a", [128, KC], U32)
    fx("idx_h", [128, NTB], U32)
    fx("exp16", [8, 2, 128], BF16)
    assert off <= 229344, off

    sm = T["small"]

    psl = []
    for i in range(8):
        cm = nc.psum_tensor("psb%d" % i, [128, 512], F32)
        psl.append(cm.__enter__()); guards.append(cm)

    semnames = ["Wld", "Xld", "Park", "Trp", "Bnc", "Gth", "CC", "Tmm", "Sev",
                "Vev", "Gev", "Out", "Hfree", "Qfree", "PkA", "PkB", "WlA", "WlB",
                "WdA", "WdB", "XsA", "XsB", "XlA", "XlB", "TrA", "TrB", "Wq", "Wst"]
    sems = {}
    for name in semnames:
        cm = nc.semaphore(name)
        sems[name] = cm.__enter__(); guards.append(cm)

    def walk(mk: MK):
        emit, wait, drain = mk.emit, mk.wait, mk.drain
        P = psl
        pstr = psl[6][:, :].bitcast(BF16)[:, 0:512]

        def tap(name, src_fn, after=None):
            if name in tapd:
                if after:
                    wait("sync", after[0], after[1])
                emit("sync", lambda e: e.dma_start(out=tapd[name], in_=src_fn()), inc=("Out", 16))

        emit("gpsimd", lambda g: g.memset(T["consts"][:, 0:1], EPS_LN))
        emit("gpsimd", lambda g: g.memset(T["consts"][:, 1:2], -0.5))
        emit("gpsimd", lambda g: g.memset(T["consts"][:, 2:3], EPS_RMS))
        emit("gpsimd", lambda g: g.memset(T["consts"][:, 3:4], 1.0 / 2048.0))
        emit("gpsimd", lambda g: g.memset(T["onesb"][:, :], 1.0))
        emit("gpsimd", lambda g: g.memset(T["onecol"][:, :], 1.0))
        emit("gpsimd", lambda g: g.memset(T["ident"][:, :], 1.0))
        drain("gpsimd")
        emit("gpsimd", lambda g: g.affine_select(
            T["ident"][:, :], T["ident"][:, :], compare_op=ALU.is_equal, fill=0.0,
            pattern=[[1, 128]], base=0, channel_multiplier=-1))
        emit("gpsimd", lambda g: g.memset(T["identf"][:, :], 1.0))
        drain("gpsimd")
        emit("gpsimd", lambda g: g.affine_select(
            T["identf"][:, :], T["identf"][:, :], compare_op=ALU.is_equal, fill=0.0,
            pattern=[[1, 128]], base=0, channel_multiplier=-1))
        emit("gpsimd", lambda g: g.memset(T["exp16"][:, :, :], 2.0))
        drain("gpsimd")
        emit("gpsimd", lambda g: g.affine_select(
            T["exp16"][:, 0, :], T["exp16"][:, 0, :], compare_op=ALU.is_equal, fill=0.0,
            pattern=[[2, 4], [0, 32]], base=0, channel_multiplier=-1))
        drain("gpsimd")
        emit("gpsimd", lambda g: g.affine_select(
            T["exp16"][:, 1, :], T["exp16"][:, 1, :], compare_op=ALU.is_equal, fill=0.0,
            pattern=[[2, 4], [0, 32]], base=1, channel_multiplier=-1))
        drain("gpsimd")
        emit("gpsimd", lambda g: g.nop(), inc=("Gev", 1))

        emit("sync", lambda e: e.dma_start(out=T["slop"][:, 0:HPC], in_=d["slopes"]), inc=("Xld", 16))
        emit("sync", lambda e: e.dma_start(out=T["pm2c"][:, :], in_=d["pm2"]), inc=("Xld", 16))
        emit("sync", lambda e: e.dma_start(out=T["idx_m"][:, :], in_=d["modidx"]), inc=("Xld", 16))
        emit("sync", lambda e: e.dma_start(out=T["idx_a"][:, :], in_=d["aidx"]), inc=("Xld", 16))
        emit("sync", lambda e: e.dma_start(out=T["idx_h"][:, :], in_=d["hidx"]), inc=("Xld", 16))
        emit("sync", lambda e: e.dma_start(out=T["nw_sb"][:, :], in_=d["nw"]), inc=("Xld", 16))
        emit("sync", lambda e: e.dma_start(
            out=T["c2_sb"][:, :, :], in_=d["c2"]), inc=("Xld", 16))
        XLD0 = mk.cnt["Xld"]
        emit("gpsimd", lambda g: g.dma_start(
            out=T["wmod_sb"][:, :, :], in_=d["wmod_sl"]),
            inc=("Wst", 16))
        WMOD_W = mk.cnt["Wst"]

        wait("scalar", "Xld", XLD0)
        emit("scalar", lambda s: s.activation(T["c2sil"][:, :, :], T["c2_sb"][:, :, :], AF.Silu),
             inc=("Sev", 1))
        C2S_S = mk.cnt["Sev"]
        wait("tensor", "Sev", C2S_S)
        wait("tensor", "Wst", WMOD_W)
        for n0 in range(3):
            def modmm(e, n0=n0):
                for k in range(KC):
                    last = e.matmul(P[n0][0:B, :], T["c2sil"][:, k, :],
                             T["wmod_sb"][:, k, n0 * 512:(n0 + 1) * 512],
                             start=(k == 0), stop=(k == KC - 1))
                return last
            emit("tensor", modmm, inc=("Tmm", 1))
        wait("vector", "Tmm", mk.cnt["Tmm"])
        for n0 in range(3):
            emit("vector", lambda v, n0=n0: v.tensor_copy(
                T["modrows"][0:B, n0 * 512:(n0 + 1) * 512], P[n0][0:B, :]))
        drain("vector")
        emit("vector", lambda v: v.nop(), inc=("Vev", 1))
        wait("sync", "Vev", mk.cnt["Vev"])
        emit("sync", lambda e: e.dma_start(out=d["mod_gi"], in_=T["modrows"][0:B, 0:1536]), inc=("Park", 16))
        MODGI_P = mk.cnt["Park"]
        wait("gpsimd", "Park", MODGI_P)
        emit("gpsimd", lambda g: g.collective_compute(
            "AllGather", ALU.bypass, replica_groups=[list(range(8))],
            ins=[d["mod_gi"].opt()], outs=[d["mod_go"].opt()]), inc=("CC", 1))
        wait("gpsimd", "CC", 1)
        wait("gpsimd", "Xld", XLD0)
        emit("gpsimd", lambda g: g.indirect_dma_start(
            out=T["g8"][:, :], out_offset=None, in_=d["mod_go"],
            in_offset=bass.IndirectOffsetOnAxis(ap=T["idx_m"][:, :], axis=0)), inc=("Gth", 16))
        MODG_G = mk.cnt["Gth"]
        wait("tensor", "Gth", MODG_G)
        for j in range(12):
            emit("tensor", lambda e, j=j: e.transpose(
                P[1][:, j * 8:(j + 1) * 8], T["g8"][:, j * 128:(j + 1) * 128],
                T["identf"][0:8, 0:8]), inc=("Tmm", 1))
        MODTR_T = mk.cnt["Tmm"]
        wait("vector", "Tmm", MODTR_T)
        for j in range(12):
            emit("vector", lambda v, j=j: v.tensor_copy(
                T["modc"][:, j:96:12], P[1][:, j * 8:(j + 1) * 8]))
        drain("vector")
        emit("vector", lambda v: v.tensor_copy(T["modd"][:, 0:16], T["modc"][:, 0:16]))
        emit("vector", lambda v: v.tensor_scalar(T["modd"][:, 16:32], T["modc"][:, 16:32],
                                                 1.0, None, ALU.add))
        emit("vector", lambda v: v.tensor_copy(T["modd"][:, 48:64], T["modc"][:, 48:64]))
        emit("vector", lambda v: v.tensor_scalar(T["modd"][:, 64:80], T["modc"][:, 64:80],
                                                 1.0, None, ALU.add))
        drain("vector")
        emit("vector", lambda v: v.nop(), inc=("Vev", 1))
        MODC_V = mk.cnt["Vev"]
        wait("scalar", "Vev", MODC_V)
        emit("scalar", lambda s: s.sign(T["isc"][:, 0:16], T["modc"][:, 32:48]))
        emit("scalar", lambda s: s.sign(T["isc"][:, 16:32], T["modc"][:, 80:96]))
        drain("scalar")
        emit("scalar", lambda s: s.nop(), inc=("Sev", 1))
        wait("vector", "Sev", mk.cnt["Sev"])
        emit("vector", lambda v: v.tensor_tensor(T["isc"][:, 32:48], T["modc"][:, 32:48],
                                                 T["isc"][:, 0:16], ALU.mult))
        emit("vector", lambda v: v.tensor_tensor(T["isc"][:, 48:64], T["modc"][:, 80:96],
                                                 T["isc"][:, 16:32], ALU.mult))
        drain("vector")
        emit("vector", lambda v: v.tensor_scalar(T["isc"][:, 32:48], T["isc"][:, 32:48],
                                                 0.25, None, ALU.max))
        emit("vector", lambda v: v.tensor_scalar(T["isc"][:, 48:64], T["isc"][:, 48:64],
                                                 0.25, None, ALU.max))
        drain("vector")
        emit("vector", lambda v: v.tensor_tensor(T["modd"][:, 32:48], T["isc"][:, 32:48],
                                                 T["isc"][:, 0:16], ALU.mult))
        emit("vector", lambda v: v.tensor_tensor(T["modd"][:, 80:96], T["isc"][:, 48:64],
                                                 T["isc"][:, 16:32], ALU.mult))
        drain("vector")
        emit("vector", lambda v: v.nop(), inc=("Vev", 1))
        MODD_V = mk.cnt["Vev"]
        tap("modd", lambda: T["modd"][:, :], after=("Vev" if mk.me == "sync" else "Vev", MODD_V))

        if STOP_AFTER == "P1":
            return
        wait("vector", "Xld", XLD0)
        emit("vector", lambda v: v.tensor_scalar(T["slop"][:, 4:8], T["slop"][:, 0:4],
                                                 -1.0, None, ALU.mult))
        drain("vector")
        emit("vector", lambda v: v.nop(), inc=("Vev", 1))
        NEGS_V = mk.cnt["Vev"]
        wait("scalar", "Vev", NEGS_V)
        emit("scalar", lambda s: s.copy(T["smallb"][0:1, 0:8], T["slop"][0:1, 0:8]))
        drain("scalar")
        emit("scalar", lambda s: s.nop(), inc=("Sev", 1))
        wait("tensor", "Sev", mk.cnt["Sev"])
        wait("tensor", "Gev", 1)
        emit("tensor", lambda e: e.matmul(P[3][:, 0:8], T["onesb"][0:1, :],
                                          T["smallb"][0:1, 0:8], start=True, stop=True),
             inc=("Tmm", 1))
        wait("vector", "Tmm", mk.cnt["Tmm"])
        emit("vector", lambda v: v.tensor_copy(T["slopc"][:, :], P[3][:, 0:8]))
        drain("vector")
        emit("vector", lambda v: v.nop(), inc=("Vev", 1))
        SLOPC_V = mk.cnt["Vev"]
        if STOP_AFTER == "P2a":
            return
        wait("gpsimd", "Park", MODGI_P)
        emit("gpsimd", lambda g: g.iota(sm[0:1, 0:SBLK], pattern=[[1, SBLK]], base=1,
                                        channel_multiplier=0, allow_small_or_imprecise_dtypes=True))
        emit("gpsimd", lambda g: g.iota(sm[:, SBLK:SBLK + 2], pattern=[[-128, 2]], base=255,
                                        channel_multiplier=-1, allow_small_or_imprecise_dtypes=True))
        drain("gpsimd")
        emit("gpsimd", lambda g: g.nop(), inc=("Gev", 1))
        IOTA1_G = mk.cnt["Gev"]
        wait("scalar", "Gev", IOTA1_G)
        wait("scalar", "Vev", SLOPC_V)
        for hh in range(HPC):
            emit("scalar", lambda s, hh=hh: s.activation(
                T["smallb"][0:1, hh * SBLK:(hh + 1) * SBLK], sm[0:1, 0:SBLK],
                AF.Exp, bias=0.0, scale=T["slop"][0:1, 4 + hh:5 + hh]))
            emit("scalar", lambda s, hh=hh: s.activation(
                T["kdec"][:, hh, :], sm[:, SBLK:SBLK + 2],
                AF.Exp, bias=0.0, scale=T["slopc"][:, 4 + hh:5 + hh]))
        drain("scalar")
        emit("scalar", lambda s: s.nop(), inc=("Sev", 1))
        QDR_S = mk.cnt["Sev"]
        if STOP_AFTER == "P2b":
            return
        wait("tensor", "Sev", QDR_S)
        wait("tensor", "Gev", 1)
        QDEC_BASE_V = mk.cnt["Vev"]
        for hh in range(HPC):
            if hh >= 1:
                wait("tensor", "Vev", QDEC_BASE_V + hh)
            emit("tensor", lambda e, hh=hh: e.matmul(
                P[2][:, (hh % 2) * SBLK:(hh % 2 + 1) * SBLK], T["onesb"][0:1, :],
                T["smallb"][0:1, hh * SBLK:(hh + 1) * SBLK], start=True, stop=True),
                inc=("Tmm", 1))
            wait("vector", "Tmm", mk.cnt["Tmm"])
            emit("vector", lambda v, hh=hh: v.tensor_copy(
                T["qdec"][:, hh, :], P[2][:, (hh % 2) * SBLK:(hh % 2 + 1) * SBLK]))
            drain("vector")
            emit("vector", lambda v: v.nop(), inc=("Vev", 1))
        QDEC_V = mk.cnt["Vev"]
        if STOP_AFTER == "P2c":
            return
        for half in range(2):
            wait("gpsimd", "Sev", mk.cnt["Sev"])
            emit("gpsimd", lambda g, half=half: g.iota(
                sm[:, 0:SBLK], pattern=[[1, SBLK]], base=-128 * half,
                channel_multiplier=-1, allow_small_or_imprecise_dtypes=True))
            drain("gpsimd")
            emit("gpsimd", lambda g: g.nop(), inc=("Gev", 1))
            iota_g = mk.cnt["Gev"]
            for hh in range(HPC):
                wait("scalar", "Gev", iota_g)
                wait("scalar", "Gev", mk.cnt["Gev"])
                emit("scalar", lambda s, hh=hh: s.activation(
                    sm[:, 512:512 + SBLK], sm[:, 0:SBLK], AF.Exp, bias=0.0,
                    scale=T["slopc"][:, 4 + hh:5 + hh]))
                drain("scalar")
                emit("scalar", lambda s: s.nop(), inc=("Sev", 1))
                wait("gpsimd", "Sev", mk.cnt["Sev"])
                emit("gpsimd", lambda g, hh=hh, half=half: g.affine_select(
                    T["dtt"][:, hh, half, :], sm[:, 512:512 + SBLK],
                    compare_op=ALU.is_ge, fill=0.0, pattern=[[1, SBLK]],
                    base=-128 * half, channel_multiplier=-1))
                drain("gpsimd")
                emit("gpsimd", lambda g: g.nop(), inc=("Gev", 1))
        DTT_G = mk.cnt["Gev"]
        if STOP_AFTER == "P2d":
            return
        emit("scalar", lambda s: s.activation(sm[0:1, 776:780], T["slop"][0:1, 4:8],
                                              AF.Exp, bias=0.0, scale=256.0))
        drain("scalar")
        emit("scalar", lambda s: s.copy(T["smallb"][0:1, 1024:1028], sm[0:1, 776:780]))
        drain("scalar")
        emit("scalar", lambda s: s.nop(), inc=("Sev", 1))
        wait("tensor", "Sev", mk.cnt["Sev"])
        wait("tensor", "Vev", QDEC_V)
        emit("tensor", lambda e: e.matmul(P[2][:, 0:4], T["onesb"][0:1, :],
                                          T["smallb"][0:1, 1024:1028], start=True, stop=True),
             inc=("Tmm", 1))
        wait("vector", "Tmm", mk.cnt["Tmm"])
        emit("vector", lambda v: v.tensor_copy(sm[:, 772:776], P[2][:, 0:4]))
        drain("vector")
        emit("vector", lambda v: v.memset(T["kv"][:, :, :], 0.0))
        emit("vector", lambda v: v.memset(T["kvb"][:, :, :], 0.0))
        drain("vector")
        emit("vector", lambda v: v.nop(), inc=("Vev", 1))
        BD_V = mk.cnt["Vev"]

        if STOP_AFTER == "P2":
            return
        wait("gpsimd", "Tmm", MODTR_T)
        emit("gpsimd", lambda g: g.dma_start(
            out=T["wqkv_sb"][:, :, :, :],
            in_=d["wqkv_T"]), inc=("Wq", 16))
        WQKV_W = mk.cnt["Wq"]

        p7b = psl[7][:, :].bitcast(BF16)[:, 0:512]
        xs_sems = ["XsA", "XsB", "TrA"]
        dma_sem = {}
        scat_p = {}
        stat_vv = {}
        bias_v = {}
        aff_v = {}
        xa_read_t = {}
        evac_vv = {}
        qmm_last = {}

        def ln_dma(g):
            sl = g % 3
            xsem = xs_sems[sl]
            if g >= 3:
                wait("sync", "Vev", aff_v[g - 3])
            emit("sync", lambda e, g=g, sl=sl: e.dma_start(
                out=T["xstage"][:, sl, :], in_=d["x_b"][g * 128:(g + 1) * 128, :]),
                inc=(xsem, 16))
            dma_sem[g] = (xsem, mk.cnt[xsem])

        def ln_stats(g):
            sl = g % 3
            xsem, cnt = dma_sem[g]
            wait("vector", xsem, cnt)
            s6 = (g % 2) * 24
            for pc in range(4):
                emit("vector", lambda v, sl=sl, pc=pc, s6=s6: v.bn_stats(
                    T["stat6"][:, s6 + pc * 6:s6 + (pc + 1) * 6],
                    T["xstage"][:, sl, pc * 512:(pc + 1) * 512]))
            drain("vector")
            emit("vector", lambda v, g=g, s6=s6: v.bn_aggr(
                T["stats"][:, 2 * (g % 16):2 * (g % 16) + 2], T["stat6"][:, s6:s6 + 24]),
                inc=("Vev", 1))
            stat_vv[g] = mk.cnt["Vev"]

        def ln_istd_pair(g):
            c = 2 * (g % 16)
            wait("vector", "Vev", stat_vv[g])
            emit("vector", lambda v, c=c: v.tensor_scalar(
                T["isc"][:, c - 2:c + 2:2], T["stats"][:, c - 1:c + 2:2],
                1e-5, None, ALU.add))
            drain("vector")
            emit("vector", lambda v, c=c: v.reciprocal(
                T["isc"][:, c - 2:c + 2:2], T["isc"][:, c - 2:c + 2:2]),
                inc=("Vev", 1))
            rv = mk.cnt["Vev"]
            wait("scalar", "Vev", rv)
            emit("scalar", lambda s, c=c: s.activation(
                T["isc"][:, c - 2:c + 2:2], T["isc"][:, c - 2:c + 2:2],
                AF.Sqrt), inc=("Sev", 1))
            rs = mk.cnt["Sev"]
            wait("vector", "Sev", rs)
            emit("vector", lambda v, c=c: v.scalar_tensor_tensor(
                T["isc"][:, c - 1:c + 2:2], T["stats"][:, c - 2:c + 2:2],
                -1.0, T["isc"][:, c - 2:c + 2:2], ALU.mult, ALU.mult),
                inc=("Vev", 1))
            bias_v[g] = mk.cnt["Vev"]

        def ln_affine(g):
            sl = g % 3
            c = 2 * (g % 16)
            wait("vector", "Vev", bias_v[g | 1])
            if g >= 3:
                wait("vector", "Tmm", xa_read_t[g - 3])
            emit("vector", lambda v, g=g, sl=sl, c=c: v.tensor_scalar(
                T["xaff"][:, g % 3, :], T["xstage"][:, sl, :],
                T["isc"][:, c:c + 1], T["isc"][:, c + 1:c + 2],
                ALU.mult, ALU.add), inc=("Vev", 1))
            aff_v[g] = mk.cnt["Vev"]

        def xn_pass(blk, pi):
            jj, p = pi // 4, pi % 4
            g = 4 * blk + jj
            wait("tensor", "Vev", aff_v[g])
            prev = evac_vv.get((blk, pi - 1), evac_vv.get((blk - 1, 15), 0))
            if prev:
                wait("tensor", "Vev", prev)
            def trp(e, g=g, p=p):
                for i, k in enumerate(range(4 * p, 4 * p + 4)):
                    last = e.transpose(p7b[:, 128 * i:128 * (i + 1)],
                                       T["xaff"][:, g % 3, 128 * k:128 * (k + 1)],
                                       T["ident"][:, :])
                return last
            emit("tensor", trp, inc=("Tmm", 1))
            if p == 3:
                xa_read_t[g] = mk.cnt["Tmm"]
            wait("vector", "Tmm", mk.cnt["Tmm"])
            if pi == 0 and blk >= 2:
                wait("vector", "Tmm", qmm_last[blk - 2])
                wait("vector", "PkA" if blk % 2 == 0 else "PkB", scat_p[blk - 2])
            emit("vector", lambda v, blk=blk, jj=jj, p=p: v.tensor_copy(
                T["hblk"][:, blk % 2, 4 * p:4 * p + 4, 128 * jj:128 * (jj + 1)],
                p7b[:, 0:512]), inc=("Vev", 1))
            evac_vv[(blk, pi)] = mk.cnt["Vev"]

        for g in range(3):
            ln_dma(g)
        ln_stats(0)
        ln_stats(1)
        ln_istd_pair(1)
        ln_affine(0)
        ln_affine(1)
        ln_dma(3)
        ln_stats(2)
        ln_stats(3)
        ln_istd_pair(3)
        ln_affine(2)
        for pi in range(4):
            xn_pass(0, pi)
        ln_affine(3)
        for pi in range(4, 16):
            xn_pass(0, pi)

        if STOP_AFTER == "P3":
            return
        hv = [0, 0]
        qkv_sev = [0] * NTB
        scan_state = dict(tr_evac_v=0, mask_v=0, kvupd_v=0)
        att_evac = {3: 0, 4: 0}
        kvb_sev = [0] * HPC
        bank_evac_s = [0, 0]

        for tb in range(NTB):
            buf = tb % 2
            blk = tb + 1 if tb + 1 < NTB else None
            if blk is not None:
                for jj in range(3):
                    ln_dma(4 * blk + jj)
            wait("vector", "Vev", MODD_V)
            wait("vector", "Vev", evac_vv[(tb, 15)])
            for k in range(KC):
                emit("vector", lambda v, buf=buf, k=k: v.tensor_scalar(
                    T["hblk"][:, buf, k, :], T["hblk"][:, buf, k, :],
                    T["modd"][:, 16 + k:17 + k], T["modd"][:, 0 + k:1 + k], ALU.mult, ALU.add),
                    inc=("Vev", 1) if k == KC - 1 else None)
            hv[buf] = mk.cnt["Vev"]
            wait("gpsimd", "Vev", hv[buf])
            wait("gpsimd", "Xld", XLD0)
            hsem = "PkA" if buf == 0 else "PkB"
            emit("gpsimd", lambda gg, tb=tb, buf=buf: gg.indirect_dma_start(
                out=d["h1mp"].rearrange("r a b -> r (a b)"),
                out_offset=bass.IndirectOffsetOnAxis(ap=T["idx_h"][:, tb:tb + 1], axis=0),
                in_=T["hblk"][:, buf, :, :].rearrange("p a b -> p (a b)"), in_offset=None,
                bounds_check=256, oob_is_err=False), inc=(hsem, 16))
            scat_p[tb] = mk.cnt[hsem]

            wait("tensor", "Vev", hv[buf])
            wait("tensor", "Wq", WQKV_W)
            qbuf = tb % 2
            if tb >= 2:
                wait("tensor", "Qfree", tb - 1)
            for m in range(12):
                bk = m % 2
                wait("tensor", "Sev", bank_evac_s[bk])
                def qmm(e, m=m, buf=buf, bk=bk):
                    for k in range(KC):
                        last = e.matmul(P[bk][:, :], T["wqkv_sb"][:, m, k, :],
                                 T["hblk"][:, buf, k, :], start=(k == 0), stop=(k == KC - 1))
                    return last
                emit("tensor", qmm, inc=("Tmm", 1))
                wait("scalar", "Tmm", mk.cnt["Tmm"])
                emit("scalar", lambda s, m=m, qbuf=qbuf, bk=bk: s.activation(
                    T["qkvblk"][:, qbuf, m, :], P[bk][:, :], AF.Silu), inc=("Sev", 1))
                bank_evac_s[bk] = mk.cnt["Sev"]
            qkv_sev[tb] = mk.cnt["Sev"]
            qmm_last[tb] = mk.cnt["Tmm"]

            for jj in range(2):
                j = 2 * tb + jj
                for hh in range(HPC):
                    st = 4 * jj + hh
                    wait("tensor", "Sev", qkv_sev[tb])
                    wait("tensor", "Vev", scan_state["tr_evac_v"])
                    def tr4(e, hh=hh, jj=jj, qbuf=qbuf):
                        for half in range(2):
                            last = e.transpose(pstr[:, half * 128:(half + 1) * 128],
                                        T["qkvblk"][:, qbuf, 4 + hh,
                                                    jj * SBLK + half * 128:jj * SBLK + (half + 1) * 128],
                                        T["ident"][:, :])
                        for half in range(2):
                            last = e.transpose(pstr[:, 256 + half * 128:256 + (half + 1) * 128],
                                        T["qkvblk"][:, qbuf, 8 + hh,
                                                    jj * SBLK + half * 128:jj * SBLK + (half + 1) * 128],
                                        T["ident"][:, :])
                        return last
                    emit("tensor", tr4, inc=("Tmm", 1))
                    wait("vector", "Tmm", mk.cnt["Tmm"])
                    wait("vector", "Vev", QDEC_V)
                    for half in range(2):
                        emit("vector", lambda v, hh=hh, half=half: v.tensor_scalar(
                            T["scanbuf"][:, hh, 2, half * 128:(half + 1) * 128],
                            pstr[:, half * 128:(half + 1) * 128],
                            T["kdec"][:, hh, half:half + 1], None, ALU.mult))
                    for half in range(2):
                        emit("vector", lambda v, hh=hh, half=half: v.tensor_copy(
                            T["scanbuf"][:, hh, 3, half * 128:(half + 1) * 128],
                            pstr[:, 256 + half * 128:256 + (half + 1) * 128]))
                    emit("vector", lambda v, hh=hh, jj=jj, qbuf=qbuf: v.tensor_tensor(
                        T["scanbuf"][:, hh, 4, :],
                        T["qkvblk"][:, qbuf, hh, jj * SBLK:(jj + 1) * SBLK],
                        T["qdec"][:, hh, :], ALU.mult), inc=("Vev", 1))
                    scan_state["tr_evac_v"] = mk.cnt["Vev"]
                    wait("tensor", "Vev", scan_state["tr_evac_v"])
                    wait("tensor", "Vev", scan_state["mask_v"])
                    def smm(e, hh=hh, jj=jj, qbuf=qbuf):
                        for half in range(2):
                            last = e.matmul(P[2][:, half * SBLK:(half + 1) * SBLK],
                                     T["qkvblk"][:, qbuf, 4 + hh,
                                                 jj * SBLK + half * 128:jj * SBLK + (half + 1) * 128],
                                     T["qkvblk"][:, qbuf, hh, jj * SBLK:(jj + 1) * SBLK],
                                     start=True, stop=True)
                        return last
                    emit("tensor", smm, inc=("Tmm", 1))
                    wait("vector", "Tmm", mk.cnt["Tmm"])
                    wait("vector", "Gev", DTT_G)
                    for half in range(2):
                        ii = emit("vector", lambda v, hh=hh, half=half: v.tensor_tensor(
                            T["scanbuf"][:, hh, half, :], P[2][:, half * SBLK:(half + 1) * SBLK],
                            T["dtt"][:, hh, half, :], ALU.mult),
                            inc=("Vev", 1) if half == 1 else None)
                    scan_state["mask_v"] = mk.cnt["Vev"]
                    abank = 3 + (hh % 2)
                    wait("tensor", "Vev", scan_state["mask_v"])
                    wait("tensor", "Sev", kvb_sev[hh])
                    wait("tensor", "Sev", att_evac[abank])
                    def amm(e, hh=hh, abank=abank):
                        last = e.matmul(P[abank][:, 0:SBLK], T["kvb"][:, hh, :],
                                 T["scanbuf"][:, hh, 4, :], start=True, stop=False)
                        for half in range(2):
                            last = e.matmul(P[abank][:, 0:SBLK],
                                     T["scanbuf"][:, hh, 3, half * 128:(half + 1) * 128],
                                     T["scanbuf"][:, hh, half, :],
                                     start=False, stop=(half == 1))
                        return last
                    emit("tensor", amm, inc=("Tmm", 1))
                    att_t = mk.cnt["Tmm"]
                    wait("tensor", "Vev", scan_state["kvupd_v"])
                    def kmm(e, hh=hh):
                        for half in range(2):
                            last = e.matmul(P[5][:, 0:128],
                                     T["scanbuf"][:, hh, 2, half * 128:(half + 1) * 128],
                                     T["scanbuf"][:, hh, 3, half * 128:(half + 1) * 128],
                                     start=(half == 0), stop=(half == 1))
                        return last
                    emit("tensor", kmm, inc=("Tmm", 1))
                    kv_t = mk.cnt["Tmm"]
                    if blk is not None:
                        if st == 4:
                            for pi in range(4):
                                xn_pass(blk, pi)
                        elif st == 5:
                            for pi in range(4, 8):
                                xn_pass(blk, pi)
                    wait("scalar", "Tmm", att_t)
                    wait("scalar", "Bnc", mk.cnt["Bnc"])
                    emit("scalar", lambda s, hh=hh, abank=abank: s.activation(
                        T["bnc_st"][:, hh, :],
                        P[abank][:, 0:SBLK], AF.Copy), inc=("Sev", 1))
                    att_evac[abank] = mk.cnt["Sev"]
                    wait("sync", "Sev", att_evac[abank])
                    emit("sync", lambda e, hh=hh, j=j: e.dma_start(
                        out=d["att_gi"][hh * 128:(hh + 1) * 128, j * SBLK:(j + 1) * SBLK],
                        in_=T["bnc_st"][:, hh, :]),
                        inc=("Bnc", 16))
                    wait("vector", "Tmm", kv_t)
                    wait("vector", "Vev", BD_V)
                    emit("vector", lambda v, hh=hh: v.scalar_tensor_tensor(
                        T["kv"][:, hh, :], T["kv"][:, hh, :], sm[:, 772 + hh:773 + hh],
                        P[5][:, 0:128], ALU.mult, ALU.add), inc=("Vev", 1))
                    scan_state["kvupd_v"] = mk.cnt["Vev"]
                    wait("scalar", "Vev", scan_state["kvupd_v"])
                    emit("scalar", lambda s, hh=hh: s.copy(T["kvb"][:, hh, :], T["kv"][:, hh, :]),
                         inc=("Sev", 1))
                    kvb_sev[hh] = mk.cnt["Sev"]
                    if blk is not None:
                        g0 = 4 * blk
                        if st == 0:
                            ln_stats(g0)
                        elif st == 1:
                            ln_stats(g0 + 1)
                            ln_istd_pair(g0 + 1)
                        elif st == 2:
                            ln_affine(g0)
                            ln_affine(g0 + 1)
                            ln_dma(g0 + 3)
                        elif st == 3:
                            ln_stats(g0 + 2)
                        elif st == 6:
                            ln_stats(g0 + 3)
                            ln_istd_pair(g0 + 3)
                        elif st == 7:
                            ln_affine(g0 + 2)
            if blk is not None:
                for pi in range(8, 12):
                    xn_pass(blk, pi)
                ln_affine(4 * blk + 3)
                for pi in range(12, 16):
                    xn_pass(blk, pi)
            emit("vector", lambda v: v.nop(), inc=("Qfree", 1))
        SCAN_T = mk.cnt["Tmm"]

        PARK_ALL = (mk.cnt["PkA"], mk.cnt["PkB"])
        BNC_ALL = mk.cnt["Bnc"]

        if STOP_AFTER == "P4":
            return
        wait("sync", "Tmm", mk.cnt["Tmm"])
        wait("sync", "Vev", mk.cnt["Vev"])
        wait("sync", "Sev", mk.cnt["Sev"])
        wait("sync", "Bnc", mk.cnt["Bnc"])
        for pk_i, pk_n in enumerate(["PkA", "PkB"]):
            wait("sync", pk_n, PARK_ALL[pk_i])
        for half in range(2):
            emit("sync", lambda e, half=half: e.dma_start(
                out=T["h1m"][:, :, half * TBLK:(half + 1) * TBLK],
                in_=d["h1mp"][half * 128:(half + 1) * 128, :, :]), inc=("Trp", 16))
        H1M_TRP = mk.cnt["Trp"]

        wait("gpsimd", "Bnc", BNC_ALL)
        emit("gpsimd", lambda g: g.collective_compute(
            "AllGather", ALU.bypass, replica_groups=[[0, 1, 2, 3], [4, 5, 6, 7]],
            ins=[d["att_gi"].opt()], outs=[d["att_go"].opt()]), inc=("CC", 1))
        ATTCC = mk.cnt["CC"]

        wait("gpsimd", "Vev", aff_v[31])
        emit("gpsimd", lambda g: g.dma_start(
            out=T["wcolA"][:, 0, :, :], in_=d["wg_T"][0]),
            inc=("WlA", 16))
        gate_evac = [0, 0]
        gate_t = []
        for m in range(KC):
            if m + 1 < KC:
                if m >= 1:
                    wait("gpsimd", "Tmm", gate_t[m - 1])
                wl = "WlA" if (m + 1) % 2 == 0 else "WlB"
                emit("gpsimd", lambda g, m=m: g.dma_start(
                    out=T["wcolA"][:, (m + 1) % 2, :, :],
                    in_=d["wg_T"][m + 1]), inc=(wl, 16))
            wl_m = "WlA" if m % 2 == 0 else "WlB"
            wait("tensor", wl_m, mk.cnt[wl_m])
            wait("tensor", "Trp", H1M_TRP)
            wait("tensor", "Sev", gate_evac[m % 2])
            def gmm(e, m=m):
                for half in range(2):
                    for k in range(KC):
                        last = e.matmul(P[2 * (m % 2) + half][:, :], T["wcolA"][:, m % 2, k, :],
                                 T["h1m"][:, k, half * TBLK:(half + 1) * TBLK],
                                 start=(k == 0), stop=(k == KC - 1))
                return last
            emit("tensor", gmm, inc=("Tmm", 1))
            gate_t.append(mk.cnt["Tmm"])
            wait("scalar", "Tmm", mk.cnt["Tmm"])
            for half in range(2):
                emit("scalar", lambda s, m=m, half=half: s.activation(
                    T["gate_h2"][:, m, half * TBLK:(half + 1) * TBLK],
                    P[2 * (m % 2) + half][:, :], AF.Sigmoid))
            drain("scalar")
            emit("scalar", lambda s: s.nop(), inc=("Sev", 1))
            gate_evac[m % 2] = mk.cnt["Sev"]
        GATE_S = mk.cnt["Sev"]
        tap("gate", lambda: T["gate_h2"][:, :, :], after=("Sev", GATE_S))

        if STOP_AFTER == "P5":
            return
        wait("gpsimd", "CC", ATTCC)
        for k in range(KC):
            emit("gpsimd", lambda g, k=k: g.indirect_dma_start(
                out=T["attnz"][:, k, :], out_offset=None,
                in_=d["att_go"].rearrange("a (q c) -> (a q) c", c=NT),
                in_offset=bass.IndirectOffsetOnAxis(ap=T["idx_a"][:, k:k + 1], axis=0)),
                inc=("Gth", 16))
        GTH_A = mk.cnt["Gth"]
        tap("attng", lambda: T["attnz"][:, :, :], after=("Gth", GTH_A))

        rms_t = 0
        for k in range(KC):
            wait("vector", "Gth", GTH_A)
            if k > 0:
                wait("vector", "Tmm", rms_t)
            emit("vector", lambda v, k=k: v.tensor_tensor(
                T["smallb"][:, 0:NT], T["attnz"][:, k, :], T["attnz"][:, k, :], ALU.mult),
                inc=("Vev", 1))
            wait("tensor", "Vev", mk.cnt["Vev"])
            def rmm(e, k=k):
                last = e.matmul(P[4][0:1, :], T["onecol"][:, 0:1], T["smallb"][:, 0:512],
                         start=(k == 0), stop=(k == KC - 1))
                last = e.matmul(P[5][0:1, :], T["onecol"][:, 0:1], T["smallb"][:, 512:1024],
                         start=(k == 0), stop=(k == KC - 1))
                return last
            emit("tensor", rmm, inc=("Tmm", 1))
            rms_t = mk.cnt["Tmm"]
        wait("scalar", "Tmm", rms_t)
        for half, bank in ((0, 4), (1, 5)):
            emit("scalar", lambda s, half=half, bank=bank: s.activation(
                T["rstat"][0:1, half * 512:(half + 1) * 512], P[bank][0:1, :],
                AF.Ln, bias=T["consts"][0:1, 2:3], scale=T["consts"][0:1, 3:4]))
        drain("scalar")
        for half in range(2):
            emit("scalar", lambda s, half=half: s.activation(
                T["rstat"][0:1, half * 512:(half + 1) * 512],
                T["rstat"][0:1, half * 512:(half + 1) * 512],
                AF.Exp, bias=0.0, scale=T["consts"][0:1, 1:2]))
        drain("scalar")
        emit("scalar", lambda s: s.copy(T["smallb"][0:1, 0:NT], T["rstat"][0:1, 0:NT]))
        drain("scalar")
        emit("scalar", lambda s: s.nop(), inc=("Sev", 1))
        wait("tensor", "Sev", mk.cnt["Sev"])
        for half in range(2):
            if half == 1:
                wait("tensor", "Vev", mk.cnt["Vev"])
            emit("tensor", lambda e, half=half: e.matmul(
                P[6][:, :], T["onesb"][0:1, :], T["smallb"][0:1, half * 512:(half + 1) * 512],
                start=True, stop=True), inc=("Tmm", 1))
            wait("vector", "Tmm", mk.cnt["Tmm"])
            emit("vector", lambda v, half=half: v.tensor_copy(
                T["rbc"][:, half * 512:(half + 1) * 512], P[6][:, :]))
            drain("vector")
            emit("vector", lambda v: v.nop(), inc=("Vev", 1))
        RBC_V = mk.cnt["Vev"]

        wait("vector", "Sev", GATE_S)
        wait("vector", "Vev", RBC_V)
        for k in range(KC):
            emit("vector", lambda v, k=k: v.scalar_tensor_tensor(
                T["attnz"][:, k, :], T["attnz"][:, k, :], T["nw_sb"][:, k:k + 1],
                T["gate_h2"][:, k, :], ALU.mult, ALU.mult),
                inc=("Vev", 1) if k == KC - 1 else None)
        wait("vector", "Vev", mk.cnt["Vev"])
        for k in range(KC):
            emit("vector", lambda v, k=k: v.tensor_tensor(
                T["attnz"][:, k, :], T["attnz"][:, k, :], T["rbc"][:, :], ALU.mult),
                inc=("Vev", 1) if k == KC - 1 else None)
        Z_V = mk.cnt["Vev"]
        tap("z", lambda: T["attnz"][:, :, :], after=("Vev", Z_V))

        wait("gpsimd", "Tmm", gate_t[-1])
        emit("gpsimd", lambda g: g.dma_start(
            out=T["wcolA"][:, 0, :, :], in_=d["wo_T"][0]),
            inc=("WlA", 16))
        xp_evac = [0, 0]
        xp_done = [0, 0]
        op_t = []
        ln2_t = 0
        for m in range(KC):
            if m + 1 < KC:
                if m >= 1:
                    wait("gpsimd", "Tmm", op_t[m - 1])
                wl = "WlA" if (m + 1) % 2 == 0 else "WlB"
                emit("gpsimd", lambda g, m=m: g.dma_start(
                    out=T["wcolA"][:, (m + 1) % 2, :, :],
                    in_=d["wo_T"][m + 1]), inc=(wl, 16))
            xl_m = "XlA" if m % 2 == 0 else "XlB"
            if m >= 2:
                wait("sync", "Vev", xp_done[m % 2])
                wait("sync", "Park", mk.cnt["Park"])
            emit("sync", lambda e, m=m: e.dma_start(
                out=T["resid"][:, m % 2, :], in_=d["xT_m"][m * 128:(m + 1) * 128, :]),
                inc=(xl_m, 16))
            xld_m = mk.cnt[xl_m]
            wl_m = "WlA" if m % 2 == 0 else "WlB"
            wait("tensor", wl_m, mk.cnt[wl_m])
            wait("tensor", "Vev", Z_V)
            for half in range(2):
                wait("tensor", "Vev", xp_evac[half])
                def omm(e, m=m, half=half):
                    for k in range(KC):
                        last = e.matmul(P[half][:, :], T["wcolA"][:, m % 2, k, :],
                                 T["attnz"][:, k, half * TBLK:(half + 1) * TBLK],
                                 start=(k == 0), stop=(k == KC - 1))
                    return last
                emit("tensor", omm, inc=("Tmm", 1))
                omm_t = mk.cnt["Tmm"]
                wait("vector", "Tmm", omm_t)
                wait("vector", xl_m, xld_m)
                emit("vector", lambda v, m=m, half=half: v.scalar_tensor_tensor(
                    T["resid"][:, m % 2, half * TBLK:(half + 1) * TBLK],
                    P[half][:, :], T["modd"][:, 32 + m:33 + m],
                    T["resid"][:, m % 2, half * TBLK:(half + 1) * TBLK], ALU.mult, ALU.add))
                drain("vector")
                emit("vector", lambda v: v.nop(), inc=("Vev", 1))
                xp_evac[half] = mk.cnt["Vev"]
            emit("vector", lambda v, m=m: v.tensor_copy(
                T["smallb"][:, 0:NT], T["resid"][:, m % 2, :]))
            emit("vector", lambda v, m=m: v.tensor_tensor(
                T["smallb"][:, NT:2 * NT], T["resid"][:, m % 2, :], T["resid"][:, m % 2, :],
                ALU.mult), inc=("Vev", 1))
            xp_done[m % 2] = mk.cnt["Vev"]
            wait("sync", "Vev", xp_done[m % 2])
            emit("sync", lambda e, m=m: e.dma_start(
                out=d["xp_park"][m * 128:(m + 1) * 128, :], in_=T["resid"][:, m % 2, :]),
                inc=("Park", 16))
            wait("tensor", "Vev", xp_done[m % 2])
            def lsmm(e, m=m):
                last = e.matmul(P[2][0:1, :], T["onecol"][:, 0:1], T["smallb"][:, 0:512],
                         start=(m == 0), stop=(m == KC - 1))
                last = e.matmul(P[3][0:1, :], T["onecol"][:, 0:1], T["smallb"][:, 512:1024],
                         start=(m == 0), stop=(m == KC - 1))
                last = e.matmul(P[6][0:1, :], T["onecol"][:, 0:1], T["smallb"][:, NT:NT + 512],
                         start=(m == 0), stop=(m == KC - 1))
                last = e.matmul(P[4][0:1, :], T["onecol"][:, 0:1], T["smallb"][:, NT + 512:2 * NT],
                         start=(m == 0), stop=(m == KC - 1))
                return last
            emit("tensor", lsmm, inc=("Tmm", 1))
            ln2_t = mk.cnt["Tmm"]
            op_t.append(mk.cnt["Tmm"])
        XPARK_ALL = mk.cnt["Park"]

        wait("vector", "Tmm", ln2_t)
        emit("vector", lambda v: v.tensor_scalar(T["lrow"][0:1, 0:512], P[2][0:1, :],
                                                 1.0 / 2048.0, None, ALU.mult))
        emit("vector", lambda v: v.tensor_scalar(T["lrow"][0:1, 512:1024], P[3][0:1, :],
                                                 1.0 / 2048.0, None, ALU.mult))
        emit("vector", lambda v: v.tensor_scalar(T["lrow"][0:1, 1024:1536], P[6][0:1, :],
                                                 1.0 / 2048.0, None, ALU.mult))
        emit("vector", lambda v: v.tensor_scalar(T["lrow"][0:1, 1536:2048], P[4][0:1, :],
                                                 1.0 / 2048.0, None, ALU.mult))
        drain("vector")
        emit("vector", lambda v: v.tensor_tensor(sm[0:1, 0:512], T["lrow"][0:1, 0:512],
                                                 T["lrow"][0:1, 0:512], ALU.mult))
        drain("vector")
        emit("vector", lambda v: v.tensor_tensor(T["lrow"][0:1, 1024:1536],
                                                 T["lrow"][0:1, 1024:1536],
                                                 sm[0:1, 0:512], ALU.subtract))
        drain("vector")
        emit("vector", lambda v: v.tensor_tensor(sm[0:1, 0:512], T["lrow"][0:1, 512:1024],
                                                 T["lrow"][0:1, 512:1024], ALU.mult))
        drain("vector")
        emit("vector", lambda v: v.tensor_tensor(T["lrow"][0:1, 1536:2048],
                                                 T["lrow"][0:1, 1536:2048],
                                                 sm[0:1, 0:512], ALU.subtract))
        drain("vector")
        emit("vector", lambda v: v.nop(), inc=("Vev", 1))
        wait("scalar", "Vev", mk.cnt["Vev"])
        emit("scalar", lambda s: s.activation(T["lrow"][0:1, 1024:2048], T["lrow"][0:1, 1024:2048],
                                              AF.Ln, bias=T["consts"][0:1, 0:1], scale=1.0))
        drain("scalar")
        emit("scalar", lambda s: s.activation(T["lrow"][0:1, 1024:2048], T["lrow"][0:1, 1024:2048],
                                              AF.Exp, bias=0.0, scale=T["consts"][0:1, 1:2]))
        drain("scalar")
        emit("scalar", lambda s: s.nop(), inc=("Sev", 1))
        wait("vector", "Sev", mk.cnt["Sev"])
        emit("vector", lambda v: v.tensor_tensor(T["lrow"][0:1, 0:1024], T["lrow"][0:1, 0:1024],
                                                 T["lrow"][0:1, 1024:2048], ALU.mult))
        drain("vector")
        emit("vector", lambda v: v.tensor_scalar(T["lrow"][0:1, 0:1024], T["lrow"][0:1, 0:1024],
                                                 -1.0, None, ALU.mult))
        drain("vector")
        emit("vector", lambda v: v.tensor_copy(T["smallb"][0:1, 0:NT], T["lrow"][0:1, 1024:2048]))
        emit("vector", lambda v: v.tensor_copy(T["smallb"][0:1, NT:2 * NT], T["lrow"][0:1, 0:1024]))
        drain("vector")
        emit("vector", lambda v: v.nop(), inc=("Vev", 1))
        wait("tensor", "Vev", mk.cnt["Vev"])
        for half in range(2):
            if half == 1:
                wait("tensor", "Vev", mk.cnt["Vev"])
            emit("tensor", lambda e, half=half: e.matmul(
                P[5][:, :], T["onesb"][0:1, :], T["smallb"][0:1, half * 512:(half + 1) * 512],
                start=True, stop=True), inc=("Tmm", 1))
            wait("vector", "Tmm", mk.cnt["Tmm"])
            emit("vector", lambda v, half=half: v.tensor_copy(
                T["rbc"][:, half * 512:(half + 1) * 512], P[5][:, :]))
            drain("vector")
            emit("vector", lambda v: v.nop(), inc=("Vev", 1))
            wait("tensor", "Vev", mk.cnt["Vev"])
            emit("tensor", lambda e, half=half: e.matmul(
                P[5][:, :], T["onesb"][0:1, :],
                T["smallb"][0:1, NT + half * 512:NT + (half + 1) * 512],
                start=True, stop=True), inc=("Tmm", 1))
            wait("vector", "Tmm", mk.cnt["Tmm"])
            emit("vector", lambda v, half=half: v.tensor_copy(
                T["smallb"][:, 2048 + half * 512:2048 + (half + 1) * 512], P[5][:, :]))
            drain("vector")
            emit("vector", lambda v: v.nop(), inc=("Vev", 1))
        IBC_V = mk.cnt["Vev"]

        for m in range(KC):
            xl_m = "XlA" if m % 2 == 0 else "XlB"
            wait("sync", "Park", XPARK_ALL)
            if m >= 2:
                wait("sync", "Vev", mk.cnt["Vev"])
            emit("sync", lambda e, m=m: e.dma_start(
                out=T["resid"][:, m % 2, :], in_=d["xp_park"][m * 128:(m + 1) * 128, :]),
                inc=(xl_m, 16))
            wait("vector", xl_m, mk.cnt[xl_m])
            wait("vector", "Vev", IBC_V)
            emit("vector", lambda v, m=m: v.tensor_tensor(
                T["gate_h2"][:, m, :], T["resid"][:, m % 2, :], T["rbc"][:, :], ALU.mult),
                inc=("Vev", 1))
            wait("vector", "Vev", mk.cnt["Vev"])
            emit("vector", lambda v, m=m: v.tensor_tensor(
                T["gate_h2"][:, m, :], T["gate_h2"][:, m, :],
                T["smallb"][:, 2048:3072], ALU.add), inc=("Vev", 1))
            wait("vector", "Vev", mk.cnt["Vev"])
            emit("vector", lambda v, m=m: v.tensor_scalar(
                T["gate_h2"][:, m, :], T["gate_h2"][:, m, :],
                T["modd"][:, 64 + m:65 + m], T["modd"][:, 48 + m:49 + m], ALU.mult, ALU.add),
                inc=("Vev", 1))
        H2_V = mk.cnt["Vev"]
        tap("h2", lambda: T["gate_h2"][:, :, :], after=("Vev", H2_V))

        if STOP_AFTER == "P6":
            return
        wait("gpsimd", "Sev", GATE_S)
        emit("gpsimd", lambda g: g.dma_start(
            out=T["wr_sb"][:, :, :], in_=d["wr_T"]),
            inc=("Wst", 16))
        for i, nm in enumerate(["agp1", "agp2", "aup1", "aup2"]):
            emit("gpsimd", lambda g, i=i, nm=nm: g.dma_start(
                out=T["agp_sb"][:, i, :, :], in_=d[nm]),
                inc=("Wst", 16))
        emit("gpsimd", lambda g: g.dma_start(out=T["adS_sb"][:, :, :, :],
                                             in_=d["adS"]), inc=("Wst", 16))
        emit("gpsimd", lambda g: g.dma_start(out=T["bdS_sb"][:, :, :], in_=d["bdS"]), inc=("Wst", 16))
        WFFN_W = mk.cnt["Wst"]

        wait("tensor", "Wst", WFFN_W)
        wait("tensor", "Vev", H2_V)
        def rlmm2(e):
            for half in range(2):
                bank = P[4] if half == 0 else P[5]
                for k in range(KC):
                    last = e.matmul(bank[0:E, :], T["wr_sb"][:, k, :],
                             T["gate_h2"][:, k, half * TBLK:(half + 1) * TBLK],
                             start=(k == 0), stop=(k == KC - 1))
            return last
        emit("tensor", rlmm2, inc=("Tmm", 1))
        wait("vector", "Tmm", mk.cnt["Tmm"])
        emit("vector", lambda v: v.tensor_copy(T["rl8"][:, 0:512], P[4][0:E, :]))
        emit("vector", lambda v: v.tensor_copy(T["rl8"][:, 512:1024], P[5][0:E, :]))
        drain("vector")
        emit("vector", lambda v: v.nop(), inc=("Vev", 1))
        wait("tensor", "Vev", mk.cnt["Vev"])
        for c in range(8):
            emit("tensor", lambda e, c=c: e.transpose(
                P[4][:, c * 8:(c + 1) * 8], T["rl8"][:, c * 128:(c + 1) * 128],
                T["identf"][0:8, 0:8]), inc=("Tmm", 1))
        wait("vector", "Tmm", mk.cnt["Tmm"])
        for c in range(8):
            emit("vector", lambda v, c=c: v.tensor_copy(T["rt"][:, c, :], P[4][:, c * 8:(c + 1) * 8]))
        drain("vector")
        for c in range(8):
            emit("vector", lambda v, c=c: v.reduce_max(T["rcol"][:, c, 0:1], T["rt"][:, c, :], AX.X))
        drain("vector")
        for c in range(8):
            emit("vector", lambda v, c=c: v.tensor_scalar(T["rcol"][:, c, 1:2], T["rcol"][:, c, 0:1],
                                                          -1.0, None, ALU.mult))
        drain("vector")
        emit("vector", lambda v: v.nop(), inc=("Vev", 1))
        wait("scalar", "Vev", mk.cnt["Vev"])
        for c in range(8):
            emit("scalar", lambda s, c=c: s.activation(
                T["rsc"][:, c, :], T["rt"][:, c, :], AF.Exp, bias=T["rcol"][:, c, 1:2], scale=1.0))
        drain("scalar")
        emit("scalar", lambda s: s.nop(), inc=("Sev", 1))
        wait("vector", "Sev", mk.cnt["Sev"])
        for c in range(8):
            emit("vector", lambda v, c=c: v.tensor_reduce(T["rcol"][:, c, 0:1], T["rsc"][:, c, :],
                                                          AX.X, ALU.add))
        drain("vector")
        for c in range(8):
            emit("vector", lambda v, c=c: v.reciprocal(T["rcol"][:, c, 1:2], T["rcol"][:, c, 0:1]))
        drain("vector")
        for c in range(8):
            emit("vector", lambda v, c=c: v.tensor_scalar(T["rt"][:, c, :], T["rsc"][:, c, :],
                                                          T["rcol"][:, c, 1:2], None, ALU.mult))
        drain("vector")
        for c in range(8):
            emit("vector", lambda v, c=c: v.reduce_max(T["rcol"][:, c, 0:1], T["rt"][:, c, :], AX.X))
        drain("vector")
        for c in range(8):
            emit("vector", lambda v, c=c: v.tensor_scalar(T["rsc"][:, c, :], T["rt"][:, c, :],
                                                          T["rcol"][:, c, 0:1], None, ALU.is_equal))
        drain("vector")
        for c in range(8):
            emit("vector", lambda v, c=c: v.scalar_tensor_tensor(
                T["rsc"][:, c, :], T["rsc"][:, c, :], -1e30, T["rt"][:, c, :], ALU.mult, ALU.add))
        drain("vector")
        for c in range(8):
            emit("vector", lambda v, c=c: v.reduce_max(T["rcol"][:, c, 2:3], T["rsc"][:, c, :], AX.X))
        drain("vector")
        for c in range(8):
            emit("vector", lambda v, c=c: v.tensor_scalar(T["rsc"][:, c, :], T["rt"][:, c, :],
                                                          T["rcol"][:, c, 2:3], None, ALU.is_ge))
        drain("vector")
        for c in range(8):
            emit("vector", lambda v, c=c: v.tensor_tensor(T["rt"][:, c, :], T["rt"][:, c, :],
                                                          T["rsc"][:, c, :], ALU.mult))
        drain("vector")
        emit("vector", lambda v: v.nop(), inc=("Vev", 1))
        TOPW_V = mk.cnt["Vev"]
        tap("topw", lambda: T["rt"][:, :, :], after=("Vev", TOPW_V))
        wait("tensor", "Vev", TOPW_V)
        for c in range(8):
            bank = P[4] if c < 4 else P[5]
            emit("tensor", lambda e, c=c, bank=bank: e.transpose(
                bank[0:8, (c % 4) * 128:(c % 4 + 1) * 128], T["rt"][:, c, :], T["identf"][:, :]),
                inc=("Tmm", 1))
        wait("vector", "Tmm", mk.cnt["Tmm"])
        emit("vector", lambda v: v.tensor_copy(T["wef"][:, 0:512], P[4][0:8, 0:512]))
        emit("vector", lambda v: v.tensor_copy(T["wef"][:, 512:1024], P[5][0:8, 0:512]))
        drain("vector")
        emit("vector", lambda v: v.nop(), inc=("Vev", 1))
        wait("tensor", "Vev", mk.cnt["Vev"])
        for par, dst in ((0, "wexpE"), (1, "wexpO")):
            for half in range(2):
                if not (par == 0 and half == 0):
                    wait("tensor", "Vev", mk.cnt["Vev"])
                emit("tensor", lambda e, par=par, half=half: e.matmul(
                    P[6][:, :], T["exp16"][:, par, :], T["wef"][:, half * 512:(half + 1) * 512],
                    start=True, stop=True), inc=("Tmm", 1))
                wait("vector", "Tmm", mk.cnt["Tmm"])
                emit("vector", lambda v, dst=dst, half=half: v.tensor_copy(
                    T[dst][:, half * 512:(half + 1) * 512], P[6][:, :]))
                drain("vector")
                emit("vector", lambda v: v.nop(), inc=("Vev", 1))
        WEXP_V = mk.cnt["Vev"]

        xa_dst = [("xagP", 0), ("xagP", 1), ("xauP", 0), ("xauP", 1)]
        for si in range(4):
            dstn, dsti = xa_dst[si]
            for half in range(2):
                if not (si == 0 and half == 0):
                    wait("tensor", "Vev", mk.cnt["Vev"])
                def xamm(e, si=si, half=half):
                    for k in range(KC):
                        last = e.matmul(P[3][:, :], T["agp_sb"][:, si, k, :],
                                 T["gate_h2"][:, k, half * TBLK:(half + 1) * TBLK],
                                 start=(k == 0), stop=(k == KC - 1))
                    return last
                emit("tensor", xamm, inc=("Tmm", 1))
                wait("vector", "Tmm", mk.cnt["Tmm"])
                wait("vector", "Xld", XLD0)
                emit("vector", lambda v, dstn=dstn, dsti=dsti, half=half: v.tensor_scalar(
                    T[dstn][:, dsti, half * TBLK:(half + 1) * TBLK], P[3][:, :],
                    T["pm2c"][:, 0:1], None, ALU.mult))
                drain("vector")
                emit("vector", lambda v: v.nop(), inc=("Vev", 1))
        XA_V = mk.cnt["Vev"]
        tap("xa", lambda: T["xagP"][:, :, :], after=("Vev", XA_V))

        if STOP_AFTER == "P7":
            return
        xat_gth = 0
        carry_sev = carry_mbv = carry_uadd = 0
        prev_silu = [0] * E
        prev_uadd = [0] * E
        prev_mult = [0] * E
        for half in range(2):
            wait("gpsimd", "Vev", XA_V)
            if half == 1:
                wait("gpsimd", "Tmm", mk.cnt["Tmm"])
            for tg in range(E):
                srcn = "xagP"; srci = 0 if tg < 4 else 1
                emit("gpsimd", lambda g, tg=tg, srcn=srcn, srci=srci, half=half: g.dma_start(
                    out=T["xat"][:, tg, :],
                    in_=T[srcn][32 * (tg % 4):32 * (tg % 4) + 32, srci,
                                half * TBLK:(half + 1) * TBLK]), inc=("Gth", 16))
            for tg in range(E):
                srcn = "xauP"; srci = 0 if tg < 4 else 1
                emit("gpsimd", lambda g, tg=tg, srcn=srcn, srci=srci, half=half: g.dma_start(
                    out=T["xat"][:, E + tg, :],
                    in_=T[srcn][32 * (tg % 4):32 * (tg % 4) + 32, srci,
                                half * TBLK:(half + 1) * TBLK]), inc=("Gth", 16))
            xat_gth = mk.cnt["Gth"]

            if half == 1:
                wait("gpsimd", "Tmm", mk.cnt["Tmm"])
            wait("gpsimd", "Tmm", op_t[-1])
            emit("gpsimd", lambda g: g.dma_start(
                out=T["wcolA"][:, 0, :, :], in_=d["Wg_T"][0]),
                inc=("WlA", 16))
            emit("gpsimd", lambda g: g.dma_start(
                out=T["wcolB"][:, 0, :, :], in_=d["Wu_T"][0]),
                inc=("WlA", 16))
            emit("gpsimd", lambda g: g.dma_start(
                out=T["bgp_sb"][:, 0, :, :], in_=d["bgp"][0]),
                inc=("WdA", 16))
            emit("gpsimd", lambda g: g.dma_start(
                out=T["bup_sb"][:, 0, :, :], in_=d["bup"][0]),
                inc=("WdA", 16))
            wait("vector", "Tmm", mk.cnt["Tmm"])
            emit("vector", lambda v: v.memset(P[6][:, :], 0.0))
            emit("vector", lambda v: v.memset(P[7][:, :], 0.0))
            drain("vector")
            emit("vector", lambda v: v.nop(), inc=("Vev", 1))
            HAZ_V = mk.cnt["Vev"]
            ffn_t = []

            chunk_t = [0] * (FC + 1)
            base_done_t = [0] * (FC + 1)
            base_g_done_t = [0] * (FC + 1)
            gbe_s = [0] * FC
            ube_s = [0] * FC
            mult_base_v = [0] * FC
            uadd_last = [0] * FC
            for i in range(FC + 1):
                if i + 1 < FC:
                    if i >= 1:
                        wait("gpsimd", "Tmm", base_done_t[i - 1])
                    wl = "WlA" if (i + 1) % 2 == 0 else "WlB"
                    for nm, dst in (("Wg_T", "wcolA"), ("Wu_T", "wcolB")):
                        emit("gpsimd", lambda g, i=i, nm=nm, dst=dst: g.dma_start(
                            out=T[dst][:, (i + 1) % 2, :, :], in_=d[nm][i + 1]), inc=(wl, 16))
                silu_s = [0] * E
                uadd_v = [0] * E
                mult_v = [0] * E
                ad_t = [0] * E
                if i >= 1:
                    pp = (i - 1) % 2
                    wait("scalar", "Tmm", base_done_t[i - 1])
                    if i >= 3:
                        wait("scalar", "Vev", uadd_last[i - 3])
                    elif carry_uadd:
                        wait("scalar", "Vev", carry_uadd)
                    emit("scalar", lambda s, pp=pp: s.copy(
                        T["ubb"][:, pp, :], P[1][:, :]), inc=("Sev", 1))
                    ube_s[i - 1] = mk.cnt["Sev"]
                    wait("vector", "Tmm", base_done_t[i - 1])
                    wait("vector", "Sev", ube_s[i - 1])
                    emit("vector", lambda v, i=i, pp=pp: v.tensor_tensor(
                        T["hbase"][:, i - 1, :], T["sgbb"][:, pp, :], T["ubb"][:, pp, :],
                        ALU.mult), inc=("Vev", 1))
                    mult_base_v[i - 1] = mk.cnt["Vev"]
                def bmmB(e, i=i, q=0, half=half):
                    bank, wc = (0, "wcolA") if q < 4 else (1, "wcolB")
                    qq = q % 4
                    for k in range(4 * qq, 4 * qq + 4):
                        last = e.matmul(P[bank][:, :], T[wc][:, i % 2, k, :],
                                        T["gate_h2"][:, k, half * TBLK:(half + 1) * TBLK],
                                        start=(k == 0), stop=(k == KC - 1),
                                        skip_group_check=True)
                    return last
                if i == 0:
                    wl_fc = "WlA"
                    wait("tensor", wl_fc, mk.cnt[wl_fc])
                    wait("tensor", "Vev", H2_V)
                    if carry_sev:
                        wait("tensor", "Sev", carry_sev)
                    def bmm(e, i=i, half=half):
                        for q in range(2 * 4):
                            last = bmmB(e, i=i, q=q, half=half)
                        return last
                    emit("tensor", bmm, inc=("Tmm", 1))
                    base_g_done_t[i] = base_done_t[i] = mk.cnt["Tmm"]
                    wait("scalar", "Tmm", base_done_t[0])
                    if carry_mbv:
                        wait("scalar", "Vev", carry_mbv)
                    emit("scalar", lambda s: s.activation(
                        T["sgbb"][:, 0, :], P[0][:, :], AF.Silu))
                    emit("scalar", lambda s: s.copy(
                        T["gbb"][:, 0, :], P[0][:, :]), inc=("Sev", 1))
                    gbe_s[0] = mk.cnt["Sev"]
                if i >= 1:
                    pp = (i - 1) % 2
                    wdp = "WdA" if pp == 0 else "WdB"
                    wait("tensor", wdp, mk.cnt[wdp])
                    wait("tensor", "Gth", xat_gth)
                    wait("tensor", "Vev", HAZ_V)
                    wait("tensor", "Sev", gbe_s[i - 1])
                    for e_i in range(E):
                        lgb = 2 + (e_i % 2)
                        lub = 4 + (e_i % 2)
                        if e_i >= 2:
                            wait("tensor", "Sev", silu_s[e_i - 2])
                            wait("tensor", "Vev", uadd_v[e_i - 2])
                        else:
                            wait("tensor", "Sev", prev_silu[e_i - 2 + E])
                            wait("tensor", "Vev", prev_uadd[e_i - 2 + E])
                        def lmm(e, i=i, e_i=e_i, lgb=lgb, lub=lub, pp=pp):
                            e.matmul(P[lgb][:, :], T["ident"][:, :], T["gbb"][:, pp, :],
                                     start=True, stop=False, skip_group_check=True)
                            e.matmul(P[lgb][:, :], T["bgp_sb"][:, pp, e_i, :],
                                     T["xat"][:, e_i, :],
                                     start=False, stop=True, skip_group_check=True)
                            last = e.matmul(P[lub][:, :], T["bup_sb"][:, pp, e_i, :],
                                            T["xat"][:, E + e_i, :],
                                            start=True, stop=True, skip_group_check=True)
                            return last
                        emit("tensor", lmm, inc=("Tmm", 1))
                        lora_t = mk.cnt["Tmm"]
                        if e_i >= 1:
                            wait("tensor", "Vev", mult_v[e_i - 1])
                            emit("tensor", lambda e, i=i, e_i=e_i: e.matmul(
                                P[6 + ((e_i - 1) % 2)][32 * ((e_i - 1) // 2):32 * ((e_i - 1) // 2) + 16, :],
                                T["adS_sb"][:, i - 1, e_i - 1, :], T["hexp"][:, (e_i - 1) % 2, :],
                                start=(i == 1), stop=True, skip_group_check=True,
                                tile_position=(0, 32 * ((e_i - 1) // 2))), inc=("Tmm", 1))
                            ad_t[e_i - 1] = mk.cnt["Tmm"]
                        if i < FC:
                            if e_i == 0:
                                wl_fc = "WlA" if i % 2 == 0 else "WlB"
                                wait("tensor", wl_fc, mk.cnt[wl_fc])
                                wait("tensor", "Sev", gbe_s[i - 1])
                            if e_i == 4:
                                wait("tensor", "Sev", ube_s[i - 1])
                            emit("tensor", lambda e, i=i, e_i=e_i, half=half: bmmB(
                                e, i=i, q=e_i, half=half), inc=("Tmm", 1))
                            if e_i == 3:
                                base_g_done_t[i] = mk.cnt["Tmm"]
                            if e_i == E - 1:
                                base_done_t[i] = mk.cnt["Tmm"]
                        wait("vector", "Tmm", lora_t)
                        if e_i >= 2:
                            wait("vector", "Vev", mult_v[e_i - 2])
                        emit("vector", lambda v, e_i=e_i, lub=lub, pp=pp: v.tensor_tensor(
                            T["usb"][:, e_i % 2, :], P[lub][:, :], T["ubb"][:, pp, :], ALU.add),
                            inc=("Vev", 1))
                        uadd_v[e_i] = mk.cnt["Vev"]
                        wait("scalar", "Tmm", lora_t)
                        if e_i >= 2:
                            wait("scalar", "Vev", mult_v[e_i - 2])
                        else:
                            wait("scalar", "Vev", prev_mult[e_i - 2 + E])
                        emit("scalar", lambda s, e_i=e_i, lgb=lgb: s.activation(
                            T["sgb"][:, e_i % 2, :], P[lgb][:, :], AF.Silu),
                            inc=("Sev", 1))
                        silu_s[e_i] = mk.cnt["Sev"]
                        if i < FC and e_i == 3:
                            wait("scalar", "Tmm", base_g_done_t[i])
                            if i >= 2:
                                wait("scalar", "Vev", mult_base_v[i - 2])
                            elif carry_mbv:
                                wait("scalar", "Vev", carry_mbv)
                            emit("scalar", lambda s, i=i: s.activation(
                                T["sgbb"][:, i % 2, :], P[0][:, :], AF.Silu))
                            emit("scalar", lambda s, i=i: s.copy(
                                T["gbb"][:, i % 2, :], P[0][:, :]), inc=("Sev", 1))
                            gbe_s[i] = mk.cnt["Sev"]
                        wait("vector", "Vev", uadd_v[e_i])
                        wait("vector", "Sev", silu_s[e_i])
                        if e_i >= 2:
                            wait("vector", "Tmm", ad_t[e_i - 2])
                        emit("vector", lambda v, e_i=e_i: v.tensor_tensor(
                            T["hexp"][:, e_i % 2, :], T["sgb"][:, e_i % 2, :],
                            T["usb"][:, e_i % 2, :], ALU.mult), inc=("Vev", 1))
                        mult_v[e_i] = mk.cnt["Vev"]
                    wait("tensor", "Vev", mult_v[E - 1])
                    emit("tensor", lambda e, i=i: e.matmul(
                        P[6 + ((E - 1) % 2)][32 * ((E - 1) // 2):32 * ((E - 1) // 2) + 16, :],
                        T["adS_sb"][:, i - 1, E - 1, :], T["hexp"][:, (E - 1) % 2, :],
                        start=(i == 1), stop=True, skip_group_check=True,
                        tile_position=(0, 32 * ((E - 1) // 2))), inc=("Tmm", 1))
                    ad_t[E - 1] = mk.cnt["Tmm"]
                    uadd_last[i - 1] = uadd_v[E - 1]
                    prev_silu = list(silu_s)
                    prev_uadd = list(uadd_v)
                    prev_mult = list(mult_v)
                chunk_t[i] = mk.cnt["Tmm"]
                if i + 1 < FC:
                    wait("gpsimd", "Tmm", chunk_t[i])
                    wlp = "WdA" if (i + 1) % 2 == 0 else "WdB"
                    for nm, dst in (("bgp", "bgp_sb"), ("bup", "bup_sb")):
                        emit("gpsimd", lambda g, i=i, nm=nm, dst=dst: g.dma_start(
                            out=T[dst][:, (i + 1) % 2, :, :], in_=d[nm][i + 1]), inc=(wlp, 16))
            carry_sev = ube_s[FC - 1]
            carry_mbv = mult_base_v[FC - 1]
            carry_uadd = uadd_last[FC - 1]
            AD_T = mk.cnt["Tmm"]
            wait("vector", "Tmm", AD_T)
            wait("vector", "Vev", WEXP_V)
            emit("vector", lambda v, half=half: v.tensor_tensor(
                T["hAwE"][:, :], P[6][:, :], T["wexpE"][:, half * TBLK:(half + 1) * TBLK], ALU.mult))
            emit("vector", lambda v, half=half: v.tensor_tensor(
                T["hAwO"][:, :], P[7][:, :], T["wexpO"][:, half * TBLK:(half + 1) * TBLK], ALU.mult))
            drain("vector")
            emit("vector", lambda v: v.nop(), inc=("Vev", 1))
            HAW_V = mk.cnt["Vev"]

            wait("gpsimd", "Tmm", AD_T)
            emit("gpsimd", lambda g: g.dma_start(
                out=T["wdcol"][:, 0, :, :], in_=d["Wd_T"][0]),
                inc=("WdA", 16))
            out_evac = [0, 0]
            dm_t = []
            for mo in range(KC):
                if mo + 1 < KC:
                    if mo >= 1:
                        wait("gpsimd", "Tmm", dm_t[mo - 1])
                    wd = "WdA" if (mo + 1) % 2 == 0 else "WdB"
                    emit("gpsimd", lambda g, mo=mo: g.dma_start(
                        out=T["wdcol"][:, (mo + 1) % 2, :, :],
                        in_=d["Wd_T"][mo + 1]), inc=(wd, 16))
                xl_mo = "XlA" if mo % 2 == 0 else "XlB"
                if mo >= 2:
                    wait("sync", "Vev", out_evac[mo % 2])
                else:
                    wait("sync", "Vev", H2_V)
                    if half == 1:
                        wait("sync", "Vev", out_evac[mo % 2])
                wait("sync", "Out", mk.cnt["Out"])
                emit("sync", lambda e, mo=mo, half=half: e.dma_start(
                    out=T["resid"][:, mo % 2, 0:TBLK],
                    in_=d["xp_park"][mo * 128:(mo + 1) * 128, half * TBLK:(half + 1) * TBLK]),
                    inc=(xl_mo, 16))
                wd_mo = "WdA" if mo % 2 == 0 else "WdB"
                wait("tensor", wd_mo, mk.cnt[wd_mo])
                wait("tensor", "Vev", HAW_V)
                if mo >= 2:
                    wait("tensor", "Vev", out_evac[mo % 2])
                def dmm(e, mo=mo):
                    for k in range(FC):
                        last = e.matmul(P[4 + (mo % 2)][:, :], T["wdcol"][:, mo % 2, k, :],
                                 T["hbase"][:, k, :], start=(k == 0), stop=False)
                    last = e.matmul(P[4 + (mo % 2)][:, :], T["bdS_sb"][:, 0, mo * 128:(mo + 1) * 128],
                             T["hAwE"][:, :], start=False, stop=False)
                    last = e.matmul(P[4 + (mo % 2)][:, :], T["bdS_sb"][:, 1, mo * 128:(mo + 1) * 128],
                             T["hAwO"][:, :], start=False, stop=True)
                    return last
                emit("tensor", dmm, inc=("Tmm", 1))
                dm_t.append(mk.cnt["Tmm"])
                wait("vector", "Tmm", mk.cnt["Tmm"])
                wait("vector", xl_mo, mk.cnt[xl_mo])
                emit("vector", lambda v, mo=mo: v.scalar_tensor_tensor(
                    T["resid"][:, mo % 2, 0:TBLK], P[4 + (mo % 2)][:, :],
                    T["modd"][:, 80 + mo:81 + mo], T["resid"][:, mo % 2, 0:TBLK],
                    ALU.mult, ALU.add))
                drain("vector")
                emit("vector", lambda v: v.nop(), inc=("Vev", 1))
                out_evac[mo % 2] = mk.cnt["Vev"]
                wait("sync", "Vev", out_evac[mo % 2])
                emit("sync", lambda e, mo=mo, half=half: e.dma_start(
                    out=d["y"][mo * 128:(mo + 1) * 128, half * TBLK:(half + 1) * TBLK],
                    in_=T["resid"][:, mo % 2, 0:TBLK]), inc=("Out", 16))
        wait("sync", "Out", mk.cnt["Out"])
        wait("gpsimd", "Out", mk.cnt["Out"])
        wait("scalar", "Out", mk.cnt["Out"])
        wait("vector", "Out", mk.cnt["Out"])
        wait("tensor", "Out", mk.cnt["Out"])

    handles = {}
    with nc.Block() as block:
        def runner(eng):
            def f(h):
                hs = {eng: h}
                mk = MK(eng, hs, sems)
                walk(mk)
            return f
        block.gpsimd(runner("gpsimd"))
        block.sync(runner("sync"))
        block.scalar(runner("scalar"))
        block.vector(runner("vector"))
        block.tensor(runner("tensor"))

    for cm in reversed(guards):
        cm.__exit__(None, None, None)
    return nc



def host_prep(inp):
    x = np.asarray(inp["L0_attn_hidden_states"], np.float32)
    c = np.asarray(inp["L0_c"], np.float32)
    wmod = np.asarray(inp["L0_wmod"], np.float32)
    wqkv = np.asarray(inp["L0_attn_w_qkv"], np.float32)
    slope = np.asarray(inp["L0_attn_slope_rate"], np.float32)
    wgate = np.asarray(inp["L0_attn_w_gate"], np.float32)
    wout = np.asarray(inp["L0_attn_w_out"], np.float32)
    normw = np.asarray(inp["L0_attn_norm_w"], np.float32)
    Wg = np.asarray(inp["L0_ffn_Wg"], np.float32)
    Wu = np.asarray(inp["L0_ffn_Wu"], np.float32)
    Wd = np.asarray(inp["L0_ffn_Wd"], np.float32)
    Wr = np.asarray(inp["L0_ffn_Wr"], np.float32)
    Ag = np.asarray(inp["L0_ffn_Ag"], np.float32)
    Bg = np.asarray(inp["L0_ffn_Bg"], np.float32)
    Au = np.asarray(inp["L0_ffn_Au"], np.float32)
    Bu = np.asarray(inp["L0_ffn_Bu"], np.float32)
    Ad = np.asarray(inp["L0_ffn_Ad"], np.float32)
    Bd = np.asarray(inp["L0_ffn_Bd"], np.float32)

    co = np.ascontiguousarray

    def mblocks(wT, nblk):
        K, N = wT.shape
        kc = K // 128
        return co(wT.reshape(kc, 128, nblk, 128).transpose(2, 1, 0, 3))

    wg_T = mblocks(wgate.T, KC)
    wo_T = mblocks(wout.T, KC)
    Wg_T = mblocks(Wg.T, FC)
    Wu_T = mblocks(Wu.T, FC)
    Wd_T = mblocks(Wd.T, KC)
    wr_T = co(Wr.T.reshape(KC, 128, E).transpose(1, 0, 2))
    nw_cols = co(normw.reshape(KC, 128).T)

    def pair_cols(A):
        AT = A.transpose(2, 0, 1)
        def mk(tt0):
            cols = []
            for tt in range(tt0, tt0 + 4):
                cols.append(AT[:, (tt - 1) % E, :])
                cols.append(AT[:, tt, :])
            m = np.concatenate(cols, axis=1)
            return co(m.reshape(KC, 128, 128).transpose(1, 0, 2))
        return mk(0), mk(4)
    agp1, agp2 = pair_cols(Ag)
    aup1, aup2 = pair_cols(Au)

    def pair_rows(Bm):
        BT = Bm.transpose(0, 2, 1)
        out = np.empty((FC, E, 32, 128), np.float32)
        for tt in range(E):
            stack = np.concatenate([BT[(tt - 1) % E], BT[tt]], axis=0)
            out[:, tt, :, :] = stack.reshape(32, FC, 128).transpose(1, 0, 2)
        return co(out.transpose(0, 2, 1, 3))
    bgp = pair_rows(Bg)
    bup = pair_rows(Bu)

    AdT = Ad.transpose(2, 0, 1).reshape(FC, 128, E, RK)
    adS = co(AdT.transpose(1, 0, 2, 3))
    bdS = np.zeros((128, 2, D), np.float32)
    for e_i in range(E):
        for i in range(RK):
            bdS[32 * (e_i // 2) + i, e_i % 2, :] = Bd[e_i, :, i]
    bdS = co(bdS)

    pm2c = co(np.where((np.arange(128) % 32) < 16, -2.0, 2.0).astype(np.float32).reshape(128, 1))
    in_maps = []
    for cid in range(8):
        b, q = cid // 4, cid % 4
        heads = list(range(4 * q, 4 * q + 4))
        rows = []
        for sec in range(3):
            for hh in heads:
                rows.extend(range(sec * D + hh * DH, sec * D + (hh + 1) * DH))
        wq = mblocks(co(wqkv[rows, :]).T, 12)
        wqkv_T = co(wq.transpose(1, 0, 2, 3))
        modidx = np.array([[2 * j + b] for j in range(8)], np.uint32)
        hidxm = np.full((128, 8), 10 ** 6, np.uint32)
        for m in (2 * q, 2 * q + 1):
            hidxm[:, m] = (m - 2 * q) * 128 + np.arange(128)
        pidx = np.arange(128, dtype=np.uint32)[:, None]
        kidx = np.arange(KC, dtype=np.uint32)[None, :]
        aidx = (4 * (128 * kidx + pidx) + q).astype(np.uint32)
        im = {
            "x_b": co(x[b]),
            "x_m": co(x[b, NT * q:NT * (q + 1), :]),
            "xT_m": co(x[b, NT * q:NT * (q + 1), :].T),
            "c2": co(c.T.reshape(KC, 128, B).transpose(1, 0, 2)),
            "wmod_sl": co(wmod[:, 1536 * cid:1536 * (cid + 1)].reshape(KC, 128, 1536).transpose(1, 0, 2)),
            "modidx": modidx,
            "wqkv_T": wqkv_T,
            "slopes": co(slope[heads].reshape(1, HPC)),
            "wg_T": wg_T, "wo_T": wo_T, "nw": nw_cols, "aidx": aidx, "hidx": hidxm,
            "Wg_T": Wg_T, "Wu_T": Wu_T, "Wd_T": Wd_T, "wr_T": wr_T,
            "agp1": agp1, "agp2": agp2, "aup1": aup1, "aup2": aup2,
            "bgp": bgp, "bup": bup, "adS": adS, "bdS": bdS,
            "pm2": pm2c,
        }
        in_maps.append(im)
    return in_maps


_NC_CACHE = {}


def kernel(**inputs):
    in_maps = host_prep(inputs)
    if "main" not in _NC_CACHE:
        _NC_CACHE["main"] = build_program(taps=DEBUG_TAPS)
    nc = _NC_CACHE["main"]
    res = run_bass_kernel_spmd(nc, in_maps, core_ids=list(range(8)))
    out = np.empty((B, S, D), np.float32)
    for cid in range(8):
        b, q = cid // 4, cid % 4
        out[b, NT * q:NT * (q + 1), :] = res.results[cid]["y"].T
    return out


# version 12
# speedup vs baseline: 1.3176x; 1.0009x over previous
import numpy as np
import ml_dtypes
import concourse.bass as bass
import concourse.mybir as mybir
from concourse.bass_utils import run_bass_kernel_spmd

F32 = mybir.dt.float32
BF16 = mybir.dt.bfloat16
U32 = mybir.dt.uint32
U8 = mybir.dt.uint8
AF = mybir.ActivationFunctionType
ALU = mybir.AluOpType
AX = mybir.AxisListType

B, S, D, H, DH = 2, 4096, 2048, 16, 128
HPC = 4
NT = 1024
DFF, E, RK = 5632, 8, 16
KC = D // 128
FC = DFF // 128
TBLK = 512
NTB = S // TBLK
SBLK = 256
EPS_LN = 1e-5
EPS_RMS = 1e-6

DEBUG_TAPS = ()
STOP_AFTER = None


EVENTS = {}


class MK:
    def __init__(self, eng_name, handles, sems):
        self.me = eng_name
        self.h = handles
        self.sems = sems
        self.cnt = {k: 0 for k in sems}
        self.ev = []
        EVENTS[eng_name] = self.ev

    def emit(self, eng, fn, inc=None):
        if self.me == eng:
            ins = fn(self.h[eng])
            if inc is not None and ins is not None:
                ins.then_inc(self.sems[inc[0]], inc[1])
            if inc is not None:
                self.ev.append(("inc", inc[0], inc[1]))
        if inc is not None:
            self.cnt[inc[0]] += inc[1]

    def wait(self, eng, sem, val=None):
        v = self.cnt[sem] if val is None else val
        if self.me == eng and v > 0:
            self.h[eng].wait_ge(self.sems[sem], v)
            self.ev.append(("wait", sem, v))

    def drain(self, eng):
        if self.me == eng:
            self.h[eng].drain()


def build_program(taps=()):
    nc = bass.Bass("TRN2", target_bir_lowering=False)
    d = {}

    def din(name, shape, dt=F32):
        d[name] = nc.dram_tensor(name, shape, dt, kind="ExternalInput").ap()

    def dint(name, shape, dt=F32):
        d[name] = nc.dram_tensor(name, shape, dt).ap()

    din("x_b", [S, D]);  din("x_m", [NT, D]);  din("xT_m", [D, NT])
    din("c2", [128, KC, B]);   din("wmod_sl", [128, KC, 1536]);  din("modidx", [8, 1], U32)
    din("wqkv_T", [128, 12, KC, 128]);  din("slopes", [1, HPC])
    din("wg_T", [KC, 128, KC, 128]);    din("wo_T", [KC, 128, KC, 128])
    din("nw", [128, KC]);         din("aidx", [128, KC], U32);  din("aidxB", [128, KC], U32);  din("hidx", [128, NTB], U32)
    din("Wg_T", [FC, 128, KC, 128]);    din("Wu_T", [FC, 128, KC, 128]);  din("Wd_T", [KC, 128, FC, 128])
    din("wr_T", [128, KC, E])
    din("agp1", [128, KC, 128]); din("agp2", [128, KC, 128]); din("aup1", [128, KC, 128]); din("aup2", [128, KC, 128])
    din("bgp", [FC, 32, E, 128]);  din("bup", [FC, 32, E, 128])
    din("adS", [128, FC, E, 16]);  din("bdS", [128, 2, D]);  din("pm2", [128, 1])
    d["y"] = nc.dram_tensor("y", [D, NT], F32, kind="ExternalOutput").ap()

    dint("xnb_park", [S, D], BF16)
    dint("xnm_park", [NT, D], BF16)
    dint("xp_park", [D, NT])
    dint("mod_gi", [B, 1536]);  dint("mod_go", [16, 1536])
    dint("att_giA", [HPC * DH, S // 2], BF16);  dint("att_goA", [D, S // 2], BF16)
    dint("att_giB", [HPC * DH, S // 2], BF16);  dint("att_goB", [D, S // 2], BF16)
    dint("h1mp", [256, KC, TBLK], BF16)

    tapd = {}
    for tname, shape, dt in taps:
        tapd[tname] = nc.dram_tensor("tap_" + tname, shape, dt, kind="ExternalOutput").ap()

    guards = []
    arena_cm = nc.sbuf_tensor("arena", [128, 212800], U8)
    arena_cm.__enter__(); guards.append(arena_cm)
    BASE = 16512
    RA = BASE + 0
    RB = BASE + 49152
    RC = BASE + 81920
    RD = BASE + 116736
    RE = BASE + 141312
    RF = BASE + 157696

    T = {}

    def at(name, shape, dt, off):
        T[name] = nc.alloc_sbuf_tensor_at(name, shape, dt, offset=off)

    at("wmod_sb", [128, KC, 1536], BF16, RA)
    at("g8", [8, 1536], F32, RA)
    at("wqkv_sb", [128, 12, KC, 128], BF16, RA)
    at("attnz", [128, KC, NT], BF16, RA)
    at("rstat", [4, 1024], F32, RA + 40960)
    at("lrow", [1, 2048], F32, RA + 40960)
    at("rl8", [8, NT], F32, RA)
    at("wef", [8, NT], BF16, RA + 8192)
    at("hbase", [128, FC, TBLK], BF16, RA)

    at("gate_h2", [128, KC, NT], BF16, RB)
    at("modrows", [2, 1536], F32, RB)
    at("kv", [128, HPC, 128], F32, RB + 8192)
    at("kvb", [128, HPC, 128], BF16, RB + 10240)
    at("dtt", [128, HPC, 2, SBLK], BF16, RB + 11264)
    at("qdec", [128, HPC, SBLK], BF16, RB + 15360)
    at("kdec", [128, HPC, 2], F32, RB + 17408)
    at("xaff", [128, 3, D], BF16, RB + 17472)
    at("bnc_st", [128, HPC, SBLK], BF16, RB + 29760)

    at("qkvblk", [128, 2, 12, TBLK], BF16, RC)
    at("scanbuf", [128, HPC, 5, SBLK], BF16, RC + 24576)
    at("h1m", [128, KC, NT], BF16, RC)
    at("agp_sb", [128, 4, KC, 128], BF16, RC)
    at("wr_sb", [128, KC, E], BF16, RC + 16384)
    at("xagP", [128, 2, NT], BF16, RC + 16672)
    at("xauP", [128, 2, NT], BF16, RC + 20768)
    at("hexp", [128, 2, TBLK], BF16, RC + 24864)
    at("sgb", [128, 2, TBLK], BF16, RC + 26912)
    at("tadd", [128, 2, TBLK], BF16, RC + 29984)
    at("usb", [128, 2, TBLK], BF16, RC + 32032)
    at("wexpE", [128, NT], BF16, RA + 47104)
    at("gbb", [128, 2, TBLK], BF16, RC)
    at("sgbb", [128, 2, TBLK], BF16, RC + 2048)
    at("ubb", [128, 2, TBLK], BF16, RC + 4096)
    at("hAwE", [128, TBLK], BF16, RA + 45056)
    at("hAwO", [128, TBLK], BF16, RA + 46080)

    at("xstage", [128, 3, D], F32, RD)
    at("wcolA", [128, 2, KC, 128], BF16, RD)
    at("wcolB", [128, 2, KC, 128], BF16, RD + 8192)
    at("bgp_sb", [32, 2, E, 128], BF16, RD + 16384)
    at("bup_sb", [32, 2, E, 128], BF16, RD + 20480)
    at("wdcol", [128, 2, FC, 128], BF16, RD)

    at("hblk", [128, 2, KC, TBLK], BF16, RE)
    at("xat", [32, 2 * E, TBLK], BF16, RE)

    off = RF
    def fx(name, shape, dt):
        nonlocal off
        n = int(np.prod(shape[1:]))
        sz = n * (4 if dt == F32 else (2 if dt in (BF16,) else 4))
        at(name, shape, dt, off)
        off += (sz + 31) // 32 * 32

    fx("resid", [128, 2, NT], F32)
    fx("smallb", [128, 3072], BF16)
    fx("small", [128, 784], F32)
    fx("rbc", [128, NT], BF16)
    fx("adS_sb", [128, FC, E, 16], BF16)
    fx("bdS_sb", [128, 2, D], BF16)
    fx("c2_sb", [128, KC, B], F32)
    fx("c2sil", [128, KC, B], BF16)
    fx("modc", [128, 96], F32)
    fx("modd", [128, 96], F32)
    fx("stats", [128, 64], F32)
    fx("statm", [128, 16], F32)
    fx("stat6", [128, 48], F32)
    fx("isc", [128, 64], F32)
    fx("iscm", [128, 16], F32)
    fx("wexpO", [128, NT], BF16)
    fx("consts", [128, 8], F32)
    fx("nw_sb", [128, KC], F32)
    fx("rt", [128, 8, E], F32)
    fx("rsc", [128, 8, E], F32)
    fx("rcol", [128, 8, 4], F32)
    fx("ident", [128, 128], BF16)
    fx("identf", [128, 128], F32)
    fx("onesb", [1, 128], BF16)
    fx("onecol", [128, 1], BF16)
    fx("slop", [1, 8], F32)
    fx("pm2c", [128, 1], F32)
    fx("slopc", [128, 8], F32)
    fx("idx_m", [8, 1], U32)
    fx("idx_a", [128, KC], U32)
    fx("idx_b", [128, KC], U32)
    fx("idx_h", [128, NTB], U32)
    fx("exp16", [8, 2, 128], BF16)
    assert off <= 229344, off

    sm = T["small"]

    psl = []
    for i in range(8):
        cm = nc.psum_tensor("psb%d" % i, [128, 512], F32)
        psl.append(cm.__enter__()); guards.append(cm)

    semnames = ["Wld", "Xld", "Park", "Trp", "Bnc", "Gth", "CC", "Tmm", "Sev",
                "Vev", "Gev", "Out", "Hfree", "Qfree", "PkA", "PkB", "WlA", "WlB",
                "WdA", "WdB", "XsA", "XsB", "XlA", "XlB", "TrA", "TrB", "Wq", "Wst"]
    sems = {}
    for name in semnames:
        cm = nc.semaphore(name)
        sems[name] = cm.__enter__(); guards.append(cm)

    def walk(mk: MK):
        emit, wait, drain = mk.emit, mk.wait, mk.drain
        P = psl
        pstr = psl[6][:, :].bitcast(BF16)[:, 0:512]

        def tap(name, src_fn, after=None):
            if name in tapd:
                if after:
                    wait("sync", after[0], after[1])
                emit("sync", lambda e: e.dma_start(out=tapd[name], in_=src_fn()), inc=("Out", 16))

        emit("gpsimd", lambda g: g.memset(T["consts"][:, 0:1], EPS_LN))
        emit("gpsimd", lambda g: g.memset(T["consts"][:, 1:2], -0.5))
        emit("gpsimd", lambda g: g.memset(T["consts"][:, 2:3], EPS_RMS))
        emit("gpsimd", lambda g: g.memset(T["consts"][:, 3:4], 1.0 / 2048.0))
        emit("gpsimd", lambda g: g.memset(T["onesb"][:, :], 1.0))
        emit("gpsimd", lambda g: g.memset(T["onecol"][:, :], 1.0))
        emit("gpsimd", lambda g: g.memset(T["ident"][:, :], 1.0))
        drain("gpsimd")
        emit("gpsimd", lambda g: g.affine_select(
            T["ident"][:, :], T["ident"][:, :], compare_op=ALU.is_equal, fill=0.0,
            pattern=[[1, 128]], base=0, channel_multiplier=-1))
        emit("gpsimd", lambda g: g.memset(T["identf"][:, :], 1.0))
        drain("gpsimd")
        emit("gpsimd", lambda g: g.affine_select(
            T["identf"][:, :], T["identf"][:, :], compare_op=ALU.is_equal, fill=0.0,
            pattern=[[1, 128]], base=0, channel_multiplier=-1))
        emit("gpsimd", lambda g: g.memset(T["exp16"][:, :, :], 2.0))
        drain("gpsimd")
        emit("gpsimd", lambda g: g.affine_select(
            T["exp16"][:, 0, :], T["exp16"][:, 0, :], compare_op=ALU.is_equal, fill=0.0,
            pattern=[[2, 4], [0, 32]], base=0, channel_multiplier=-1))
        drain("gpsimd")
        emit("gpsimd", lambda g: g.affine_select(
            T["exp16"][:, 1, :], T["exp16"][:, 1, :], compare_op=ALU.is_equal, fill=0.0,
            pattern=[[2, 4], [0, 32]], base=1, channel_multiplier=-1))
        drain("gpsimd")
        emit("gpsimd", lambda g: g.nop(), inc=("Gev", 1))

        emit("sync", lambda e: e.dma_start(out=T["slop"][:, 0:HPC], in_=d["slopes"]), inc=("Xld", 16))
        emit("sync", lambda e: e.dma_start(out=T["pm2c"][:, :], in_=d["pm2"]), inc=("Xld", 16))
        emit("sync", lambda e: e.dma_start(out=T["idx_m"][:, :], in_=d["modidx"]), inc=("Xld", 16))
        emit("sync", lambda e: e.dma_start(out=T["idx_a"][:, :], in_=d["aidx"]), inc=("Xld", 16))
        emit("sync", lambda e: e.dma_start(out=T["idx_b"][:, :], in_=d["aidxB"]), inc=("Xld", 16))
        emit("sync", lambda e: e.dma_start(out=T["idx_h"][:, :], in_=d["hidx"]), inc=("Xld", 16))
        emit("sync", lambda e: e.dma_start(out=T["nw_sb"][:, :], in_=d["nw"]), inc=("Xld", 16))
        emit("sync", lambda e: e.dma_start(
            out=T["c2_sb"][:, :, :], in_=d["c2"]), inc=("Xld", 16))
        XLD0 = mk.cnt["Xld"]
        emit("gpsimd", lambda g: g.dma_start(
            out=T["wmod_sb"][:, :, :], in_=d["wmod_sl"]),
            inc=("Wst", 16))
        WMOD_W = mk.cnt["Wst"]

        wait("scalar", "Xld", XLD0)
        emit("scalar", lambda s: s.activation(T["c2sil"][:, :, :], T["c2_sb"][:, :, :], AF.Silu),
             inc=("Sev", 1))
        C2S_S = mk.cnt["Sev"]
        wait("tensor", "Sev", C2S_S)
        wait("tensor", "Wst", WMOD_W)
        for n0 in range(3):
            def modmm(e, n0=n0):
                for k in range(KC):
                    last = e.matmul(P[n0][0:B, :], T["c2sil"][:, k, :],
                             T["wmod_sb"][:, k, n0 * 512:(n0 + 1) * 512],
                             start=(k == 0), stop=(k == KC - 1))
                return last
            emit("tensor", modmm, inc=("Tmm", 1))
        wait("vector", "Tmm", mk.cnt["Tmm"])
        for n0 in range(3):
            emit("vector", lambda v, n0=n0: v.tensor_copy(
                T["modrows"][0:B, n0 * 512:(n0 + 1) * 512], P[n0][0:B, :]))
        drain("vector")
        emit("vector", lambda v: v.nop(), inc=("Vev", 1))
        wait("sync", "Vev", mk.cnt["Vev"])
        emit("sync", lambda e: e.dma_start(out=d["mod_gi"], in_=T["modrows"][0:B, 0:1536]), inc=("Park", 16))
        MODGI_P = mk.cnt["Park"]
        wait("gpsimd", "Park", MODGI_P)
        emit("gpsimd", lambda g: g.collective_compute(
            "AllGather", ALU.bypass, replica_groups=[list(range(8))],
            ins=[d["mod_gi"].opt()], outs=[d["mod_go"].opt()]), inc=("CC", 1))
        wait("gpsimd", "CC", 1)
        wait("gpsimd", "Xld", XLD0)
        emit("gpsimd", lambda g: g.indirect_dma_start(
            out=T["g8"][:, :], out_offset=None, in_=d["mod_go"],
            in_offset=bass.IndirectOffsetOnAxis(ap=T["idx_m"][:, :], axis=0)), inc=("Gth", 16))
        MODG_G = mk.cnt["Gth"]
        wait("tensor", "Gth", MODG_G)
        for j in range(12):
            emit("tensor", lambda e, j=j: e.transpose(
                P[1][:, j * 8:(j + 1) * 8], T["g8"][:, j * 128:(j + 1) * 128],
                T["identf"][0:8, 0:8]), inc=("Tmm", 1))
        MODTR_T = mk.cnt["Tmm"]
        wait("vector", "Tmm", MODTR_T)
        for j in range(12):
            emit("vector", lambda v, j=j: v.tensor_copy(
                T["modc"][:, j:96:12], P[1][:, j * 8:(j + 1) * 8]))
        drain("vector")
        emit("vector", lambda v: v.tensor_copy(T["modd"][:, 0:16], T["modc"][:, 0:16]))
        emit("vector", lambda v: v.tensor_scalar(T["modd"][:, 16:32], T["modc"][:, 16:32],
                                                 1.0, None, ALU.add))
        emit("vector", lambda v: v.tensor_copy(T["modd"][:, 48:64], T["modc"][:, 48:64]))
        emit("vector", lambda v: v.tensor_scalar(T["modd"][:, 64:80], T["modc"][:, 64:80],
                                                 1.0, None, ALU.add))
        drain("vector")
        emit("vector", lambda v: v.nop(), inc=("Vev", 1))
        MODC_V = mk.cnt["Vev"]
        wait("scalar", "Vev", MODC_V)
        emit("scalar", lambda s: s.sign(T["isc"][:, 0:16], T["modc"][:, 32:48]))
        emit("scalar", lambda s: s.sign(T["isc"][:, 16:32], T["modc"][:, 80:96]))
        drain("scalar")
        emit("scalar", lambda s: s.nop(), inc=("Sev", 1))
        wait("vector", "Sev", mk.cnt["Sev"])
        emit("vector", lambda v: v.tensor_tensor(T["isc"][:, 32:48], T["modc"][:, 32:48],
                                                 T["isc"][:, 0:16], ALU.mult))
        emit("vector", lambda v: v.tensor_tensor(T["isc"][:, 48:64], T["modc"][:, 80:96],
                                                 T["isc"][:, 16:32], ALU.mult))
        drain("vector")
        emit("vector", lambda v: v.tensor_scalar(T["isc"][:, 32:48], T["isc"][:, 32:48],
                                                 0.25, None, ALU.max))
        emit("vector", lambda v: v.tensor_scalar(T["isc"][:, 48:64], T["isc"][:, 48:64],
                                                 0.25, None, ALU.max))
        drain("vector")
        emit("vector", lambda v: v.tensor_tensor(T["modd"][:, 32:48], T["isc"][:, 32:48],
                                                 T["isc"][:, 0:16], ALU.mult))
        emit("vector", lambda v: v.tensor_tensor(T["modd"][:, 80:96], T["isc"][:, 48:64],
                                                 T["isc"][:, 16:32], ALU.mult))
        drain("vector")
        emit("vector", lambda v: v.nop(), inc=("Vev", 1))
        MODD_V = mk.cnt["Vev"]
        tap("modd", lambda: T["modd"][:, :], after=("Vev" if mk.me == "sync" else "Vev", MODD_V))

        if STOP_AFTER == "P1":
            return
        wait("vector", "Xld", XLD0)
        emit("vector", lambda v: v.tensor_scalar(T["slop"][:, 4:8], T["slop"][:, 0:4],
                                                 -1.0, None, ALU.mult))
        drain("vector")
        emit("vector", lambda v: v.nop(), inc=("Vev", 1))
        NEGS_V = mk.cnt["Vev"]
        wait("scalar", "Vev", NEGS_V)
        emit("scalar", lambda s: s.copy(T["smallb"][0:1, 0:8], T["slop"][0:1, 0:8]))
        drain("scalar")
        emit("scalar", lambda s: s.nop(), inc=("Sev", 1))
        wait("tensor", "Sev", mk.cnt["Sev"])
        wait("tensor", "Gev", 1)
        emit("tensor", lambda e: e.matmul(P[3][:, 0:8], T["onesb"][0:1, :],
                                          T["smallb"][0:1, 0:8], start=True, stop=True),
             inc=("Tmm", 1))
        wait("vector", "Tmm", mk.cnt["Tmm"])
        emit("vector", lambda v: v.tensor_copy(T["slopc"][:, :], P[3][:, 0:8]))
        drain("vector")
        emit("vector", lambda v: v.nop(), inc=("Vev", 1))
        SLOPC_V = mk.cnt["Vev"]
        if STOP_AFTER == "P2a":
            return
        wait("gpsimd", "Park", MODGI_P)
        emit("gpsimd", lambda g: g.iota(sm[0:1, 0:SBLK], pattern=[[1, SBLK]], base=1,
                                        channel_multiplier=0, allow_small_or_imprecise_dtypes=True))
        emit("gpsimd", lambda g: g.iota(sm[:, SBLK:SBLK + 2], pattern=[[-128, 2]], base=255,
                                        channel_multiplier=-1, allow_small_or_imprecise_dtypes=True))
        drain("gpsimd")
        emit("gpsimd", lambda g: g.nop(), inc=("Gev", 1))
        IOTA1_G = mk.cnt["Gev"]
        wait("scalar", "Gev", IOTA1_G)
        wait("scalar", "Vev", SLOPC_V)
        for hh in range(HPC):
            emit("scalar", lambda s, hh=hh: s.activation(
                T["smallb"][0:1, hh * SBLK:(hh + 1) * SBLK], sm[0:1, 0:SBLK],
                AF.Exp, bias=0.0, scale=T["slop"][0:1, 4 + hh:5 + hh]))
            emit("scalar", lambda s, hh=hh: s.activation(
                T["kdec"][:, hh, :], sm[:, SBLK:SBLK + 2],
                AF.Exp, bias=0.0, scale=T["slopc"][:, 4 + hh:5 + hh]))
        drain("scalar")
        emit("scalar", lambda s: s.nop(), inc=("Sev", 1))
        QDR_S = mk.cnt["Sev"]
        if STOP_AFTER == "P2b":
            return
        wait("tensor", "Sev", QDR_S)
        wait("tensor", "Gev", 1)
        QDEC_BASE_V = mk.cnt["Vev"]
        for hh in range(HPC):
            if hh >= 1:
                wait("tensor", "Vev", QDEC_BASE_V + hh)
            emit("tensor", lambda e, hh=hh: e.matmul(
                P[2][:, (hh % 2) * SBLK:(hh % 2 + 1) * SBLK], T["onesb"][0:1, :],
                T["smallb"][0:1, hh * SBLK:(hh + 1) * SBLK], start=True, stop=True),
                inc=("Tmm", 1))
            wait("vector", "Tmm", mk.cnt["Tmm"])
            emit("vector", lambda v, hh=hh: v.tensor_copy(
                T["qdec"][:, hh, :], P[2][:, (hh % 2) * SBLK:(hh % 2 + 1) * SBLK]))
            drain("vector")
            emit("vector", lambda v: v.nop(), inc=("Vev", 1))
        QDEC_V = mk.cnt["Vev"]
        if STOP_AFTER == "P2c":
            return
        for half in range(2):
            wait("gpsimd", "Sev", mk.cnt["Sev"])
            emit("gpsimd", lambda g, half=half: g.iota(
                sm[:, 0:SBLK], pattern=[[1, SBLK]], base=-128 * half,
                channel_multiplier=-1, allow_small_or_imprecise_dtypes=True))
            drain("gpsimd")
            emit("gpsimd", lambda g: g.nop(), inc=("Gev", 1))
            iota_g = mk.cnt["Gev"]
            for hh in range(HPC):
                wait("scalar", "Gev", iota_g)
                wait("scalar", "Gev", mk.cnt["Gev"])
                emit("scalar", lambda s, hh=hh: s.activation(
                    sm[:, 512:512 + SBLK], sm[:, 0:SBLK], AF.Exp, bias=0.0,
                    scale=T["slopc"][:, 4 + hh:5 + hh]))
                drain("scalar")
                emit("scalar", lambda s: s.nop(), inc=("Sev", 1))
                wait("gpsimd", "Sev", mk.cnt["Sev"])
                emit("gpsimd", lambda g, hh=hh, half=half: g.affine_select(
                    T["dtt"][:, hh, half, :], sm[:, 512:512 + SBLK],
                    compare_op=ALU.is_ge, fill=0.0, pattern=[[1, SBLK]],
                    base=-128 * half, channel_multiplier=-1))
                drain("gpsimd")
                emit("gpsimd", lambda g: g.nop(), inc=("Gev", 1))
        DTT_G = mk.cnt["Gev"]
        if STOP_AFTER == "P2d":
            return
        emit("scalar", lambda s: s.activation(sm[0:1, 776:780], T["slop"][0:1, 4:8],
                                              AF.Exp, bias=0.0, scale=256.0))
        drain("scalar")
        emit("scalar", lambda s: s.copy(T["smallb"][0:1, 1024:1028], sm[0:1, 776:780]))
        drain("scalar")
        emit("scalar", lambda s: s.nop(), inc=("Sev", 1))
        wait("tensor", "Sev", mk.cnt["Sev"])
        wait("tensor", "Vev", QDEC_V)
        emit("tensor", lambda e: e.matmul(P[2][:, 0:4], T["onesb"][0:1, :],
                                          T["smallb"][0:1, 1024:1028], start=True, stop=True),
             inc=("Tmm", 1))
        wait("vector", "Tmm", mk.cnt["Tmm"])
        emit("vector", lambda v: v.tensor_copy(sm[:, 772:776], P[2][:, 0:4]))
        drain("vector")
        emit("vector", lambda v: v.memset(T["kv"][:, :, :], 0.0))
        emit("vector", lambda v: v.memset(T["kvb"][:, :, :], 0.0))
        drain("vector")
        emit("vector", lambda v: v.nop(), inc=("Vev", 1))
        BD_V = mk.cnt["Vev"]

        if STOP_AFTER == "P2":
            return
        wait("gpsimd", "Tmm", MODTR_T)
        emit("gpsimd", lambda g: g.dma_start(
            out=T["wqkv_sb"][:, :, :, :],
            in_=d["wqkv_T"]), inc=("Wq", 16))
        WQKV_W = mk.cnt["Wq"]

        p7b = psl[7][:, :].bitcast(BF16)[:, 0:512]
        xs_sems = ["XsA", "XsB", "TrA"]
        dma_sem = {}
        scat_p = {}
        stat_vv = {}
        bias_v = {}
        aff_v = {}
        xa_read_t = {}
        evac_vv = {}
        qmm_last = {}

        def ln_dma(g):
            sl = g % 3
            xsem = xs_sems[sl]
            if g >= 3:
                wait("sync", "Vev", aff_v[g - 3])
            emit("sync", lambda e, g=g, sl=sl: e.dma_start(
                out=T["xstage"][:, sl, :], in_=d["x_b"][g * 128:(g + 1) * 128, :]),
                inc=(xsem, 16))
            dma_sem[g] = (xsem, mk.cnt[xsem])

        def ln_stats(g):
            sl = g % 3
            xsem, cnt = dma_sem[g]
            wait("vector", xsem, cnt)
            s6 = (g % 2) * 24
            for pc in range(4):
                emit("vector", lambda v, sl=sl, pc=pc, s6=s6: v.bn_stats(
                    T["stat6"][:, s6 + pc * 6:s6 + (pc + 1) * 6],
                    T["xstage"][:, sl, pc * 512:(pc + 1) * 512]))
            drain("vector")
            emit("vector", lambda v, g=g, s6=s6: v.bn_aggr(
                T["stats"][:, 2 * (g % 16):2 * (g % 16) + 2], T["stat6"][:, s6:s6 + 24]),
                inc=("Vev", 1))
            stat_vv[g] = mk.cnt["Vev"]

        def ln_istd_pair(g):
            c = 2 * (g % 16)
            wait("vector", "Vev", stat_vv[g])
            emit("vector", lambda v, c=c: v.tensor_scalar(
                T["isc"][:, c - 2:c + 2:2], T["stats"][:, c - 1:c + 2:2],
                1e-5, None, ALU.add))
            drain("vector")
            emit("vector", lambda v, c=c: v.reciprocal(
                T["isc"][:, c - 2:c + 2:2], T["isc"][:, c - 2:c + 2:2]),
                inc=("Vev", 1))
            rv = mk.cnt["Vev"]
            wait("scalar", "Vev", rv)
            emit("scalar", lambda s, c=c: s.activation(
                T["isc"][:, c - 2:c + 2:2], T["isc"][:, c - 2:c + 2:2],
                AF.Sqrt), inc=("Sev", 1))
            rs = mk.cnt["Sev"]
            wait("vector", "Sev", rs)
            emit("vector", lambda v, c=c: v.scalar_tensor_tensor(
                T["isc"][:, c - 1:c + 2:2], T["stats"][:, c - 2:c + 2:2],
                -1.0, T["isc"][:, c - 2:c + 2:2], ALU.mult, ALU.mult),
                inc=("Vev", 1))
            bias_v[g] = mk.cnt["Vev"]

        def ln_affine(g):
            sl = g % 3
            c = 2 * (g % 16)
            wait("vector", "Vev", bias_v[g | 1])
            if g >= 3:
                wait("vector", "Tmm", xa_read_t[g - 3])
            emit("vector", lambda v, g=g, sl=sl, c=c: v.tensor_scalar(
                T["xaff"][:, g % 3, :], T["xstage"][:, sl, :],
                T["isc"][:, c:c + 1], T["isc"][:, c + 1:c + 2],
                ALU.mult, ALU.add), inc=("Vev", 1))
            aff_v[g] = mk.cnt["Vev"]

        def xn_pass(blk, pi):
            jj, p = pi // 4, pi % 4
            g = 4 * blk + jj
            wait("tensor", "Vev", aff_v[g])
            prev = evac_vv.get((blk, pi - 1), evac_vv.get((blk - 1, 15), 0))
            if prev:
                wait("tensor", "Vev", prev)
            def trp(e, g=g, p=p):
                for i, k in enumerate(range(4 * p, 4 * p + 4)):
                    last = e.transpose(p7b[:, 128 * i:128 * (i + 1)],
                                       T["xaff"][:, g % 3, 128 * k:128 * (k + 1)],
                                       T["ident"][:, :])
                return last
            emit("tensor", trp, inc=("Tmm", 1))
            if p == 3:
                xa_read_t[g] = mk.cnt["Tmm"]
            wait("vector", "Tmm", mk.cnt["Tmm"])
            if pi == 0 and blk >= 2:
                wait("vector", "Tmm", qmm_last[blk - 2])
                wait("vector", "PkA" if blk % 2 == 0 else "PkB", scat_p[blk - 2])
            emit("vector", lambda v, blk=blk, jj=jj, p=p: v.tensor_copy(
                T["hblk"][:, blk % 2, 4 * p:4 * p + 4, 128 * jj:128 * (jj + 1)],
                p7b[:, 0:512]), inc=("Vev", 1))
            evac_vv[(blk, pi)] = mk.cnt["Vev"]

        for g in range(3):
            ln_dma(g)
        ln_stats(0)
        ln_stats(1)
        ln_istd_pair(1)
        ln_affine(0)
        ln_affine(1)
        ln_dma(3)
        ln_stats(2)
        ln_stats(3)
        ln_istd_pair(3)
        ln_affine(2)
        for pi in range(4):
            xn_pass(0, pi)
        ln_affine(3)
        for pi in range(4, 16):
            xn_pass(0, pi)

        if STOP_AFTER == "P3":
            return
        hv = [0, 0]
        qkv_sev = [0] * NTB
        scan_state = dict(tr_evac_v=0, mask_v=0, kvupd_v=0)
        att_evac = {3: 0, 4: 0}
        kvb_sev = [0] * HPC
        bank_evac_s = [0, 0]

        for tb in range(NTB):
            buf = tb % 2
            blk = tb + 1 if tb + 1 < NTB else None
            if blk is not None:
                for jj in range(3):
                    ln_dma(4 * blk + jj)
            wait("vector", "Vev", MODD_V)
            wait("vector", "Vev", evac_vv[(tb, 15)])
            for k in range(KC):
                emit("vector", lambda v, buf=buf, k=k: v.tensor_scalar(
                    T["hblk"][:, buf, k, :], T["hblk"][:, buf, k, :],
                    T["modd"][:, 16 + k:17 + k], T["modd"][:, 0 + k:1 + k], ALU.mult, ALU.add),
                    inc=("Vev", 1) if k == KC - 1 else None)
            hv[buf] = mk.cnt["Vev"]
            wait("gpsimd", "Vev", hv[buf])
            wait("gpsimd", "Xld", XLD0)
            hsem = "PkA" if buf == 0 else "PkB"
            emit("gpsimd", lambda gg, tb=tb, buf=buf: gg.indirect_dma_start(
                out=d["h1mp"].rearrange("r a b -> r (a b)"),
                out_offset=bass.IndirectOffsetOnAxis(ap=T["idx_h"][:, tb:tb + 1], axis=0),
                in_=T["hblk"][:, buf, :, :].rearrange("p a b -> p (a b)"), in_offset=None,
                bounds_check=256, oob_is_err=False), inc=(hsem, 16))
            scat_p[tb] = mk.cnt[hsem]

            wait("tensor", "Vev", hv[buf])
            wait("tensor", "Wq", WQKV_W)
            qbuf = tb % 2
            if tb >= 2:
                wait("tensor", "Qfree", tb - 1)
            for m in range(12):
                bk = m % 2
                wait("tensor", "Sev", bank_evac_s[bk])
                def qmm(e, m=m, buf=buf, bk=bk):
                    for k in range(KC):
                        last = e.matmul(P[bk][:, :], T["wqkv_sb"][:, m, k, :],
                                 T["hblk"][:, buf, k, :], start=(k == 0), stop=(k == KC - 1))
                    return last
                emit("tensor", qmm, inc=("Tmm", 1))
                wait("scalar", "Tmm", mk.cnt["Tmm"])
                emit("scalar", lambda s, m=m, qbuf=qbuf, bk=bk: s.activation(
                    T["qkvblk"][:, qbuf, m, :], P[bk][:, :], AF.Silu), inc=("Sev", 1))
                bank_evac_s[bk] = mk.cnt["Sev"]
            qkv_sev[tb] = mk.cnt["Sev"]
            qmm_last[tb] = mk.cnt["Tmm"]

            for jj in range(2):
                j = 2 * tb + jj
                for hh in range(HPC):
                    st = 4 * jj + hh
                    wait("tensor", "Sev", qkv_sev[tb])
                    wait("tensor", "Vev", scan_state["tr_evac_v"])
                    def tr4(e, hh=hh, jj=jj, qbuf=qbuf):
                        for half in range(2):
                            last = e.transpose(pstr[:, half * 128:(half + 1) * 128],
                                        T["qkvblk"][:, qbuf, 4 + hh,
                                                    jj * SBLK + half * 128:jj * SBLK + (half + 1) * 128],
                                        T["ident"][:, :])
                        for half in range(2):
                            last = e.transpose(pstr[:, 256 + half * 128:256 + (half + 1) * 128],
                                        T["qkvblk"][:, qbuf, 8 + hh,
                                                    jj * SBLK + half * 128:jj * SBLK + (half + 1) * 128],
                                        T["ident"][:, :])
                        return last
                    emit("tensor", tr4, inc=("Tmm", 1))
                    wait("vector", "Tmm", mk.cnt["Tmm"])
                    wait("vector", "Vev", QDEC_V)
                    for half in range(2):
                        emit("vector", lambda v, hh=hh, half=half: v.tensor_scalar(
                            T["scanbuf"][:, hh, 2, half * 128:(half + 1) * 128],
                            pstr[:, half * 128:(half + 1) * 128],
                            T["kdec"][:, hh, half:half + 1], None, ALU.mult))
                    for half in range(2):
                        emit("vector", lambda v, hh=hh, half=half: v.tensor_copy(
                            T["scanbuf"][:, hh, 3, half * 128:(half + 1) * 128],
                            pstr[:, 256 + half * 128:256 + (half + 1) * 128]))
                    emit("vector", lambda v, hh=hh, jj=jj, qbuf=qbuf: v.tensor_tensor(
                        T["scanbuf"][:, hh, 4, :],
                        T["qkvblk"][:, qbuf, hh, jj * SBLK:(jj + 1) * SBLK],
                        T["qdec"][:, hh, :], ALU.mult), inc=("Vev", 1))
                    scan_state["tr_evac_v"] = mk.cnt["Vev"]
                    wait("tensor", "Vev", scan_state["tr_evac_v"])
                    wait("tensor", "Vev", scan_state["mask_v"])
                    def smm(e, hh=hh, jj=jj, qbuf=qbuf):
                        for half in range(2):
                            last = e.matmul(P[2][:, half * SBLK:(half + 1) * SBLK],
                                     T["qkvblk"][:, qbuf, 4 + hh,
                                                 jj * SBLK + half * 128:jj * SBLK + (half + 1) * 128],
                                     T["qkvblk"][:, qbuf, hh, jj * SBLK:(jj + 1) * SBLK],
                                     start=True, stop=True)
                        return last
                    emit("tensor", smm, inc=("Tmm", 1))
                    wait("vector", "Tmm", mk.cnt["Tmm"])
                    wait("vector", "Gev", DTT_G)
                    for half in range(2):
                        ii = emit("vector", lambda v, hh=hh, half=half: v.tensor_tensor(
                            T["scanbuf"][:, hh, half, :], P[2][:, half * SBLK:(half + 1) * SBLK],
                            T["dtt"][:, hh, half, :], ALU.mult),
                            inc=("Vev", 1) if half == 1 else None)
                    scan_state["mask_v"] = mk.cnt["Vev"]
                    abank = 3 + (hh % 2)
                    wait("tensor", "Vev", scan_state["mask_v"])
                    wait("tensor", "Sev", kvb_sev[hh])
                    wait("tensor", "Sev", att_evac[abank])
                    def amm(e, hh=hh, abank=abank):
                        last = e.matmul(P[abank][:, 0:SBLK], T["kvb"][:, hh, :],
                                 T["scanbuf"][:, hh, 4, :], start=True, stop=False)
                        for half in range(2):
                            last = e.matmul(P[abank][:, 0:SBLK],
                                     T["scanbuf"][:, hh, 3, half * 128:(half + 1) * 128],
                                     T["scanbuf"][:, hh, half, :],
                                     start=False, stop=(half == 1))
                        return last
                    emit("tensor", amm, inc=("Tmm", 1))
                    att_t = mk.cnt["Tmm"]
                    wait("tensor", "Vev", scan_state["kvupd_v"])
                    def kmm(e, hh=hh):
                        for half in range(2):
                            last = e.matmul(P[5][:, 0:128],
                                     T["scanbuf"][:, hh, 2, half * 128:(half + 1) * 128],
                                     T["scanbuf"][:, hh, 3, half * 128:(half + 1) * 128],
                                     start=(half == 0), stop=(half == 1))
                        return last
                    emit("tensor", kmm, inc=("Tmm", 1))
                    kv_t = mk.cnt["Tmm"]
                    if blk is not None:
                        if st == 4:
                            for pi in range(4):
                                xn_pass(blk, pi)
                        elif st == 5:
                            for pi in range(4, 8):
                                xn_pass(blk, pi)
                    wait("scalar", "Tmm", att_t)
                    wait("scalar", "Bnc", mk.cnt["Bnc"])
                    emit("scalar", lambda s, hh=hh, abank=abank: s.activation(
                        T["bnc_st"][:, hh, :],
                        P[abank][:, 0:SBLK], AF.Copy), inc=("Sev", 1))
                    att_evac[abank] = mk.cnt["Sev"]
                    wait("sync", "Sev", att_evac[abank])
                    emit("sync", lambda e, hh=hh, j=j: e.dma_start(
                        out=(d["att_giA"] if j < 8 else d["att_giB"])[
                            hh * 128:(hh + 1) * 128, (j % 8) * SBLK:(j % 8 + 1) * SBLK],
                        in_=T["bnc_st"][:, hh, :]),
                        inc=("Bnc", 16))
                    wait("vector", "Tmm", kv_t)
                    wait("vector", "Vev", BD_V)
                    emit("vector", lambda v, hh=hh: v.scalar_tensor_tensor(
                        T["kv"][:, hh, :], T["kv"][:, hh, :], sm[:, 772 + hh:773 + hh],
                        P[5][:, 0:128], ALU.mult, ALU.add), inc=("Vev", 1))
                    scan_state["kvupd_v"] = mk.cnt["Vev"]
                    wait("scalar", "Vev", scan_state["kvupd_v"])
                    emit("scalar", lambda s, hh=hh: s.copy(T["kvb"][:, hh, :], T["kv"][:, hh, :]),
                         inc=("Sev", 1))
                    kvb_sev[hh] = mk.cnt["Sev"]
                    if blk is not None:
                        g0 = 4 * blk
                        if st == 0:
                            ln_stats(g0)
                        elif st == 1:
                            ln_stats(g0 + 1)
                            ln_istd_pair(g0 + 1)
                        elif st == 2:
                            ln_affine(g0)
                            ln_affine(g0 + 1)
                            ln_dma(g0 + 3)
                        elif st == 3:
                            ln_stats(g0 + 2)
                        elif st == 6:
                            ln_stats(g0 + 3)
                            ln_istd_pair(g0 + 3)
                        elif st == 7:
                            ln_affine(g0 + 2)
            if blk is not None:
                for pi in range(8, 12):
                    xn_pass(blk, pi)
                ln_affine(4 * blk + 3)
                for pi in range(12, 16):
                    xn_pass(blk, pi)
            emit("vector", lambda v: v.nop(), inc=("Qfree", 1))
            if tb == 3:
                wait("gpsimd", "Bnc", mk.cnt["Bnc"])
                emit("gpsimd", lambda g: g.collective_compute(
                    "AllGather", ALU.bypass, replica_groups=[[0, 1, 2, 3], [4, 5, 6, 7]],
                    ins=[d["att_giA"].opt()], outs=[d["att_goA"].opt()]), inc=("CC", 1))
        SCAN_T = mk.cnt["Tmm"]

        PARK_ALL = (mk.cnt["PkA"], mk.cnt["PkB"])
        BNC_ALL = mk.cnt["Bnc"]

        if STOP_AFTER == "P4":
            return
        wait("sync", "Tmm", mk.cnt["Tmm"])
        wait("sync", "Vev", mk.cnt["Vev"])
        wait("sync", "Sev", mk.cnt["Sev"])
        wait("sync", "Bnc", mk.cnt["Bnc"])
        for pk_i, pk_n in enumerate(["PkA", "PkB"]):
            wait("sync", pk_n, PARK_ALL[pk_i])
        for half in range(2):
            emit("sync", lambda e, half=half: e.dma_start(
                out=T["h1m"][:, :, half * TBLK:(half + 1) * TBLK],
                in_=d["h1mp"][half * 128:(half + 1) * 128, :, :]), inc=("Trp", 16))
        H1M_TRP = mk.cnt["Trp"]

        wait("gpsimd", "Bnc", BNC_ALL)
        emit("gpsimd", lambda g: g.collective_compute(
            "AllGather", ALU.bypass, replica_groups=[[0, 1, 2, 3], [4, 5, 6, 7]],
            ins=[d["att_giB"].opt()], outs=[d["att_goB"].opt()]), inc=("CC", 1))
        ATTCC = mk.cnt["CC"]

        wait("gpsimd", "Vev", aff_v[31])
        emit("gpsimd", lambda g: g.dma_start(
            out=T["wcolA"][:, 0, :, :], in_=d["wg_T"][0]),
            inc=("WlA", 16))
        gate_evac = [0, 0]
        gate_t = []
        for m in range(KC):
            if m + 1 < KC:
                if m >= 1:
                    wait("gpsimd", "Tmm", gate_t[m - 1])
                wl = "WlA" if (m + 1) % 2 == 0 else "WlB"
                emit("gpsimd", lambda g, m=m: g.dma_start(
                    out=T["wcolA"][:, (m + 1) % 2, :, :],
                    in_=d["wg_T"][m + 1]), inc=(wl, 16))
            wl_m = "WlA" if m % 2 == 0 else "WlB"
            wait("tensor", wl_m, mk.cnt[wl_m])
            wait("tensor", "Trp", H1M_TRP)
            wait("tensor", "Sev", gate_evac[m % 2])
            def gmm(e, m=m):
                for half in range(2):
                    for k in range(KC):
                        last = e.matmul(P[2 * (m % 2) + half][:, :], T["wcolA"][:, m % 2, k, :],
                                 T["h1m"][:, k, half * TBLK:(half + 1) * TBLK],
                                 start=(k == 0), stop=(k == KC - 1))
                return last
            emit("tensor", gmm, inc=("Tmm", 1))
            gate_t.append(mk.cnt["Tmm"])
            wait("scalar", "Tmm", mk.cnt["Tmm"])
            for half in range(2):
                emit("scalar", lambda s, m=m, half=half: s.activation(
                    T["gate_h2"][:, m, half * TBLK:(half + 1) * TBLK],
                    P[2 * (m % 2) + half][:, :], AF.Sigmoid))
            drain("scalar")
            emit("scalar", lambda s: s.nop(), inc=("Sev", 1))
            gate_evac[m % 2] = mk.cnt["Sev"]
        GATE_S = mk.cnt["Sev"]
        tap("gate", lambda: T["gate_h2"][:, :, :], after=("Sev", GATE_S))

        if STOP_AFTER == "P5":
            return
        wait("gpsimd", "CC", ATTCC)
        for k in range(KC):
            emit("gpsimd", lambda g, k=k: g.indirect_dma_start(
                out=T["attnz"][:, k, :], out_offset=None,
                in_=d["att_goA"].rearrange("a (q c) -> (a q) c", c=NT),
                in_offset=bass.IndirectOffsetOnAxis(ap=T["idx_a"][:, k:k + 1], axis=0),
                bounds_check=2 * D, oob_is_err=False),
                inc=("Gth", 16))
        wait("gpsimd", "Gth", mk.cnt["Gth"])
        for k in range(KC):
            emit("gpsimd", lambda g, k=k: g.indirect_dma_start(
                out=T["attnz"][:, k, :], out_offset=None,
                in_=d["att_goB"].rearrange("a (q c) -> (a q) c", c=NT),
                in_offset=bass.IndirectOffsetOnAxis(ap=T["idx_b"][:, k:k + 1], axis=0),
                bounds_check=2 * D, oob_is_err=False),
                inc=("Gth", 16))
        GTH_A = mk.cnt["Gth"]
        tap("attng", lambda: T["attnz"][:, :, :], after=("Gth", GTH_A))

        rms_t = 0
        for k in range(KC):
            wait("vector", "Gth", GTH_A)
            if k > 0:
                wait("vector", "Tmm", rms_t)
            emit("vector", lambda v, k=k: v.tensor_tensor(
                T["smallb"][:, 0:NT], T["attnz"][:, k, :], T["attnz"][:, k, :], ALU.mult),
                inc=("Vev", 1))
            wait("tensor", "Vev", mk.cnt["Vev"])
            def rmm(e, k=k):
                last = e.matmul(P[4][0:1, :], T["onecol"][:, 0:1], T["smallb"][:, 0:512],
                         start=(k == 0), stop=(k == KC - 1))
                last = e.matmul(P[5][0:1, :], T["onecol"][:, 0:1], T["smallb"][:, 512:1024],
                         start=(k == 0), stop=(k == KC - 1))
                return last
            emit("tensor", rmm, inc=("Tmm", 1))
            rms_t = mk.cnt["Tmm"]
        wait("scalar", "Tmm", rms_t)
        for half, bank in ((0, 4), (1, 5)):
            emit("scalar", lambda s, half=half, bank=bank: s.activation(
                T["rstat"][0:1, half * 512:(half + 1) * 512], P[bank][0:1, :],
                AF.Ln, bias=T["consts"][0:1, 2:3], scale=T["consts"][0:1, 3:4]))
        drain("scalar")
        for half in range(2):
            emit("scalar", lambda s, half=half: s.activation(
                T["rstat"][0:1, half * 512:(half + 1) * 512],
                T["rstat"][0:1, half * 512:(half + 1) * 512],
                AF.Exp, bias=0.0, scale=T["consts"][0:1, 1:2]))
        drain("scalar")
        emit("scalar", lambda s: s.copy(T["smallb"][0:1, 0:NT], T["rstat"][0:1, 0:NT]))
        drain("scalar")
        emit("scalar", lambda s: s.nop(), inc=("Sev", 1))
        wait("tensor", "Sev", mk.cnt["Sev"])
        for half in range(2):
            if half == 1:
                wait("tensor", "Vev", mk.cnt["Vev"])
            emit("tensor", lambda e, half=half: e.matmul(
                P[6][:, :], T["onesb"][0:1, :], T["smallb"][0:1, half * 512:(half + 1) * 512],
                start=True, stop=True), inc=("Tmm", 1))
            wait("vector", "Tmm", mk.cnt["Tmm"])
            emit("vector", lambda v, half=half: v.tensor_copy(
                T["rbc"][:, half * 512:(half + 1) * 512], P[6][:, :]))
            drain("vector")
            emit("vector", lambda v: v.nop(), inc=("Vev", 1))
        RBC_V = mk.cnt["Vev"]

        wait("vector", "Sev", GATE_S)
        wait("vector", "Vev", RBC_V)
        for k in range(KC):
            emit("vector", lambda v, k=k: v.scalar_tensor_tensor(
                T["attnz"][:, k, :], T["attnz"][:, k, :], T["nw_sb"][:, k:k + 1],
                T["gate_h2"][:, k, :], ALU.mult, ALU.mult),
                inc=("Vev", 1) if k == KC - 1 else None)
        wait("vector", "Vev", mk.cnt["Vev"])
        for k in range(KC):
            emit("vector", lambda v, k=k: v.tensor_tensor(
                T["attnz"][:, k, :], T["attnz"][:, k, :], T["rbc"][:, :], ALU.mult),
                inc=("Vev", 1) if k == KC - 1 else None)
        Z_V = mk.cnt["Vev"]
        tap("z", lambda: T["attnz"][:, :, :], after=("Vev", Z_V))

        wait("gpsimd", "Tmm", gate_t[-1])
        emit("gpsimd", lambda g: g.dma_start(
            out=T["wcolA"][:, 0, :, :], in_=d["wo_T"][0]),
            inc=("WlA", 16))
        xp_evac = [0, 0]
        xp_done = [0, 0]
        op_t = []
        ln2_t = 0
        for m in range(KC):
            if m + 1 < KC:
                if m >= 1:
                    wait("gpsimd", "Tmm", op_t[m - 1])
                wl = "WlA" if (m + 1) % 2 == 0 else "WlB"
                emit("gpsimd", lambda g, m=m: g.dma_start(
                    out=T["wcolA"][:, (m + 1) % 2, :, :],
                    in_=d["wo_T"][m + 1]), inc=(wl, 16))
            xl_m = "XlA" if m % 2 == 0 else "XlB"
            if m >= 2:
                wait("sync", "Vev", xp_done[m % 2])
                wait("sync", "Park", mk.cnt["Park"])
            emit("sync", lambda e, m=m: e.dma_start(
                out=T["resid"][:, m % 2, :], in_=d["xT_m"][m * 128:(m + 1) * 128, :]),
                inc=(xl_m, 16))
            xld_m = mk.cnt[xl_m]
            wl_m = "WlA" if m % 2 == 0 else "WlB"
            wait("tensor", wl_m, mk.cnt[wl_m])
            wait("tensor", "Vev", Z_V)
            for half in range(2):
                wait("tensor", "Vev", xp_evac[half])
                def omm(e, m=m, half=half):
                    for k in range(KC):
                        last = e.matmul(P[half][:, :], T["wcolA"][:, m % 2, k, :],
                                 T["attnz"][:, k, half * TBLK:(half + 1) * TBLK],
                                 start=(k == 0), stop=(k == KC - 1))
                    return last
                emit("tensor", omm, inc=("Tmm", 1))
                omm_t = mk.cnt["Tmm"]
                wait("vector", "Tmm", omm_t)
                wait("vector", xl_m, xld_m)
                emit("vector", lambda v, m=m, half=half: v.scalar_tensor_tensor(
                    T["resid"][:, m % 2, half * TBLK:(half + 1) * TBLK],
                    P[half][:, :], T["modd"][:, 32 + m:33 + m],
                    T["resid"][:, m % 2, half * TBLK:(half + 1) * TBLK], ALU.mult, ALU.add))
                drain("vector")
                emit("vector", lambda v: v.nop(), inc=("Vev", 1))
                xp_evac[half] = mk.cnt["Vev"]
            emit("vector", lambda v, m=m: v.tensor_copy(
                T["smallb"][:, 0:NT], T["resid"][:, m % 2, :]))
            emit("vector", lambda v, m=m: v.tensor_tensor(
                T["smallb"][:, NT:2 * NT], T["resid"][:, m % 2, :], T["resid"][:, m % 2, :],
                ALU.mult), inc=("Vev", 1))
            xp_done[m % 2] = mk.cnt["Vev"]
            wait("sync", "Vev", xp_done[m % 2])
            emit("sync", lambda e, m=m: e.dma_start(
                out=d["xp_park"][m * 128:(m + 1) * 128, :], in_=T["resid"][:, m % 2, :]),
                inc=("Park", 16))
            wait("tensor", "Vev", xp_done[m % 2])
            def lsmm(e, m=m):
                last = e.matmul(P[2][0:1, :], T["onecol"][:, 0:1], T["smallb"][:, 0:512],
                         start=(m == 0), stop=(m == KC - 1))
                last = e.matmul(P[3][0:1, :], T["onecol"][:, 0:1], T["smallb"][:, 512:1024],
                         start=(m == 0), stop=(m == KC - 1))
                last = e.matmul(P[6][0:1, :], T["onecol"][:, 0:1], T["smallb"][:, NT:NT + 512],
                         start=(m == 0), stop=(m == KC - 1))
                last = e.matmul(P[4][0:1, :], T["onecol"][:, 0:1], T["smallb"][:, NT + 512:2 * NT],
                         start=(m == 0), stop=(m == KC - 1))
                return last
            emit("tensor", lsmm, inc=("Tmm", 1))
            ln2_t = mk.cnt["Tmm"]
            op_t.append(mk.cnt["Tmm"])
        XPARK_ALL = mk.cnt["Park"]

        wait("vector", "Tmm", ln2_t)
        emit("vector", lambda v: v.tensor_scalar(T["lrow"][0:1, 0:512], P[2][0:1, :],
                                                 1.0 / 2048.0, None, ALU.mult))
        emit("vector", lambda v: v.tensor_scalar(T["lrow"][0:1, 512:1024], P[3][0:1, :],
                                                 1.0 / 2048.0, None, ALU.mult))
        emit("vector", lambda v: v.tensor_scalar(T["lrow"][0:1, 1024:1536], P[6][0:1, :],
                                                 1.0 / 2048.0, None, ALU.mult))
        emit("vector", lambda v: v.tensor_scalar(T["lrow"][0:1, 1536:2048], P[4][0:1, :],
                                                 1.0 / 2048.0, None, ALU.mult))
        drain("vector")
        emit("vector", lambda v: v.tensor_tensor(sm[0:1, 0:512], T["lrow"][0:1, 0:512],
                                                 T["lrow"][0:1, 0:512], ALU.mult))
        drain("vector")
        emit("vector", lambda v: v.tensor_tensor(T["lrow"][0:1, 1024:1536],
                                                 T["lrow"][0:1, 1024:1536],
                                                 sm[0:1, 0:512], ALU.subtract))
        drain("vector")
        emit("vector", lambda v: v.tensor_tensor(sm[0:1, 0:512], T["lrow"][0:1, 512:1024],
                                                 T["lrow"][0:1, 512:1024], ALU.mult))
        drain("vector")
        emit("vector", lambda v: v.tensor_tensor(T["lrow"][0:1, 1536:2048],
                                                 T["lrow"][0:1, 1536:2048],
                                                 sm[0:1, 0:512], ALU.subtract))
        drain("vector")
        emit("vector", lambda v: v.nop(), inc=("Vev", 1))
        wait("scalar", "Vev", mk.cnt["Vev"])
        emit("scalar", lambda s: s.activation(T["lrow"][0:1, 1024:2048], T["lrow"][0:1, 1024:2048],
                                              AF.Ln, bias=T["consts"][0:1, 0:1], scale=1.0))
        drain("scalar")
        emit("scalar", lambda s: s.activation(T["lrow"][0:1, 1024:2048], T["lrow"][0:1, 1024:2048],
                                              AF.Exp, bias=0.0, scale=T["consts"][0:1, 1:2]))
        drain("scalar")
        emit("scalar", lambda s: s.nop(), inc=("Sev", 1))
        wait("vector", "Sev", mk.cnt["Sev"])
        emit("vector", lambda v: v.tensor_tensor(T["lrow"][0:1, 0:1024], T["lrow"][0:1, 0:1024],
                                                 T["lrow"][0:1, 1024:2048], ALU.mult))
        drain("vector")
        emit("vector", lambda v: v.tensor_scalar(T["lrow"][0:1, 0:1024], T["lrow"][0:1, 0:1024],
                                                 -1.0, None, ALU.mult))
        drain("vector")
        emit("vector", lambda v: v.tensor_copy(T["smallb"][0:1, 0:NT], T["lrow"][0:1, 1024:2048]))
        emit("vector", lambda v: v.tensor_copy(T["smallb"][0:1, NT:2 * NT], T["lrow"][0:1, 0:1024]))
        drain("vector")
        emit("vector", lambda v: v.nop(), inc=("Vev", 1))
        wait("tensor", "Vev", mk.cnt["Vev"])
        for half in range(2):
            if half == 1:
                wait("tensor", "Vev", mk.cnt["Vev"])
            emit("tensor", lambda e, half=half: e.matmul(
                P[5][:, :], T["onesb"][0:1, :], T["smallb"][0:1, half * 512:(half + 1) * 512],
                start=True, stop=True), inc=("Tmm", 1))
            wait("vector", "Tmm", mk.cnt["Tmm"])
            emit("vector", lambda v, half=half: v.tensor_copy(
                T["rbc"][:, half * 512:(half + 1) * 512], P[5][:, :]))
            drain("vector")
            emit("vector", lambda v: v.nop(), inc=("Vev", 1))
            wait("tensor", "Vev", mk.cnt["Vev"])
            emit("tensor", lambda e, half=half: e.matmul(
                P[5][:, :], T["onesb"][0:1, :],
                T["smallb"][0:1, NT + half * 512:NT + (half + 1) * 512],
                start=True, stop=True), inc=("Tmm", 1))
            wait("vector", "Tmm", mk.cnt["Tmm"])
            emit("vector", lambda v, half=half: v.tensor_copy(
                T["smallb"][:, 2048 + half * 512:2048 + (half + 1) * 512], P[5][:, :]))
            drain("vector")
            emit("vector", lambda v: v.nop(), inc=("Vev", 1))
        IBC_V = mk.cnt["Vev"]

        for m in range(KC):
            xl_m = "XlA" if m % 2 == 0 else "XlB"
            wait("sync", "Park", XPARK_ALL)
            if m >= 2:
                wait("sync", "Vev", mk.cnt["Vev"])
            emit("sync", lambda e, m=m: e.dma_start(
                out=T["resid"][:, m % 2, :], in_=d["xp_park"][m * 128:(m + 1) * 128, :]),
                inc=(xl_m, 16))
            wait("vector", xl_m, mk.cnt[xl_m])
            wait("vector", "Vev", IBC_V)
            emit("vector", lambda v, m=m: v.tensor_tensor(
                T["gate_h2"][:, m, :], T["resid"][:, m % 2, :], T["rbc"][:, :], ALU.mult),
                inc=("Vev", 1))
            wait("vector", "Vev", mk.cnt["Vev"])
            emit("vector", lambda v, m=m: v.tensor_tensor(
                T["gate_h2"][:, m, :], T["gate_h2"][:, m, :],
                T["smallb"][:, 2048:3072], ALU.add), inc=("Vev", 1))
            wait("vector", "Vev", mk.cnt["Vev"])
            emit("vector", lambda v, m=m: v.tensor_scalar(
                T["gate_h2"][:, m, :], T["gate_h2"][:, m, :],
                T["modd"][:, 64 + m:65 + m], T["modd"][:, 48 + m:49 + m], ALU.mult, ALU.add),
                inc=("Vev", 1))
        H2_V = mk.cnt["Vev"]
        tap("h2", lambda: T["gate_h2"][:, :, :], after=("Vev", H2_V))

        if STOP_AFTER == "P6":
            return
        wait("gpsimd", "Sev", GATE_S)
        emit("gpsimd", lambda g: g.dma_start(
            out=T["wr_sb"][:, :, :], in_=d["wr_T"]),
            inc=("Wst", 16))
        for i, nm in enumerate(["agp1", "agp2", "aup1", "aup2"]):
            emit("gpsimd", lambda g, i=i, nm=nm: g.dma_start(
                out=T["agp_sb"][:, i, :, :], in_=d[nm]),
                inc=("Wst", 16))
        emit("gpsimd", lambda g: g.dma_start(out=T["adS_sb"][:, :, :, :],
                                             in_=d["adS"]), inc=("Wst", 16))
        emit("gpsimd", lambda g: g.dma_start(out=T["bdS_sb"][:, :, :], in_=d["bdS"]), inc=("Wst", 16))
        WFFN_W = mk.cnt["Wst"]

        wait("tensor", "Wst", WFFN_W)
        wait("tensor", "Vev", H2_V)
        def rlmm2(e):
            for half in range(2):
                bank = P[4] if half == 0 else P[5]
                for k in range(KC):
                    last = e.matmul(bank[0:E, :], T["wr_sb"][:, k, :],
                             T["gate_h2"][:, k, half * TBLK:(half + 1) * TBLK],
                             start=(k == 0), stop=(k == KC - 1))
            return last
        emit("tensor", rlmm2, inc=("Tmm", 1))
        wait("vector", "Tmm", mk.cnt["Tmm"])
        emit("vector", lambda v: v.tensor_copy(T["rl8"][:, 0:512], P[4][0:E, :]))
        emit("vector", lambda v: v.tensor_copy(T["rl8"][:, 512:1024], P[5][0:E, :]))
        drain("vector")
        emit("vector", lambda v: v.nop(), inc=("Vev", 1))
        wait("tensor", "Vev", mk.cnt["Vev"])
        for c in range(8):
            emit("tensor", lambda e, c=c: e.transpose(
                P[4][:, c * 8:(c + 1) * 8], T["rl8"][:, c * 128:(c + 1) * 128],
                T["identf"][0:8, 0:8]), inc=("Tmm", 1))
        wait("vector", "Tmm", mk.cnt["Tmm"])
        for c in range(8):
            emit("vector", lambda v, c=c: v.tensor_copy(T["rt"][:, c, :], P[4][:, c * 8:(c + 1) * 8]))
        drain("vector")
        for c in range(8):
            emit("vector", lambda v, c=c: v.reduce_max(T["rcol"][:, c, 0:1], T["rt"][:, c, :], AX.X))
        drain("vector")
        for c in range(8):
            emit("vector", lambda v, c=c: v.tensor_scalar(T["rcol"][:, c, 1:2], T["rcol"][:, c, 0:1],
                                                          -1.0, None, ALU.mult))
        drain("vector")
        emit("vector", lambda v: v.nop(), inc=("Vev", 1))
        wait("scalar", "Vev", mk.cnt["Vev"])
        for c in range(8):
            emit("scalar", lambda s, c=c: s.activation(
                T["rsc"][:, c, :], T["rt"][:, c, :], AF.Exp, bias=T["rcol"][:, c, 1:2], scale=1.0))
        drain("scalar")
        emit("scalar", lambda s: s.nop(), inc=("Sev", 1))
        wait("vector", "Sev", mk.cnt["Sev"])
        for c in range(8):
            emit("vector", lambda v, c=c: v.tensor_reduce(T["rcol"][:, c, 0:1], T["rsc"][:, c, :],
                                                          AX.X, ALU.add))
        drain("vector")
        for c in range(8):
            emit("vector", lambda v, c=c: v.reciprocal(T["rcol"][:, c, 1:2], T["rcol"][:, c, 0:1]))
        drain("vector")
        for c in range(8):
            emit("vector", lambda v, c=c: v.tensor_scalar(T["rt"][:, c, :], T["rsc"][:, c, :],
                                                          T["rcol"][:, c, 1:2], None, ALU.mult))
        drain("vector")
        for c in range(8):
            emit("vector", lambda v, c=c: v.reduce_max(T["rcol"][:, c, 0:1], T["rt"][:, c, :], AX.X))
        drain("vector")
        for c in range(8):
            emit("vector", lambda v, c=c: v.tensor_scalar(T["rsc"][:, c, :], T["rt"][:, c, :],
                                                          T["rcol"][:, c, 0:1], None, ALU.is_equal))
        drain("vector")
        for c in range(8):
            emit("vector", lambda v, c=c: v.scalar_tensor_tensor(
                T["rsc"][:, c, :], T["rsc"][:, c, :], -1e30, T["rt"][:, c, :], ALU.mult, ALU.add))
        drain("vector")
        for c in range(8):
            emit("vector", lambda v, c=c: v.reduce_max(T["rcol"][:, c, 2:3], T["rsc"][:, c, :], AX.X))
        drain("vector")
        for c in range(8):
            emit("vector", lambda v, c=c: v.tensor_scalar(T["rsc"][:, c, :], T["rt"][:, c, :],
                                                          T["rcol"][:, c, 2:3], None, ALU.is_ge))
        drain("vector")
        for c in range(8):
            emit("vector", lambda v, c=c: v.tensor_tensor(T["rt"][:, c, :], T["rt"][:, c, :],
                                                          T["rsc"][:, c, :], ALU.mult))
        drain("vector")
        emit("vector", lambda v: v.nop(), inc=("Vev", 1))
        TOPW_V = mk.cnt["Vev"]
        tap("topw", lambda: T["rt"][:, :, :], after=("Vev", TOPW_V))
        wait("tensor", "Vev", TOPW_V)
        for c in range(8):
            bank = P[4] if c < 4 else P[5]
            emit("tensor", lambda e, c=c, bank=bank: e.transpose(
                bank[0:8, (c % 4) * 128:(c % 4 + 1) * 128], T["rt"][:, c, :], T["identf"][:, :]),
                inc=("Tmm", 1))
        wait("vector", "Tmm", mk.cnt["Tmm"])
        emit("vector", lambda v: v.tensor_copy(T["wef"][:, 0:512], P[4][0:8, 0:512]))
        emit("vector", lambda v: v.tensor_copy(T["wef"][:, 512:1024], P[5][0:8, 0:512]))
        drain("vector")
        emit("vector", lambda v: v.nop(), inc=("Vev", 1))
        wait("tensor", "Vev", mk.cnt["Vev"])
        for par, dst in ((0, "wexpE"), (1, "wexpO")):
            for half in range(2):
                if not (par == 0 and half == 0):
                    wait("tensor", "Vev", mk.cnt["Vev"])
                emit("tensor", lambda e, par=par, half=half: e.matmul(
                    P[6][:, :], T["exp16"][:, par, :], T["wef"][:, half * 512:(half + 1) * 512],
                    start=True, stop=True), inc=("Tmm", 1))
                wait("vector", "Tmm", mk.cnt["Tmm"])
                emit("vector", lambda v, dst=dst, half=half: v.tensor_copy(
                    T[dst][:, half * 512:(half + 1) * 512], P[6][:, :]))
                drain("vector")
                emit("vector", lambda v: v.nop(), inc=("Vev", 1))
        WEXP_V = mk.cnt["Vev"]

        xa_dst = [("xagP", 0), ("xagP", 1), ("xauP", 0), ("xauP", 1)]
        for si in range(4):
            dstn, dsti = xa_dst[si]
            for half in range(2):
                if not (si == 0 and half == 0):
                    wait("tensor", "Vev", mk.cnt["Vev"])
                def xamm(e, si=si, half=half):
                    for k in range(KC):
                        last = e.matmul(P[3][:, :], T["agp_sb"][:, si, k, :],
                                 T["gate_h2"][:, k, half * TBLK:(half + 1) * TBLK],
                                 start=(k == 0), stop=(k == KC - 1))
                    return last
                emit("tensor", xamm, inc=("Tmm", 1))
                wait("vector", "Tmm", mk.cnt["Tmm"])
                wait("vector", "Xld", XLD0)
                emit("vector", lambda v, dstn=dstn, dsti=dsti, half=half: v.tensor_scalar(
                    T[dstn][:, dsti, half * TBLK:(half + 1) * TBLK], P[3][:, :],
                    T["pm2c"][:, 0:1], None, ALU.mult))
                drain("vector")
                emit("vector", lambda v: v.nop(), inc=("Vev", 1))
        XA_V = mk.cnt["Vev"]
        tap("xa", lambda: T["xagP"][:, :, :], after=("Vev", XA_V))

        if STOP_AFTER == "P7":
            return
        xat_gth = 0
        carry_sev = carry_mbv = carry_uadd = 0
        prev_silu = [0] * E
        prev_uadd = [0] * E
        prev_mult = [0] * E
        for half in range(2):
            wait("gpsimd", "Vev", XA_V)
            if half == 1:
                wait("gpsimd", "Tmm", mk.cnt["Tmm"])
            for tg in range(E):
                srcn = "xagP"; srci = 0 if tg < 4 else 1
                emit("gpsimd", lambda g, tg=tg, srcn=srcn, srci=srci, half=half: g.dma_start(
                    out=T["xat"][:, tg, :],
                    in_=T[srcn][32 * (tg % 4):32 * (tg % 4) + 32, srci,
                                half * TBLK:(half + 1) * TBLK]), inc=("Gth", 16))
            for tg in range(E):
                srcn = "xauP"; srci = 0 if tg < 4 else 1
                emit("gpsimd", lambda g, tg=tg, srcn=srcn, srci=srci, half=half: g.dma_start(
                    out=T["xat"][:, E + tg, :],
                    in_=T[srcn][32 * (tg % 4):32 * (tg % 4) + 32, srci,
                                half * TBLK:(half + 1) * TBLK]), inc=("Gth", 16))
            xat_gth = mk.cnt["Gth"]

            if half == 1:
                wait("gpsimd", "Tmm", mk.cnt["Tmm"])
            wait("gpsimd", "Tmm", op_t[-1])
            emit("gpsimd", lambda g: g.dma_start(
                out=T["wcolA"][:, 0, :, :], in_=d["Wg_T"][0]),
                inc=("WlA", 16))
            emit("gpsimd", lambda g: g.dma_start(
                out=T["wcolB"][:, 0, :, :], in_=d["Wu_T"][0]),
                inc=("WlA", 16))
            emit("gpsimd", lambda g: g.dma_start(
                out=T["bgp_sb"][:, 0, :, :], in_=d["bgp"][0]),
                inc=("WdA", 16))
            emit("gpsimd", lambda g: g.dma_start(
                out=T["bup_sb"][:, 0, :, :], in_=d["bup"][0]),
                inc=("WdA", 16))
            wait("vector", "Tmm", mk.cnt["Tmm"])
            emit("vector", lambda v: v.memset(P[6][:, :], 0.0))
            emit("vector", lambda v: v.memset(P[7][:, :], 0.0))
            drain("vector")
            emit("vector", lambda v: v.nop(), inc=("Vev", 1))
            HAZ_V = mk.cnt["Vev"]
            ffn_t = []

            chunk_t = [0] * (FC + 1)
            base_done_t = [0] * (FC + 1)
            base_g_done_t = [0] * (FC + 1)
            gbe_s = [0] * FC
            ube_s = [0] * FC
            mult_base_v = [0] * FC
            uadd_last = [0] * FC
            for i in range(FC + 1):
                if i + 1 < FC:
                    if i >= 1:
                        wait("gpsimd", "Tmm", base_done_t[i - 1])
                    wl = "WlA" if (i + 1) % 2 == 0 else "WlB"
                    for nm, dst in (("Wg_T", "wcolA"), ("Wu_T", "wcolB")):
                        emit("gpsimd", lambda g, i=i, nm=nm, dst=dst: g.dma_start(
                            out=T[dst][:, (i + 1) % 2, :, :], in_=d[nm][i + 1]), inc=(wl, 16))
                silu_s = [0] * E
                uadd_v = [0] * E
                mult_v = [0] * E
                ad_t = [0] * E
                if i >= 1:
                    pp = (i - 1) % 2
                    wait("scalar", "Tmm", base_done_t[i - 1])
                    if i >= 3:
                        wait("scalar", "Vev", uadd_last[i - 3])
                    elif carry_uadd:
                        wait("scalar", "Vev", carry_uadd)
                    emit("scalar", lambda s, pp=pp: s.copy(
                        T["ubb"][:, pp, :], P[1][:, :]), inc=("Sev", 1))
                    ube_s[i - 1] = mk.cnt["Sev"]
                    wait("vector", "Tmm", base_done_t[i - 1])
                    wait("vector", "Sev", ube_s[i - 1])
                    emit("vector", lambda v, i=i, pp=pp: v.tensor_tensor(
                        T["hbase"][:, i - 1, :], T["sgbb"][:, pp, :], T["ubb"][:, pp, :],
                        ALU.mult), inc=("Vev", 1))
                    mult_base_v[i - 1] = mk.cnt["Vev"]
                def bmmB(e, i=i, q=0, half=half):
                    bank, wc = (0, "wcolA") if q < 4 else (1, "wcolB")
                    qq = q % 4
                    for k in range(4 * qq, 4 * qq + 4):
                        last = e.matmul(P[bank][:, :], T[wc][:, i % 2, k, :],
                                        T["gate_h2"][:, k, half * TBLK:(half + 1) * TBLK],
                                        start=(k == 0), stop=(k == KC - 1),
                                        skip_group_check=True)
                    return last
                if i == 0:
                    wl_fc = "WlA"
                    wait("tensor", wl_fc, mk.cnt[wl_fc])
                    wait("tensor", "Vev", H2_V)
                    if carry_sev:
                        wait("tensor", "Sev", carry_sev)
                    def bmm(e, i=i, half=half):
                        for q in range(2 * 4):
                            last = bmmB(e, i=i, q=q, half=half)
                        return last
                    emit("tensor", bmm, inc=("Tmm", 1))
                    base_g_done_t[i] = base_done_t[i] = mk.cnt["Tmm"]
                    wait("scalar", "Tmm", base_done_t[0])
                    if carry_mbv:
                        wait("scalar", "Vev", carry_mbv)
                    emit("scalar", lambda s: s.activation(
                        T["sgbb"][:, 0, :], P[0][:, :], AF.Silu))
                    emit("scalar", lambda s: s.copy(
                        T["gbb"][:, 0, :], P[0][:, :]), inc=("Sev", 1))
                    gbe_s[0] = mk.cnt["Sev"]
                if i >= 1:
                    pp = (i - 1) % 2
                    wdp = "WdA" if pp == 0 else "WdB"
                    wait("tensor", wdp, mk.cnt[wdp])
                    wait("tensor", "Gth", xat_gth)
                    wait("tensor", "Vev", HAZ_V)
                    wait("tensor", "Sev", gbe_s[i - 1])
                    for e_i in range(E):
                        lgb = 2 + (e_i % 2)
                        lub = 4 + (e_i % 2)
                        if e_i >= 2:
                            wait("tensor", "Sev", silu_s[e_i - 2])
                            wait("tensor", "Vev", uadd_v[e_i - 2])
                        else:
                            wait("tensor", "Sev", prev_silu[e_i - 2 + E])
                            wait("tensor", "Vev", prev_uadd[e_i - 2 + E])
                        def lmm(e, i=i, e_i=e_i, lgb=lgb, lub=lub, pp=pp):
                            e.matmul(P[lgb][:, :], T["ident"][:, :], T["gbb"][:, pp, :],
                                     start=True, stop=False, skip_group_check=True)
                            e.matmul(P[lgb][:, :], T["bgp_sb"][:, pp, e_i, :],
                                     T["xat"][:, e_i, :],
                                     start=False, stop=True, skip_group_check=True)
                            last = e.matmul(P[lub][:, :], T["bup_sb"][:, pp, e_i, :],
                                            T["xat"][:, E + e_i, :],
                                            start=True, stop=True, skip_group_check=True)
                            return last
                        emit("tensor", lmm, inc=("Tmm", 1))
                        lora_t = mk.cnt["Tmm"]
                        if e_i >= 1:
                            wait("tensor", "Vev", mult_v[e_i - 1])
                            emit("tensor", lambda e, i=i, e_i=e_i: e.matmul(
                                P[6 + ((e_i - 1) % 2)][32 * ((e_i - 1) // 2):32 * ((e_i - 1) // 2) + 16, :],
                                T["adS_sb"][:, i - 1, e_i - 1, :], T["hexp"][:, (e_i - 1) % 2, :],
                                start=(i == 1), stop=True, skip_group_check=True,
                                tile_position=(0, 32 * ((e_i - 1) // 2))), inc=("Tmm", 1))
                            ad_t[e_i - 1] = mk.cnt["Tmm"]
                        if i < FC:
                            if e_i == 0:
                                wl_fc = "WlA" if i % 2 == 0 else "WlB"
                                wait("tensor", wl_fc, mk.cnt[wl_fc])
                                wait("tensor", "Sev", gbe_s[i - 1])
                            if e_i == 4:
                                wait("tensor", "Sev", ube_s[i - 1])
                            emit("tensor", lambda e, i=i, e_i=e_i, half=half: bmmB(
                                e, i=i, q=e_i, half=half), inc=("Tmm", 1))
                            if e_i == 3:
                                base_g_done_t[i] = mk.cnt["Tmm"]
                            if e_i == E - 1:
                                base_done_t[i] = mk.cnt["Tmm"]
                        wait("vector", "Tmm", lora_t)
                        if e_i >= 2:
                            wait("vector", "Vev", mult_v[e_i - 2])
                        emit("vector", lambda v, e_i=e_i, lub=lub, pp=pp: v.tensor_tensor(
                            T["usb"][:, e_i % 2, :], P[lub][:, :], T["ubb"][:, pp, :], ALU.add),
                            inc=("Vev", 1))
                        uadd_v[e_i] = mk.cnt["Vev"]
                        wait("scalar", "Tmm", lora_t)
                        if e_i >= 2:
                            wait("scalar", "Vev", mult_v[e_i - 2])
                        else:
                            wait("scalar", "Vev", prev_mult[e_i - 2 + E])
                        emit("scalar", lambda s, e_i=e_i, lgb=lgb: s.activation(
                            T["sgb"][:, e_i % 2, :], P[lgb][:, :], AF.Silu),
                            inc=("Sev", 1))
                        silu_s[e_i] = mk.cnt["Sev"]
                        if i < FC and e_i == 3:
                            wait("scalar", "Tmm", base_g_done_t[i])
                            if i >= 2:
                                wait("scalar", "Vev", mult_base_v[i - 2])
                            elif carry_mbv:
                                wait("scalar", "Vev", carry_mbv)
                            emit("scalar", lambda s, i=i: s.activation(
                                T["sgbb"][:, i % 2, :], P[0][:, :], AF.Silu))
                            emit("scalar", lambda s, i=i: s.copy(
                                T["gbb"][:, i % 2, :], P[0][:, :]), inc=("Sev", 1))
                            gbe_s[i] = mk.cnt["Sev"]
                        wait("vector", "Vev", uadd_v[e_i])
                        wait("vector", "Sev", silu_s[e_i])
                        if e_i >= 2:
                            wait("vector", "Tmm", ad_t[e_i - 2])
                        emit("vector", lambda v, e_i=e_i: v.tensor_tensor(
                            T["hexp"][:, e_i % 2, :], T["sgb"][:, e_i % 2, :],
                            T["usb"][:, e_i % 2, :], ALU.mult), inc=("Vev", 1))
                        mult_v[e_i] = mk.cnt["Vev"]
                    wait("tensor", "Vev", mult_v[E - 1])
                    emit("tensor", lambda e, i=i: e.matmul(
                        P[6 + ((E - 1) % 2)][32 * ((E - 1) // 2):32 * ((E - 1) // 2) + 16, :],
                        T["adS_sb"][:, i - 1, E - 1, :], T["hexp"][:, (E - 1) % 2, :],
                        start=(i == 1), stop=True, skip_group_check=True,
                        tile_position=(0, 32 * ((E - 1) // 2))), inc=("Tmm", 1))
                    ad_t[E - 1] = mk.cnt["Tmm"]
                    uadd_last[i - 1] = uadd_v[E - 1]
                    prev_silu = list(silu_s)
                    prev_uadd = list(uadd_v)
                    prev_mult = list(mult_v)
                chunk_t[i] = mk.cnt["Tmm"]
                if i + 1 < FC:
                    wait("gpsimd", "Tmm", chunk_t[i])
                    wlp = "WdA" if (i + 1) % 2 == 0 else "WdB"
                    for nm, dst in (("bgp", "bgp_sb"), ("bup", "bup_sb")):
                        emit("gpsimd", lambda g, i=i, nm=nm, dst=dst: g.dma_start(
                            out=T[dst][:, (i + 1) % 2, :, :], in_=d[nm][i + 1]), inc=(wlp, 16))
            carry_sev = ube_s[FC - 1]
            carry_mbv = mult_base_v[FC - 1]
            carry_uadd = uadd_last[FC - 1]
            AD_T = mk.cnt["Tmm"]
            wait("vector", "Tmm", AD_T)
            wait("vector", "Vev", WEXP_V)
            emit("vector", lambda v, half=half: v.tensor_tensor(
                T["hAwE"][:, :], P[6][:, :], T["wexpE"][:, half * TBLK:(half + 1) * TBLK], ALU.mult))
            emit("vector", lambda v, half=half: v.tensor_tensor(
                T["hAwO"][:, :], P[7][:, :], T["wexpO"][:, half * TBLK:(half + 1) * TBLK], ALU.mult))
            drain("vector")
            emit("vector", lambda v: v.nop(), inc=("Vev", 1))
            HAW_V = mk.cnt["Vev"]

            wait("gpsimd", "Tmm", AD_T)
            emit("gpsimd", lambda g: g.dma_start(
                out=T["wdcol"][:, 0, :, :], in_=d["Wd_T"][0]),
                inc=("WdA", 16))
            out_evac = [0, 0]
            dm_t = []
            for mo in range(KC):
                if mo + 1 < KC:
                    if mo >= 1:
                        wait("gpsimd", "Tmm", dm_t[mo - 1])
                    wd = "WdA" if (mo + 1) % 2 == 0 else "WdB"
                    emit("gpsimd", lambda g, mo=mo: g.dma_start(
                        out=T["wdcol"][:, (mo + 1) % 2, :, :],
                        in_=d["Wd_T"][mo + 1]), inc=(wd, 16))
                xl_mo = "XlA" if mo % 2 == 0 else "XlB"
                if mo >= 2:
                    wait("sync", "Vev", out_evac[mo % 2])
                else:
                    wait("sync", "Vev", H2_V)
                    if half == 1:
                        wait("sync", "Vev", out_evac[mo % 2])
                wait("sync", "Out", mk.cnt["Out"])
                emit("sync", lambda e, mo=mo, half=half: e.dma_start(
                    out=T["resid"][:, mo % 2, 0:TBLK],
                    in_=d["xp_park"][mo * 128:(mo + 1) * 128, half * TBLK:(half + 1) * TBLK]),
                    inc=(xl_mo, 16))
                wd_mo = "WdA" if mo % 2 == 0 else "WdB"
                wait("tensor", wd_mo, mk.cnt[wd_mo])
                wait("tensor", "Vev", HAW_V)
                if mo >= 2:
                    wait("tensor", "Vev", out_evac[mo % 2])
                def dmm(e, mo=mo):
                    for k in range(FC):
                        last = e.matmul(P[4 + (mo % 2)][:, :], T["wdcol"][:, mo % 2, k, :],
                                 T["hbase"][:, k, :], start=(k == 0), stop=False)
                    last = e.matmul(P[4 + (mo % 2)][:, :], T["bdS_sb"][:, 0, mo * 128:(mo + 1) * 128],
                             T["hAwE"][:, :], start=False, stop=False)
                    last = e.matmul(P[4 + (mo % 2)][:, :], T["bdS_sb"][:, 1, mo * 128:(mo + 1) * 128],
                             T["hAwO"][:, :], start=False, stop=True)
                    return last
                emit("tensor", dmm, inc=("Tmm", 1))
                dm_t.append(mk.cnt["Tmm"])
                wait("vector", "Tmm", mk.cnt["Tmm"])
                wait("vector", xl_mo, mk.cnt[xl_mo])
                emit("vector", lambda v, mo=mo: v.scalar_tensor_tensor(
                    T["resid"][:, mo % 2, 0:TBLK], P[4 + (mo % 2)][:, :],
                    T["modd"][:, 80 + mo:81 + mo], T["resid"][:, mo % 2, 0:TBLK],
                    ALU.mult, ALU.add))
                drain("vector")
                emit("vector", lambda v: v.nop(), inc=("Vev", 1))
                out_evac[mo % 2] = mk.cnt["Vev"]
                wait("sync", "Vev", out_evac[mo % 2])
                emit("sync", lambda e, mo=mo, half=half: e.dma_start(
                    out=d["y"][mo * 128:(mo + 1) * 128, half * TBLK:(half + 1) * TBLK],
                    in_=T["resid"][:, mo % 2, 0:TBLK]), inc=("Out", 16))
        wait("sync", "Out", mk.cnt["Out"])
        wait("gpsimd", "Out", mk.cnt["Out"])
        wait("scalar", "Out", mk.cnt["Out"])
        wait("vector", "Out", mk.cnt["Out"])
        wait("tensor", "Out", mk.cnt["Out"])

    handles = {}
    with nc.Block() as block:
        def runner(eng):
            def f(h):
                hs = {eng: h}
                mk = MK(eng, hs, sems)
                walk(mk)
            return f
        block.gpsimd(runner("gpsimd"))
        block.sync(runner("sync"))
        block.scalar(runner("scalar"))
        block.vector(runner("vector"))
        block.tensor(runner("tensor"))

    for cm in reversed(guards):
        cm.__exit__(None, None, None)
    return nc



def host_prep(inp):
    x = np.asarray(inp["L0_attn_hidden_states"], np.float32)
    c = np.asarray(inp["L0_c"], np.float32)
    wmod = np.asarray(inp["L0_wmod"], np.float32)
    wqkv = np.asarray(inp["L0_attn_w_qkv"], np.float32)
    slope = np.asarray(inp["L0_attn_slope_rate"], np.float32)
    wgate = np.asarray(inp["L0_attn_w_gate"], np.float32)
    wout = np.asarray(inp["L0_attn_w_out"], np.float32)
    normw = np.asarray(inp["L0_attn_norm_w"], np.float32)
    Wg = np.asarray(inp["L0_ffn_Wg"], np.float32)
    Wu = np.asarray(inp["L0_ffn_Wu"], np.float32)
    Wd = np.asarray(inp["L0_ffn_Wd"], np.float32)
    Wr = np.asarray(inp["L0_ffn_Wr"], np.float32)
    Ag = np.asarray(inp["L0_ffn_Ag"], np.float32)
    Bg = np.asarray(inp["L0_ffn_Bg"], np.float32)
    Au = np.asarray(inp["L0_ffn_Au"], np.float32)
    Bu = np.asarray(inp["L0_ffn_Bu"], np.float32)
    Ad = np.asarray(inp["L0_ffn_Ad"], np.float32)
    Bd = np.asarray(inp["L0_ffn_Bd"], np.float32)

    co = np.ascontiguousarray

    def mblocks(wT, nblk):
        K, N = wT.shape
        kc = K // 128
        return co(wT.reshape(kc, 128, nblk, 128).transpose(2, 1, 0, 3))

    wg_T = mblocks(wgate.T, KC)
    wo_T = mblocks(wout.T, KC)
    Wg_T = mblocks(Wg.T, FC)
    Wu_T = mblocks(Wu.T, FC)
    Wd_T = mblocks(Wd.T, KC)
    wr_T = co(Wr.T.reshape(KC, 128, E).transpose(1, 0, 2))
    nw_cols = co(normw.reshape(KC, 128).T)

    def pair_cols(A):
        AT = A.transpose(2, 0, 1)
        def mk(tt0):
            cols = []
            for tt in range(tt0, tt0 + 4):
                cols.append(AT[:, (tt - 1) % E, :])
                cols.append(AT[:, tt, :])
            m = np.concatenate(cols, axis=1)
            return co(m.reshape(KC, 128, 128).transpose(1, 0, 2))
        return mk(0), mk(4)
    agp1, agp2 = pair_cols(Ag)
    aup1, aup2 = pair_cols(Au)

    def pair_rows(Bm):
        BT = Bm.transpose(0, 2, 1)
        out = np.empty((FC, E, 32, 128), np.float32)
        for tt in range(E):
            stack = np.concatenate([BT[(tt - 1) % E], BT[tt]], axis=0)
            out[:, tt, :, :] = stack.reshape(32, FC, 128).transpose(1, 0, 2)
        return co(out.transpose(0, 2, 1, 3))
    bgp = pair_rows(Bg)
    bup = pair_rows(Bu)

    AdT = Ad.transpose(2, 0, 1).reshape(FC, 128, E, RK)
    adS = co(AdT.transpose(1, 0, 2, 3))
    bdS = np.zeros((128, 2, D), np.float32)
    for e_i in range(E):
        for i in range(RK):
            bdS[32 * (e_i // 2) + i, e_i % 2, :] = Bd[e_i, :, i]
    bdS = co(bdS)

    pm2c = co(np.where((np.arange(128) % 32) < 16, -2.0, 2.0).astype(np.float32).reshape(128, 1))
    in_maps = []
    for cid in range(8):
        b, q = cid // 4, cid % 4
        heads = list(range(4 * q, 4 * q + 4))
        rows = []
        for sec in range(3):
            for hh in heads:
                rows.extend(range(sec * D + hh * DH, sec * D + (hh + 1) * DH))
        wq = mblocks(co(wqkv[rows, :]).T, 12)
        wqkv_T = co(wq.transpose(1, 0, 2, 3))
        modidx = np.array([[2 * j + b] for j in range(8)], np.uint32)
        hidxm = np.full((128, 8), 10 ** 6, np.uint32)
        for m in (2 * q, 2 * q + 1):
            hidxm[:, m] = (m - 2 * q) * 128 + np.arange(128)
        pidx = np.arange(128, dtype=np.uint32)[:, None]
        kidx = np.arange(KC, dtype=np.uint32)[None, :]
        arow = 128 * kidx + pidx
        if q < 2:
            aidx = (2 * arow + q).astype(np.uint32)
            aidxB = np.full_like(aidx, 10 ** 7)
        else:
            aidx = np.full((128, KC), 10 ** 7, np.uint32)
            aidxB = (2 * arow + (q - 2)).astype(np.uint32)
        im = {
            "x_b": co(x[b]),
            "x_m": co(x[b, NT * q:NT * (q + 1), :]),
            "xT_m": co(x[b, NT * q:NT * (q + 1), :].T),
            "c2": co(c.T.reshape(KC, 128, B).transpose(1, 0, 2)),
            "wmod_sl": co(wmod[:, 1536 * cid:1536 * (cid + 1)].reshape(KC, 128, 1536).transpose(1, 0, 2)),
            "modidx": modidx,
            "wqkv_T": wqkv_T,
            "slopes": co(slope[heads].reshape(1, HPC)),
            "wg_T": wg_T, "wo_T": wo_T, "nw": nw_cols, "aidx": aidx, "aidxB": aidxB, "hidx": hidxm,
            "Wg_T": Wg_T, "Wu_T": Wu_T, "Wd_T": Wd_T, "wr_T": wr_T,
            "agp1": agp1, "agp2": agp2, "aup1": aup1, "aup2": aup2,
            "bgp": bgp, "bup": bup, "adS": adS, "bdS": bdS,
            "pm2": pm2c,
        }
        in_maps.append(im)
    return in_maps


_NC_CACHE = {}


def kernel(**inputs):
    in_maps = host_prep(inputs)
    if "main" not in _NC_CACHE:
        _NC_CACHE["main"] = build_program(taps=DEBUG_TAPS)
    nc = _NC_CACHE["main"]
    res = run_bass_kernel_spmd(nc, in_maps, core_ids=list(range(8)))
    out = np.empty((B, S, D), np.float32)
    for cid in range(8):
        b, q = cid // 4, cid % 4
        out[b, NT * q:NT * (q + 1), :] = res.results[cid]["y"].T
    return out
